# version 18
# speedup vs baseline: 1.1744x; 1.0091x over previous
import contextlib
import math
import numpy as np
import ml_dtypes
import concourse.bass as bass
import concourse.mybir as mybir
from concourse.bass_utils import run_bass_kernel_spmd

F32 = mybir.dt.float32
BF16 = mybir.dt.bfloat16
I32 = mybir.dt.int32
AF = mybir.ActivationFunctionType
ALU = mybir.AluOpType
AX = mybir.AxisListType.X

NCORES = 8
TOK = 4096
D = 1024
NT = TOK // 128
N_ST = 8
SEQ_ST = 4

_BLOCK_METH = {"t": "tensor", "v": "vector", "s": "scalar", "p": "gpsimd", "y": "sync"}
DRAIN_FREE_THRESH = 2048


class Plan:

    def __init__(self, nc):
        self.nc = nc
        self.engs = list("tvspy")
        self.ops = {e: [] for e in self.engs}
        self.dma_idx = {e: 0 for e in self.engs}
        self.last_w = {}
        self.readers = {}
        self.tok_free = {}
        self.last_drain = {e: 0 for e in self.engs}
        self.marked = set()

    def _drain(self, e):
        self.ops[e].append(dict(fn=lambda eng: eng.drain(), deps=set(), dma=False,
                                tag="drain"))
        self.last_drain[e] = len(self.ops[e])

    @staticmethod
    def _ck(b):
        return (b,) if isinstance(b, str) else tuple(b)

    def op(self, e, fn, reads=(), writes=(), sreads=(), dma=False, out_free=1 << 30,
           tag=""):
        reads = tuple(self._ck(b) for b in reads)
        sreads = tuple(self._ck(b) for b in sreads)
        writes = tuple(self._ck(b) for b in writes)
        deps = set()
        raw_deps = set()
        for b in tuple(reads) + tuple(sreads):
            t = self.last_w.get(b)
            if t is not None:
                deps.add(t)
                raw_deps.add(t)
        for b in tuple(writes):
            t = self.last_w.get(b)
            if t is not None:
                deps.add(t)
            for t2 in self.readers.get(b, ()):
                deps.add(t2)
        if e in "vsp":
            need = False
            for d in deps:
                if d[0] == e and d[1] >= self.last_drain[e]:
                    if (d in raw_deps) or (self.tok_free.get(d, 1 << 30) < DRAIN_FREE_THRESH):
                        need = True
            if need:
                self._drain(e)
        deps = {d for d in deps if d[0] != e}
        for d in deps:
            self.marked.add(d)
        idx = len(self.ops[e])
        rec = dict(fn=fn, deps=deps, dma=dma, tag=tag)
        self.ops[e].append(rec)
        if dma:
            i = self.dma_idx[e]
            token = (e + "D" + str(i % 8), i // 8)
            self.dma_idx[e] += 1
            rec["dtok"] = token
            self.marked.add(token)
        else:
            token = (e, idx)
        self.tok_free[token] = out_free
        for b in tuple(reads) + tuple(sreads):
            self.readers.setdefault(b, []).append(token)
        for b in tuple(writes):
            self.last_w[b] = token
            self.readers[b] = []
        return token

    def tail_token(self, e):
        for i in range(len(self.ops[e]) - 1, -1, -1):
            o = self.ops[e][i]
            if o["tag"] == "barrier":
                continue
            if o["dma"]:
                return o["dtok"]
            return (e, i)
        return None

    def dma_tail_tokens(self):
        toks = []
        for e in self.engs:
            n = self.dma_idx[e]
            for k in range(8):
                nk = (n + 7 - k) // 8
                if nk > 0:
                    toks.append((e + "D" + str(k), nk - 1))
        return toks

    def block_barrier(self):
        tails = [self.tail_token(e) for e in self.engs] + self.dma_tail_tokens()
        tails = [t for t in tails if t is not None]
        for e in self.engs:
            deps = {t for t in tails if t[0] != e}
            for d in deps:
                self.marked.add(d)
            self.ops[e].append(dict(fn=lambda eng: eng.nop(), deps=deps,
                                    dma=False, tag="barrier"))

    def finalize(self, out_tokens):
        deps = set(t for t in out_tokens if t is not None)
        deps.update(self.dma_tail_tokens())
        for e in self.engs:
            if e == "y":
                continue
            t = self.tail_token(e)
            if t is not None:
                deps.add(t)
        for d in deps:
            self.marked.add(d)
        self.ops["y"].append(dict(fn=lambda eng: eng.nop(), deps=deps,
                                  dma=False, tag="final"))

    def emit(self, ctx):
        nc = self.nc
        doms = {d[0] for d in self.marked}
        sems = {d: ctx.enter_context(nc.semaphore(f"sem_{d}")) for d in sorted(doms)}
        rank = {}
        for d in self.marked:
            rank.setdefault(d[0], {})[d[1]] = 0
        for dom in rank:
            for r, idx in enumerate(sorted(rank[dom].keys())):
                rank[dom][idx] = r + 1

        def wait_val(tok):
            return rank[tok[0]][tok[1]] * (16 if len(tok[0]) > 1 else 1)

        plan = self

        def make_runner(e):
            def run(eng):
                wm = {}
                for i, o in enumerate(plan.ops[e]):
                    for tok in sorted(o["deps"]):
                        dom = tok[0]
                        v = wait_val(tok)
                        if wm.get(dom, 0) < v:
                            eng.wait_ge(sems[dom], v)
                            wm[dom] = v
                    if o["dma"]:
                        dom, j = o["dtok"]
                        if j > 0 and wm.get(dom, 0) < 16 * j:
                            eng.wait_ge(sems[dom], 16 * j)
                            wm[dom] = 16 * j
                    inst = o["fn"](eng)
                    if o["dma"]:
                        inst.then_inc(sems[o["dtok"][0]], 16)
                    elif (e, i) in plan.marked:
                        inst.then_inc(sems[e], 1)
            return run

        with nc.Block() as block:
            for e in self.engs:
                getattr(block, _BLOCK_METH[e])(make_runner(e))


def emit_rsqrt(pl, out_ap, in_ap, t_ap, key_out, key_in, key_t, newton=2):
    MAGIC = 0x5F3759DF
    fr = 1
    pl.op("v", lambda E, o=out_ap, i=in_ap: E.tensor_scalar(
        out=o.bitcast(I32), in0=i.bitcast(I32), scalar1=1, scalar2=None,
        op0=ALU.arith_shift_right), reads=(key_in,), writes=(key_out,), out_free=fr)
    pl.op("v", lambda E, o=out_ap: E.tensor_scalar(
        out=o.bitcast(I32), in0=o.bitcast(I32), scalar1=-1, scalar2=None,
        op0=ALU.bitwise_xor), reads=(key_out,), writes=(key_out,), out_free=fr)
    pl.op("v", lambda E, o=out_ap: E.tensor_scalar(
        out=o.bitcast(I32), in0=o.bitcast(I32), scalar1=MAGIC + 1, scalar2=None,
        op0=ALU.add), reads=(key_out,), writes=(key_out,), out_free=fr)
    for _ in range(newton):
        pl.op("v", lambda E, o=out_ap, t=t_ap: E.tensor_tensor(
            out=t, in0=o, in1=o, op=ALU.mult), reads=(key_out,), writes=(key_t,),
            out_free=fr)
        pl.op("v", lambda E, t=t_ap, i=in_ap: E.tensor_tensor(
            out=t, in0=t, in1=i, op=ALU.mult), reads=(key_t, key_in), writes=(key_t,),
            out_free=fr)
        pl.op("v", lambda E, t=t_ap: E.tensor_scalar(
            out=t, in0=t, scalar1=-0.5, scalar2=1.5, op0=ALU.mult, op1=ALU.add),
            reads=(key_t,), writes=(key_t,), out_free=fr)
        pl.op("v", lambda E, o=out_ap, t=t_ap: E.tensor_tensor(
            out=o, in0=o, in1=t, op=ALU.mult), reads=(key_out, key_t),
            writes=(key_out,), out_free=fr)


def ln_finalize(pl, SUM, SQ, M, R, T, pref, sum_keys, sq_keys, eps=1e-5, ncols=4,
                inv_n=1.0 / 1024):
    kM, kR, kT = (pref, "M"), (pref, "R"), (pref, "T")
    sl = (slice(None), slice(0, ncols))
    pl.op("v", lambda E: E.tensor_scalar(out=M[sl], in0=SUM[sl], scalar1=inv_n,
          scalar2=None, op0=ALU.mult), reads=tuple(sum_keys), writes=(kM,),
          out_free=ncols)
    pl.op("v", lambda E: E.tensor_scalar(out=T[sl], in0=SQ[sl], scalar1=inv_n,
          scalar2=None, op0=ALU.mult), reads=tuple(sq_keys), writes=(kT,),
          out_free=ncols)
    pl.op("v", lambda E: E.tensor_tensor(out=R[sl], in0=M[sl], in1=M[sl],
          op=ALU.mult), reads=(kM,), writes=(kR,), out_free=ncols)
    pl.op("v", lambda E: E.tensor_tensor(out=T[sl], in0=T[sl], in1=R[sl],
          op=ALU.subtract), reads=(kR, kT), writes=(kT,), out_free=ncols)
    pl.op("v", lambda E: E.tensor_scalar(out=T[sl], in0=T[sl], scalar1=eps,
          scalar2=None, op0=ALU.add), reads=(kT,), writes=(kT,), out_free=ncols)
    emit_rsqrt(pl, R[sl], T[sl], SUM[sl], kR, kT, (pref, "scr"))
    return kM, kR


def slopes16():
    start = 2.0 ** (-(2.0 ** (-(math.log2(16) - 3))))
    s = np.array([start ** (i + 1) for i in range(16)], dtype=np.float64)
    return s * (1.0 + 1e-5)


def host_consts():
    s = slopes16()
    i = np.arange(128.0)
    diff = i[None, :] - i[:, None]
    maskT = np.where(diff >= 0, np.exp(-s[:, None, None] * diff[None]), 0.0)
    maskT_t = maskT.transpose(1, 0, 2).reshape(128, 16 * 128)
    lamq = np.zeros((128, 1024))
    for ofc in range(8):
        for half in range(2):
            h = 2 * ofc + half
            lamq[half * 64:(half + 1) * 64, ofc * 128:(ofc + 1) * 128] = \
                np.exp(-s[h] * (i[None, :] + 1.0))
    lamk = np.zeros((128, 1024))
    for h in range(16):
        lamk[:, h * 64:(h + 1) * 64] = np.exp(-s[h] * (127.0 - i))[:, None]
    lamc = np.zeros((128, 512))
    for p2 in range(2):
        for f8 in range(8):
            h = p2 + 2 * f8
            lamc[p2 * 64:(p2 + 1) * 64, f8 * 64:(f8 + 1) * 64] = np.exp(-128.0 * s[h])
    sel8 = np.zeros((8, 1024))
    for e in range(8):
        sel8[e, e * 128:(e + 1) * 128] = 1.0
    bf = ml_dtypes.bfloat16
    return {
        "c_idn": np.eye(128, dtype=np.float32).astype(bf),
        "c_maskT": maskT_t.astype(np.float32).astype(bf),
        "c_lamq": lamq.astype(np.float32).astype(bf),
        "c_lamk": lamk.astype(np.float32).astype(bf),
        "c_lamc": lamc.astype(np.float32),
        "c_sel8": sel8.astype(np.float32).astype(bf),
    }



def emit_l0_attn(pl, nc, ctx, x_in, x_out, kx_in, kx_out, W, C):
    import os
    SKIP = set(os.environ.get("L0A_SKIP", "").split(","))
    E_ = ctx.enter_context
    sb = lambda name, shape, dt: E_(nc.sbuf_tensor(name, shape, dt)).ap()
    ps = lambda name, shape, dt: E_(nc.psum_tensor(name, shape, dt)).ap()

    WQ = sb("wq", [128, 8 * 1024], BF16)
    WK = sb("wk", [128, 8 * 1024], BF16)
    WV = sb("wv", [128, 8 * 1024], BF16)
    WO = sb("wo", [128, 8 * 1024], BF16)
    WG1 = sb("wg1", [128, 8 * 64], BF16)
    WG2 = sb("wg2", [64, 1024], BF16)
    GREP = sb("gamma_rep", [128, 1024], F32)
    BETA = sb("beta_rep", [128, 1024], F32)
    MASKT = sb("maskt", [128, 16 * 128], BF16)
    LAMQ = sb("lamq", [128, 1024], BF16)
    LAMK = sb("lamk", [128, 1024], BF16)
    LAMC = sb("lamc", [128, 512], F32)

    def wld(dst, src_name, kname):
        src = W[src_name]
        if src.shape[0] > 128:
            kc = src.shape[0] // 128
            view = src.rearrange("(kc p) o -> p kc o", p=128)
            d = dst.rearrange("p (kc o) -> p kc o", kc=kc)
        else:
            view, d = src, dst
        pl.op("p", lambda E, d=d, s=view: E.dma_start(out=d, in_=s), dma=True,
              writes=(kname,), tag=f"ld_{kname}")

    wld(WQ, "L0_attn_Wq", "WQ")
    wld(WK, "L0_attn_Wk", "WK")
    wld(WV, "L0_attn_Wv", "WV")
    wld(WO, "L0_attn_Wo", "WO")
    wld(WG1, "L0_attn_Wg1", "WG1")
    wld(WG2, "L0_attn_Wg2", "WG2")
    pl.op("y", lambda E: E.dma_start(
        out=GREP, in_=W["L0_attn_gamma"].unsqueeze(0).broadcast_to([128, 1024])),
        dma=True, writes=("GREP",))
    pl.op("y", lambda E: E.dma_start(
        out=BETA, in_=W["L0_attn_beta"].unsqueeze(0).broadcast_to([128, 1024])),
        dma=True, writes=("BETA",))
    pl.op("y", lambda E: E.dma_start(out=MASKT, in_=W["c_maskT"]), dma=True,
          writes=("MASKT",))
    pl.op("y", lambda E: E.dma_start(out=LAMQ, in_=W["c_lamq"]), dma=True,
          writes=("LAMQ",))
    pl.op("y", lambda E: E.dma_start(out=LAMK, in_=W["c_lamk"]), dma=True,
          writes=("LAMK",))
    pl.op("y", lambda E: E.dma_start(out=LAMC, in_=W["c_lamc"]), dma=True,
          writes=("LAMC",))
    for kc in range(8):
        pl.op("v", lambda E, kc=kc: E.tensor_scalar(
            out=WO[:, kc * 1024:(kc + 1) * 1024], in0=WO[:, kc * 1024:(kc + 1) * 1024],
            scalar1=0.5, scalar2=None, op0=ALU.mult),
            reads=("WO",), writes=("WO",), out_free=1024)

    XT = [sb(f"xt{i}", [128, 1024], F32) for i in range(4)]
    XTB = [sb(f"xtb{i}", [128, 1024], F32) for i in range(2)]
    XHB = sb("xhb", [128, 1024], BF16)
    XHB2 = sb("xhb2", [128, 1024], BF16)
    NMRS = sb("nmrs", [128, 4], F32)
    XHF = sb("xhf", [128, 8 * 512], BF16)
    QFM = sb("qfm", [128, 8 * 512], BF16)
    KFM = sb("kfm", [128, 8 * 512], BF16)
    QLAM = sb("qlam", [128, 8 * 512], BF16)
    KLAM = sb("klam", [128, 4 * 1024], BF16)
    VTM = sb("vtm", [128, 4 * 1024], BF16)
    SCBF = [sb(f"scbf{i}", [128, 512], BF16) for i in range(2)]
    OTM = sb("otm", [128, 4 * 1024], BF16)
    OB = sb("ob", [128, 4 * 1024], F32)
    ZBF = sb("zbf", [128, 1024], BF16)
    ZFM = sb("zfm", [128, 1024], BF16)
    G1BF = sb("g1bf", [64, 512], BF16)
    GT = [sb(f"gt{i}", [128, 1024], BF16) for i in range(4)]
    KV32 = sb("kv32", [128, 512], F32)
    KVBF = sb("kvbf", [128, 512], BF16)
    JV = sb("jv", [128, 1024], BF16)
    JS = sb("js", [128, 1024], BF16)
    S_SUM = sb("s_sum", [128, 4], F32)
    S_SQ = sb("s_sq", [128, 4], F32)
    S_M = sb("s_m", [128, 4], F32)
    S_R = sb("s_r", [128, 4], F32)
    S_T = sb("s_t", [128, 4], F32)
    I_SUM = sb("i_sum", [128, 4], F32)
    I_SQ = sb("i_sq", [128, 4], F32)
    I_M = sb("i_m", [128, 4], F32)
    I_R = sb("i_r", [128, 4], F32)
    B_SUM = sb("b_sum", [128, 4], F32)
    B_SQ = sb("b_sq", [128, 4], F32)
    B_M = sb("b_m", [128, 4], F32)
    B_R = sb("b_r", [128, 4], F32)

    PA = ps("pa", [128, 1024], F32)
    PB = ps("pb", [128, 1024], F32)
    PC = ps("pc", [128, 512], F32)
    PC2 = ps("pc2", [128, 512], F32)
    PD = ps("pd", [128, 512], F32)
    PT = ps("pt", [128, 1024], BF16)

    out_toks = []
    XHF_R = tuple(("XHF", u) for u in range(4))

    for st in range(N_ST):
        if st % SEQ_ST == 0:
            pl.op("v", lambda E: E.memset(KV32, 0.0), writes=(("KV32",),),
                  out_free=512)
            pl.op("v", lambda E: E.memset(KVBF, 0.0), writes=(("KVBF",),),
                  out_free=512)

        for u in range(4):
            tl = st * 4 + u
            kx = ("XT", u)
            pl.op("y", lambda E, tl=tl, u=u: E.dma_start(
                out=XT[u], in_=x_in[tl * 128:(tl + 1) * 128, :]), dma=True,
                reads=((kx_in, tl),), writes=(kx,), tag=f"ldx{tl}")
            pl.op("s", lambda E, u=u: E.activation(
                out=JV, in_=XT[u], func=AF.Square, accum_out=S_SQ[:, u:u + 1]),
                reads=(kx,), writes=(("JV",), ("S_SQ", u)), out_free=1024)
            pl.op("s", lambda E, u=u: E.activation(
                out=JS, in_=XT[u], func=AF.Copy, accum_out=S_SUM[:, u:u + 1]),
                reads=(kx,), writes=(("JS",), ("S_SUM", u)), out_free=1024)
        ln_finalize(pl, S_SUM, S_SQ, S_M, S_R, S_T, "Sst",
                    [("S_SUM", u) for u in range(4)], [("S_SQ", u) for u in range(4)])

        if 'lnT' not in SKIP:
            pl.op("v", lambda E: E.scalar_tensor_tensor(
                out=NMRS, in0=S_M, scalar=-1.0, in1=S_R, op0=ALU.mult,
                op1=ALU.mult), reads=(("Sst", "M"), ("Sst", "R")),
                writes=(("NMRS",),), out_free=4)
        for u in range(4) if 'lnT' not in SKIP else []:
            kx = ("XT", u)
            xhb = XHB if u % 2 == 0 else XHB2
            kxhb = ("XHB", u % 2)
            pl.op("s", lambda E, u=u, xhb=xhb: E.activation(
                out=xhb, in_=XT[u], func=AF.Identity, scale=S_R[:, u:u + 1],
                bias=NMRS[:, u:u + 1]), reads=(kx,),
                sreads=(("Sst", "R"), ("NMRS",)), writes=(kxhb,), out_free=1024)
            for fc in range(8):
                pl.op("t", lambda E, fc=fc, xhb=xhb, kxhb=kxhb: E.transpose(
                    PT[:, fc * 128:(fc + 1) * 128], xhb[:, fc * 128:(fc + 1) * 128],
                    C["IDN"]), reads=(kxhb, "IDN"), writes=(("PT",),),
                    out_free=128)
            pl.op("v", lambda E, u=u: E.tensor_copy(
                out=XHF.rearrange("p (f c) -> p f c", f=8)[:, :, u * 128:(u + 1) * 128],
                in_=PT.rearrange("p (f c) -> p f c", f=8)),
                reads=(("PT",),), writes=(("XHF", u),), out_free=1024)

        for ofc in range(8) if 'proj' not in SKIP else []:
            for kc in range(8):
                pl.op("t", lambda E, ofc=ofc, kc=kc: E.matmul(
                    PA[:, 0:512],
                    WQ[:, kc * 1024 + ofc * 128: kc * 1024 + (ofc + 1) * 128],
                    XHF[:, kc * 512:(kc + 1) * 512], start=(kc == 0), stop=(kc == 7)),
                    reads=(("WQ",),) + XHF_R, writes=(("PA", 0),), out_free=512)
            pl.op("s", lambda E, ofc=ofc: E.activation(
                out=QFM[:, ofc * 512:(ofc + 1) * 512], in_=PA[:, 0:512], func=AF.Silu),
                reads=(("PA", 0),), writes=(("QFM", ofc),), out_free=512)
            for kc in range(8):
                pl.op("t", lambda E, ofc=ofc, kc=kc: E.matmul(
                    PA[:, 512:1024],
                    WK[:, kc * 1024 + ofc * 128: kc * 1024 + (ofc + 1) * 128],
                    XHF[:, kc * 512:(kc + 1) * 512], start=(kc == 0), stop=(kc == 7)),
                    reads=(("WK",),) + XHF_R, writes=(("PA", 1),), out_free=512)
            pl.op("s", lambda E, ofc=ofc: E.activation(
                out=KFM[:, ofc * 512:(ofc + 1) * 512], in_=PA[:, 512:1024],
                func=AF.Silu),
                reads=(("PA", 1),), writes=(("KFM", ofc),), out_free=512)
        if 'proj' not in SKIP: pl.op("v", lambda E: E.tensor_tensor(
            out=QLAM.rearrange("p (o c i) -> p o c i", o=8, c=4),
            in0=QFM.rearrange("p (o c i) -> p o c i", o=8, c=4),
            in1=LAMQ.rearrange("p (o i) -> p o i", o=8).unsqueeze(2).broadcast_to(
                [128, 8, 4, 128]),
            op=ALU.mult),
            reads=tuple(("QFM", ofc) for ofc in range(8)) + (("LAMQ",),),
            writes=(("QLAM",),), out_free=4096)
        for u in range(4) if 'proj' not in SKIP else []:
            for half in range(2):
                for kc in range(8):
                    pl.op("t", lambda E, u=u, kc=kc, half=half: E.matmul(
                        PB[:, half * 512:(half + 1) * 512],
                        XHF[:, kc * 512 + u * 128: kc * 512 + (u + 1) * 128],
                        WV[:, kc * 1024 + half * 512: kc * 1024 + (half + 1) * 512],
                        start=(kc == 0), stop=(kc == 7)),
                        reads=(("WV",), ("XHF", u)), writes=(("PB",),), out_free=512)
            pl.op("s", lambda E, u=u: E.activation(
                out=VTM[:, u * 1024:(u + 1) * 1024], in_=PB, func=AF.Silu),
                reads=(("PB",),), writes=(("VTM", u),), out_free=1024)
            for fc in range(8):
                pl.op("t", lambda E, u=u, fc=fc: E.transpose(
                    PT[:, fc * 128:(fc + 1) * 128],
                    KFM[:, fc * 512 + u * 128: fc * 512 + (u + 1) * 128], C["IDN"]),
                    reads=(("KFM", fc), "IDN"), writes=(("PT",),), out_free=128)
            pl.op("v", lambda E, u=u: E.tensor_tensor(
                out=KLAM[:, u * 1024:(u + 1) * 1024], in0=PT, in1=LAMK, op=ALU.mult),
                reads=(("PT",), ("LAMK",)), writes=(("KLAM", u),), out_free=1024)
        for kc in range(8) if 'proj' not in SKIP else []:
            pl.op("t", lambda E, kc=kc: E.matmul(
                PC[0:64, :], WG1[:, kc * 64:(kc + 1) * 64],
                XHF[:, kc * 512:(kc + 1) * 512], start=(kc == 0), stop=(kc == 7)),
                reads=(("WG1",),) + XHF_R, writes=(("PC",),), out_free=512)
        if 'proj' not in SKIP: pl.op("v", lambda E: E.tensor_copy(out=G1BF, in_=PC[0:64, :]),
              reads=(("PC",),), writes=(("G1BF",),), out_free=512)
        for u in range(4):
            for half in range(2):
                pl.op("t", lambda E, u=u, half=half: E.matmul(
                    PB[:, half * 512:(half + 1) * 512], G1BF[:, u * 128:(u + 1) * 128],
                    WG2[:, half * 512:(half + 1) * 512], start=True, stop=True),
                    reads=(("G1BF",), ("WG2",)), writes=(("PB",),), out_free=512)
            pl.op("s", lambda E, u=u: E.activation(
                out=GT[u], in_=PB, func=AF.Tanh, scale=0.5),
                reads=(("PB",),), writes=(("GT", u),), out_free=1024)

        for u in range(4) if 'scan' not in SKIP else []:
            for hg in range(4):
                sc = SCBF[hg % 2]
                ksc = ("SCBF", hg % 2)
                pc = PC if hg % 2 == 0 else PC2
                kpc = ("PC", hg % 2)
                for hh in range(4):
                    h = hg * 4 + hh
                    tp = (0, (h % 2) * 64)
                    pl.op("t", lambda E, h=h, u=u, tp=tp: E.matmul(
                        PD[(h % 2) * 64:(h % 2) * 64 + 64,
                           (h // 2) * 64:(h // 2) * 64 + 64],
                        KLAM[:, u * 1024 + h * 64: u * 1024 + (h + 1) * 64],
                        VTM[:, u * 1024 + h * 64: u * 1024 + (h + 1) * 64],
                        start=True, stop=True, tile_position=tp),
                        reads=(("KLAM", u), ("VTM", u)), writes=(("PD", h),),
                        out_free=64)
                    qs = QFM[(h % 2) * 64:(h % 2) * 64 + 64,
                             (h // 2) * 512 + u * 128:(h // 2) * 512 + (u + 1) * 128]
                    ks = KFM[(h % 2) * 64:(h % 2) * 64 + 64,
                             (h // 2) * 512 + u * 128:(h // 2) * 512 + (u + 1) * 128]
                    pl.op("t", lambda E, hh=hh, qs=qs, ks=ks, pc=pc: E.matmul(
                        pc[:, hh * 128:(hh + 1) * 128], ks, qs, start=True, stop=True),
                        reads=(("KFM", h // 2), ("QFM", h // 2)), writes=(kpc,),
                        out_free=128)
                pl.op("v", lambda E, hg=hg, sc=sc, pc=pc: E.tensor_tensor(
                    out=sc, in0=pc, in1=MASKT[:, hg * 512:(hg + 1) * 512],
                    op=ALU.mult),
                    reads=(kpc, ("MASKT",)), writes=(ksc,), out_free=512)
                for hh in range(4):
                    h = hg * 4 + hh
                    pl.op("t", lambda E, h=h, u=u, sc=sc, hh=hh: E.matmul(
                        PA[:, h * 64:(h + 1) * 64], sc[:, hh * 128:(hh + 1) * 128],
                        VTM[:, u * 1024 + h * 64: u * 1024 + (h + 1) * 64],
                        start=True, stop=False),
                        reads=(ksc, ("VTM", u)), writes=(("PA", h // 8),),
                        out_free=64)
                    qlam_s = QLAM[(h % 2) * 64:(h % 2) * 64 + 64,
                                  (h // 2) * 512 + u * 128:(h // 2) * 512 + (u + 1) * 128]
                    kv_s = KVBF[(h % 2) * 64:(h % 2) * 64 + 64,
                                (h // 2) * 64:(h // 2) * 64 + 64]
                    pl.op("t", lambda E, qlam_s=qlam_s, kv_s=kv_s, h=h: E.matmul(
                        PA[:, h * 64:(h + 1) * 64], qlam_s, kv_s, start=False,
                        stop=True),
                        reads=(("QLAM",), ("KVBF",)), writes=(("PA", h // 8),),
                        out_free=64)
            pl.op("v", lambda E: E.tensor_tensor(out=KV32, in0=KV32, in1=LAMC,
                  op=ALU.mult), reads=(("KV32",), ("LAMC",)), writes=(("KV32",),),
                  out_free=512)
            pl.op("v", lambda E: E.tensor_tensor(out=KV32, in0=KV32, in1=PD,
                  op=ALU.add),
                  reads=(("KV32",),) + tuple(("PD", h) for h in range(16)),
                  writes=(("KV32",),), out_free=512)
            pl.op("v", lambda E: E.tensor_copy(out=KVBF, in_=KV32),
                  reads=(("KV32",),), writes=(("KVBF",),), out_free=512)
            pl.op("s", lambda E, u=u: E.activation(
                out=OTM[:, u * 1024:(u + 1) * 1024], in_=PA, func=AF.Copy,
                accum_out=I_SUM[:, u:u + 1]),
                reads=(("PA", 0), ("PA", 1)), writes=(("OTM", u), ("I_SUM", u)),
                out_free=1024)
            pl.op("v", lambda E, u=u: E.scalar_tensor_tensor(
                out=JV, in0=OTM[:, u * 1024:(u + 1) * 1024], scalar=1.0,
                in1=OTM[:, u * 1024:(u + 1) * 1024], op0=ALU.mult, op1=ALU.mult,
                accum_out=I_SQ[:, u:u + 1]), reads=(("OTM", u),),
                writes=(("JV",), ("I_SQ", u)), out_free=1024)
        ln_finalize(pl, I_SUM, I_SQ, I_M, I_R, S_T, "Ist",
                    [("I_SUM", u) for u in range(4)], [("I_SQ", u) for u in range(4)])

        for u in range(4) if 'gz' not in SKIP else []:
            tl = st * 4 + u
            gt = GT[u]
            kgt = ("GT", u)
            pl.op("v", lambda E, u=u: E.tensor_scalar(
                out=OB[:, u * 1024:(u + 1) * 1024],
                in0=OTM[:, u * 1024:(u + 1) * 1024],
                scalar1=I_M[:, u:u + 1], scalar2=I_R[:, u:u + 1],
                op0=ALU.subtract, op1=ALU.mult), reads=(("OTM", u),),
                sreads=(("Ist", "M"), ("Ist", "R")), writes=(("OB", u),),
                out_free=1024)
            pl.op("v", lambda E, u=u: E.tensor_tensor(
                out=OB[:, u * 1024:(u + 1) * 1024],
                in0=OB[:, u * 1024:(u + 1) * 1024], in1=GREP, op=ALU.mult),
                reads=(("OB", u), ("GREP",)), writes=(("OB", u),), out_free=1024)
            pl.op("v", lambda E, u=u: E.tensor_tensor(
                out=OB[:, u * 1024:(u + 1) * 1024],
                in0=OB[:, u * 1024:(u + 1) * 1024], in1=BETA, op=ALU.add),
                reads=(("OB", u), ("BETA",)), writes=(("OB", u),), out_free=1024)
            pl.op("v", lambda E, u=u, gt=gt: E.scalar_tensor_tensor(
                out=ZBF, in0=gt, scalar=1.0, in1=OB[:, u * 1024:(u + 1) * 1024],
                op0=ALU.add, op1=ALU.mult), reads=(kgt, ("OB", u)),
                writes=(("ZBF",),), out_free=1024)
            for fc in range(8):
                pl.op("t", lambda E, fc=fc: E.transpose(
                    PT[:, fc * 128:(fc + 1) * 128], ZBF[:, fc * 128:(fc + 1) * 128],
                    C["IDN"]), reads=(("ZBF",), "IDN"), writes=(("PT",),),
                    out_free=128)
            pl.op("v", lambda E: E.tensor_copy(out=ZFM, in_=PT),
                  reads=(("PT",),), writes=(("ZFM",),), out_free=1024)
            for half in range(2):
                for fc in range(8):
                    pl.op("t", lambda E, fc=fc, half=half: E.matmul(
                        PB[:, half * 512:(half + 1) * 512],
                        ZFM[:, fc * 128:(fc + 1) * 128],
                        WO[:, fc * 1024 + half * 512: fc * 1024 + (half + 1) * 512],
                        start=(fc == 0), stop=(fc == 7)),
                        reads=(("ZFM",), ("WO",)), writes=(("PB",),), out_free=512)
            pl.op("s", lambda E, u=u: E.activation(
                out=OB[:, u * 1024:(u + 1) * 1024], in_=PB, func=AF.Copy,
                accum_out=B_SUM[:, u:u + 1]), reads=(("PB",),),
                writes=(("OB", u), ("B_SUM", u)), out_free=1024)
            pl.op("s", lambda E, u=u: E.activation(
                out=JV, in_=OB[:, u * 1024:(u + 1) * 1024], func=AF.Square,
                accum_out=B_SQ[:, u:u + 1]), reads=(("OB", u),),
                writes=(("JV",), ("B_SQ", u)), out_free=1024)
        ln_finalize(pl, B_SUM, B_SQ, B_M, B_R, S_T, "Bst",
                    [("B_SUM", u) for u in range(4)], [("B_SQ", u) for u in range(4)])
        for u in range(4):
            tl = st * 4 + u
            xb = XTB[u % 2]
            kx = ("XTB", u % 2)
            pl.op("s", lambda E, tl=tl, xb=xb: E.dma_start(
                out=xb, in_=x_in[tl * 128:(tl + 1) * 128, :]), dma=True,
                reads=((kx_in, tl),), writes=(kx,))
            pl.op("v", lambda E, u=u: E.tensor_scalar(
                out=OB[:, u * 1024:(u + 1) * 1024],
                in0=OB[:, u * 1024:(u + 1) * 1024],
                scalar1=B_M[:, u:u + 1], scalar2=B_R[:, u:u + 1],
                op0=ALU.subtract, op1=ALU.mult), reads=(("OB", u),),
                sreads=(("Bst", "M"), ("Bst", "R")), writes=(("OB", u),),
                out_free=1024)
            pl.op("p", lambda E, u=u, xb=xb: E.tensor_tensor(
                out=xb, in0=OB[:, u * 1024:(u + 1) * 1024], in1=xb, op=ALU.add),
                reads=(("OB", u), kx), writes=(kx,), out_free=1024)
            tok = pl.op("y", lambda E, tl=tl, xb=xb: E.dma_start(
                out=x_out[tl * 128:(tl + 1) * 128, :], in_=xb), dma=True,
                reads=(kx,), writes=((kx_out, tl),), tag="stx")
            out_toks.append(tok)
    return out_toks



def emit_l0_ffn(pl, nc, ctx, x_in, x_out, kx_in, kx_out, W, C):
    E_ = ctx.enter_context
    sb = lambda name, shape, dt: E_(nc.sbuf_tensor(name, shape, dt)).ap()
    ps = lambda name, shape, dt: E_(nc.psum_tensor(name, shape, dt)).ap()
    ntok = N_ST * 512

    xhat = nc.dram_tensor("l0f_xhat", [N_ST * 128, 4096], BF16, kind="Internal").ap()
    accA = nc.dram_tensor("l0f_accA", [ntok, D], F32, kind="Internal").ap()
    accB = nc.dram_tensor("l0f_accB", [ntok, D], F32, kind="Internal").ap()
    cc_in = nc.dram_tensor("l0f_ccin", [128, 16], F32, kind="Internal").ap()
    cc_out = nc.dram_tensor("l0f_ccout", [128, 16], F32, kind="Internal").ap()

    GWT = sb("gwt", [128, 64], BF16)
    GW32 = sb("gw32", [128, 64], F32)
    GWLO = sb("gwlo", [128, 64], BF16)
    XH32 = sb("fxh32", [128, 1024], F32)
    XLOB = sb("fxlob", [128, 1024], BF16)
    XLOFS = sb("fxlofs", [128, 4096], BF16)
    LBB = sb("lbb", [128, 8], F32)
    SEL_ALL = sb("sel_all", [128, NT * 8], BF16)
    ONES1 = sb("ones1", [1, 128], BF16)
    ONESC = sb("onesc", [128, 1], BF16)
    WMR = sb("wmr", [128, 8], F32)
    WMS = sb("wms", [1, 16], F32)
    NAT2 = [sb(f"natw{i}", [128, 4096], BF16) for i in range(2)]
    XT = [sb(f"fxt{i}", [128, 1024], F32) for i in range(4)]
    XHB = sb("fxhb", [128, 1024], BF16)
    XHFS = sb("fxhfs", [128, 4096], BF16)
    GSB = sb("fgsb", [128, 2048], BF16)
    USB = sb("fusb", [128, 2048], BF16)
    H2ALL = sb("fh2", [128, 16 * 512], BF16)
    CB = [sb(f"fcb{i}", [128, 512], F32) for i in range(4)]
    CF1 = sb("fcf1", [1, 512], BF16)
    CF8 = sb("fcf8", [8, 512], BF16)
    SEL8 = sb("fsel8", [8, 1024], BF16)
    COEFT = sb("fcoeft", [128, 32], BF16)
    ACC_SB = sb("facc", [128, 1024], F32)
    ACCR = [sb(f"faccr{i}", [128, 1024], F32) for i in range(2)]
    LG = sb("flg", [128, 8], F32)
    EQ1 = sb("feq1", [128, 8], F32)
    EQ2 = sb("feq2", [128, 8], F32)
    LGM = sb("flgm", [128, 8], F32)
    M12 = sb("fm12", [128, 2], F32)
    TH = sb("fth", [128, 2], F32)
    S12 = sb("fs12", [128, 2], F32)
    DEN = sb("fden", [128, 1], F32)
    WEB = sb("fweb", [128, 8], BF16)
    ST2 = sb("fst2", [128, 16], BF16)
    WEF = sb("fwef", [128, 8], F32)
    JV = sb("fjv", [128, 1024], BF16)
    JS = sb("fjs", [128, 1024], BF16)
    STR = sb("fstr", [1, 16], F32)
    CCZ = sb("fccz", [128, 16], F32)
    S_SUM = sb("fs_sum", [128, 4], F32)
    S_SQ = sb("fs_sq", [128, 4], F32)
    S_M = sb("fs_m", [128, 4], F32)
    S_R = sb("fs_r", [128, 4], F32)
    S_T = sb("fs_t", [128, 4], F32)
    B_SUM = sb("fb_sum", [128, 4], F32)
    B_SQ = sb("fb_sq", [128, 4], F32)
    B_M = sb("fb_m", [128, 4], F32)
    B_R = sb("fb_r", [128, 4], F32)

    PGa = ps("fpga", [128, 1024], F32)
    PGb = ps("fpgb", [128, 1024], F32)
    PACC = ps("fpacc", [128, 1024], F32)
    PT = ps("fpt", [128, 1024], BF16)
    PST = ps("fpst", [1, 16], F32)
    PTH = [PT[:, 0:512], PACC.bitcast(BF16)[:, 0:512]]

    out_toks = []

    pl.op("v", lambda E: E.memset(ONES1, 1.0), writes=("ONES1",), out_free=128)
    pl.op("v", lambda E: E.memset(ONESC, 1.0), writes=("ONESC",), out_free=1)
    pl.op("y", lambda E: E.dma_start(out=SEL8, in_=W["c_sel8"]), dma=True,
          writes=(("SEL8",),))
    pl.op("y", lambda E: E.dma_start(out=GW32, in_=W["d_gwT"]), dma=True,
          writes=("GW32",))
    pl.op("v", lambda E: E.tensor_copy(out=GWT, in_=GW32),
          reads=("GW32",), writes=("GWT",), out_free=64)
    pl.op("v", lambda E: E.scalar_tensor_tensor(out=GWLO, in0=GWT, scalar=-1.0,
          in1=GW32, op0=ALU.mult, op1=ALU.add), reads=("GWT", "GW32"),
          writes=("GWLO",), out_free=64)
    pl.op("v", lambda E: E.memset(CCZ, 0.0), writes=("CCZ",), out_free=16)
    pl.op("y", lambda E: E.dma_start(out=cc_in, in_=CCZ), dma=True,
          reads=("CCZ",), writes=(("cc_in",),))
    pl.op("y", lambda E: E.dma_start(
        out=LBB, in_=W["L0_ffn_lb_bias"].unsqueeze(0).broadcast_to([128, 8])),
        dma=True, writes=("LBB",))

    for st in range(N_ST):
        for u in range(4):
            tl = st * 4 + u
            kx = ("fXT", u)
            pl.op("y", lambda E, tl=tl, u=u: E.dma_start(
                out=XT[u], in_=x_in[tl * 128:(tl + 1) * 128, :]), dma=True,
                reads=((kx_in, tl),), writes=(kx,))
            pl.op("s", lambda E, u=u: E.activation(
                out=JV, in_=XT[u], func=AF.Square, accum_out=S_SQ[:, u:u + 1]),
                reads=(kx,), writes=(("fJV",), ("fS_SQ", u)), out_free=1024)
            pl.op("s", lambda E, u=u: E.activation(
                out=JS, in_=XT[u], func=AF.Copy, accum_out=S_SUM[:, u:u + 1]),
                reads=(kx,), writes=(("fJS",), ("fS_SUM", u)), out_free=1024)
        ln_finalize(pl, S_SUM, S_SQ, S_M, S_R, S_T, "fSst",
                    [("fS_SUM", u) for u in range(4)],
                    [("fS_SQ", u) for u in range(4)])
        for u in range(4):
            kx = ("fXT", u)
            pl.op("v", lambda E, u=u: E.tensor_scalar(
                out=XH32, in0=XT[u], scalar1=S_M[:, u:u + 1], scalar2=S_R[:, u:u + 1],
                op0=ALU.subtract, op1=ALU.mult), reads=(kx,),
                sreads=(("fSst", "M"), ("fSst", "R")), writes=(("fXH32",),),
                out_free=512)
            pl.op("v", lambda E: E.tensor_copy(out=XHB, in_=XH32),
                  reads=(("fXH32",),), writes=(("fXHB",),), out_free=512)
            pl.op("v", lambda E: E.scalar_tensor_tensor(out=XLOB, in0=XHB,
                  scalar=-1.0, in1=XH32, op0=ALU.mult, op1=ALU.add),
                  reads=(("fXHB",), ("fXH32",)), writes=(("fXLOB",),),
                  out_free=1024)
            for half in range(2):
                for fc2 in range(4):
                    fc = half * 4 + fc2
                    pl.op("t", lambda E, fc=fc, fc2=fc2, half=half: E.transpose(
                        PTH[half][:, fc2 * 128:(fc2 + 1) * 128],
                        XHB[:, fc * 128:(fc + 1) * 128],
                        C["IDN"]), reads=(("fXHB",), "IDN"),
                        writes=(("fPT", half),), out_free=128)
                pl.op("v", lambda E, u=u, half=half: E.tensor_copy(
                    out=XHFS.rearrange("p (f c) -> p f c", f=8)[
                        :, half * 4:(half + 1) * 4, u * 128:(u + 1) * 128],
                    in_=PTH[half].rearrange("p (f c) -> p f c", f=4)),
                    reads=(("fPT", half),), writes=(("fXHFS", 0),), out_free=512)
            for half in range(2):
                for fc2 in range(4):
                    fc = half * 4 + fc2
                    pl.op("t", lambda E, fc=fc, fc2=fc2, half=half: E.transpose(
                        PTH[half][:, fc2 * 128:(fc2 + 1) * 128],
                        XLOB[:, fc * 128:(fc + 1) * 128],
                        C["IDN"]), reads=(("fXLOB",), "IDN"),
                        writes=(("fPT", half),), out_free=128)
                pl.op("v", lambda E, u=u, half=half: E.tensor_copy(
                    out=XLOFS.rearrange("p (f c) -> p f c", f=8)[
                        :, half * 4:(half + 1) * 4, u * 128:(u + 1) * 128],
                    in_=PTH[half].rearrange("p (f c) -> p f c", f=4)),
                    reads=(("fPT", half),), writes=(("fXLOFS",),), out_free=512)
        pl.op("y", lambda E, st=st: E.dma_start(
            out=xhat[st * 128:(st + 1) * 128, :], in_=XHFS), dma=True,
            reads=(("fXHFS", 0),), writes=(("xhat", st),))
        for u in range(4):
            tl = st * 4 + u
            for trm in range(3):
                xs, ws, kxs = ((XHFS, GWT, "fXHFS"), (XHFS, GWLO, "fXHFS"),
                               (XLOFS, GWT, "fXLOFS"))[trm]
                for kc in range(8):
                    pl.op("t", lambda E, kc=kc, u=u, xs=xs, ws=ws, trm=trm:
                          E.matmul(
                        PGa[:, 0:8], xs[:, kc * 512 + u * 128: kc * 512 + (u + 1) * 128],
                        ws[:, kc * 8:(kc + 1) * 8],
                        start=(trm == 0 and kc == 0), stop=(trm == 2 and kc == 7)),
                        reads=((kxs if kxs != "fXHFS" else ("fXHFS", 0)),
                               ("GWT",), ("GWLO",)), writes=(("fPGpl",),),
                        out_free=8)
            pl.op("v", lambda E: E.tensor_tensor(out=LG, in0=PGa[:, 0:8], in1=LBB,
                  op=ALU.add), reads=(("fPGpl",), ("LBB",)), writes=(("fLG",),),
                  out_free=8)
            pl.op("v", lambda E: E.reduce_max(out=M12[:, 0:1], in_=LG, axis=AX),
                  reads=(("fLG",),), writes=(("fM12",),), out_free=1)
            pl.op("v", lambda E: E.tensor_scalar(out=EQ1, in0=LG,
                  scalar1=M12[:, 0:1], scalar2=None, op0=ALU.is_equal),
                  reads=(("fLG",),), sreads=(("fM12",),), writes=(("fEQ1",),),
                  out_free=8)
            pl.op("v", lambda E: E.scalar_tensor_tensor(out=LGM, in0=EQ1,
                  scalar=-1e30, in1=LG, op0=ALU.mult, op1=ALU.add),
                  reads=(("fEQ1",), ("fLG",)), writes=(("fLGM",),), out_free=8)
            pl.op("v", lambda E: E.reduce_max(out=M12[:, 1:2], in_=LGM, axis=AX),
                  reads=(("fLGM",),), writes=(("fM12",),), out_free=1)
            pl.op("v", lambda E: E.tensor_scalar(out=EQ2, in0=LGM,
                  scalar1=M12[:, 1:2], scalar2=None, op0=ALU.is_equal),
                  reads=(("fLGM",),), sreads=(("fM12",),), writes=(("fEQ2",),),
                  out_free=8)
            pl.op("s", lambda E: E.activation(out=TH, in_=M12, func=AF.Tanh,
                  scale=0.5), reads=(("fM12",),), writes=(("fTH",),), out_free=2)
            pl.op("v", lambda E: E.tensor_scalar(out=S12, in0=TH, scalar1=0.5,
                  scalar2=0.5, op0=ALU.mult, op1=ALU.add), reads=(("fTH",),),
                  writes=(("fS12",),), out_free=2)
            pl.op("v", lambda E: E.reduce_sum(out=DEN, in_=S12, axis=AX),
                  reads=(("fS12",),), writes=(("fDEN",),), out_free=1)
            pl.op("v", lambda E: E.tensor_scalar(out=DEN, in0=DEN, scalar1=1e-8,
                  scalar2=None, op0=ALU.add), reads=(("fDEN",),),
                  writes=(("fDEN",),), out_free=1)
            pl.op("v", lambda E: E.reciprocal(out=DEN, in_=DEN),
                  reads=(("fDEN",),), writes=(("fDEN",),), out_free=1)
            pl.op("v", lambda E: E.tensor_scalar(out=S12, in0=S12, scalar1=DEN,
                  scalar2=None, op0=ALU.mult), reads=(("fS12",),),
                  sreads=(("fDEN",),), writes=(("fS12",),), out_free=2)
            pl.op("v", lambda E: E.tensor_scalar(out=WEF, in0=EQ1,
                  scalar1=S12[:, 0:1], scalar2=None, op0=ALU.mult),
                  reads=(("fEQ1",),), sreads=(("fS12",),), writes=(("fWEF",),),
                  out_free=8)
            pl.op("v", lambda E: E.scalar_tensor_tensor(out=WEF, in0=EQ2,
                  scalar=S12[:, 1:2], in1=WEF, op0=ALU.mult, op1=ALU.add),
                  reads=(("fEQ2",), ("fWEF",)), sreads=(("fS12",),),
                  writes=(("fWEF",),), out_free=8)
            pl.op("v", lambda E: E.tensor_copy(out=ST2[:, 8:16], in_=WEF),
                  reads=(("fWEF",),), writes=(("fST2",),), out_free=8)
            pl.op("v", lambda E, tl=tl: E.tensor_tensor(
                out=ST2[:, 0:8], in0=EQ1, in1=EQ2, op=ALU.add),
                reads=(("fEQ1",), ("fEQ2",)), writes=(("fST2",),), out_free=8)
            pl.op("v", lambda E, tl=tl: E.tensor_copy(
                out=SEL_ALL[:, tl * 8:(tl + 1) * 8], in_=ST2[:, 0:8]),
                reads=(("fST2",),), writes=(("SEL", tl),), out_free=8)
            pl.op("t", lambda E, tl=tl: E.matmul(
                PST[0:1, 0:16], ONESC, ST2,
                start=(tl == 0), stop=(tl == N_ST * 4 - 1)),
                reads=(("fST2",), ("ONESC",)), writes=(("fPST",),), out_free=16)
    pl.op("v", lambda E: E.tensor_copy(out=STR, in_=PST),
          reads=(("fPST",),), writes=(("STR",),), out_free=16)
    pl.op("y", lambda E: E.dma_start(out=cc_in[0:1, 0:16], in_=STR), dma=True,
          reads=(("STR",),), writes=(("cc_in",),))
    pl.op("p", lambda E: E.collective_compute(
        "AllReduce", ALU.add, replica_groups=[[0, 1, 2, 3, 4, 5, 6, 7]],
        ins=[cc_in.opt()], outs=[cc_out.opt()]),
        reads=(("cc_in",),), writes=(("cc_out",),), out_free=16, tag="allreduce")
    pl.op("y", lambda E: E.dma_start(out=WMS, in_=cc_out[0:1, 0:16]), dma=True,
          reads=(("cc_out",),), writes=(("WMS",),))
    pl.op("v", lambda E: E.tensor_scalar(out=WMS[0:1, 0:8], in0=WMS[0:1, 0:8],
          scalar1=1.0, scalar2=None, op0=ALU.max), reads=(("WMS",),),
          writes=(("WMS",),), out_free=8)
    pl.op("v", lambda E: E.reciprocal(out=WMS[0:1, 0:8], in_=WMS[0:1, 0:8]),
          reads=(("WMS",),), writes=(("WMS",),), out_free=8)
    pl.op("v", lambda E: E.tensor_tensor(out=WMS[0:1, 0:8], in0=WMS[0:1, 8:16],
          in1=WMS[0:1, 0:8], op=ALU.mult), reads=(("WMS",),), writes=(("WMS",),),
          out_free=8)
    pl.op("v", lambda E: E.tensor_copy(out=CF1[0:1, 0:8], in_=WMS[0:1, 0:8]),
          reads=(("WMS",),), writes=(("CF1",),), out_free=8)
    pl.op("t", lambda E: E.matmul(PGa[:, 0:8], ONES1, CF1[0:1, 0:8], start=True,
          stop=True), reads=(("CF1",), ("ONES1",)), writes=(("fPGpl",),),
          out_free=8)
    pl.op("v", lambda E: E.tensor_copy(out=WMR, in_=PGa[:, 0:8]),
          reads=(("fPGpl",),), writes=(("WMR",),), out_free=8)
    pl.block_barrier()

    natctr = [0]

    def load_mat_T(dst, kdst, src_ap, out_dim, in_dim, tag):
        noc = out_dim // 128
        nkc = in_dim // 128
        oc_grp = max(1, 4096 // in_dim)
        for g0 in range(0, noc, oc_grp):
            g1 = min(noc, g0 + oc_grp)
            natctr[0] += 1
            NAT = NAT2[natctr[0] % 2]
            knat = ("NAT", natctr[0] % 2)
            view = src_ap[g0 * 128:g1 * 128, :].rearrange("(oc p) i -> p oc i",
                                                          p=128)
            pl.op("p", lambda E, view=view, n=(g1 - g0), NAT=NAT: E.dma_start(
                out=NAT.rearrange("p (oc i) -> p oc i", oc=oc_grp if in_dim * oc_grp == 4096 else 1)[:, 0:n, 0:in_dim]
                if in_dim <= 4096 else None, in_=view),
                dma=True, writes=(knat,), tag=f"ld_{tag}")
            kw = (("fPT", 0), ("fPT", 1))
            for kc in range(nkc):
                for oc in range(g0, g1):
                    pl.op("t", lambda E, kc=kc, oc=oc, g0=g0,
                          NAT=NAT: E.transpose(
                        PT[:, (oc - g0) * 128:(oc - g0 + 1) * 128],
                        NAT[:, (oc - g0) * in_dim + kc * 128:
                            (oc - g0) * in_dim + (kc + 1) * 128],
                        C["IDN"]), reads=(knat, "IDN"),
                        writes=kw, out_free=128)
                pl.op("v", lambda E, kc=kc, g0=g0, g1=g1: E.tensor_copy(
                    out=dst[:, kc * out_dim + g0 * 128: kc * out_dim + g1 * 128],
                    in_=PT[:, 0:(g1 - g0) * 128]),
                    reads=kw,
                    writes=(kdst,), out_free=(g1 - g0) * 128)

    def ldxh(st):
        pl.op("p", lambda E, st=st: E.dma_start(
            out=XHFS, in_=xhat[st * 128:(st + 1) * 128, :]), dma=True,
            reads=(("xhat", st),), writes=(("fXHFS", 0),))

    def expert_pass(EG, EU, DW, experts, acc, acc_key):
        for i, e in enumerate(experts):
            load_mat_T(EG[i], ("EGWT", i), W["L0_ffn_expert_gate_w"][e], 512,
                       1024, f"eg{e}")
            load_mat_T(EU[i], ("EUWT", i), W["L0_ffn_expert_up_w"][e], 512,
                       1024, f"eu{e}")
            load_mat_T(DW[i], ("DWT", i), W["L0_ffn_expert_down_w"][e], 1024,
                       512, f"ed{e}")
        for st in range(N_ST):
            XHFSx = XHFS
            kxh = ("fXHFS", 0)
            ldxh(st)
            for u in range(4):
                tl = st * 4 + u
                pl.op("v", lambda E, tl=tl, u=u: E.tensor_tensor(
                    out=COEFT[:, u * 8:(u + 1) * 8],
                    in0=SEL_ALL[:, tl * 8:(tl + 1) * 8], in1=WMR,
                    op=ALU.mult), reads=(("SEL", tl), ("WMR",)),
                    writes=(("COEFT", u),), out_free=8)
                pl.op("t", lambda E, u=u: E.transpose(
                    PT[0:8, u * 128:(u + 1) * 128],
                    COEFT[:, u * 8:(u + 1) * 8],
                    C["IDN"]), reads=(("COEFT", u), "IDN"),
                    writes=(("fPT", 0), ("fPT", 1)), out_free=128)
            pl.op("v", lambda E: E.tensor_copy(out=CF8, in_=PT[0:8, 0:512]),
                  reads=(("fPT", 0), ("fPT", 1)), writes=(("CF8",),),
                  out_free=512)
            for i, e in enumerate(experts):
                pl.op("t", lambda E, e=e: E.matmul(
                    PACC[:, 0:512], SEL8[:, e * 128:(e + 1) * 128], CF8,
                    start=True, stop=True), reads=(("CF8",), ("SEL8",)),
                    writes=(("fPACC",),), out_free=512)
                pl.op("s", lambda E, i=i: E.activation(out=CB[i],
                      in_=PACC[:, 0:512], func=AF.Copy), reads=(("fPACC",),),
                      writes=(("CBk", i),), out_free=512)
                for hc in range(4):
                    pg = PGa if hc < 2 else PGb
                    kpg = ("fPG", hc // 2)
                    for kc in range(8):
                        pl.op("t", lambda E, i=i, hc=hc, kc=kc, pg=pg,
                              XHFSx=XHFSx: E.matmul(
                            pg[:, (hc % 2) * 512:(hc % 2 + 1) * 512],
                            EG[i][:, kc * 512 + hc * 128: kc * 512 + (hc + 1) * 128],
                            XHFSx[:, kc * 512:(kc + 1) * 512],
                            start=(kc == 0), stop=(kc == 7)),
                            reads=(("EGWT", i), kxh), writes=(kpg,),
                            out_free=512)
                    if hc % 2 == 1:
                        pl.op("s", lambda E, hc=hc, pg=pg: E.activation(
                            out=GSB[:, (hc - 1) * 512:(hc + 1) * 512], in_=pg,
                            func=AF.Silu), reads=(kpg,), writes=(("GSB",),),
                            out_free=1024)
                for hc in range(4):
                    pg = PGa if hc < 2 else PGb
                    kpg = ("fPG", hc // 2)
                    for kc in range(8):
                        pl.op("t", lambda E, i=i, hc=hc, kc=kc, pg=pg,
                              XHFSx=XHFSx: E.matmul(
                            pg[:, (hc % 2) * 512:(hc % 2 + 1) * 512],
                            EU[i][:, kc * 512 + hc * 128: kc * 512 + (hc + 1) * 128],
                            XHFSx[:, kc * 512:(kc + 1) * 512],
                            start=(kc == 0), stop=(kc == 7)),
                            reads=(("EUWT", i), kxh, ("GSB",)),
                            writes=(kpg,), out_free=512)
                    if hc % 2 == 1:
                        for h2 in (hc - 1, hc):
                            pl.op("v", lambda E, i=i, h2=h2, pg=pg: E.tensor_tensor(
                                out=USB[:, h2 * 512:(h2 + 1) * 512],
                                in0=pg[:, (h2 % 2) * 512:(h2 % 2 + 1) * 512],
                                in1=CB[i], op=ALU.mult),
                                reads=(kpg, ("CBk", i)),
                                writes=(("USB",),), out_free=512)
                pl.op("v", lambda E, i=i: E.tensor_tensor(
                    out=H2ALL[:, i * 2048:(i + 1) * 2048], in0=USB, in1=GSB,
                    op=ALU.mult), reads=(("USB",), ("GSB",)),
                    writes=(("H2", i),), out_free=2048)
            for tt in range(4):
                for half in range(2):
                    for ch in range(16):
                        i, hc = ch // 4, ch % 4
                        pl.op("t", lambda E, i=i, hc=hc, tt=tt, half=half, ch=ch:
                              E.matmul(
                            PACC[:, half * 512:(half + 1) * 512],
                            H2ALL[:, (i * 4 + hc) * 512 + tt * 128:
                                  (i * 4 + hc) * 512 + (tt + 1) * 128],
                            DW[i][:, hc * 1024 + half * 512:
                                  hc * 1024 + (half + 1) * 512],
                            start=(ch == 0), stop=(ch == 15)),
                            reads=(("H2", i), ("DWT", i)), writes=(("fPACC",),),
                            out_free=512)
                pl.op("s", lambda E: E.activation(out=ACC_SB, in_=PACC,
                      func=AF.Copy), reads=(("fPACC",),), writes=(("ACC_SB",),),
                      out_free=1024)
                tl = st * 4 + tt
                tok = pl.op("s", lambda E, tl=tl, acc=acc: E.dma_start(
                    out=acc[tl * 128:(tl + 1) * 128, :], in_=ACC_SB), dma=True,
                    reads=(("ACC_SB",),), writes=((acc_key, tl),))

    with contextlib.ExitStack() as ctx_ab:
        sb2 = lambda name, shape, dt: ctx_ab.enter_context(
            nc.sbuf_tensor(name, shape, dt)).ap()
        EGWT = [sb2(f"egwt{i}", [128, 8 * 512], BF16) for i in range(4)]
        EUWT = [sb2(f"euwt{i}", [128, 8 * 512], BF16) for i in range(4)]
        DWT = [sb2(f"dwt{i}", [128, 4 * 1024], BF16) for i in range(4)]
        expert_pass(EGWT, EUWT, DWT, [0, 1, 2, 3], accA, "accA")
        expert_pass(EGWT, EUWT, DWT, [4, 5, 6, 7], accB, "accB")
    pl.block_barrier()

    with contextlib.ExitStack() as ctx_c:
        sb3 = lambda name, shape, dt: ctx_c.enter_context(
            nc.sbuf_tensor(name, shape, dt)).ap()
        SGWT = sb3("sgwt", [128, 8 * 1024], BF16)
        SUWT = sb3("suwt", [128, 8 * 1024], BF16)
        SDWT = sb3("sdwt", [128, 8 * 1024], BF16)
        OB4 = [sb3(f"fob{i}", [128, 1024], F32) for i in range(4)]
        load_mat_T(SGWT, ("SGWT",), W["L0_ffn_shared_gate_w"], 1024, 1024, "sg")
        load_mat_T(SUWT, ("SUWT",), W["L0_ffn_shared_up_w"], 1024, 1024, "su")
        load_mat_T(SDWT, ("SDWT",), W["L0_ffn_shared_down_w"], 1024, 1024, "sd")
        XHFS2 = sb3("fxhfs2c", [128, 4096], BF16)

        def stageG(st):
            q = st % 2
            XHFSx = XHFS if q == 0 else XHFS2
            kxh = ("fXHFS", q)
            pl.op("p", lambda E, st=st, XHFSx=XHFSx: E.dma_start(
                out=XHFSx, in_=xhat[st * 128:(st + 1) * 128, :]), dma=True,
                reads=(("xhat", st),), writes=(kxh,))
            for rnd in range(2):
                for hc in range(4):
                    hco = rnd * 4 + hc
                    pg = PGa if hc < 2 else PGb
                    kpg = ("fPG", hc // 2)
                    for kc in range(8):
                        pl.op("t", lambda E, hco=hco, hc=hc, kc=kc, pg=pg,
                              XHFSx=XHFSx, kxh=kxh: E.matmul(
                            pg[:, (hc % 2) * 512:(hc % 2 + 1) * 512],
                            SGWT[:, kc * 1024 + hco * 128: kc * 1024 + (hco + 1) * 128],
                            XHFSx[:, kc * 512:(kc + 1) * 512],
                            start=(kc == 0), stop=(kc == 7)),
                            reads=(("SGWT",), kxh), writes=(kpg,),
                            out_free=512)
                    if hc % 2 == 1:
                        pl.op("s", lambda E, hc=hc, pg=pg: E.activation(
                            out=GSB[:, (hc - 1) * 512:(hc + 1) * 512], in_=pg,
                            func=AF.Silu), reads=(kpg,), writes=(("GSB",),),
                            out_free=1024)
                for hc in range(4):
                    hco = rnd * 4 + hc
                    pg = PGa if hc < 2 else PGb
                    kpg = ("fPG", hc // 2)
                    for kc in range(8):
                        pl.op("t", lambda E, hco=hco, hc=hc, kc=kc, pg=pg,
                              XHFSx=XHFSx, kxh=kxh: E.matmul(
                            pg[:, (hc % 2) * 512:(hc % 2 + 1) * 512],
                            SUWT[:, kc * 1024 + hco * 128: kc * 1024 + (hco + 1) * 128],
                            XHFSx[:, kc * 512:(kc + 1) * 512],
                            start=(kc == 0), stop=(kc == 7)),
                            reads=(("SUWT",), kxh, ("GSB",)),
                            writes=(kpg,), out_free=512)
                    if hc % 2 == 1:
                        pl.op("s", lambda E, hc=hc, pg=pg: E.activation(
                            out=USB[:, (hc - 1) * 512:(hc + 1) * 512], in_=pg,
                            func=AF.Copy), reads=(kpg,), writes=(("USB",),),
                            out_free=1024)
                pl.op("v", lambda E, rnd=rnd, q=q: E.tensor_tensor(
                    out=H2ALL[:, (q * 2 + rnd) * 2048:(q * 2 + rnd + 1) * 2048],
                    in0=USB, in1=GSB,
                    op=ALU.mult), reads=(("USB",), ("GSB",)),
                    writes=(("H2", q * 2 + rnd),), out_free=2048)

        def stageD(st):
            q = st % 2
            for tt in range(4):
                tl = st * 4 + tt
                for j in range(2):
                    pl.op("y" if j == 0 else "s", lambda E, tl=tl, j=j,
                          acc=(accA if j == 0 else accB): E.dma_start(
                          out=ACCR[j], in_=acc[tl * 128:(tl + 1) * 128, :]),
                          dma=True, reads=((("accA" if j == 0 else "accB"), tl),),
                          writes=(("ACCR", j),))
                for half in range(2):
                    for ch in range(8):
                        rnd, hc = ch // 4, ch % 4
                        pl.op("t", lambda E, rnd=rnd, hc=hc, tt=tt, half=half,
                              ch=ch, q=q: E.matmul(
                            PACC[:, half * 512:(half + 1) * 512],
                            H2ALL[:, (q * 8 + rnd * 4 + hc) * 512 + tt * 128:
                                  (q * 8 + rnd * 4 + hc) * 512 + (tt + 1) * 128],
                            SDWT[:, (rnd * 4 + hc) * 1024 + half * 512:
                                  (rnd * 4 + hc) * 1024 + (half + 1) * 512],
                            start=(ch == 0), stop=(ch == 7)),
                            reads=(("H2", q * 2 + rnd), ("SDWT",)),
                            writes=(("fPACC",),),
                            out_free=512)
                pl.op("v", lambda E, tt=tt: E.tensor_tensor(
                    out=OB4[tt], in0=PACC, in1=ACCR[0], op=ALU.add),
                    reads=(("fPACC",), ("ACCR", 0)), writes=(("OBS", tt),),
                    out_free=1024)
                pl.op("p", lambda E, tt=tt: E.tensor_tensor(
                    out=OB4[tt], in0=OB4[tt], in1=ACCR[1], op=ALU.add),
                    reads=(("OBS", tt), ("ACCR", 1)), writes=(("OBS", tt),),
                    out_free=1024)
                pl.op("s", lambda E, tt=tt: E.activation(out=JS, in_=OB4[tt],
                      func=AF.Copy, accum_out=B_SUM[:, tt:tt + 1]),
                      reads=(("OBS", tt),), writes=(("fJS",), ("fB_SUM", tt)),
                      out_free=1024)
                pl.op("s", lambda E, tt=tt: E.activation(
                    out=JV, in_=OB4[tt], func=AF.Square,
                    accum_out=B_SQ[:, tt:tt + 1]),
                    reads=(("OBS", tt),), writes=(("fJV",), ("fB_SQ", tt)),
                    out_free=1024)
                pl.op("p", lambda E, tl=tl, tt=tt: E.dma_start(
                    out=XT[tt], in_=x_in[tl * 128:(tl + 1) * 128, :]), dma=True,
                    reads=((kx_in, tl),), writes=(("fXT", tt),))

        stageG(0)
        for st in range(N_ST):
            if st + 1 < N_ST:
                stageG(st + 1)
            stageD(st)
            ln_finalize(pl, B_SUM, B_SQ, B_M, B_R, S_T, "fBst",
                        [("fB_SUM", tt) for tt in range(4)],
                        [("fB_SQ", tt) for tt in range(4)])
            for tt in range(4):
                tl = st * 4 + tt
                pl.op("v", lambda E, tt=tt: E.tensor_scalar(
                    out=OB4[tt], in0=OB4[tt], scalar1=B_M[:, tt:tt + 1],
                    scalar2=B_R[:, tt:tt + 1], op0=ALU.subtract, op1=ALU.mult),
                    reads=(("OBS", tt),), sreads=(("fBst", "M"), ("fBst", "R")),
                    writes=(("OBS", tt),), out_free=1024)
                pl.op("p", lambda E, tt=tt: E.tensor_tensor(
                    out=XT[tt], in0=OB4[tt], in1=XT[tt], op=ALU.add),
                    reads=(("OBS", tt), ("fXT", tt)), writes=(("fXT", tt),),
                    out_free=1024)
                tok = pl.op("y", lambda E, tl=tl, tt=tt: E.dma_start(
                    out=x_out[tl * 128:(tl + 1) * 128, :], in_=XT[tt]), dma=True,
                    reads=(("fXT", tt),), writes=((kx_out, tl),))
                out_toks.append(tok)
    return out_toks



def emit_l1_attn(pl, nc, ctx, x_in, x_out, kx_in, kx_out, W, C):
    E_ = ctx.enter_context
    sb = lambda name, shape, dt: E_(nc.sbuf_tensor(name, shape, dt)).ap()
    ps = lambda name, shape, dt: E_(nc.psum_tensor(name, shape, dt)).ap()
    nseq = max(1, N_ST // SEQ_ST)
    ntile_seq = SEQ_ST * 4
    slen = ntile_seq * 128

    WQ1 = sb("awq", [128, 8 * 512], BF16)
    WKV = sb("awkv", [128, 8 * 1024], BF16)
    WO1 = sb("awo", [128, 4 * 1024], BF16)
    ONESC = sb("aonesc", [128, 1], BF16)
    XT = [sb(f"axt{i}", [128, 1024], F32) for i in range(4)]
    XHB = sb("axhb", [128, 1024], BF16)
    XHB2a = sb("axhb2", [128, 1024], BF16)
    NMRA = sb("anmra", [128, 4], F32)
    XHF = sb("axhf", [128, 4096], BF16)
    QTM = sb("aqtm", [128, 512], F32)
    KVTM = sb("akvtm", [128, 1024], F32)
    SQSC = sb("asqsc", [128, 512], F32)
    QSS = sb("aqss", [128, 8], F32)
    KSS = sb("akss", [128, 8], F32)
    QRS = sb("aqrs", [128, 8], F32)
    KRS = sb("akrs", [128, 8], F32)
    SST = sb("asst", [128, 8], F32)
    ONESB = sb("aonesb", [128, 64], BF16)
    NEG16 = sb("aneg16", [128, 1], F32)
    QHB = sb("aqhb", [128, 512], BF16)
    KHB = sb("akhb", [128, 512], BF16)
    VEX = sb("avex", [128, ntile_seq * 520], BF16)
    QHF = sb("aqhf", [128, 4 * slen], BF16)
    KHF = sb("akhf", [128, 4 * slen], BF16)
    PJ = [sb(f"apj{i}", [128, 1024], BF16) for i in range(2)]
    OSB = sb("aosb", [128, 512], F32)
    RS = sb("ars", [128, 512], F32)
    RH = sb("arh", [128, 512], BF16)
    RL = sb("arl", [128, 512], BF16)
    OFM = sb("aofm", [128, 4 * 512], BF16)
    OB = sb("aob", [128, 4 * 1024], F32)
    JV = sb("ajv", [128, 1024], BF16)
    JS = sb("ajs", [128, 1024], BF16)
    S_SUM = sb("as_sum", [128, 4], F32)
    S_SQ = sb("as_sq", [128, 4], F32)
    S_M = sb("as_m", [128, 4], F32)
    S_R = sb("as_r", [128, 4], F32)
    S_T = sb("as_t", [128, 4], F32)
    B_SUM = sb("ab_sum", [128, 4], F32)
    B_SQ = sb("ab_sq", [128, 4], F32)
    B_M = sb("ab_m", [128, 4], F32)
    B_R = sb("ab_r", [128, 4], F32)

    PA = ps("apa", [128, 1024], F32)
    PL1 = ps("apl1", [128, 1024], F32)
    PL2 = ps("apl2", [128, 1024], F32)
    PO1 = ps("apo1", [128, 512], F32)
    PO2 = ps("apo2", [128, 512], F32)
    PB1 = None
    _PLB = PL1.bitcast(BF16)
    PTH = [_PLB[:, 0:512], _PLB[:, 1024:1536]]

    out_toks = []

    def wld(dst, src_name, kname):
        src = W[src_name]
        kc = src.shape[0] // 128
        view = src.rearrange("(kc p) o -> p kc o", p=128)
        d = dst.rearrange("p (kc o) -> p kc o", kc=kc)
        pl.op("p", lambda E, d=d, s=view: E.dma_start(out=d, in_=s), dma=True,
              writes=(kname,), tag=f"ld_{kname}")

    wld(WQ1, "L1_attn_Wq", "aWQ")
    wld(WKV, "L1_attn_Wkv", "aWKV")
    wld(WO1, "L1_attn_Wo", "aWO")
    pl.op("v", lambda E: E.memset(ONESC, 1.0), writes=("aONESC",), out_free=1)
    pl.op("v", lambda E: E.memset(ONESB, 1.0), writes=("aONESB",), out_free=64)
    pl.op("v", lambda E: E.memset(NEG16, -16.0), writes=("aNEG16",), out_free=1)
    pl.op("v", lambda E: E.memset(
        VEX.rearrange("p (t h e) -> p t h e", t=ntile_seq, h=8)[:, :, :, 64:65],
        1.0), writes=(("aVEXones",),), out_free=1024)

    for s in range(nseq):
        for stq in range(SEQ_ST):
            st = s * SEQ_ST + stq
            for u in range(4):
                tl = st * 4 + u
                kx = ("aXT", u)
                pl.op("y", lambda E, tl=tl, u=u: E.dma_start(
                    out=XT[u], in_=x_in[tl * 128:(tl + 1) * 128, :]), dma=True,
                    reads=((kx_in, tl),), writes=(kx,))
                pl.op("s", lambda E, u=u: E.activation(
                    out=JV, in_=XT[u], func=AF.Square,
                    accum_out=S_SQ[:, u:u + 1]), reads=(kx,),
                    writes=(("aJV",), ("aS_SQ", u)), out_free=1024)
                pl.op("s", lambda E, u=u: E.activation(
                    out=JS, in_=XT[u], func=AF.Copy, accum_out=S_SUM[:, u:u + 1]),
                    reads=(kx,), writes=(("aJS",), ("aS_SUM", u)), out_free=1024)
            ln_finalize(pl, S_SUM, S_SQ, S_M, S_R, S_T, "aSst",
                        [("aS_SUM", u) for u in range(4)],
                        [("aS_SQ", u) for u in range(4)])
            pl.op("v", lambda E: E.scalar_tensor_tensor(
                out=NMRA, in0=S_M, scalar=-1.0, in1=S_R, op0=ALU.mult,
                op1=ALU.mult), reads=(("aSst", "M"), ("aSst", "R")),
                writes=(("aNMRA",),), out_free=4)
            for u in range(4):
                tl = st * 4 + u
                tls = stq * 4 + u
                kx = ("aXT", u)
                XHBu = XHB if u % 2 == 0 else XHB2a
                kxhb = ("aXHB", u % 2)
                pl.op("s", lambda E, u=u, XHBu=XHBu: E.activation(
                    out=XHBu, in_=XT[u], func=AF.Identity,
                    scale=S_R[:, u:u + 1], bias=NMRA[:, u:u + 1]),
                    reads=(kx,), sreads=(("aSst", "R"), ("aNMRA",)),
                    writes=(kxhb,), out_free=1024)
                for half in range(2):
                    for fc2 in range(4):
                        fc = half * 4 + fc2
                        pl.op("t", lambda E, fc=fc, fc2=fc2, half=half,
                              XHBu=XHBu, kxhb=kxhb: E.transpose(
                            PTH[half][:, fc2 * 128:(fc2 + 1) * 128],
                            XHBu[:, fc * 128:(fc + 1) * 128], C["IDN"]),
                            reads=(kxhb, "IDN"), writes=(("aPT", half),),
                            out_free=128)
                    pl.op("v", lambda E, half=half: E.tensor_copy(
                        out=XHF.rearrange("p (f c) -> p f c", f=8)[
                            :, half * 4:(half + 1) * 4, 0:128],
                        in_=PTH[half].rearrange("p (f c) -> p f c", f=4)),
                        reads=(("aPT", half),), writes=(("aXHF",),),
                        out_free=512)
                for kc in range(8):
                    pl.op("t", lambda E, kc=kc: E.matmul(
                        PA[:, 0:512], XHF[:, kc * 512:kc * 512 + 128],
                        WQ1[:, kc * 512:(kc + 1) * 512], start=(kc == 0),
                        stop=(kc == 7)),
                        reads=(("aXHF",), ("aWQ",)), writes=(("aPA",),),
                        out_free=512)
                for half in range(2):
                    for kc in range(8):
                        pl.op("t", lambda E, kc=kc, half=half: E.matmul(
                            PA[:, 512:1024] if half == 0 else PL1[:, 0:512],
                            XHF[:, kc * 512:kc * 512 + 128],
                            WKV[:, kc * 1024 + half * 512:
                                kc * 1024 + (half + 1) * 512],
                            start=(kc == 0), stop=(kc == 7)),
                            reads=(("aXHF",), ("aWKV",)),
                            writes=(("aPA",) if half == 0 else ("aPL1", 0),),
                            out_free=512)
                pl.op("v", lambda E: E.tensor_copy(out=QTM, in_=PA[:, 0:512]),
                      reads=(("aPA",),), writes=(("aQTM",),), out_free=512)
                pl.op("v", lambda E: E.tensor_copy(out=KVTM[:, 0:512],
                      in_=PA[:, 512:1024]), reads=(("aPA",),),
                      writes=(("aKVTM",),), out_free=512)
                pl.op("v", lambda E: E.tensor_copy(out=KVTM[:, 512:1024],
                      in_=PL1[:, 0:512]), reads=(("aPL1", 0),),
                      writes=(("aKVTM",),), out_free=512)
                for (TMM, SS, RSD, kss, krs, ktm) in (
                        (QTM, QSS, QRS, "aQSS", "aQRS", "aQTM"),
                        (KVTM, KSS, KRS, "aKSS", "aKRS", "aKVTM")):
                    tin = TMM[:, 0:512]
                    pl.op("v", lambda E, tin=tin: E.scalar_tensor_tensor(
                        out=SQSC, in0=tin, scalar=1.0, in1=tin, op0=ALU.mult,
                        op1=ALU.mult), reads=((ktm,),), writes=(("aSQSC",),),
                        out_free=512)
                    pl.op("v", lambda E, SS=SS: E.reduce_sum(
                        out=SS.unsqueeze(2), in_=SQSC.rearrange(
                            "p (h d) -> p h d", h=8), axis=AX),
                        reads=(("aSQSC",),), writes=((kss,),), out_free=8)
                    pl.op("v", lambda E, SS=SS: E.tensor_scalar(
                        out=SS, in0=SS, scalar1=1e-24, scalar2=None,
                        op0=ALU.max), reads=((kss,),), writes=((kss,),),
                        out_free=8)
                    emit_rsqrt(pl, RSD, SS, SST, (krs,), (kss,), ("aSST",))
                pl.op("v", lambda E: E.tensor_tensor(
                    out=QHB.rearrange("p (h d) -> p d h", h=8),
                    in0=QTM.rearrange("p (h d) -> p d h", h=8),
                    in1=QRS.unsqueeze(1).broadcast_to([128, 64, 8]),
                    op=ALU.mult), reads=(("aQTM",),), sreads=(("aQRS",),),
                    writes=(("aQHB",),), out_free=512)
                pl.op("v", lambda E: E.tensor_tensor(
                    out=KHB.rearrange("p (h d) -> p d h", h=8),
                    in0=KVTM[:, 0:512].rearrange("p (h d) -> p d h", h=8),
                    in1=KRS.unsqueeze(1).broadcast_to([128, 64, 8]),
                    op=ALU.mult), reads=(("aKVTM",),), sreads=(("aKRS",),),
                    writes=(("aKHB",),), out_free=512)
                pl.op("v", lambda E, tls=tls: E.tensor_copy(
                    out=VEX[:, tls * 520:(tls + 1) * 520].rearrange(
                        "p (h e) -> p h e", h=8)[:, :, 0:64],
                    in_=KVTM[:, 512:1024].rearrange("p (h d) -> p h d", h=8)),
                    reads=(("aKVTM",),), writes=(("aVEX", tls),), out_free=512)
                for fc in range(4):
                    pl.op("t", lambda E, fc=fc: E.transpose(
                        PTH[0][:, fc * 128:(fc + 1) * 128],
                        QHB[:, fc * 128:(fc + 1) * 128], C["IDN"]),
                        reads=(("aQHB",), "IDN"), writes=(("aPT", 0),),
                        out_free=128)
                pl.op("v", lambda E, tls=tls: E.tensor_copy(
                    out=QHF.rearrange("p (f t) -> p f t", f=4)[
                        :, :, tls * 128:(tls + 1) * 128],
                    in_=PTH[0].rearrange("p (f t) -> p f t", f=4)),
                    reads=(("aPT", 0),), writes=(("aQHF", tls),), out_free=512)
                for fc in range(4):
                    pl.op("t", lambda E, fc=fc: E.transpose(
                        PTH[1][:, fc * 128:(fc + 1) * 128],
                        KHB[:, fc * 128:(fc + 1) * 128], C["IDN"]),
                        reads=(("aKHB",), "IDN"), writes=(("aPT", 1),),
                        out_free=128)
                pl.op("v", lambda E, tls=tls: E.tensor_copy(
                    out=KHF.rearrange("p (f t) -> p f t", f=4)[
                        :, :, tls * 128:(tls + 1) * 128],
                    in_=PTH[1].rearrange("p (f t) -> p f t", f=4)),
                    reads=(("aPT", 1),), writes=(("aKHF", tls),), out_free=512)

        QHF_R = tuple(("aQHF", i) for i in range(ntile_seq))
        KHF_R = tuple(("aKHF", i) for i in range(ntile_seq))
        VEX_R = tuple(("aVEX", i) for i in range(ntile_seq)) + (("aVEXones",),)
        for ib in range(SEQ_ST):
            for hx, h in enumerate([1, 3, 5, 7, 0, 2, 4, 6]):
                hb = (h % 2) * 64
                hf = h // 2
                po = PO1 if hx % 2 == 0 else PO2
                kpo = ("aPO1", hx % 2)
                def emit_pv(jg):
                    pj = PJ[jg % 2]
                    kpj = ("aPJ", jg % 2)
                    for j2 in range(2):
                        jc = jg * 2 + j2
                        pl.op("t", lambda E, jc=jc, h=h, pj=pj, po=po, j2=j2:
                              E.matmul(
                            po[0:65, :], VEX[:, jc * 520 + h * 65:
                                             jc * 520 + (h + 1) * 65],
                            pj[:, j2 * 512:(j2 + 1) * 512],
                            start=(jc == 0), stop=(jc == ntile_seq - 1)),
                            reads=VEX_R + (kpj,), writes=(kpo,), out_free=512)
                for jg in range(ntile_seq // 2):
                    plx = PL1 if jg % 2 == 0 else PL2
                    kpl = ("aPL1", jg % 2)
                    pj = PJ[jg % 2]
                    kpj = ("aPJ", jg % 2)
                    for j2 in range(2):
                        jc = jg * 2 + j2
                        pl.op("t", lambda E, hb=hb, hf=hf, jc=jc, ib=ib,
                              plx=plx, j2=j2: E.matmul(
                            plx[:, j2 * 512:(j2 + 1) * 512],
                            KHF[hb:hb + 64,
                                hf * slen + jc * 128: hf * slen + (jc + 1) * 128],
                            QHF[hb:hb + 64,
                                hf * slen + ib * 512: hf * slen + (ib + 1) * 512],
                            start=True, stop=True),
                            reads=KHF_R + QHF_R, writes=(kpl,), out_free=512)
                    pl.op("s", lambda E, pj=pj, plx=plx: E.activation(
                        out=pj, in_=plx, func=AF.Exp, scale=16.0, bias=NEG16),
                        reads=(kpl,), sreads=(("aNEG16",),),
                        writes=(kpj,), out_free=1024)
                    if jg > 0:
                        emit_pv(jg - 1)
                emit_pv(ntile_seq // 2 - 1)
                pl.op("v", lambda E, po=po: E.reciprocal(out=RS[64:65, :],
                      in_=po[64:65, :]), reads=(kpo,), writes=(("aRS",),),
                      out_free=512)
                pl.op("v", lambda E: E.tensor_copy(out=RH[64:65, :],
                      in_=RS[64:65, :]), reads=(("aRS",),), writes=(("aRH",),),
                      out_free=512)
                pl.op("v", lambda E: E.scalar_tensor_tensor(
                    out=RL[64:65, :], in0=RH[64:65, :], scalar=-1.0,
                    in1=RS[64:65, :], op0=ALU.mult, op1=ALU.add),
                    reads=(("aRH",), ("aRS",)), writes=(("aRL",),), out_free=512)
                pl.op("v", lambda E, po=po: E.tensor_copy(out=OSB[0:64, :],
                      in_=po[0:64, :]), reads=(kpo,),
                      writes=(("aOSB",),), out_free=512)
                pl.op("t", lambda E: E.matmul(
                    PA[0:64, 0:512], ONESB[64:65, 0:64], RH[64:65, :],
                    start=True, stop=False),
                    reads=(("aRH",), "aONESB"), writes=(("aPA",),),
                    out_free=512)
                pl.op("t", lambda E: E.matmul(
                    PA[0:64, 0:512], ONESB[64:65, 0:64], RL[64:65, :],
                    start=False, stop=True),
                    reads=(("aRL",), "aONESB"), writes=(("aPA",),),
                    out_free=512)
                pl.op("v", lambda E, hb=hb, hf=hf: E.tensor_tensor(
                    out=OFM[hb:hb + 64, hf * 512:(hf + 1) * 512],
                    in0=OSB[0:64, :], in1=PA[0:64, 0:512], op=ALU.mult),
                    reads=(("aOSB",), ("aPA",)), writes=(("aOFM", h),),
                    out_free=512)
                pl.op("t", lambda E: E.matmul(PL1[0:1, 0:1], ONESC, ONESC,
                      start=True, stop=True),
                      reads=(("aONESC",),), writes=(("aPL1", 0),), out_free=1)
            OFM_R = tuple(("aOFM", h) for h in range(8))
            for u in range(4):
                tls = ib * 4 + u
                tl = (s * SEQ_ST) * 4 + tls
                pwo = PA if u % 2 == 0 else PL1
                kpwo = ("aPA",) if u % 2 == 0 else ("aPL1", 0)
                for half in range(2):
                    for fc in range(4):
                        pl.op("t", lambda E, fc=fc, half=half, u=u, pwo=pwo:
                              E.matmul(
                            pwo[:, half * 512:(half + 1) * 512],
                            OFM[:, fc * 512 + u * 128: fc * 512 + (u + 1) * 128],
                            WO1[:, fc * 1024 + half * 512:
                                fc * 1024 + (half + 1) * 512],
                            start=(fc == 0), stop=(fc == 3)),
                            reads=OFM_R + (("aWO",),), writes=(kpwo,),
                            out_free=512)
                pl.op("s", lambda E, u=u, pwo=pwo: E.activation(
                    out=OB[:, u * 1024:(u + 1) * 1024], in_=pwo, func=AF.Copy,
                    accum_out=B_SUM[:, u:u + 1]), reads=(kpwo,),
                    writes=(("aOB", u), ("aB_SUM", u)), out_free=1024)
                pl.op("s", lambda E, u=u: E.activation(
                    out=JV, in_=OB[:, u * 1024:(u + 1) * 1024], func=AF.Square,
                    accum_out=B_SQ[:, u:u + 1]),
                    reads=(("aOB", u),), writes=(("aJV",), ("aB_SQ", u)),
                    out_free=1024)
                pl.op("s", lambda E, tl=tl, u=u: E.dma_start(
                    out=XT[u], in_=x_in[tl * 128:(tl + 1) * 128, :]), dma=True,
                    reads=((kx_in, tl),), writes=(("aXT", u),))
            ln_finalize(pl, B_SUM, B_SQ, B_M, B_R, S_T, "aBst",
                        [("aB_SUM", u) for u in range(4)],
                        [("aB_SQ", u) for u in range(4)])
            for u in range(4):
                tls = ib * 4 + u
                tl = (s * SEQ_ST) * 4 + tls
                pl.op("v", lambda E, u=u: E.tensor_scalar(
                    out=OB[:, u * 1024:(u + 1) * 1024],
                    in0=OB[:, u * 1024:(u + 1) * 1024],
                    scalar1=B_M[:, u:u + 1], scalar2=B_R[:, u:u + 1],
                    op0=ALU.subtract, op1=ALU.mult), reads=(("aOB", u),),
                    sreads=(("aBst", "M"), ("aBst", "R")), writes=(("aOB", u),),
                    out_free=1024)
                pl.op("p", lambda E, u=u: E.tensor_tensor(
                    out=XT[u], in0=OB[:, u * 1024:(u + 1) * 1024], in1=XT[u],
                    op=ALU.add), reads=(("aOB", u), ("aXT", u)),
                    writes=(("aXT", u),), out_free=1024)
                tok = pl.op("y", lambda E, tl=tl, u=u: E.dma_start(
                    out=x_out[tl * 128:(tl + 1) * 128, :], in_=XT[u]), dma=True,
                    reads=(("aXT", u),), writes=((kx_out, tl),))
                out_toks.append(tok)
    return out_toks



def emit_l1_ffn(pl, nc, ctx, x_in, x_out, kx_in, kx_out, W, C):
    E_ = ctx.enter_context
    sb = lambda name, shape, dt: E_(nc.sbuf_tensor(name, shape, dt)).ap()
    ps = lambda name, shape, dt: E_(nc.psum_tensor(name, shape, dt)).ap()
    ntok = N_ST * 512

    x1f_d = nc.dram_tensor("l1f_x1f", [N_ST * 128, 4096], BF16, kind="Internal").ap()
    x1t_d = nc.dram_tensor("l1f_x1t", [ntok, D], F32, kind="Internal").ap()
    accC = nc.dram_tensor("l1f_accC", [ntok, D], F32, kind="Internal").ap()
    accD = nc.dram_tensor("l1f_accD", [ntok, D], F32, kind="Internal").ap()

    ONES1 = sb("nones1", [1, 128], BF16)
    XT = [sb(f"nxt{i}", [128, 1024], F32) for i in range(4)]
    X1F = sb("nx1f", [128, 4096], BF16)
    X1T = sb("nx1t", [128, 1024], F32)
    COEF_ALL = sb("ncoef", [128, NT * 8], F32)
    JV = sb("njv", [128, 1024], BF16)
    JS = sb("njs", [128, 1024], BF16)
    S_SUM = sb("ns_sum", [128, 4], F32)
    S_SQ = sb("ns_sq", [128, 4], F32)
    S_M = sb("ns_m", [128, 4], F32)
    S_R = sb("ns_r", [128, 4], F32)
    S_T = sb("ns_t", [128, 4], F32)
    PA = ps("npa", [128, 1024], F32)
    PB = ps("npb", [128, 1024], F32)
    PT = ps("npt", [128, 1024], BF16)
    PT2 = ps("npt2", [128, 1024], BF16)
    PTH = [PT[:, 0:512], PT2[:, 0:512]]
    out_toks = []
    pl.op("v", lambda E: E.memset(ONES1, 1.0), writes=("nONES1",), out_free=128)

    def wld(dst, name, kname, src=None):
        src = W[name] if src is None else src
        if src.shape[0] > 128:
            kc = src.shape[0] // 128
            view = src.rearrange("(kc p) o -> p kc o", p=128)
            d = dst.rearrange("p (kc o) -> p kc o", kc=kc)
        else:
            view, d = src, dst
        pl.op("p", lambda E, d=d, s=view: E.dma_start(out=d, in_=s), dma=True,
              writes=(kname,), tag=f"ld_{kname}")

    with contextlib.ExitStack() as c1:
        sb1 = lambda name, shape, dt: c1.enter_context(
            nc.sbuf_tensor(name, shape, dt)).ap()
        PWT = sb1("npwt", [128, 8 * 1024], BF16)
        RWT = sb1("nrwt", [128, 64], BF16)
        RBB = sb1("nrbb", [128, 8], F32)
        DWK = sb1("ndwk", [128, 24], F32)
        DWKB = sb1("ndwkb", [128, 8], F32)
        PWBC = sb1("npwbc", [128, 8], F32)
        XHB = sb1("nxhb", [128, 1024], BF16)
        XHB2n = sb1("nxhb2", [128, 1024], BF16)
        NMRN = sb1("nnmrn", [128, 4], F32)
        XCF = [sb1(f"nxcf{i}", [128, 8 * 514], BF16) for i in range(2)]
        XCONV = sb1("nxconv", [128, 4096], BF16)
        CVT = [sb1(f"ncvt{i}", [128, 512], F32) for i in range(2)]
        XPF = sb1("nxpf", [128, 4096], BF16)
        XPB = sb1("nxpb", [128, 1024], BF16)
        LG = sb1("nlg", [128, 8], F32)
        EQ1 = sb1("neq1", [128, 8], F32)
        EQ2 = sb1("neq2", [128, 8], F32)
        LGM = sb1("nlgm", [128, 8], F32)
        M12 = sb1("nm12", [128, 2], F32)
        TH = sb1("nth", [128, 1], F32)
        W12 = sb1("nw12", [128, 2], F32)
        DWDG = sb1("ndwdg", [128, 3072], BF16)
        wld(PWT, "d_pwT", "nPWT")
        wld(RWT, "d_rwT", "nRWT")
        wld(DWDG, "d_dwdg", "nDWDG")
        pl.op("y", lambda E: E.dma_start(out=RBB, in_=W["L1_ffn_router_b"]
              .unsqueeze(0).broadcast_to([128, 8])), dma=True, writes=("nRBB",))
        pl.op("y", lambda E: E.dma_start(out=DWK, in_=W["d_dwk"]), dma=True,
              writes=("nDWK",))
        pl.op("y", lambda E: E.dma_start(out=DWKB, in_=W["d_dwbc"]), dma=True,
              writes=("nDWKB",))
        pl.op("y", lambda E: E.dma_start(out=PWBC, in_=W["d_pwbc"]), dma=True,
              writes=("nPWBC",))

        for st in range(N_ST):
            seq_start = (st % SEQ_ST == 0)
            cur = XCF[st % 2]
            prv = XCF[1 - st % 2]
            kcur = ("nXCF", st % 2)
            kprv = ("nXCF", 1 - st % 2)
            if seq_start:
                pl.op("v", lambda E, cur=cur: E.memset(
                    cur.rearrange("p (f t) -> p f t", f=8)[:, :, 0:2], 0.0),
                    writes=(kcur,), out_free=16)
            else:
                pl.op("v", lambda E, cur=cur, prv=prv: E.tensor_copy(
                    out=cur.rearrange("p (f t) -> p f t", f=8)[:, :, 0:2],
                    in_=prv.rearrange("p (f t) -> p f t", f=8)[:, :, 512:514]),
                    reads=(kprv,), writes=(kcur,), out_free=16)
            for u in range(4):
                tl = st * 4 + u
                kx = ("nXT", u)
                pl.op("y", lambda E, tl=tl, u=u: E.dma_start(
                    out=XT[u], in_=x_in[tl * 128:(tl + 1) * 128, :]), dma=True,
                    reads=((kx_in, tl),), writes=(kx,))
                pl.op("s", lambda E, u=u: E.activation(
                    out=JV, in_=XT[u], func=AF.Square,
                    accum_out=S_SQ[:, u:u + 1]), reads=(kx,),
                    writes=(("nJV",), ("nS_SQ", u)), out_free=1024)
                pl.op("s", lambda E, u=u: E.activation(
                    out=JS, in_=XT[u], func=AF.Copy,
                    accum_out=S_SUM[:, u:u + 1]),
                    reads=(kx,), writes=(("nJS",), ("nS_SUM", u)), out_free=1024)
            ln_finalize(pl, S_SUM, S_SQ, S_M, S_R, S_T, "nSst",
                        [("nS_SUM", u) for u in range(4)],
                        [("nS_SQ", u) for u in range(4)])
            pl.op("v", lambda E: E.scalar_tensor_tensor(
                out=NMRN, in0=S_M, scalar=-1.0, in1=S_R, op0=ALU.mult,
                op1=ALU.mult), reads=(("nSst", "M"), ("nSst", "R")),
                writes=(("nNMRN",),), out_free=4)
            for u in range(4):
                kx = ("nXT", u)
                XHBu = XHB if u % 2 == 0 else XHB2n
                kxhb = ("nXHB", u % 2)
                pl.op("s", lambda E, u=u, XHBu=XHBu: E.activation(
                    out=XHBu, in_=XT[u], func=AF.Identity,
                    scale=S_R[:, u:u + 1], bias=NMRN[:, u:u + 1]),
                    reads=(kx,), sreads=(("nSst", "R"), ("nNMRN",)),
                    writes=(kxhb,), out_free=1024)
                for half in range(2):
                    for fc2 in range(4):
                        fc = half * 4 + fc2
                        pl.op("t", lambda E, fc=fc, fc2=fc2, half=half,
                              XHBu=XHBu, kxhb=kxhb: E.transpose(
                            PTH[half][:, fc2 * 128:(fc2 + 1) * 128],
                            XHBu[:, fc * 128:(fc + 1) * 128], C["IDN"]),
                            reads=(kxhb, "IDN"), writes=(("nPT", half),),
                            out_free=128)
                    pl.op("v", lambda E, u=u, cur=cur, half=half: E.tensor_copy(
                        out=cur.rearrange("p (f t) -> p f t", f=8)[
                            :, half * 4:(half + 1) * 4,
                            2 + u * 128:2 + (u + 1) * 128],
                        in_=PTH[half].rearrange("p (f c) -> p f c", f=4)),
                        reads=(("nPT", half),), writes=(kcur,), out_free=512)
            for fc in range(8):
                pb = PB[:, (fc % 2) * 512:(fc % 2) * 512 + 512]
                kpb = ("nPBc", fc % 2)
                for t in range(3):
                    pl.op("t", lambda E, fc=fc, t=t, cur=cur, pb=pb: E.matmul(
                        pb, DWDG[:, (fc * 3 + t) * 128:(fc * 3 + t + 1) * 128],
                        cur[:, fc * 514 + t: fc * 514 + t + 512],
                        start=(t == 0), stop=(t == 2)),
                        reads=(kcur, ("nDWDG",)), writes=(kpb,), out_free=512)
                pl.op("s", lambda E, fc=fc, pb=pb: E.activation(
                    out=XCONV[:, fc * 512:(fc + 1) * 512], in_=pb,
                    func=AF.Identity, scale=1.0, bias=DWKB[:, fc:fc + 1]),
                    reads=(kpb,), sreads=(("nDWKB",),),
                    writes=(("nXCONV",),), out_free=512)
            for ofc in range(8):
                for kc in range(8):
                    pl.op("t", lambda E, ofc=ofc, kc=kc: E.matmul(
                        PA[:, 0:512],
                        PWT[:, kc * 1024 + ofc * 128: kc * 1024 + (ofc + 1) * 128],
                        XCONV[:, kc * 512:(kc + 1) * 512], start=(kc == 0),
                        stop=(kc == 7)),
                        reads=(("nPWT",), ("nXCONV",)), writes=(("nPA",),),
                        out_free=512)
                pl.op("s", lambda E, ofc=ofc: E.activation(
                    out=XPF[:, ofc * 512:(ofc + 1) * 512], in_=PA[:, 0:512],
                    func=AF.Silu, bias=PWBC[:, ofc:ofc + 1]),
                    reads=(("nPA",),), sreads=(("nPWBC",),),
                    writes=(("nXPF", ofc),), out_free=512)
            pl.op("v", lambda E, cur=cur: E.tensor_tensor(
                out=X1F.rearrange("p (f t) -> p f t", f=8),
                in0=cur.rearrange("p (f t) -> p f t", f=8)[:, :, 2:514],
                in1=XPF.rearrange("p (f t) -> p f t", f=8), op=ALU.add),
                reads=(kcur,) + tuple(("nXPF", o) for o in range(8)),
                writes=(("nX1F",),), out_free=4096)
            pl.op("y", lambda E, st=st: E.dma_start(
                out=x1f_d[st * 128:(st + 1) * 128, :], in_=X1F), dma=True,
                reads=(("nX1F",),), writes=(("x1f", st),))
            for u in range(4):
                tl = st * 4 + u
                for half in range(2):
                    for fc2 in range(4):
                        fc = half * 4 + fc2
                        pl.op("t", lambda E, fc=fc, fc2=fc2, half=half, u=u:
                              E.transpose(
                            PTH[half][:, fc2 * 128:(fc2 + 1) * 128],
                            XPF[:, fc * 512 + u * 128: fc * 512 + (u + 1) * 128],
                            C["IDN"]), reads=(("nXPF", fc), "IDN"),
                            writes=(("nPT", half),), out_free=128)
                    pl.op("v", lambda E, half=half: E.tensor_copy(
                        out=XPB[:, half * 512:(half + 1) * 512],
                        in_=PTH[half]),
                        reads=(("nPT", half),), writes=(("nXPB",),),
                        out_free=512)
                pl.op("v", lambda E, u=u: E.tensor_scalar(
                    out=X1T, in0=XT[u], scalar1=S_M[:, u:u + 1],
                    scalar2=S_R[:, u:u + 1], op0=ALU.subtract, op1=ALU.mult),
                    reads=(("nXT", u),), sreads=(("nSst", "M"), ("nSst", "R")),
                    writes=(("nX1T",),), out_free=512)
                pl.op("p", lambda E: E.tensor_tensor(
                    out=X1T, in0=X1T, in1=XPB, op=ALU.add),
                    reads=(("nX1T",), ("nXPB",)), writes=(("nX1T",),),
                    out_free=1024)
                pl.op("y", lambda E, tl=tl: E.dma_start(
                    out=x1t_d[tl * 128:(tl + 1) * 128, :], in_=X1T), dma=True,
                    reads=(("nX1T",),), writes=(("x1t", tl),))
                for kc in range(8):
                    pl.op("t", lambda E, kc=kc, u=u: E.matmul(
                        PB[:, 0:8],
                        X1F[:, kc * 512 + u * 128: kc * 512 + (u + 1) * 128],
                        RWT[:, kc * 8:(kc + 1) * 8], start=(kc == 0),
                        stop=(kc == 7)),
                        reads=(("nX1F",), ("nRWT",)), writes=(("nPB",),),
                        out_free=8)
                pl.op("v", lambda E: E.tensor_tensor(out=LG, in0=PB[:, 0:8],
                      in1=RBB, op=ALU.add), reads=(("nPB",), ("nRBB",)),
                      writes=(("nLG",),), out_free=8)
                pl.op("v", lambda E: E.reduce_max(out=M12[:, 0:1], in_=LG,
                      axis=AX), reads=(("nLG",),), writes=(("nM12",),),
                      out_free=1)
                pl.op("v", lambda E: E.tensor_scalar(out=EQ1, in0=LG,
                      scalar1=M12[:, 0:1], scalar2=None, op0=ALU.is_equal),
                      reads=(("nLG",),), sreads=(("nM12",),),
                      writes=(("nEQ1",),), out_free=8)
                pl.op("v", lambda E: E.scalar_tensor_tensor(out=LGM, in0=EQ1,
                      scalar=-1e30, in1=LG, op0=ALU.mult, op1=ALU.add),
                      reads=(("nEQ1",), ("nLG",)), writes=(("nLGM",),),
                      out_free=8)
                pl.op("v", lambda E: E.reduce_max(out=M12[:, 1:2], in_=LGM,
                      axis=AX), reads=(("nLGM",),), writes=(("nM12",),),
                      out_free=1)
                pl.op("v", lambda E: E.tensor_scalar(out=EQ2, in0=LGM,
                      scalar1=M12[:, 1:2], scalar2=None, op0=ALU.is_equal),
                      reads=(("nLGM",),), sreads=(("nM12",),),
                      writes=(("nEQ2",),), out_free=8)
                pl.op("v", lambda E: E.tensor_scalar(out=TH, in0=M12[:, 0:1],
                      scalar1=M12[:, 1:2], scalar2=0.5, op0=ALU.subtract,
                      op1=ALU.mult), reads=(("nM12",),), sreads=(("nM12",),),
                      writes=(("nTH",),), out_free=1)
                pl.op("s", lambda E: E.activation(out=TH, in_=TH, func=AF.Tanh),
                      reads=(("nTH",),), writes=(("nTH",),), out_free=1)
                pl.op("v", lambda E: E.tensor_scalar(out=W12[:, 0:1], in0=TH,
                      scalar1=0.5, scalar2=0.5, op0=ALU.mult, op1=ALU.add),
                      reads=(("nTH",),), writes=(("nW12",),), out_free=1)
                pl.op("v", lambda E: E.tensor_scalar(out=W12[:, 1:2], in0=TH,
                      scalar1=-0.5, scalar2=0.5, op0=ALU.mult, op1=ALU.add),
                      reads=(("nTH",),), writes=(("nW12",),), out_free=1)
                pl.op("v", lambda E, tl=tl: E.tensor_scalar(
                    out=COEF_ALL[:, tl * 8:(tl + 1) * 8], in0=EQ1,
                    scalar1=W12[:, 0:1], scalar2=None, op0=ALU.mult),
                    reads=(("nEQ1",),), sreads=(("nW12",),),
                    writes=(("nCOEF", tl),), out_free=8)
                pl.op("v", lambda E, tl=tl: E.scalar_tensor_tensor(
                    out=COEF_ALL[:, tl * 8:(tl + 1) * 8], in0=EQ2,
                    scalar=W12[:, 1:2], in1=COEF_ALL[:, tl * 8:(tl + 1) * 8],
                    op0=ALU.mult, op1=ALU.add),
                    reads=(("nEQ2",), ("nCOEF", tl)), sreads=(("nW12",),),
                    writes=(("nCOEF", tl),), out_free=8)
    pl.block_barrier()

    with contextlib.ExitStack() as c2:
        sb2 = lambda name, shape, dt: c2.enter_context(
            nc.sbuf_tensor(name, shape, dt)).ap()
        EWT = [sb2(f"newt{i}", [128, 8 * 1024], BF16) for i in range(8)]
        EBH = sb2("nebh", [1, 8192], BF16)
        YE = [sb2(f"nye{i}", [128, 1024], BF16) for i in range(2)]
        X1F2 = sb2("nx1f2", [128, 4096], BF16)
        ACC = sb2("nacc", [128, 4 * 1024], F32)
        PA2 = ps("npa2", [128, 1024], F32)
        pl.op("p", lambda E: E.dma_start(
            out=EBH, in_=W["L1_ffn_expert_b"].rearrange("a b -> (a b)").unsqueeze(0)),
            dma=True, writes=(("nEB",),))

        def expert_load(experts, base):
            for i, e in enumerate(experts):
                wld(EWT[base + i], None, ("nEWT", base + i), src=W["d_ewT"][e])

        def expert_pass(experts, acc, acc_key, base):
            def ldx1f(st):
                xf = X1F if st % 2 == 0 else X1F2
                pl.op("y", lambda E, st=st, xf=xf: E.dma_start(
                    out=xf, in_=x1f_d[st * 128:(st + 1) * 128, :]), dma=True,
                    reads=(("x1f", st),), writes=(("nX1F", st % 2),))
            ldx1f(0)
            for st in range(N_ST):
                xf = X1F if st % 2 == 0 else X1F2
                kxf = ("nX1F", st % 2)
                if st + 1 < N_ST:
                    ldx1f(st + 1)
                for u in range(4):
                    tl = st * 4 + u
                    for i, e in enumerate(experts):
                        pa = PA if i % 2 == 0 else PA2
                        kpa = ("nPA", i % 2)
                        ye = YE[i % 2]
                        kye = ("nYE", i % 2)
                        for kc in range(8):
                            for half in range(2):
                                pl.op("t", lambda E, i=i, kc=kc, half=half, u=u,
                                      pa=pa, xf=xf, base=base: E.matmul(
                                    pa[:, half * 512:(half + 1) * 512],
                                    xf[:, kc * 512 + u * 128:
                                       kc * 512 + (u + 1) * 128],
                                    EWT[base + i][:, kc * 1024 + half * 512:
                                                  kc * 1024 + (half + 1) * 512],
                                    start=(kc == 0), stop=False),
                                    reads=(kxf, ("nEWT", base + i)),
                                    writes=(kpa,), out_free=512)
                        for half in range(2):
                            pl.op("t", lambda E, e=e, half=half, pa=pa: E.matmul(
                                pa[:, half * 512:(half + 1) * 512], ONES1,
                                EBH[0:1, e * 1024 + half * 512:
                                     e * 1024 + (half + 1) * 512],
                                start=False, stop=True),
                                reads=(("nEB",), ("nONES1",)),
                                writes=(kpa,), out_free=512)
                        pl.op("s", lambda E, pa=pa, ye=ye: E.activation(
                              out=ye, in_=pa, func=AF.Silu), reads=(kpa,),
                              writes=(kye,), out_free=1024)
                        if i == 0:
                            pl.op("v", lambda E, u=u, tl=tl, e=e, ye=ye:
                                  E.tensor_scalar(
                                out=ACC[:, u * 1024:(u + 1) * 1024], in0=ye,
                                scalar1=COEF_ALL[:, tl * 8 + e: tl * 8 + e + 1],
                                scalar2=None, op0=ALU.mult),
                                reads=(kye,), sreads=(("nCOEF", tl),),
                                writes=(("nACC", u),), out_free=1024)
                        else:
                            pl.op("v", lambda E, u=u, tl=tl, e=e, ye=ye:
                                  E.scalar_tensor_tensor(
                                out=ACC[:, u * 1024:(u + 1) * 1024], in0=ye,
                                scalar=COEF_ALL[:, tl * 8 + e: tl * 8 + e + 1],
                                in1=ACC[:, u * 1024:(u + 1) * 1024],
                                op0=ALU.mult, op1=ALU.add),
                                reads=(kye, ("nACC", u)),
                                sreads=(("nCOEF", tl),), writes=(("nACC", u),),
                                out_free=1024)
                    tok = pl.op("s", lambda E, tl=tl, u=u, acc=acc: E.dma_start(
                        out=acc[tl * 128:(tl + 1) * 128, :],
                        in_=ACC[:, u * 1024:(u + 1) * 1024]), dma=True,
                        reads=(("nACC", u),), writes=((acc_key, tl),))

        expert_load([0, 1, 2, 3], 0)
        expert_load([4, 5, 6, 7], 4)
        expert_pass([0, 1, 2, 3], accC, "accC", 0)
        expert_pass([4, 5, 6, 7], accD, "accD", 4)
    pl.block_barrier()

    with contextlib.ExitStack() as c3:
        sb3 = lambda name, shape, dt: c3.enter_context(
            nc.sbuf_tensor(name, shape, dt)).ap()
        OWT = sb3("nowt", [128, 8 * 1024], BF16)
        FDWT = sb3("nfdwt", [128, 8 * 256], BF16)
        FUWT = sb3("nfuwt", [128, 2 * 1024], BF16)
        NWREP = sb3("nnwrep", [128, 1024], F32)
        OBH = sb3("nobh", [1, 1024], BF16)
        FDBH = sb3("nfdbh", [1, 256], BF16)
        FUBH = sb3("nfubh", [1, 1024], BF16)
        ACCR = [[sb3(f"caccr{q}{j}", [128, 1024], F32) for j in range(2)]
                for q in range(2)]
        ACCB = sb3("caccb", [128, 2 * 1024], BF16)
        ACCFM = sb3("caccfm", [128, 2 * 1024], BF16)
        H1 = sb3("ch1", [128, 2 * 1024], F32)
        HN = sb3("chn", [128, 2 * 1024], F32)
        HNB = sb3("chnb", [128, 2 * 1024], BF16)
        HFM = sb3("chfm", [128, 2 * 1024], BF16)
        FF1 = sb3("cff1", [128, 2 * 256], BF16)
        FF1F = sb3("cff1f", [128, 2 * 256], BF16)
        OB = sb3("cob", [128, 2 * 1024], F32)
        X1TC = sb3("cx1tc", [128, 2 * 1024], F32)
        JS2 = sb3("cjs2", [128, 1024], BF16)
        NMR = sb3("cnmr", [128, 4], F32)
        _PA2B = PA2.bitcast(BF16)
        PTHQ = [[PT[:, 0:512], PT2[:, 0:512]],
                [_PA2B[:, 0:512], _PA2B[:, 1024:1536]]]
        B1Q = [PA, PB]
        RT = sb3("nrt", [128, 4], F32)
        RSCR = sb3("nrscr", [128, 4], F32)
        R_R = sb3("nr_r", [128, 4], F32)
        B_SUM = sb3("nb_sum", [128, 4], F32)
        B_SQ = sb3("nb_sq", [128, 4], F32)
        B_M = sb3("nb_m", [128, 4], F32)
        B_R = sb3("nb_r", [128, 4], F32)
        BT = sb3("nbt", [128, 4], F32)
        BSCR = sb3("nbscr", [128, 4], F32)
        wld(OWT, "d_owT", "nOWT")
        wld(FDWT, "d_fdwT", "nFDWT")
        wld(FUWT, "d_fuwT", "nFUWT")
        pl.op("y", lambda E: E.dma_start(out=NWREP, in_=W["L1_ffn_norm_w"]
              .unsqueeze(0).broadcast_to([128, 1024])), dma=True,
              writes=("nNWREP",))
        pl.op("p", lambda E: E.dma_start(out=OBH,
              in_=W["L1_ffn_out_b"].unsqueeze(0)), dma=True, writes=("nOBb2",))
        pl.op("p", lambda E: E.dma_start(out=FDBH,
              in_=W["L1_ffn_ff_down_b"].unsqueeze(0)), dma=True,
              writes=("nFDB",))
        pl.op("p", lambda E: E.dma_start(out=FUBH,
              in_=W["L1_ffn_ff_up_b"].unsqueeze(0)), dma=True, writes=("nFUB",))

        def stageA(tl):
            q, u = tl % 2, tl % 4
            B1, acb = B1Q[q], ACCB[:, q * 1024:(q + 1) * 1024]
            acf = ACCFM[:, q * 1024:(q + 1) * 1024]
            h1 = H1[:, q * 1024:(q + 1) * 1024]
            x1c = X1TC[:, q * 1024:(q + 1) * 1024]
            for j, (acc, ak) in enumerate(((accC, "accC"), (accD, "accD"))):
                pl.op("y" if j == 0 else "s",
                    lambda E, tl=tl, j=j, q=q, acc=acc: E.dma_start(
                    out=ACCR[q][j], in_=acc[tl * 128:(tl + 1) * 128, :]),
                    dma=True, reads=((ak, tl),), writes=(("cACCR", q, j),))
            pl.op("p", lambda E, q=q, acb=acb: E.tensor_tensor(
                out=acb, in0=ACCR[q][0], in1=ACCR[q][1], op=ALU.add),
                reads=(("cACCR", q, 0), ("cACCR", q, 1)),
                writes=(("cACCB", q),), out_free=1024)
            for half in range(2):
                for fc2 in range(4):
                    fc = half * 4 + fc2
                    pl.op("t", lambda E, fc=fc, fc2=fc2, half=half, q=q,
                          acb=acb: E.transpose(
                        PTHQ[q][half][:, fc2 * 128:(fc2 + 1) * 128],
                        acb[:, fc * 128:(fc + 1) * 128], C["IDN"]),
                        reads=(("cACCB", q), "IDN"), writes=(("cPT", q, half),),
                        out_free=128)
                pl.op("v", lambda E, half=half, q=q, acf=acf: E.tensor_copy(
                    out=acf[:, half * 512:(half + 1) * 512], in_=PTHQ[q][half]),
                    reads=(("cPT", q, half),), writes=(("cACCFM", q),),
                    out_free=512)
            for half in range(2):
                for kc in range(8):
                    pl.op("t", lambda E, kc=kc, half=half, q=q, B1=B1,
                          acf=acf: E.matmul(
                        B1[:, half * 512:(half + 1) * 512],
                        acf[:, kc * 128:(kc + 1) * 128],
                        OWT[:, kc * 1024 + half * 512:
                            kc * 1024 + (half + 1) * 512],
                        start=(kc == 0), stop=False),
                        reads=(("cACCFM", q), ("nOWT",)), writes=(("cP", q),),
                        out_free=512)
                pl.op("t", lambda E, half=half, q=q, B1=B1: E.matmul(
                    B1[:, half * 512:(half + 1) * 512], ONES1,
                    OBH[0:1, half * 512:(half + 1) * 512],
                    start=False, stop=True),
                    reads=(("nOBb2",), ("nONES1",)), writes=(("cP", q),),
                    out_free=512)
            pl.op("p", lambda E, tl=tl, x1c=x1c: E.dma_start(
                out=x1c, in_=x1t_d[tl * 128:(tl + 1) * 128, :]), dma=True,
                reads=(("x1t", tl),), writes=(("cX1T", q),))

        def stageA2(tl):
            q, u = tl % 2, tl % 4
            B1 = B1Q[q]
            h1 = H1[:, q * 1024:(q + 1) * 1024]
            x1c = X1TC[:, q * 1024:(q + 1) * 1024]
            pl.op("v", lambda E, q=q, B1=B1, h1=h1, x1c=x1c: E.tensor_tensor(
                out=h1, in0=B1, in1=x1c, op=ALU.add),
                reads=(("cP", q), ("cX1T", q)), writes=(("cH1", q),),
                out_free=1024)
            pl.op("s", lambda E, q=q, u=u, h1=h1: E.activation(
                out=JS2, in_=h1, func=AF.Square, accum_out=RT[:, u:u + 1]),
                reads=(("cH1", q),), writes=(("nJS2",), ("nRT", u)),
                out_free=1024)

        def stageB1(tl):
            q, u = tl % 2, tl % 4
            B1 = B1Q[q]
            h1 = H1[:, q * 1024:(q + 1) * 1024]
            hn = HN[:, q * 1024:(q + 1) * 1024]
            hnb = HNB[:, q * 1024:(q + 1) * 1024]
            hfm = HFM[:, q * 1024:(q + 1) * 1024]
            ff1 = FF1[:, q * 256:(q + 1) * 256]
            ff1f = FF1F[:, q * 256:(q + 1) * 256]
            ob = OB[:, q * 1024:(q + 1) * 1024]
            pl.op("v", lambda E, u=u: E.tensor_scalar(
                out=RT[:, u:u + 1], in0=RT[:, u:u + 1], scalar1=1.0 / 1024,
                scalar2=1e-6, op0=ALU.mult, op1=ALU.add),
                reads=(("nRT", u),), writes=(("nRT", u),), out_free=1)
            emit_rsqrt(pl, R_R[:, u:u + 1], RT[:, u:u + 1],
                       RSCR[:, u:u + 1], ("nR_R", u), ("nRT", u),
                       ("nRscr", u))
            pl.op("v", lambda E, q=q, u=u, h1=h1, hn=hn: E.tensor_scalar(
                out=hn, in0=h1, scalar1=R_R[:, u:u + 1], scalar2=None,
                op0=ALU.mult), reads=(("cH1", q),), sreads=(("nR_R", u),),
                writes=(("cHN", q),), out_free=512)
            pl.op("v", lambda E, q=q, hn=hn: E.tensor_tensor(out=hn, in0=hn,
                  in1=NWREP, op=ALU.mult), reads=(("cHN", q), ("nNWREP",)),
                  writes=(("cHN", q),), out_free=1024)
            pl.op("s", lambda E, q=q, hn=hn, hnb=hnb: E.activation(
                out=hnb, in_=hn, func=AF.Copy), reads=(("cHN", q),),
                writes=(("cHNB", q),), out_free=1024)

        def stageB2(tl):
            q, u = tl % 2, tl % 4
            B1 = B1Q[q]
            hn = HN[:, q * 1024:(q + 1) * 1024]
            hnb = HNB[:, q * 1024:(q + 1) * 1024]
            hfm = HFM[:, q * 1024:(q + 1) * 1024]
            ff1 = FF1[:, q * 256:(q + 1) * 256]
            ff1f = FF1F[:, q * 256:(q + 1) * 256]
            ob = OB[:, q * 1024:(q + 1) * 1024]
            for half in range(2):
                for fc2 in range(4):
                    fc = half * 4 + fc2
                    pl.op("t", lambda E, fc=fc, fc2=fc2, half=half, q=q,
                          hnb=hnb: E.transpose(
                        PTHQ[q][half][:, fc2 * 128:(fc2 + 1) * 128],
                        hnb[:, fc * 128:(fc + 1) * 128], C["IDN"]),
                        reads=(("cHNB", q), "IDN"), writes=(("cPT", q, half),),
                        out_free=128)
                pl.op("v", lambda E, half=half, q=q, hfm=hfm: E.tensor_copy(
                    out=hfm[:, half * 512:(half + 1) * 512], in_=PTHQ[q][half]),
                    reads=(("cPT", q, half),), writes=(("cHFM", q),),
                    out_free=512)
            for kc in range(8):
                pl.op("t", lambda E, kc=kc, q=q, B1=B1, hfm=hfm: E.matmul(
                    B1[:, 0:256], hfm[:, kc * 128:(kc + 1) * 128],
                    FDWT[:, kc * 256:(kc + 1) * 256], start=(kc == 0),
                    stop=False),
                    reads=(("cHFM", q), ("nFDWT",)), writes=(("cP", q),),
                    out_free=256)
            pl.op("t", lambda E, q=q, B1=B1: E.matmul(
                B1[:, 0:256], ONES1, FDBH, start=False, stop=True),
                reads=(("nFDB",), ("nONES1",)), writes=(("cP", q),),
                out_free=256)
            pl.op("s", lambda E, q=q, B1=B1, ff1=ff1: E.activation(
                out=ff1, in_=B1[:, 0:256], func=AF.Gelu),
                reads=(("cP", q),), writes=(("cFF1", q),), out_free=256)
            for fc in range(2):
                pl.op("t", lambda E, fc=fc, q=q, ff1=ff1: E.transpose(
                    PTHQ[q][0][:, fc * 128:(fc + 1) * 128],
                    ff1[:, fc * 128:(fc + 1) * 128], C["IDN"]),
                    reads=(("cFF1", q), "IDN"), writes=(("cPT", q, 0),),
                    out_free=128)
            pl.op("v", lambda E, q=q, ff1f=ff1f: E.tensor_copy(
                out=ff1f, in_=PTHQ[q][0][:, 0:256]),
                reads=(("cPT", q, 0),), writes=(("cFF1F", q),), out_free=256)
            for half in range(2):
                for kc in range(2):
                    pl.op("t", lambda E, kc=kc, half=half, q=q, B1=B1,
                          ff1f=ff1f: E.matmul(
                        B1[:, half * 512:(half + 1) * 512],
                        ff1f[:, kc * 128:(kc + 1) * 128],
                        FUWT[:, kc * 1024 + half * 512:
                             kc * 1024 + (half + 1) * 512],
                        start=(kc == 0), stop=False),
                        reads=(("cFF1F", q), ("nFUWT",)), writes=(("cP", q),),
                        out_free=512)
                pl.op("t", lambda E, half=half, q=q, B1=B1: E.matmul(
                    B1[:, half * 512:(half + 1) * 512], ONES1,
                    FUBH[0:1, half * 512:(half + 1) * 512],
                    start=False, stop=True),
                    reads=(("nFUB",), ("nONES1",)), writes=(("cP", q),),
                    out_free=512)
            pl.op("v", lambda E, q=q, B1=B1, ob=ob, hn=hn: E.tensor_tensor(
                out=ob, in0=B1, in1=hn, op=ALU.add),
                reads=(("cP", q), ("cHN", q)), writes=(("cOB", q),),
                out_free=1024)
            pl.op("s", lambda E, q=q, u=u, ob=ob: E.activation(out=JS, in_=ob,
                  func=AF.Copy, accum_out=B_SUM[:, u:u + 1]),
                  reads=(("cOB", q),), writes=(("nJS",), ("nB_SUM", u)),
                  out_free=1024)
            pl.op("s", lambda E, q=q, u=u, ob=ob: E.activation(
                out=JS2, in_=ob, func=AF.Square, accum_out=B_SQ[:, u:u + 1]),
                reads=(("cOB", q),), writes=(("nJS2",), ("nB_SQ", u)),
                out_free=1024)
            pl.op("s", lambda E, tl=tl, u=u: E.dma_start(
                out=XT[u], in_=x_in[tl * 128:(tl + 1) * 128, :]), dma=True,
                reads=((kx_in, tl),), writes=(("nXT", u),))
            pl.op("v", lambda E, u=u: E.tensor_scalar(
                out=B_M[:, u:u + 1], in0=B_SUM[:, u:u + 1],
                scalar1=1.0 / 1024, scalar2=None, op0=ALU.mult),
                reads=(("nB_SUM", u),), writes=(("nB_M", u),), out_free=1)
            pl.op("v", lambda E, u=u: E.tensor_scalar(
                out=BT[:, u:u + 1], in0=B_SQ[:, u:u + 1],
                scalar1=1.0 / 1024, scalar2=None, op0=ALU.mult),
                reads=(("nB_SQ", u),), writes=(("nBT", u),), out_free=1)
            pl.op("v", lambda E, u=u: E.tensor_tensor(
                out=B_R[:, u:u + 1], in0=B_M[:, u:u + 1],
                in1=B_M[:, u:u + 1], op=ALU.mult), reads=(("nB_M", u),),
                writes=(("nB_R", u),), out_free=1)
            pl.op("v", lambda E, u=u: E.tensor_tensor(
                out=BT[:, u:u + 1], in0=BT[:, u:u + 1],
                in1=B_R[:, u:u + 1], op=ALU.subtract),
                reads=(("nBT", u), ("nB_R", u)), writes=(("nBT", u),),
                out_free=1)
            pl.op("v", lambda E, u=u: E.tensor_scalar(
                out=BT[:, u:u + 1], in0=BT[:, u:u + 1], scalar1=1e-5,
                scalar2=None, op0=ALU.add), reads=(("nBT", u),),
                writes=(("nBT", u),), out_free=1)
            emit_rsqrt(pl, B_R[:, u:u + 1], BT[:, u:u + 1],
                       BSCR[:, u:u + 1], ("nB_R", u), ("nBT", u),
                       ("nBscr", u))
            pl.op("v", lambda E, u=u: E.scalar_tensor_tensor(
                out=NMR[:, u:u + 1], in0=B_M[:, u:u + 1], scalar=-1.0,
                in1=B_R[:, u:u + 1], op0=ALU.mult, op1=ALU.mult),
                reads=(("nB_M", u), ("nB_R", u)), writes=(("nNMR", u),),
                out_free=1)
            pl.op("s", lambda E, q=q, u=u, ob=ob: E.activation(
                out=ob, in_=ob, func=AF.Identity, scale=B_R[:, u:u + 1],
                bias=NMR[:, u:u + 1]),
                reads=(("cOB", q),), sreads=(("nB_R", u), ("nNMR", u)),
                writes=(("cOB", q),), out_free=1024)
            pl.op("p", lambda E, q=q, u=u, ob=ob: E.tensor_tensor(
                out=XT[u], in0=ob, in1=XT[u], op=ALU.add),
                reads=(("cOB", q), ("nXT", u)), writes=(("nXT", u),),
                out_free=1024)
            tok = pl.op("y", lambda E, tl=tl, u=u: E.dma_start(
                out=x_out[tl * 128:(tl + 1) * 128, :], in_=XT[u]), dma=True,
                reads=(("nXT", u),), writes=((kx_out, tl),))
            out_toks.append(tok)

        stageA(0)
        stageA2(0)
        for tl in range(1, N_ST * 4):
            stageA(tl)
            stageB1(tl - 1)
            stageA2(tl)
            stageB2(tl - 1)
        stageB1(N_ST * 4 - 1)
        stageB2(N_ST * 4 - 1)
    return out_toks



WEIGHT_SPECS = {
    "L0_attn_Wq": [1024, 1024], "L0_attn_Wk": [1024, 1024],
    "L0_attn_Wv": [1024, 1024], "L0_attn_Wo": [1024, 1024],
    "L0_attn_gamma": [1024], "L0_attn_beta": [1024],
    "L0_attn_Wg1": [1024, 64], "L0_attn_Wg2": [64, 1024],
    "L0_ffn_gate_w": [8, 1024], "L0_ffn_lb_bias": [8],
    "L0_ffn_expert_gate_w": [8, 512, 1024], "L0_ffn_expert_up_w": [8, 512, 1024],
    "L0_ffn_expert_down_w": [8, 1024, 512],
    "L0_ffn_shared_gate_w": [1024, 1024], "L0_ffn_shared_up_w": [1024, 1024],
    "L0_ffn_shared_down_w": [1024, 1024],
    "L1_attn_Wq": [1024, 512], "L1_attn_Wkv": [1024, 1024], "L1_attn_Wo": [512, 1024],
    "L1_ffn_dw_kernel": [1024, 1, 3], "L1_ffn_dw_bias": [1024],
    "L1_ffn_pw_w": [1024, 1024], "L1_ffn_pw_b": [1024],
    "L1_ffn_router_w": [8, 1024], "L1_ffn_router_b": [8],
    "L1_ffn_expert_w": [8, 1024, 1024], "L1_ffn_expert_b": [8, 1024],
    "L1_ffn_out_w": [1024, 1024], "L1_ffn_out_b": [1024],
    "L1_ffn_norm_w": [1024],
    "L1_ffn_ff_down_w": [256, 1024], "L1_ffn_ff_down_b": [256],
    "L1_ffn_ff_up_w": [1024, 256], "L1_ffn_ff_up_b": [1024],
}
CONST_SPECS = {
    "c_idn": ([128, 128], BF16), "c_maskT": ([128, 16 * 128], BF16),
    "c_lamq": ([128, 1024], BF16), "c_lamk": ([128, 1024], BF16),
    "c_lamc": ([128, 512], F32), "c_sel8": ([8, 1024], BF16),
}
def _dwdg(inp):
    kern = np.asarray(inp["L1_ffn_dw_kernel"], np.float32)
    out = np.zeros((128, 24 * 128), np.float32)
    for fc in range(8):
        for t in range(3):
            blk = out[:, (fc * 3 + t) * 128:(fc * 3 + t + 1) * 128]
            np.fill_diagonal(blk, kern[fc * 128:(fc + 1) * 128, 0, t])
    return out


DERIVED_SPECS = {
    "d_dwdg": ([128, 3072], F32, _dwdg),
    "d_gcol": ([128, 8], F32, lambda inp: np.ascontiguousarray(
        np.asarray(inp["L0_attn_gamma"], np.float32).reshape(8, 128).T)),
    "d_gwT": ([128, 64], F32, lambda inp: np.ascontiguousarray(
        np.asarray(inp["L0_ffn_gate_w"], np.float32).T.reshape(8, 128, 8)
        .transpose(1, 0, 2).reshape(128, 64))),
    "d_pwT": ([1024, 1024], F32, lambda inp: np.ascontiguousarray(
        np.asarray(inp["L1_ffn_pw_w"], np.float32).T)),
    "d_rwT": ([128, 64], F32, lambda inp: np.ascontiguousarray(
        np.asarray(inp["L1_ffn_router_w"], np.float32).T.reshape(8, 128, 8)
        .transpose(1, 0, 2).reshape(128, 64))),
    "d_owT": ([1024, 1024], F32, lambda inp: np.ascontiguousarray(
        np.asarray(inp["L1_ffn_out_w"], np.float32).T)),
    "d_fdwT": ([1024, 256], F32, lambda inp: np.ascontiguousarray(
        np.asarray(inp["L1_ffn_ff_down_w"], np.float32).T)),
    "d_fuwT": ([256, 1024], F32, lambda inp: np.ascontiguousarray(
        np.asarray(inp["L1_ffn_ff_up_w"], np.float32).T)),
    "d_ewT": ([8, 1024, 1024], F32, lambda inp: np.ascontiguousarray(
        np.asarray(inp["L1_ffn_expert_w"], np.float32).transpose(0, 2, 1))),
    "d_pwbc": ([128, 8], F32, lambda inp: np.ascontiguousarray(
        np.asarray(inp["L1_ffn_pw_b"], np.float32).reshape(8, 128).T)),
    "d_dwk": ([128, 24], F32, lambda inp: np.ascontiguousarray(
        np.asarray(inp["L1_ffn_dw_kernel"], np.float32)[:, 0, :]
        .reshape(8, 128, 3).transpose(1, 0, 2).reshape(128, 24))),
    "d_dwbc": ([128, 8], F32, lambda inp: np.ascontiguousarray(
        np.asarray(inp["L1_ffn_dw_bias"], np.float32).reshape(8, 128).T)),
}
DERIVED_BLOCKS = {"d_gcol": "l0a", "d_gwT": "l0f", "d_pwT": "l1f", "d_dwdg": "l1f",
                  "d_rwT": "l1f", "d_owT": "l1f", "d_fdwT": "l1f",
                  "d_fuwT": "l1f", "d_ewT": "l1f", "d_pwbc": "l1f",
                  "d_dwk": "l1f", "d_dwbc": "l1f"}
BLOCK_WEIGHTS = {
    "l0a": ["L0_attn_Wq", "L0_attn_Wk", "L0_attn_Wv", "L0_attn_Wo", "L0_attn_gamma",
            "L0_attn_beta", "L0_attn_Wg1", "L0_attn_Wg2"],
    "l0f": ["L0_ffn_gate_w", "L0_ffn_lb_bias", "L0_ffn_expert_gate_w",
            "L0_ffn_expert_up_w", "L0_ffn_expert_down_w", "L0_ffn_shared_gate_w",
            "L0_ffn_shared_up_w", "L0_ffn_shared_down_w"],
    "l1a": ["L1_attn_Wq", "L1_attn_Wkv", "L1_attn_Wo"],
    "l1f": ["L1_ffn_dw_kernel", "L1_ffn_dw_bias", "L1_ffn_pw_w", "L1_ffn_pw_b",
            "L1_ffn_router_w", "L1_ffn_router_b", "L1_ffn_expert_w", "L1_ffn_expert_b",
            "L1_ffn_out_w", "L1_ffn_out_b", "L1_ffn_norm_w", "L1_ffn_ff_down_w",
            "L1_ffn_ff_down_b", "L1_ffn_ff_up_w", "L1_ffn_ff_up_b"],
}
BLOCK_EMIT = {"l0a": emit_l0_attn, "l0f": emit_l0_ffn, "l1a": emit_l1_attn,
              "l1f": emit_l1_ffn}


def build_program(blocks):
    nc = bass.Bass("TRN2", target_bir_lowering=False)
    W = {}
    for b in blocks:
        for name in BLOCK_WEIGHTS[b]:
            W[name] = nc.dram_tensor(name, WEIGHT_SPECS[name], F32,
                                     kind="ExternalInput").ap()
    for name, (shape, dt) in CONST_SPECS.items():
        W[name] = nc.dram_tensor(name, shape, dt, kind="ExternalInput").ap()
    for name, (shape, dt, _fn) in DERIVED_SPECS.items():
        if DERIVED_BLOCKS[name] in blocks:
            W[name] = nc.dram_tensor(name, shape, dt, kind="ExternalInput").ap()
    x_in = nc.dram_tensor("x", [N_ST * 512, D], F32, kind="ExternalInput").ap()
    x_out = nc.dram_tensor("xo", [N_ST * 512, D], F32, kind="ExternalOutput").ap()
    mids = [nc.dram_tensor(f"xmid{i}", [N_ST * 512, D], F32, kind="Internal").ap()
            for i in range(len(blocks) - 1)]
    chain = [x_in] + mids + [x_out]
    knames = ["x"] + [f"xmid{i}" for i in range(len(mids))] + ["xo"]

    pl = Plan(nc)
    ctx = contextlib.ExitStack()
    C = {}
    idn_sb = ctx.enter_context(nc.sbuf_tensor("c_idn_sb", [128, 128], BF16)).ap()
    C["IDN"] = idn_sb
    pl.op("y", lambda E: E.dma_start(out=idn_sb, in_=W["c_idn"]), dma=True,
          writes=("IDN",))

    out_toks = []
    for i, b in enumerate(blocks):
        if i > 0:
            pl.block_barrier()
        with contextlib.ExitStack() as bctx:
            toks = BLOCK_EMIT[b](pl, nc, bctx, chain[i], chain[i + 1],
                                 knames[i], knames[i + 1], W, C)
        if i == len(blocks) - 1:
            out_toks = toks
    pl.finalize(out_toks)
    pl.emit(ctx)
    ctx.close()
    return nc


def _shard_inputs(inputs):
    x = np.ascontiguousarray(np.asarray(inputs["L0_attn_x"], np.float32))
    xs = x.reshape(NCORES, TOK, D)
    consts = host_consts()
    maps = []
    for i in range(NCORES):
        m = {"x": np.ascontiguousarray(xs[i])}
        for name in WEIGHT_SPECS:
            if name in inputs:
                m[name] = np.ascontiguousarray(np.asarray(inputs[name], np.float32))
        m.update(consts)
        for name, (shape, dt, fn) in DERIVED_SPECS.items():
            m[name] = fn(inputs)
        maps.append(m)
    return maps


def run_blocks(inputs, blocks):
    maps = _shard_inputs(inputs)
    nc = build_program(blocks)
    need = set()
    for b in blocks:
        need.update(BLOCK_WEIGHTS[b])
    dneed = {n for n, b in DERIVED_BLOCKS.items() if b in blocks}
    in_maps = [{k: v for k, v in m.items()
                if k == "x" or k in need or k in CONST_SPECS or k in dneed}
               for m in maps]
    res = run_bass_kernel_spmd(nc, in_maps, core_ids=list(range(NCORES)))
    out = np.stack([np.asarray(r["xo"]) for r in res.results], axis=0)
    return out.reshape(16, 2048, 1024).astype(np.float32)


def kernel(**inputs):
    return run_blocks(inputs, ["l0a", "l0f", "l1a", "l1f"])


# version 19
# speedup vs baseline: 1.2082x; 1.0288x over previous
import contextlib
import math
import numpy as np
import ml_dtypes
import concourse.bass as bass
import concourse.mybir as mybir
from concourse.bass_utils import run_bass_kernel_spmd

F32 = mybir.dt.float32
BF16 = mybir.dt.bfloat16
I32 = mybir.dt.int32
AF = mybir.ActivationFunctionType
ALU = mybir.AluOpType
AX = mybir.AxisListType.X

NCORES = 8
TOK = 4096
D = 1024
NT = TOK // 128
N_ST = 8
SEQ_ST = 4

_BLOCK_METH = {"t": "tensor", "v": "vector", "s": "scalar", "p": "gpsimd", "y": "sync"}
DRAIN_FREE_THRESH = 2048


class Plan:

    def __init__(self, nc):
        self.nc = nc
        self.engs = list("tvspy")
        self.ops = {e: [] for e in self.engs}
        self.dma_idx = {e: 0 for e in self.engs}
        self.last_w = {}
        self.readers = {}
        self.tok_free = {}
        self.last_drain = {e: 0 for e in self.engs}
        self.marked = set()

    def _drain(self, e):
        self.ops[e].append(dict(fn=lambda eng: eng.drain(), deps=set(), dma=False,
                                tag="drain"))
        self.last_drain[e] = len(self.ops[e])

    @staticmethod
    def _ck(b):
        return (b,) if isinstance(b, str) else tuple(b)

    def op(self, e, fn, reads=(), writes=(), sreads=(), dma=False, out_free=1 << 30,
           tag=""):
        reads = tuple(self._ck(b) for b in reads)
        sreads = tuple(self._ck(b) for b in sreads)
        writes = tuple(self._ck(b) for b in writes)
        deps = set()
        raw_deps = set()
        for b in tuple(reads) + tuple(sreads):
            t = self.last_w.get(b)
            if t is not None:
                deps.add(t)
                raw_deps.add(t)
        for b in tuple(writes):
            t = self.last_w.get(b)
            if t is not None:
                deps.add(t)
            for t2 in self.readers.get(b, ()):
                deps.add(t2)
        if e in "vsp":
            need = False
            for d in deps:
                if d[0] == e and d[1] >= self.last_drain[e]:
                    if (d in raw_deps) or (self.tok_free.get(d, 1 << 30) < DRAIN_FREE_THRESH):
                        need = True
            if need:
                self._drain(e)
        deps = {d for d in deps if d[0] != e}
        for d in deps:
            self.marked.add(d)
        idx = len(self.ops[e])
        rec = dict(fn=fn, deps=deps, dma=dma, tag=tag)
        self.ops[e].append(rec)
        if dma:
            i = self.dma_idx[e]
            token = (e + "D" + str(i % 8), i // 8)
            self.dma_idx[e] += 1
            rec["dtok"] = token
            self.marked.add(token)
        else:
            token = (e, idx)
        self.tok_free[token] = out_free
        for b in tuple(reads) + tuple(sreads):
            self.readers.setdefault(b, []).append(token)
        for b in tuple(writes):
            self.last_w[b] = token
            self.readers[b] = []
        return token

    def tail_token(self, e):
        for i in range(len(self.ops[e]) - 1, -1, -1):
            o = self.ops[e][i]
            if o["tag"] == "barrier":
                continue
            if o["dma"]:
                return o["dtok"]
            return (e, i)
        return None

    def dma_tail_tokens(self):
        toks = []
        for e in self.engs:
            n = self.dma_idx[e]
            for k in range(8):
                nk = (n + 7 - k) // 8
                if nk > 0:
                    toks.append((e + "D" + str(k), nk - 1))
        return toks

    def block_barrier(self):
        tails = [self.tail_token(e) for e in self.engs] + self.dma_tail_tokens()
        tails = [t for t in tails if t is not None]
        for e in self.engs:
            deps = {t for t in tails if t[0] != e}
            for d in deps:
                self.marked.add(d)
            self.ops[e].append(dict(fn=lambda eng: eng.nop(), deps=deps,
                                    dma=False, tag="barrier"))

    def finalize(self, out_tokens):
        deps = set(t for t in out_tokens if t is not None)
        deps.update(self.dma_tail_tokens())
        for e in self.engs:
            if e == "y":
                continue
            t = self.tail_token(e)
            if t is not None:
                deps.add(t)
        for d in deps:
            self.marked.add(d)
        self.ops["y"].append(dict(fn=lambda eng: eng.nop(), deps=deps,
                                  dma=False, tag="final"))

    def emit(self, ctx):
        nc = self.nc
        doms = {d[0] for d in self.marked}
        sems = {d: ctx.enter_context(nc.semaphore(f"sem_{d}")) for d in sorted(doms)}
        rank = {}
        for d in self.marked:
            rank.setdefault(d[0], {})[d[1]] = 0
        for dom in rank:
            for r, idx in enumerate(sorted(rank[dom].keys())):
                rank[dom][idx] = r + 1

        def wait_val(tok):
            return rank[tok[0]][tok[1]] * (16 if len(tok[0]) > 1 else 1)

        plan = self

        def make_runner(e):
            def run(eng):
                wm = {}
                for i, o in enumerate(plan.ops[e]):
                    for tok in sorted(o["deps"]):
                        dom = tok[0]
                        v = wait_val(tok)
                        if wm.get(dom, 0) < v:
                            eng.wait_ge(sems[dom], v)
                            wm[dom] = v
                    if o["dma"]:
                        dom, j = o["dtok"]
                        if j > 0 and wm.get(dom, 0) < 16 * j:
                            eng.wait_ge(sems[dom], 16 * j)
                            wm[dom] = 16 * j
                    inst = o["fn"](eng)
                    if o["dma"]:
                        inst.then_inc(sems[o["dtok"][0]], 16)
                    elif (e, i) in plan.marked:
                        inst.then_inc(sems[e], 1)
            return run

        with nc.Block() as block:
            for e in self.engs:
                getattr(block, _BLOCK_METH[e])(make_runner(e))


def emit_rsqrt(pl, out_ap, in_ap, t_ap, key_out, key_in, key_t, newton=2):
    MAGIC = 0x5F3759DF
    fr = 1
    pl.op("v", lambda E, o=out_ap, i=in_ap: E.tensor_scalar(
        out=o.bitcast(I32), in0=i.bitcast(I32), scalar1=1, scalar2=None,
        op0=ALU.arith_shift_right), reads=(key_in,), writes=(key_out,), out_free=fr)
    pl.op("v", lambda E, o=out_ap: E.tensor_scalar(
        out=o.bitcast(I32), in0=o.bitcast(I32), scalar1=-1, scalar2=None,
        op0=ALU.bitwise_xor), reads=(key_out,), writes=(key_out,), out_free=fr)
    pl.op("v", lambda E, o=out_ap: E.tensor_scalar(
        out=o.bitcast(I32), in0=o.bitcast(I32), scalar1=MAGIC + 1, scalar2=None,
        op0=ALU.add), reads=(key_out,), writes=(key_out,), out_free=fr)
    for _ in range(newton):
        pl.op("v", lambda E, o=out_ap, t=t_ap: E.tensor_tensor(
            out=t, in0=o, in1=o, op=ALU.mult), reads=(key_out,), writes=(key_t,),
            out_free=fr)
        pl.op("v", lambda E, t=t_ap, i=in_ap: E.tensor_tensor(
            out=t, in0=t, in1=i, op=ALU.mult), reads=(key_t, key_in), writes=(key_t,),
            out_free=fr)
        pl.op("v", lambda E, t=t_ap: E.tensor_scalar(
            out=t, in0=t, scalar1=-0.5, scalar2=1.5, op0=ALU.mult, op1=ALU.add),
            reads=(key_t,), writes=(key_t,), out_free=fr)
        pl.op("v", lambda E, o=out_ap, t=t_ap: E.tensor_tensor(
            out=o, in0=o, in1=t, op=ALU.mult), reads=(key_out, key_t),
            writes=(key_out,), out_free=fr)


def ln_finalize(pl, SUM, SQ, M, R, T, pref, sum_keys, sq_keys, eps=1e-5, ncols=4,
                inv_n=1.0 / 1024):
    kM, kR, kT = (pref, "M"), (pref, "R"), (pref, "T")
    sl = (slice(None), slice(0, ncols))
    pl.op("v", lambda E: E.tensor_scalar(out=M[sl], in0=SUM[sl], scalar1=inv_n,
          scalar2=None, op0=ALU.mult), reads=tuple(sum_keys), writes=(kM,),
          out_free=ncols)
    pl.op("v", lambda E: E.tensor_scalar(out=T[sl], in0=SQ[sl], scalar1=inv_n,
          scalar2=None, op0=ALU.mult), reads=tuple(sq_keys), writes=(kT,),
          out_free=ncols)
    pl.op("v", lambda E: E.tensor_tensor(out=R[sl], in0=M[sl], in1=M[sl],
          op=ALU.mult), reads=(kM,), writes=(kR,), out_free=ncols)
    pl.op("v", lambda E: E.tensor_tensor(out=T[sl], in0=T[sl], in1=R[sl],
          op=ALU.subtract), reads=(kR, kT), writes=(kT,), out_free=ncols)
    pl.op("v", lambda E: E.tensor_scalar(out=T[sl], in0=T[sl], scalar1=eps,
          scalar2=None, op0=ALU.add), reads=(kT,), writes=(kT,), out_free=ncols)
    emit_rsqrt(pl, R[sl], T[sl], SUM[sl], kR, kT, (pref, "scr"))
    return kM, kR


def slopes16():
    start = 2.0 ** (-(2.0 ** (-(math.log2(16) - 3))))
    s = np.array([start ** (i + 1) for i in range(16)], dtype=np.float64)
    return s * (1.0 + 1e-5)


def host_consts():
    s = slopes16()
    i = np.arange(128.0)
    diff = i[None, :] - i[:, None]
    maskT = np.where(diff >= 0, np.exp(-s[:, None, None] * diff[None]), 0.0)
    maskT_t = maskT.transpose(1, 0, 2).reshape(128, 16 * 128)
    lamq = np.zeros((128, 1024))
    for ofc in range(8):
        for half in range(2):
            h = 2 * ofc + half
            lamq[half * 64:(half + 1) * 64, ofc * 128:(ofc + 1) * 128] = \
                np.exp(-s[h] * (i[None, :] + 1.0))
    lamk = np.zeros((128, 1024))
    for h in range(16):
        lamk[:, h * 64:(h + 1) * 64] = np.exp(-s[h] * (127.0 - i))[:, None]
    lamc = np.zeros((128, 512))
    for p2 in range(2):
        for f8 in range(8):
            h = p2 + 2 * f8
            lamc[p2 * 64:(p2 + 1) * 64, f8 * 64:(f8 + 1) * 64] = np.exp(-128.0 * s[h])
    sel8 = np.zeros((8, 1024))
    for e in range(8):
        sel8[e, e * 128:(e + 1) * 128] = 1.0
    bf = ml_dtypes.bfloat16
    return {
        "c_idn": np.eye(128, dtype=np.float32).astype(bf),
        "c_maskT": maskT_t.astype(np.float32).astype(bf),
        "c_lamq": lamq.astype(np.float32).astype(bf),
        "c_lamk": lamk.astype(np.float32).astype(bf),
        "c_lamc": lamc.astype(np.float32),
        "c_sel8": sel8.astype(np.float32).astype(bf),
    }



def emit_l0_attn(pl, nc, ctx, x_in, x_out, kx_in, kx_out, W, C):
    import os
    SKIP = set(os.environ.get("L0A_SKIP", "").split(","))
    E_ = ctx.enter_context
    sb = lambda name, shape, dt: E_(nc.sbuf_tensor(name, shape, dt)).ap()
    ps = lambda name, shape, dt: E_(nc.psum_tensor(name, shape, dt)).ap()

    WQ = sb("wq", [128, 8 * 1024], BF16)
    WK = sb("wk", [128, 8 * 1024], BF16)
    WV = sb("wv", [128, 8 * 1024], BF16)
    WO = sb("wo", [128, 8 * 1024], BF16)
    WG1 = sb("wg1", [128, 8 * 64], BF16)
    WG2 = sb("wg2", [64, 1024], BF16)
    GREP = sb("gamma_rep", [128, 1024], F32)
    BETA = sb("beta_rep", [128, 1024], F32)
    MASKT = sb("maskt", [128, 16 * 128], BF16)
    LAMQ = sb("lamq", [128, 1024], BF16)
    LAMK = sb("lamk", [128, 1024], BF16)
    LAMC = sb("lamc", [128, 512], F32)

    def wld(dst, src_name, kname):
        src = W[src_name]
        if src.shape[0] > 128:
            kc = src.shape[0] // 128
            view = src.rearrange("(kc p) o -> p kc o", p=128)
            d = dst.rearrange("p (kc o) -> p kc o", kc=kc)
        else:
            view, d = src, dst
        pl.op("p", lambda E, d=d, s=view: E.dma_start(out=d, in_=s), dma=True,
              writes=(kname,), tag=f"ld_{kname}")

    wld(WQ, "L0_attn_Wq", "WQ")
    wld(WK, "L0_attn_Wk", "WK")
    wld(WV, "L0_attn_Wv", "WV")
    wld(WO, "L0_attn_Wo", "WO")
    wld(WG1, "L0_attn_Wg1", "WG1")
    wld(WG2, "L0_attn_Wg2", "WG2")
    pl.op("y", lambda E: E.dma_start(
        out=GREP, in_=W["L0_attn_gamma"].unsqueeze(0).broadcast_to([128, 1024])),
        dma=True, writes=("GREP",))
    pl.op("y", lambda E: E.dma_start(
        out=BETA, in_=W["L0_attn_beta"].unsqueeze(0).broadcast_to([128, 1024])),
        dma=True, writes=("BETA",))
    pl.op("y", lambda E: E.dma_start(out=MASKT, in_=W["c_maskT"]), dma=True,
          writes=("MASKT",))
    pl.op("y", lambda E: E.dma_start(out=LAMQ, in_=W["c_lamq"]), dma=True,
          writes=("LAMQ",))
    pl.op("y", lambda E: E.dma_start(out=LAMK, in_=W["c_lamk"]), dma=True,
          writes=("LAMK",))
    pl.op("y", lambda E: E.dma_start(out=LAMC, in_=W["c_lamc"]), dma=True,
          writes=("LAMC",))
    for kc in range(8):
        pl.op("v", lambda E, kc=kc: E.tensor_scalar(
            out=WO[:, kc * 1024:(kc + 1) * 1024], in0=WO[:, kc * 1024:(kc + 1) * 1024],
            scalar1=0.5, scalar2=None, op0=ALU.mult),
            reads=("WO",), writes=("WO",), out_free=1024)

    XT = [sb(f"xt{i}", [128, 1024], F32) for i in range(4)]
    XTB = [sb(f"xtb{i}", [128, 1024], F32) for i in range(2)]
    XHB = sb("xhb", [128, 1024], BF16)
    XHB2 = sb("xhb2", [128, 1024], BF16)
    NMRS = sb("nmrs", [128, 8], F32)
    XHF = sb("xhf", [128, 8 * 512], BF16)
    QFM = sb("qfm", [128, 8 * 512], BF16)
    KFM = sb("kfm", [128, 8 * 512], BF16)
    QLAM = sb("qlam", [128, 8 * 512], BF16)
    KLAM = sb("klam", [128, 4 * 1024], BF16)
    VTM = sb("vtm", [128, 4 * 1024], BF16)
    SCBF = [sb(f"scbf{i}", [128, 512], BF16) for i in range(2)]
    OTM = sb("otm", [128, 4 * 1024], BF16)
    OB = sb("ob", [128, 4 * 1024], F32)
    ZBF = sb("zbf", [128, 1024], BF16)
    ZFM = sb("zfm", [128, 1024], BF16)
    G1BF = sb("g1bf", [64, 512], BF16)
    GT = [sb(f"gt{i}", [128, 1024], BF16) for i in range(4)]
    KV32 = sb("kv32", [128, 512], F32)
    KVBF = sb("kvbf", [128, 512], BF16)
    JV = sb("jv", [128, 1024], BF16)
    JS = sb("js", [128, 1024], BF16)
    S_SUM = sb("s_sum", [128, 8], F32)
    S_SQ = sb("s_sq", [128, 8], F32)
    S_M = sb("s_m", [128, 8], F32)
    S_R = sb("s_r", [128, 8], F32)
    S_T = sb("s_t", [128, 8], F32)
    I_SUM = sb("i_sum", [128, 4], F32)
    I_SQ = sb("i_sq", [128, 4], F32)
    I_M = sb("i_m", [128, 4], F32)
    I_R = sb("i_r", [128, 4], F32)
    B_SUM = sb("b_sum", [128, 4], F32)
    B_SQ = sb("b_sq", [128, 4], F32)
    B_M = sb("b_m", [128, 4], F32)
    B_R = sb("b_r", [128, 4], F32)

    PA = ps("pa", [128, 1024], F32)
    PB = ps("pb", [128, 1024], F32)
    PC = ps("pc", [128, 512], F32)
    PC2 = ps("pc2", [128, 512], F32)
    PD = ps("pd", [128, 512], F32)
    PT = ps("pt", [128, 1024], BF16)

    out_toks = []
    XHF_R = tuple(("XHF", u) for u in range(4))

    def pre_stats(st):
        q = st % 2
        for u in range(4):
            tl = st * 4 + u
            kx = ("XT", u)
            pl.op("y", lambda E, tl=tl, u=u: E.dma_start(
                out=XT[u], in_=x_in[tl * 128:(tl + 1) * 128, :]), dma=True,
                reads=((kx_in, tl),), writes=(kx,), tag=f"ldx{tl}")
            pl.op("s", lambda E, u=u, q=q: E.activation(
                out=JV, in_=XT[u], func=AF.Square,
                accum_out=S_SQ[:, q * 4 + u:q * 4 + u + 1]),
                reads=(kx,), writes=(("JV",), ("S_SQ", q, u)), out_free=1024)
            pl.op("s", lambda E, u=u, q=q: E.activation(
                out=JS, in_=XT[u], func=AF.Copy,
                accum_out=S_SUM[:, q * 4 + u:q * 4 + u + 1]),
                reads=(kx,), writes=(("JS",), ("S_SUM", q, u)), out_free=1024)

    for st in range(N_ST):
        if st % SEQ_ST == 0:
            pl.op("v", lambda E: E.memset(KV32, 0.0), writes=(("KV32",),),
                  out_free=512)
            pl.op("v", lambda E: E.memset(KVBF, 0.0), writes=(("KVBF",),),
                  out_free=512)

        q = st % 2
        qs = slice(q * 4, q * 4 + 4)
        if st == 0:
            pre_stats(0)
        ln_finalize(pl, S_SUM[:, qs], S_SQ[:, qs], S_M[:, qs], S_R[:, qs],
                    S_T[:, qs], f"Sst{q}",
                    [("S_SUM", q, u) for u in range(4)],
                    [("S_SQ", q, u) for u in range(4)])
        if 'lnT' not in SKIP:
            pl.op("v", lambda E, q=q, qs=qs: E.scalar_tensor_tensor(
                out=NMRS[:, qs], in0=S_M[:, qs], scalar=-1.0, in1=S_R[:, qs],
                op0=ALU.mult,
                op1=ALU.mult), reads=((f"Sst{q}", "M"), (f"Sst{q}", "R")),
                writes=(("NMRS", q),), out_free=4)
        for u in range(4) if 'lnT' not in SKIP else []:
            kx = ("XT", u)
            xhb = XHB if u % 2 == 0 else XHB2
            kxhb = ("XHB", u % 2)
            pl.op("s", lambda E, u=u, q=q, xhb=xhb: E.activation(
                out=xhb, in_=XT[u], func=AF.Identity,
                scale=S_R[:, q * 4 + u:q * 4 + u + 1],
                bias=NMRS[:, q * 4 + u:q * 4 + u + 1]), reads=(kx,),
                sreads=((f"Sst{q}", "R"), ("NMRS", q)), writes=(kxhb,),
                out_free=1024)
            for fc in range(8):
                pl.op("t", lambda E, fc=fc, xhb=xhb, kxhb=kxhb: E.transpose(
                    PT[:, fc * 128:(fc + 1) * 128], xhb[:, fc * 128:(fc + 1) * 128],
                    C["IDN"]), reads=(kxhb, "IDN"), writes=(("PT",),),
                    out_free=128)
            pl.op("v", lambda E, u=u: E.tensor_copy(
                out=XHF.rearrange("p (f c) -> p f c", f=8)[:, :, u * 128:(u + 1) * 128],
                in_=PT.rearrange("p (f c) -> p f c", f=8)),
                reads=(("PT",),), writes=(("XHF", u),), out_free=1024)
        if st + 1 < N_ST:
            pre_stats(st + 1)

        for ofc in range(8) if 'proj' not in SKIP else []:
            for kc in range(8):
                pl.op("t", lambda E, ofc=ofc, kc=kc: E.matmul(
                    PA[:, 0:512],
                    WQ[:, kc * 1024 + ofc * 128: kc * 1024 + (ofc + 1) * 128],
                    XHF[:, kc * 512:(kc + 1) * 512], start=(kc == 0), stop=(kc == 7)),
                    reads=(("WQ",),) + XHF_R, writes=(("PA", 0),), out_free=512)
            pl.op("s", lambda E, ofc=ofc: E.activation(
                out=QFM[:, ofc * 512:(ofc + 1) * 512], in_=PA[:, 0:512], func=AF.Silu),
                reads=(("PA", 0),), writes=(("QFM", ofc),), out_free=512)
            for kc in range(8):
                pl.op("t", lambda E, ofc=ofc, kc=kc: E.matmul(
                    PA[:, 512:1024],
                    WK[:, kc * 1024 + ofc * 128: kc * 1024 + (ofc + 1) * 128],
                    XHF[:, kc * 512:(kc + 1) * 512], start=(kc == 0), stop=(kc == 7)),
                    reads=(("WK",),) + XHF_R, writes=(("PA", 1),), out_free=512)
            pl.op("s", lambda E, ofc=ofc: E.activation(
                out=KFM[:, ofc * 512:(ofc + 1) * 512], in_=PA[:, 512:1024],
                func=AF.Silu),
                reads=(("PA", 1),), writes=(("KFM", ofc),), out_free=512)
        if 'proj' not in SKIP: pl.op("v", lambda E: E.tensor_tensor(
            out=QLAM.rearrange("p (o c i) -> p o c i", o=8, c=4),
            in0=QFM.rearrange("p (o c i) -> p o c i", o=8, c=4),
            in1=LAMQ.rearrange("p (o i) -> p o i", o=8).unsqueeze(2).broadcast_to(
                [128, 8, 4, 128]),
            op=ALU.mult),
            reads=tuple(("QFM", ofc) for ofc in range(8)) + (("LAMQ",),),
            writes=(("QLAM",),), out_free=4096)
        for u in range(4) if 'proj' not in SKIP else []:
            for half in range(2):
                for kc in range(8):
                    pl.op("t", lambda E, u=u, kc=kc, half=half: E.matmul(
                        PB[:, half * 512:(half + 1) * 512],
                        XHF[:, kc * 512 + u * 128: kc * 512 + (u + 1) * 128],
                        WV[:, kc * 1024 + half * 512: kc * 1024 + (half + 1) * 512],
                        start=(kc == 0), stop=(kc == 7)),
                        reads=(("WV",), ("XHF", u)), writes=(("PB",),), out_free=512)
            pl.op("s", lambda E, u=u: E.activation(
                out=VTM[:, u * 1024:(u + 1) * 1024], in_=PB, func=AF.Silu),
                reads=(("PB",),), writes=(("VTM", u),), out_free=1024)
            for fc in range(8):
                pl.op("t", lambda E, u=u, fc=fc: E.transpose(
                    PT[:, fc * 128:(fc + 1) * 128],
                    KFM[:, fc * 512 + u * 128: fc * 512 + (u + 1) * 128], C["IDN"]),
                    reads=(("KFM", fc), "IDN"), writes=(("PT",),), out_free=128)
            pl.op("v", lambda E, u=u: E.tensor_tensor(
                out=KLAM[:, u * 1024:(u + 1) * 1024], in0=PT, in1=LAMK, op=ALU.mult),
                reads=(("PT",), ("LAMK",)), writes=(("KLAM", u),), out_free=1024)
        for kc in range(8) if 'proj' not in SKIP else []:
            pl.op("t", lambda E, kc=kc: E.matmul(
                PC[0:64, :], WG1[:, kc * 64:(kc + 1) * 64],
                XHF[:, kc * 512:(kc + 1) * 512], start=(kc == 0), stop=(kc == 7)),
                reads=(("WG1",),) + XHF_R, writes=(("PC",),), out_free=512)
        if 'proj' not in SKIP: pl.op("v", lambda E: E.tensor_copy(out=G1BF, in_=PC[0:64, :]),
              reads=(("PC",),), writes=(("G1BF",),), out_free=512)
        for u in range(4):
            for half in range(2):
                pl.op("t", lambda E, u=u, half=half: E.matmul(
                    PB[:, half * 512:(half + 1) * 512], G1BF[:, u * 128:(u + 1) * 128],
                    WG2[:, half * 512:(half + 1) * 512], start=True, stop=True),
                    reads=(("G1BF",), ("WG2",)), writes=(("PB",),), out_free=512)
            pl.op("s", lambda E, u=u: E.activation(
                out=GT[u], in_=PB, func=AF.Tanh, scale=0.5),
                reads=(("PB",),), writes=(("GT", u),), out_free=1024)

        for u in range(4) if 'scan' not in SKIP else []:
            for hg in range(4):
                sc = SCBF[hg % 2]
                ksc = ("SCBF", hg % 2)
                pc = PC if hg % 2 == 0 else PC2
                kpc = ("PC", hg % 2)
                for hh in range(4):
                    h = hg * 4 + hh
                    tp = (0, (h % 2) * 64)
                    pl.op("t", lambda E, h=h, u=u, tp=tp: E.matmul(
                        PD[(h % 2) * 64:(h % 2) * 64 + 64,
                           (h // 2) * 64:(h // 2) * 64 + 64],
                        KLAM[:, u * 1024 + h * 64: u * 1024 + (h + 1) * 64],
                        VTM[:, u * 1024 + h * 64: u * 1024 + (h + 1) * 64],
                        start=True, stop=True, tile_position=tp),
                        reads=(("KLAM", u), ("VTM", u)), writes=(("PD", h),),
                        out_free=64)
                    qs = QFM[(h % 2) * 64:(h % 2) * 64 + 64,
                             (h // 2) * 512 + u * 128:(h // 2) * 512 + (u + 1) * 128]
                    ks = KFM[(h % 2) * 64:(h % 2) * 64 + 64,
                             (h // 2) * 512 + u * 128:(h // 2) * 512 + (u + 1) * 128]
                    pl.op("t", lambda E, hh=hh, qs=qs, ks=ks, pc=pc: E.matmul(
                        pc[:, hh * 128:(hh + 1) * 128], ks, qs, start=True, stop=True),
                        reads=(("KFM", h // 2), ("QFM", h // 2)), writes=(kpc,),
                        out_free=128)
                pl.op("v", lambda E, hg=hg, sc=sc, pc=pc: E.tensor_tensor(
                    out=sc, in0=pc, in1=MASKT[:, hg * 512:(hg + 1) * 512],
                    op=ALU.mult),
                    reads=(kpc, ("MASKT",)), writes=(ksc,), out_free=512)
                for hh in range(4):
                    h = hg * 4 + hh
                    pl.op("t", lambda E, h=h, u=u, sc=sc, hh=hh: E.matmul(
                        PA[:, h * 64:(h + 1) * 64], sc[:, hh * 128:(hh + 1) * 128],
                        VTM[:, u * 1024 + h * 64: u * 1024 + (h + 1) * 64],
                        start=True, stop=False),
                        reads=(ksc, ("VTM", u)), writes=(("PA", h // 8),),
                        out_free=64)
                    qlam_s = QLAM[(h % 2) * 64:(h % 2) * 64 + 64,
                                  (h // 2) * 512 + u * 128:(h // 2) * 512 + (u + 1) * 128]
                    kv_s = KVBF[(h % 2) * 64:(h % 2) * 64 + 64,
                                (h // 2) * 64:(h // 2) * 64 + 64]
                    pl.op("t", lambda E, qlam_s=qlam_s, kv_s=kv_s, h=h: E.matmul(
                        PA[:, h * 64:(h + 1) * 64], qlam_s, kv_s, start=False,
                        stop=True),
                        reads=(("QLAM",), ("KVBF",)), writes=(("PA", h // 8),),
                        out_free=64)
            pl.op("v", lambda E: E.tensor_tensor(out=KV32, in0=KV32, in1=LAMC,
                  op=ALU.mult), reads=(("KV32",), ("LAMC",)), writes=(("KV32",),),
                  out_free=512)
            pl.op("v", lambda E: E.tensor_tensor(out=KV32, in0=KV32, in1=PD,
                  op=ALU.add),
                  reads=(("KV32",),) + tuple(("PD", h) for h in range(16)),
                  writes=(("KV32",),), out_free=512)
            pl.op("v", lambda E: E.tensor_copy(out=KVBF, in_=KV32),
                  reads=(("KV32",),), writes=(("KVBF",),), out_free=512)
            pl.op("s", lambda E, u=u: E.activation(
                out=OTM[:, u * 1024:(u + 1) * 1024], in_=PA, func=AF.Copy,
                accum_out=I_SUM[:, u:u + 1]),
                reads=(("PA", 0), ("PA", 1)), writes=(("OTM", u), ("I_SUM", u)),
                out_free=1024)
            pl.op("v", lambda E, u=u: E.scalar_tensor_tensor(
                out=JV, in0=OTM[:, u * 1024:(u + 1) * 1024], scalar=1.0,
                in1=OTM[:, u * 1024:(u + 1) * 1024], op0=ALU.mult, op1=ALU.mult,
                accum_out=I_SQ[:, u:u + 1]), reads=(("OTM", u),),
                writes=(("JV",), ("I_SQ", u)), out_free=1024)
        ln_finalize(pl, I_SUM, I_SQ, I_M, I_R, S_T, "Ist",
                    [("I_SUM", u) for u in range(4)], [("I_SQ", u) for u in range(4)])

        for u in range(4) if 'gz' not in SKIP else []:
            tl = st * 4 + u
            gt = GT[u]
            kgt = ("GT", u)
            pl.op("v", lambda E, u=u: E.tensor_scalar(
                out=OB[:, u * 1024:(u + 1) * 1024],
                in0=OTM[:, u * 1024:(u + 1) * 1024],
                scalar1=I_M[:, u:u + 1], scalar2=I_R[:, u:u + 1],
                op0=ALU.subtract, op1=ALU.mult), reads=(("OTM", u),),
                sreads=(("Ist", "M"), ("Ist", "R")), writes=(("OB", u),),
                out_free=1024)
            pl.op("v", lambda E, u=u: E.tensor_tensor(
                out=OB[:, u * 1024:(u + 1) * 1024],
                in0=OB[:, u * 1024:(u + 1) * 1024], in1=GREP, op=ALU.mult),
                reads=(("OB", u), ("GREP",)), writes=(("OB", u),), out_free=1024)
            pl.op("v", lambda E, u=u: E.tensor_tensor(
                out=OB[:, u * 1024:(u + 1) * 1024],
                in0=OB[:, u * 1024:(u + 1) * 1024], in1=BETA, op=ALU.add),
                reads=(("OB", u), ("BETA",)), writes=(("OB", u),), out_free=1024)
            pl.op("v", lambda E, u=u, gt=gt: E.scalar_tensor_tensor(
                out=ZBF, in0=gt, scalar=1.0, in1=OB[:, u * 1024:(u + 1) * 1024],
                op0=ALU.add, op1=ALU.mult), reads=(kgt, ("OB", u)),
                writes=(("ZBF",),), out_free=1024)
            for fc in range(8):
                pl.op("t", lambda E, fc=fc: E.transpose(
                    PT[:, fc * 128:(fc + 1) * 128], ZBF[:, fc * 128:(fc + 1) * 128],
                    C["IDN"]), reads=(("ZBF",), "IDN"), writes=(("PT",),),
                    out_free=128)
            pl.op("v", lambda E: E.tensor_copy(out=ZFM, in_=PT),
                  reads=(("PT",),), writes=(("ZFM",),), out_free=1024)
            for half in range(2):
                for fc in range(8):
                    pl.op("t", lambda E, fc=fc, half=half: E.matmul(
                        PB[:, half * 512:(half + 1) * 512],
                        ZFM[:, fc * 128:(fc + 1) * 128],
                        WO[:, fc * 1024 + half * 512: fc * 1024 + (half + 1) * 512],
                        start=(fc == 0), stop=(fc == 7)),
                        reads=(("ZFM",), ("WO",)), writes=(("PB",),), out_free=512)
            pl.op("s", lambda E, u=u: E.activation(
                out=OB[:, u * 1024:(u + 1) * 1024], in_=PB, func=AF.Copy,
                accum_out=B_SUM[:, u:u + 1]), reads=(("PB",),),
                writes=(("OB", u), ("B_SUM", u)), out_free=1024)
            pl.op("s", lambda E, u=u: E.activation(
                out=JV, in_=OB[:, u * 1024:(u + 1) * 1024], func=AF.Square,
                accum_out=B_SQ[:, u:u + 1]), reads=(("OB", u),),
                writes=(("JV",), ("B_SQ", u)), out_free=1024)
        ln_finalize(pl, B_SUM, B_SQ, B_M, B_R, S_T, "Bst",
                    [("B_SUM", u) for u in range(4)], [("B_SQ", u) for u in range(4)])
        for u in range(4):
            tl = st * 4 + u
            xb = XTB[u % 2]
            kx = ("XTB", u % 2)
            pl.op("s", lambda E, tl=tl, xb=xb: E.dma_start(
                out=xb, in_=x_in[tl * 128:(tl + 1) * 128, :]), dma=True,
                reads=((kx_in, tl),), writes=(kx,))
            pl.op("v", lambda E, u=u: E.tensor_scalar(
                out=OB[:, u * 1024:(u + 1) * 1024],
                in0=OB[:, u * 1024:(u + 1) * 1024],
                scalar1=B_M[:, u:u + 1], scalar2=B_R[:, u:u + 1],
                op0=ALU.subtract, op1=ALU.mult), reads=(("OB", u),),
                sreads=(("Bst", "M"), ("Bst", "R")), writes=(("OB", u),),
                out_free=1024)
            pl.op("p", lambda E, u=u, xb=xb: E.tensor_tensor(
                out=xb, in0=OB[:, u * 1024:(u + 1) * 1024], in1=xb, op=ALU.add),
                reads=(("OB", u), kx), writes=(kx,), out_free=1024)
            tok = pl.op("y", lambda E, tl=tl, xb=xb: E.dma_start(
                out=x_out[tl * 128:(tl + 1) * 128, :], in_=xb), dma=True,
                reads=(kx,), writes=((kx_out, tl),), tag="stx")
            out_toks.append(tok)
    return out_toks



def emit_l0_ffn(pl, nc, ctx, x_in, x_out, kx_in, kx_out, W, C):
    E_ = ctx.enter_context
    sb = lambda name, shape, dt: E_(nc.sbuf_tensor(name, shape, dt)).ap()
    ps = lambda name, shape, dt: E_(nc.psum_tensor(name, shape, dt)).ap()
    ntok = N_ST * 512

    xhat = nc.dram_tensor("l0f_xhat", [N_ST * 128, 4096], BF16, kind="Internal").ap()
    accA = nc.dram_tensor("l0f_accA", [ntok, D], F32, kind="Internal").ap()
    accB = nc.dram_tensor("l0f_accB", [ntok, D], F32, kind="Internal").ap()
    cc_in = nc.dram_tensor("l0f_ccin", [128, 16], F32, kind="Internal").ap()
    cc_out = nc.dram_tensor("l0f_ccout", [128, 16], F32, kind="Internal").ap()

    GWT = sb("gwt", [128, 64], BF16)
    GW32 = sb("gw32", [128, 64], F32)
    GWLO = sb("gwlo", [128, 64], BF16)
    XH32 = sb("fxh32", [128, 1024], F32)
    XLOB = sb("fxlob", [128, 1024], BF16)
    XLOFS = sb("fxlofs", [128, 4096], BF16)
    LBB = sb("lbb", [128, 8], F32)
    SEL_ALL = sb("sel_all", [128, NT * 8], BF16)
    ONES1 = sb("ones1", [1, 128], BF16)
    ONESC = sb("onesc", [128, 1], BF16)
    WMR = sb("wmr", [128, 8], F32)
    WMS = sb("wms", [1, 16], F32)
    NAT2 = [sb(f"natw{i}", [128, 4096], BF16) for i in range(2)]
    XT = [sb(f"fxt{i}", [128, 1024], F32) for i in range(4)]
    XHB = sb("fxhb", [128, 1024], BF16)
    XHFS = sb("fxhfs", [128, 4096], BF16)
    GSB = sb("fgsb", [128, 2048], BF16)
    USB = sb("fusb", [128, 2048], BF16)
    H2ALL = sb("fh2", [128, 16 * 512], BF16)
    CB = [sb(f"fcb{i}", [128, 512], F32) for i in range(4)]
    CF1 = sb("fcf1", [1, 512], BF16)
    CF8 = sb("fcf8", [8, 512], BF16)
    SEL8 = sb("fsel8", [8, 1024], BF16)
    COEFT = sb("fcoeft", [128, 32], BF16)
    ACC_SB = sb("facc", [128, 1024], F32)
    ACCR = [sb(f"faccr{i}", [128, 1024], F32) for i in range(2)]
    LG = sb("flg", [128, 8], F32)
    EQ1 = sb("feq1", [128, 8], F32)
    EQ2 = sb("feq2", [128, 8], F32)
    LGM = sb("flgm", [128, 8], F32)
    M12 = sb("fm12", [128, 2], F32)
    TH = sb("fth", [128, 2], F32)
    S12 = sb("fs12", [128, 2], F32)
    DEN = sb("fden", [128, 1], F32)
    WEB = sb("fweb", [128, 8], BF16)
    ST2 = sb("fst2", [128, 16], BF16)
    WEF = sb("fwef", [128, 8], F32)
    JV = sb("fjv", [128, 1024], BF16)
    JS = sb("fjs", [128, 1024], BF16)
    STR = sb("fstr", [1, 16], F32)
    CCZ = sb("fccz", [128, 16], F32)
    S_SUM = sb("fs_sum", [128, 4], F32)
    S_SQ = sb("fs_sq", [128, 4], F32)
    S_M = sb("fs_m", [128, 4], F32)
    S_R = sb("fs_r", [128, 4], F32)
    S_T = sb("fs_t", [128, 4], F32)
    B_SUM = sb("fb_sum", [128, 4], F32)
    B_SQ = sb("fb_sq", [128, 4], F32)
    B_M = sb("fb_m", [128, 4], F32)
    B_R = sb("fb_r", [128, 4], F32)

    PGa = ps("fpga", [128, 1024], F32)
    PGb = ps("fpgb", [128, 1024], F32)
    PACC = ps("fpacc", [128, 1024], F32)
    PT = ps("fpt", [128, 1024], BF16)
    PST = ps("fpst", [1, 16], F32)
    PTH = [PT[:, 0:512], PACC.bitcast(BF16)[:, 0:512]]

    out_toks = []

    pl.op("v", lambda E: E.memset(ONES1, 1.0), writes=("ONES1",), out_free=128)
    pl.op("v", lambda E: E.memset(ONESC, 1.0), writes=("ONESC",), out_free=1)
    pl.op("y", lambda E: E.dma_start(out=SEL8, in_=W["c_sel8"]), dma=True,
          writes=(("SEL8",),))
    pl.op("y", lambda E: E.dma_start(out=GW32, in_=W["d_gwT"]), dma=True,
          writes=("GW32",))
    pl.op("v", lambda E: E.tensor_copy(out=GWT, in_=GW32),
          reads=("GW32",), writes=("GWT",), out_free=64)
    pl.op("v", lambda E: E.scalar_tensor_tensor(out=GWLO, in0=GWT, scalar=-1.0,
          in1=GW32, op0=ALU.mult, op1=ALU.add), reads=("GWT", "GW32"),
          writes=("GWLO",), out_free=64)
    pl.op("v", lambda E: E.memset(CCZ, 0.0), writes=("CCZ",), out_free=16)
    pl.op("y", lambda E: E.dma_start(out=cc_in, in_=CCZ), dma=True,
          reads=("CCZ",), writes=(("cc_in",),))
    pl.op("y", lambda E: E.dma_start(
        out=LBB, in_=W["L0_ffn_lb_bias"].unsqueeze(0).broadcast_to([128, 8])),
        dma=True, writes=("LBB",))

    for st in range(N_ST):
        for u in range(4):
            tl = st * 4 + u
            kx = ("fXT", u)
            pl.op("y", lambda E, tl=tl, u=u: E.dma_start(
                out=XT[u], in_=x_in[tl * 128:(tl + 1) * 128, :]), dma=True,
                reads=((kx_in, tl),), writes=(kx,))
            pl.op("s", lambda E, u=u: E.activation(
                out=JV, in_=XT[u], func=AF.Square, accum_out=S_SQ[:, u:u + 1]),
                reads=(kx,), writes=(("fJV",), ("fS_SQ", u)), out_free=1024)
            pl.op("s", lambda E, u=u: E.activation(
                out=JS, in_=XT[u], func=AF.Copy, accum_out=S_SUM[:, u:u + 1]),
                reads=(kx,), writes=(("fJS",), ("fS_SUM", u)), out_free=1024)
        ln_finalize(pl, S_SUM, S_SQ, S_M, S_R, S_T, "fSst",
                    [("fS_SUM", u) for u in range(4)],
                    [("fS_SQ", u) for u in range(4)])
        for u in range(4):
            kx = ("fXT", u)
            pl.op("v", lambda E, u=u: E.tensor_scalar(
                out=XH32, in0=XT[u], scalar1=S_M[:, u:u + 1], scalar2=S_R[:, u:u + 1],
                op0=ALU.subtract, op1=ALU.mult), reads=(kx,),
                sreads=(("fSst", "M"), ("fSst", "R")), writes=(("fXH32",),),
                out_free=512)
            pl.op("v", lambda E: E.tensor_copy(out=XHB, in_=XH32),
                  reads=(("fXH32",),), writes=(("fXHB",),), out_free=512)
            pl.op("v", lambda E: E.scalar_tensor_tensor(out=XLOB, in0=XHB,
                  scalar=-1.0, in1=XH32, op0=ALU.mult, op1=ALU.add),
                  reads=(("fXHB",), ("fXH32",)), writes=(("fXLOB",),),
                  out_free=1024)
            for half in range(2):
                for fc2 in range(4):
                    fc = half * 4 + fc2
                    pl.op("t", lambda E, fc=fc, fc2=fc2, half=half: E.transpose(
                        PTH[half][:, fc2 * 128:(fc2 + 1) * 128],
                        XHB[:, fc * 128:(fc + 1) * 128],
                        C["IDN"]), reads=(("fXHB",), "IDN"),
                        writes=(("fPT", half),), out_free=128)
                pl.op("v", lambda E, u=u, half=half: E.tensor_copy(
                    out=XHFS.rearrange("p (f c) -> p f c", f=8)[
                        :, half * 4:(half + 1) * 4, u * 128:(u + 1) * 128],
                    in_=PTH[half].rearrange("p (f c) -> p f c", f=4)),
                    reads=(("fPT", half),), writes=(("fXHFS", 0),), out_free=512)
            for half in range(2):
                for fc2 in range(4):
                    fc = half * 4 + fc2
                    pl.op("t", lambda E, fc=fc, fc2=fc2, half=half: E.transpose(
                        PTH[half][:, fc2 * 128:(fc2 + 1) * 128],
                        XLOB[:, fc * 128:(fc + 1) * 128],
                        C["IDN"]), reads=(("fXLOB",), "IDN"),
                        writes=(("fPT", half),), out_free=128)
                pl.op("v", lambda E, u=u, half=half: E.tensor_copy(
                    out=XLOFS.rearrange("p (f c) -> p f c", f=8)[
                        :, half * 4:(half + 1) * 4, u * 128:(u + 1) * 128],
                    in_=PTH[half].rearrange("p (f c) -> p f c", f=4)),
                    reads=(("fPT", half),), writes=(("fXLOFS",),), out_free=512)
        pl.op("y", lambda E, st=st: E.dma_start(
            out=xhat[st * 128:(st + 1) * 128, :], in_=XHFS), dma=True,
            reads=(("fXHFS", 0),), writes=(("xhat", st),))
        for u in range(4):
            tl = st * 4 + u
            for trm in range(3):
                xs, ws, kxs = ((XHFS, GWT, "fXHFS"), (XHFS, GWLO, "fXHFS"),
                               (XLOFS, GWT, "fXLOFS"))[trm]
                for kc in range(8):
                    pl.op("t", lambda E, kc=kc, u=u, xs=xs, ws=ws, trm=trm:
                          E.matmul(
                        PGa[:, 0:8], xs[:, kc * 512 + u * 128: kc * 512 + (u + 1) * 128],
                        ws[:, kc * 8:(kc + 1) * 8],
                        start=(trm == 0 and kc == 0), stop=(trm == 2 and kc == 7)),
                        reads=((kxs if kxs != "fXHFS" else ("fXHFS", 0)),
                               ("GWT",), ("GWLO",)), writes=(("fPGpl",),),
                        out_free=8)
            pl.op("v", lambda E: E.tensor_tensor(out=LG, in0=PGa[:, 0:8], in1=LBB,
                  op=ALU.add), reads=(("fPGpl",), ("LBB",)), writes=(("fLG",),),
                  out_free=8)
            pl.op("v", lambda E: E.reduce_max(out=M12[:, 0:1], in_=LG, axis=AX),
                  reads=(("fLG",),), writes=(("fM12",),), out_free=1)
            pl.op("v", lambda E: E.tensor_scalar(out=EQ1, in0=LG,
                  scalar1=M12[:, 0:1], scalar2=None, op0=ALU.is_equal),
                  reads=(("fLG",),), sreads=(("fM12",),), writes=(("fEQ1",),),
                  out_free=8)
            pl.op("v", lambda E: E.scalar_tensor_tensor(out=LGM, in0=EQ1,
                  scalar=-1e30, in1=LG, op0=ALU.mult, op1=ALU.add),
                  reads=(("fEQ1",), ("fLG",)), writes=(("fLGM",),), out_free=8)
            pl.op("v", lambda E: E.reduce_max(out=M12[:, 1:2], in_=LGM, axis=AX),
                  reads=(("fLGM",),), writes=(("fM12",),), out_free=1)
            pl.op("v", lambda E: E.tensor_scalar(out=EQ2, in0=LGM,
                  scalar1=M12[:, 1:2], scalar2=None, op0=ALU.is_equal),
                  reads=(("fLGM",),), sreads=(("fM12",),), writes=(("fEQ2",),),
                  out_free=8)
            pl.op("s", lambda E: E.activation(out=TH, in_=M12, func=AF.Tanh,
                  scale=0.5), reads=(("fM12",),), writes=(("fTH",),), out_free=2)
            pl.op("v", lambda E: E.tensor_scalar(out=S12, in0=TH, scalar1=0.5,
                  scalar2=0.5, op0=ALU.mult, op1=ALU.add), reads=(("fTH",),),
                  writes=(("fS12",),), out_free=2)
            pl.op("v", lambda E: E.reduce_sum(out=DEN, in_=S12, axis=AX),
                  reads=(("fS12",),), writes=(("fDEN",),), out_free=1)
            pl.op("v", lambda E: E.tensor_scalar(out=DEN, in0=DEN, scalar1=1e-8,
                  scalar2=None, op0=ALU.add), reads=(("fDEN",),),
                  writes=(("fDEN",),), out_free=1)
            pl.op("v", lambda E: E.reciprocal(out=DEN, in_=DEN),
                  reads=(("fDEN",),), writes=(("fDEN",),), out_free=1)
            pl.op("v", lambda E: E.tensor_scalar(out=S12, in0=S12, scalar1=DEN,
                  scalar2=None, op0=ALU.mult), reads=(("fS12",),),
                  sreads=(("fDEN",),), writes=(("fS12",),), out_free=2)
            pl.op("v", lambda E: E.tensor_scalar(out=WEF, in0=EQ1,
                  scalar1=S12[:, 0:1], scalar2=None, op0=ALU.mult),
                  reads=(("fEQ1",),), sreads=(("fS12",),), writes=(("fWEF",),),
                  out_free=8)
            pl.op("v", lambda E: E.scalar_tensor_tensor(out=WEF, in0=EQ2,
                  scalar=S12[:, 1:2], in1=WEF, op0=ALU.mult, op1=ALU.add),
                  reads=(("fEQ2",), ("fWEF",)), sreads=(("fS12",),),
                  writes=(("fWEF",),), out_free=8)
            pl.op("v", lambda E: E.tensor_copy(out=ST2[:, 8:16], in_=WEF),
                  reads=(("fWEF",),), writes=(("fST2",),), out_free=8)
            pl.op("v", lambda E, tl=tl: E.tensor_tensor(
                out=ST2[:, 0:8], in0=EQ1, in1=EQ2, op=ALU.add),
                reads=(("fEQ1",), ("fEQ2",)), writes=(("fST2",),), out_free=8)
            pl.op("v", lambda E, tl=tl: E.tensor_copy(
                out=SEL_ALL[:, tl * 8:(tl + 1) * 8], in_=ST2[:, 0:8]),
                reads=(("fST2",),), writes=(("SEL", tl),), out_free=8)
            pl.op("t", lambda E, tl=tl: E.matmul(
                PST[0:1, 0:16], ONESC, ST2,
                start=(tl == 0), stop=(tl == N_ST * 4 - 1)),
                reads=(("fST2",), ("ONESC",)), writes=(("fPST",),), out_free=16)
    pl.op("v", lambda E: E.tensor_copy(out=STR, in_=PST),
          reads=(("fPST",),), writes=(("STR",),), out_free=16)
    pl.op("y", lambda E: E.dma_start(out=cc_in[0:1, 0:16], in_=STR), dma=True,
          reads=(("STR",),), writes=(("cc_in",),))
    pl.op("p", lambda E: E.collective_compute(
        "AllReduce", ALU.add, replica_groups=[[0, 1, 2, 3, 4, 5, 6, 7]],
        ins=[cc_in.opt()], outs=[cc_out.opt()]),
        reads=(("cc_in",),), writes=(("cc_out",),), out_free=16, tag="allreduce")
    pl.op("y", lambda E: E.dma_start(out=WMS, in_=cc_out[0:1, 0:16]), dma=True,
          reads=(("cc_out",),), writes=(("WMS",),))
    pl.op("v", lambda E: E.tensor_scalar(out=WMS[0:1, 0:8], in0=WMS[0:1, 0:8],
          scalar1=1.0, scalar2=None, op0=ALU.max), reads=(("WMS",),),
          writes=(("WMS",),), out_free=8)
    pl.op("v", lambda E: E.reciprocal(out=WMS[0:1, 0:8], in_=WMS[0:1, 0:8]),
          reads=(("WMS",),), writes=(("WMS",),), out_free=8)
    pl.op("v", lambda E: E.tensor_tensor(out=WMS[0:1, 0:8], in0=WMS[0:1, 8:16],
          in1=WMS[0:1, 0:8], op=ALU.mult), reads=(("WMS",),), writes=(("WMS",),),
          out_free=8)
    pl.op("v", lambda E: E.tensor_copy(out=CF1[0:1, 0:8], in_=WMS[0:1, 0:8]),
          reads=(("WMS",),), writes=(("CF1",),), out_free=8)
    pl.op("t", lambda E: E.matmul(PGa[:, 0:8], ONES1, CF1[0:1, 0:8], start=True,
          stop=True), reads=(("CF1",), ("ONES1",)), writes=(("fPGpl",),),
          out_free=8)
    pl.op("v", lambda E: E.tensor_copy(out=WMR, in_=PGa[:, 0:8]),
          reads=(("fPGpl",),), writes=(("WMR",),), out_free=8)
    pl.block_barrier()

    natctr = [0]

    def load_mat_T(dst, kdst, src_ap, out_dim, in_dim, tag):
        noc = out_dim // 128
        nkc = in_dim // 128
        oc_grp = max(1, 4096 // in_dim)
        for g0 in range(0, noc, oc_grp):
            g1 = min(noc, g0 + oc_grp)
            natctr[0] += 1
            NAT = NAT2[natctr[0] % 2]
            knat = ("NAT", natctr[0] % 2)
            view = src_ap[g0 * 128:g1 * 128, :].rearrange("(oc p) i -> p oc i",
                                                          p=128)
            pl.op("p", lambda E, view=view, n=(g1 - g0), NAT=NAT: E.dma_start(
                out=NAT.rearrange("p (oc i) -> p oc i", oc=oc_grp if in_dim * oc_grp == 4096 else 1)[:, 0:n, 0:in_dim]
                if in_dim <= 4096 else None, in_=view),
                dma=True, writes=(knat,), tag=f"ld_{tag}")
            kw = (("fPT", 0), ("fPT", 1))
            for kc in range(nkc):
                for oc in range(g0, g1):
                    pl.op("t", lambda E, kc=kc, oc=oc, g0=g0,
                          NAT=NAT: E.transpose(
                        PT[:, (oc - g0) * 128:(oc - g0 + 1) * 128],
                        NAT[:, (oc - g0) * in_dim + kc * 128:
                            (oc - g0) * in_dim + (kc + 1) * 128],
                        C["IDN"]), reads=(knat, "IDN"),
                        writes=kw, out_free=128)
                pl.op("v", lambda E, kc=kc, g0=g0, g1=g1: E.tensor_copy(
                    out=dst[:, kc * out_dim + g0 * 128: kc * out_dim + g1 * 128],
                    in_=PT[:, 0:(g1 - g0) * 128]),
                    reads=kw,
                    writes=(kdst,), out_free=(g1 - g0) * 128)

    def ldxh(st):
        pl.op("p", lambda E, st=st: E.dma_start(
            out=XHFS, in_=xhat[st * 128:(st + 1) * 128, :]), dma=True,
            reads=(("xhat", st),), writes=(("fXHFS", 0),))

    def expert_pass(EG, EU, DW, experts, acc, acc_key):
        for i, e in enumerate(experts):
            load_mat_T(EG[i], ("EGWT", i), W["L0_ffn_expert_gate_w"][e], 512,
                       1024, f"eg{e}")
            load_mat_T(EU[i], ("EUWT", i), W["L0_ffn_expert_up_w"][e], 512,
                       1024, f"eu{e}")
            load_mat_T(DW[i], ("DWT", i), W["L0_ffn_expert_down_w"][e], 1024,
                       512, f"ed{e}")
        for st in range(N_ST):
            XHFSx = XHFS
            kxh = ("fXHFS", 0)
            ldxh(st)
            for u in range(4):
                tl = st * 4 + u
                pl.op("v", lambda E, tl=tl, u=u: E.tensor_tensor(
                    out=COEFT[:, u * 8:(u + 1) * 8],
                    in0=SEL_ALL[:, tl * 8:(tl + 1) * 8], in1=WMR,
                    op=ALU.mult), reads=(("SEL", tl), ("WMR",)),
                    writes=(("COEFT", u),), out_free=8)
                pl.op("t", lambda E, u=u: E.transpose(
                    PT[0:8, u * 128:(u + 1) * 128],
                    COEFT[:, u * 8:(u + 1) * 8],
                    C["IDN"]), reads=(("COEFT", u), "IDN"),
                    writes=(("fPT", 0), ("fPT", 1)), out_free=128)
            pl.op("v", lambda E: E.tensor_copy(out=CF8, in_=PT[0:8, 0:512]),
                  reads=(("fPT", 0), ("fPT", 1)), writes=(("CF8",),),
                  out_free=512)
            for i, e in enumerate(experts):
                pl.op("t", lambda E, e=e: E.matmul(
                    PACC[:, 0:512], SEL8[:, e * 128:(e + 1) * 128], CF8,
                    start=True, stop=True), reads=(("CF8",), ("SEL8",)),
                    writes=(("fPACC",),), out_free=512)
                pl.op("s", lambda E, i=i: E.activation(out=CB[i],
                      in_=PACC[:, 0:512], func=AF.Copy), reads=(("fPACC",),),
                      writes=(("CBk", i),), out_free=512)
                for hc in range(4):
                    pg = PGa if hc < 2 else PGb
                    kpg = ("fPG", hc // 2)
                    for kc in range(8):
                        pl.op("t", lambda E, i=i, hc=hc, kc=kc, pg=pg,
                              XHFSx=XHFSx: E.matmul(
                            pg[:, (hc % 2) * 512:(hc % 2 + 1) * 512],
                            EG[i][:, kc * 512 + hc * 128: kc * 512 + (hc + 1) * 128],
                            XHFSx[:, kc * 512:(kc + 1) * 512],
                            start=(kc == 0), stop=(kc == 7)),
                            reads=(("EGWT", i), kxh), writes=(kpg,),
                            out_free=512)
                    if hc % 2 == 1:
                        pl.op("s", lambda E, hc=hc, pg=pg: E.activation(
                            out=GSB[:, (hc - 1) * 512:(hc + 1) * 512], in_=pg,
                            func=AF.Silu), reads=(kpg,), writes=(("GSB",),),
                            out_free=1024)
                for hc in range(4):
                    pg = PGa if hc < 2 else PGb
                    kpg = ("fPG", hc // 2)
                    for kc in range(8):
                        pl.op("t", lambda E, i=i, hc=hc, kc=kc, pg=pg,
                              XHFSx=XHFSx: E.matmul(
                            pg[:, (hc % 2) * 512:(hc % 2 + 1) * 512],
                            EU[i][:, kc * 512 + hc * 128: kc * 512 + (hc + 1) * 128],
                            XHFSx[:, kc * 512:(kc + 1) * 512],
                            start=(kc == 0), stop=(kc == 7)),
                            reads=(("EUWT", i), kxh, ("GSB",)),
                            writes=(kpg,), out_free=512)
                    if hc % 2 == 1:
                        for h2 in (hc - 1, hc):
                            pl.op("v", lambda E, i=i, h2=h2, pg=pg: E.tensor_tensor(
                                out=USB[:, h2 * 512:(h2 + 1) * 512],
                                in0=pg[:, (h2 % 2) * 512:(h2 % 2 + 1) * 512],
                                in1=CB[i], op=ALU.mult),
                                reads=(kpg, ("CBk", i)),
                                writes=(("USB",),), out_free=512)
                pl.op("v", lambda E, i=i: E.tensor_tensor(
                    out=H2ALL[:, i * 2048:(i + 1) * 2048], in0=USB, in1=GSB,
                    op=ALU.mult), reads=(("USB",), ("GSB",)),
                    writes=(("H2", i),), out_free=2048)
            for tt in range(4):
                for half in range(2):
                    for ch in range(16):
                        i, hc = ch // 4, ch % 4
                        pl.op("t", lambda E, i=i, hc=hc, tt=tt, half=half, ch=ch:
                              E.matmul(
                            PACC[:, half * 512:(half + 1) * 512],
                            H2ALL[:, (i * 4 + hc) * 512 + tt * 128:
                                  (i * 4 + hc) * 512 + (tt + 1) * 128],
                            DW[i][:, hc * 1024 + half * 512:
                                  hc * 1024 + (half + 1) * 512],
                            start=(ch == 0), stop=(ch == 15)),
                            reads=(("H2", i), ("DWT", i)), writes=(("fPACC",),),
                            out_free=512)
                pl.op("s", lambda E: E.activation(out=ACC_SB, in_=PACC,
                      func=AF.Copy), reads=(("fPACC",),), writes=(("ACC_SB",),),
                      out_free=1024)
                tl = st * 4 + tt
                tok = pl.op("s", lambda E, tl=tl, acc=acc: E.dma_start(
                    out=acc[tl * 128:(tl + 1) * 128, :], in_=ACC_SB), dma=True,
                    reads=(("ACC_SB",),), writes=((acc_key, tl),))

    with contextlib.ExitStack() as ctx_ab:
        sb2 = lambda name, shape, dt: ctx_ab.enter_context(
            nc.sbuf_tensor(name, shape, dt)).ap()
        EGWT = [sb2(f"egwt{i}", [128, 8 * 512], BF16) for i in range(4)]
        EUWT = [sb2(f"euwt{i}", [128, 8 * 512], BF16) for i in range(4)]
        DWT = [sb2(f"dwt{i}", [128, 4 * 1024], BF16) for i in range(4)]
        expert_pass(EGWT, EUWT, DWT, [0, 1, 2, 3], accA, "accA")
        expert_pass(EGWT, EUWT, DWT, [4, 5, 6, 7], accB, "accB")
    pl.block_barrier()

    with contextlib.ExitStack() as ctx_c:
        sb3 = lambda name, shape, dt: ctx_c.enter_context(
            nc.sbuf_tensor(name, shape, dt)).ap()
        SGWT = sb3("sgwt", [128, 8 * 1024], BF16)
        SUWT = sb3("suwt", [128, 8 * 1024], BF16)
        SDWT = sb3("sdwt", [128, 8 * 1024], BF16)
        OB4 = [sb3(f"fob{i}", [128, 1024], F32) for i in range(4)]
        load_mat_T(SGWT, ("SGWT",), W["L0_ffn_shared_gate_w"], 1024, 1024, "sg")
        load_mat_T(SUWT, ("SUWT",), W["L0_ffn_shared_up_w"], 1024, 1024, "su")
        load_mat_T(SDWT, ("SDWT",), W["L0_ffn_shared_down_w"], 1024, 1024, "sd")
        XHFS2 = sb3("fxhfs2c", [128, 4096], BF16)

        def stageG(st):
            q = st % 2
            XHFSx = XHFS if q == 0 else XHFS2
            kxh = ("fXHFS", q)
            pl.op("p", lambda E, st=st, XHFSx=XHFSx: E.dma_start(
                out=XHFSx, in_=xhat[st * 128:(st + 1) * 128, :]), dma=True,
                reads=(("xhat", st),), writes=(kxh,))
            for rnd in range(2):
                for hc in range(4):
                    hco = rnd * 4 + hc
                    pg = PGa if hc < 2 else PGb
                    kpg = ("fPG", hc // 2)
                    for kc in range(8):
                        pl.op("t", lambda E, hco=hco, hc=hc, kc=kc, pg=pg,
                              XHFSx=XHFSx, kxh=kxh: E.matmul(
                            pg[:, (hc % 2) * 512:(hc % 2 + 1) * 512],
                            SGWT[:, kc * 1024 + hco * 128: kc * 1024 + (hco + 1) * 128],
                            XHFSx[:, kc * 512:(kc + 1) * 512],
                            start=(kc == 0), stop=(kc == 7)),
                            reads=(("SGWT",), kxh), writes=(kpg,),
                            out_free=512)
                    if hc % 2 == 1:
                        pl.op("s", lambda E, hc=hc, pg=pg: E.activation(
                            out=GSB[:, (hc - 1) * 512:(hc + 1) * 512], in_=pg,
                            func=AF.Silu), reads=(kpg,), writes=(("GSB",),),
                            out_free=1024)
                for hc in range(4):
                    hco = rnd * 4 + hc
                    pg = PGa if hc < 2 else PGb
                    kpg = ("fPG", hc // 2)
                    for kc in range(8):
                        pl.op("t", lambda E, hco=hco, hc=hc, kc=kc, pg=pg,
                              XHFSx=XHFSx, kxh=kxh: E.matmul(
                            pg[:, (hc % 2) * 512:(hc % 2 + 1) * 512],
                            SUWT[:, kc * 1024 + hco * 128: kc * 1024 + (hco + 1) * 128],
                            XHFSx[:, kc * 512:(kc + 1) * 512],
                            start=(kc == 0), stop=(kc == 7)),
                            reads=(("SUWT",), kxh, ("GSB",)),
                            writes=(kpg,), out_free=512)
                    if hc % 2 == 1:
                        pl.op("s", lambda E, hc=hc, pg=pg: E.activation(
                            out=USB[:, (hc - 1) * 512:(hc + 1) * 512], in_=pg,
                            func=AF.Copy), reads=(kpg,), writes=(("USB",),),
                            out_free=1024)
                pl.op("v", lambda E, rnd=rnd, q=q: E.tensor_tensor(
                    out=H2ALL[:, (q * 2 + rnd) * 2048:(q * 2 + rnd + 1) * 2048],
                    in0=USB, in1=GSB,
                    op=ALU.mult), reads=(("USB",), ("GSB",)),
                    writes=(("H2", q * 2 + rnd),), out_free=2048)

        def stageD(st):
            q = st % 2
            for tt in range(4):
                tl = st * 4 + tt
                for j in range(2):
                    pl.op("y" if j == 0 else "s", lambda E, tl=tl, j=j,
                          acc=(accA if j == 0 else accB): E.dma_start(
                          out=ACCR[j], in_=acc[tl * 128:(tl + 1) * 128, :]),
                          dma=True, reads=((("accA" if j == 0 else "accB"), tl),),
                          writes=(("ACCR", j),))
                for half in range(2):
                    for ch in range(8):
                        rnd, hc = ch // 4, ch % 4
                        pl.op("t", lambda E, rnd=rnd, hc=hc, tt=tt, half=half,
                              ch=ch, q=q: E.matmul(
                            PACC[:, half * 512:(half + 1) * 512],
                            H2ALL[:, (q * 8 + rnd * 4 + hc) * 512 + tt * 128:
                                  (q * 8 + rnd * 4 + hc) * 512 + (tt + 1) * 128],
                            SDWT[:, (rnd * 4 + hc) * 1024 + half * 512:
                                  (rnd * 4 + hc) * 1024 + (half + 1) * 512],
                            start=(ch == 0), stop=(ch == 7)),
                            reads=(("H2", q * 2 + rnd), ("SDWT",)),
                            writes=(("fPACC",),),
                            out_free=512)
                pl.op("v", lambda E, tt=tt: E.tensor_tensor(
                    out=OB4[tt], in0=PACC, in1=ACCR[0], op=ALU.add),
                    reads=(("fPACC",), ("ACCR", 0)), writes=(("OBS", tt),),
                    out_free=1024)
                pl.op("p", lambda E, tt=tt: E.tensor_tensor(
                    out=OB4[tt], in0=OB4[tt], in1=ACCR[1], op=ALU.add),
                    reads=(("OBS", tt), ("ACCR", 1)), writes=(("OBS", tt),),
                    out_free=1024)
                pl.op("s", lambda E, tt=tt: E.activation(out=JS, in_=OB4[tt],
                      func=AF.Copy, accum_out=B_SUM[:, tt:tt + 1]),
                      reads=(("OBS", tt),), writes=(("fJS",), ("fB_SUM", tt)),
                      out_free=1024)
                pl.op("s", lambda E, tt=tt: E.activation(
                    out=JV, in_=OB4[tt], func=AF.Square,
                    accum_out=B_SQ[:, tt:tt + 1]),
                    reads=(("OBS", tt),), writes=(("fJV",), ("fB_SQ", tt)),
                    out_free=1024)
                pl.op("p", lambda E, tl=tl, tt=tt: E.dma_start(
                    out=XT[tt], in_=x_in[tl * 128:(tl + 1) * 128, :]), dma=True,
                    reads=((kx_in, tl),), writes=(("fXT", tt),))

        stageG(0)
        for st in range(N_ST):
            if st + 1 < N_ST:
                stageG(st + 1)
            stageD(st)
            ln_finalize(pl, B_SUM, B_SQ, B_M, B_R, S_T, "fBst",
                        [("fB_SUM", tt) for tt in range(4)],
                        [("fB_SQ", tt) for tt in range(4)])
            for tt in range(4):
                tl = st * 4 + tt
                pl.op("v", lambda E, tt=tt: E.tensor_scalar(
                    out=OB4[tt], in0=OB4[tt], scalar1=B_M[:, tt:tt + 1],
                    scalar2=B_R[:, tt:tt + 1], op0=ALU.subtract, op1=ALU.mult),
                    reads=(("OBS", tt),), sreads=(("fBst", "M"), ("fBst", "R")),
                    writes=(("OBS", tt),), out_free=1024)
                pl.op("p", lambda E, tt=tt: E.tensor_tensor(
                    out=XT[tt], in0=OB4[tt], in1=XT[tt], op=ALU.add),
                    reads=(("OBS", tt), ("fXT", tt)), writes=(("fXT", tt),),
                    out_free=1024)
                tok = pl.op("y", lambda E, tl=tl, tt=tt: E.dma_start(
                    out=x_out[tl * 128:(tl + 1) * 128, :], in_=XT[tt]), dma=True,
                    reads=(("fXT", tt),), writes=((kx_out, tl),))
                out_toks.append(tok)
    return out_toks



def emit_l1_attn(pl, nc, ctx, x_in, x_out, kx_in, kx_out, W, C):
    E_ = ctx.enter_context
    sb = lambda name, shape, dt: E_(nc.sbuf_tensor(name, shape, dt)).ap()
    ps = lambda name, shape, dt: E_(nc.psum_tensor(name, shape, dt)).ap()
    nseq = max(1, N_ST // SEQ_ST)
    ntile_seq = SEQ_ST * 4
    slen = ntile_seq * 128

    WQ1 = sb("awq", [128, 8 * 512], BF16)
    WKV = sb("awkv", [128, 8 * 1024], BF16)
    WO1 = sb("awo", [128, 4 * 1024], BF16)
    ONESC = sb("aonesc", [128, 1], BF16)
    XT = [sb(f"axt{i}", [128, 1024], F32) for i in range(4)]
    XHB = sb("axhb", [128, 1024], BF16)
    XHB2a = sb("axhb2", [128, 1024], BF16)
    NMRA = sb("anmra", [128, 4], F32)
    XHF = sb("axhf", [128, 4096], BF16)
    QTM = sb("aqtm", [128, 512], F32)
    KVTM = sb("akvtm", [128, 1024], F32)
    SQSC = sb("asqsc", [128, 512], F32)
    QSS = sb("aqss", [128, 8], F32)
    KSS = sb("akss", [128, 8], F32)
    QRS = sb("aqrs", [128, 8], F32)
    KRS = sb("akrs", [128, 8], F32)
    SST = sb("asst", [128, 8], F32)
    ONESB = sb("aonesb", [128, 64], BF16)
    NEG16 = sb("aneg16", [128, 1], F32)
    QHB = sb("aqhb", [128, 512], BF16)
    KHB = sb("akhb", [128, 512], BF16)
    VEX = sb("avex", [128, ntile_seq * 520], BF16)
    QHF = sb("aqhf", [128, 4 * slen], BF16)
    KHF = sb("akhf", [128, 4 * slen], BF16)
    PJ = [sb(f"apj{i}", [128, 1024], BF16) for i in range(2)]
    OSB = sb("aosb", [128, 512], F32)
    RS = sb("ars", [128, 512], F32)
    RH = sb("arh", [128, 512], BF16)
    RL = sb("arl", [128, 512], BF16)
    OFM = sb("aofm", [128, 4 * 512], BF16)
    OB = sb("aob", [128, 4 * 1024], F32)
    JV = sb("ajv", [128, 1024], BF16)
    JS = sb("ajs", [128, 1024], BF16)
    S_SUM = sb("as_sum", [128, 4], F32)
    S_SQ = sb("as_sq", [128, 4], F32)
    S_M = sb("as_m", [128, 4], F32)
    S_R = sb("as_r", [128, 4], F32)
    S_T = sb("as_t", [128, 4], F32)
    B_SUM = sb("ab_sum", [128, 4], F32)
    B_SQ = sb("ab_sq", [128, 4], F32)
    B_M = sb("ab_m", [128, 4], F32)
    B_R = sb("ab_r", [128, 4], F32)

    PA = ps("apa", [128, 1024], F32)
    PL1 = ps("apl1", [128, 1024], F32)
    PL2 = ps("apl2", [128, 1024], F32)
    PO1 = ps("apo1", [128, 512], F32)
    PO2 = ps("apo2", [128, 512], F32)
    PB1 = None
    _PLB = PL1.bitcast(BF16)
    PTH = [_PLB[:, 0:512], _PLB[:, 1024:1536]]

    out_toks = []

    def wld(dst, src_name, kname):
        src = W[src_name]
        kc = src.shape[0] // 128
        view = src.rearrange("(kc p) o -> p kc o", p=128)
        d = dst.rearrange("p (kc o) -> p kc o", kc=kc)
        pl.op("p", lambda E, d=d, s=view: E.dma_start(out=d, in_=s), dma=True,
              writes=(kname,), tag=f"ld_{kname}")

    wld(WQ1, "L1_attn_Wq", "aWQ")
    wld(WKV, "L1_attn_Wkv", "aWKV")
    wld(WO1, "L1_attn_Wo", "aWO")
    pl.op("v", lambda E: E.memset(ONESC, 1.0), writes=("aONESC",), out_free=1)
    pl.op("v", lambda E: E.memset(ONESB, 1.0), writes=("aONESB",), out_free=64)
    pl.op("v", lambda E: E.memset(NEG16, -16.0), writes=("aNEG16",), out_free=1)
    pl.op("v", lambda E: E.memset(
        VEX.rearrange("p (t h e) -> p t h e", t=ntile_seq, h=8)[:, :, :, 64:65],
        1.0), writes=(("aVEXones",),), out_free=1024)

    for s in range(nseq):
        for stq in range(SEQ_ST):
            st = s * SEQ_ST + stq
            for u in range(4):
                tl = st * 4 + u
                kx = ("aXT", u)
                pl.op("y", lambda E, tl=tl, u=u: E.dma_start(
                    out=XT[u], in_=x_in[tl * 128:(tl + 1) * 128, :]), dma=True,
                    reads=((kx_in, tl),), writes=(kx,))
                pl.op("s", lambda E, u=u: E.activation(
                    out=JV, in_=XT[u], func=AF.Square,
                    accum_out=S_SQ[:, u:u + 1]), reads=(kx,),
                    writes=(("aJV",), ("aS_SQ", u)), out_free=1024)
                pl.op("s", lambda E, u=u: E.activation(
                    out=JS, in_=XT[u], func=AF.Copy, accum_out=S_SUM[:, u:u + 1]),
                    reads=(kx,), writes=(("aJS",), ("aS_SUM", u)), out_free=1024)
            ln_finalize(pl, S_SUM, S_SQ, S_M, S_R, S_T, "aSst",
                        [("aS_SUM", u) for u in range(4)],
                        [("aS_SQ", u) for u in range(4)])
            pl.op("v", lambda E: E.scalar_tensor_tensor(
                out=NMRA, in0=S_M, scalar=-1.0, in1=S_R, op0=ALU.mult,
                op1=ALU.mult), reads=(("aSst", "M"), ("aSst", "R")),
                writes=(("aNMRA",),), out_free=4)
            for u in range(4):
                tl = st * 4 + u
                tls = stq * 4 + u
                kx = ("aXT", u)
                XHBu = XHB if u % 2 == 0 else XHB2a
                kxhb = ("aXHB", u % 2)
                pl.op("s", lambda E, u=u, XHBu=XHBu: E.activation(
                    out=XHBu, in_=XT[u], func=AF.Identity,
                    scale=S_R[:, u:u + 1], bias=NMRA[:, u:u + 1]),
                    reads=(kx,), sreads=(("aSst", "R"), ("aNMRA",)),
                    writes=(kxhb,), out_free=1024)
                for half in range(2):
                    for fc2 in range(4):
                        fc = half * 4 + fc2
                        pl.op("t", lambda E, fc=fc, fc2=fc2, half=half,
                              XHBu=XHBu, kxhb=kxhb: E.transpose(
                            PTH[half][:, fc2 * 128:(fc2 + 1) * 128],
                            XHBu[:, fc * 128:(fc + 1) * 128], C["IDN"]),
                            reads=(kxhb, "IDN"), writes=(("aPT", half),),
                            out_free=128)
                    pl.op("v", lambda E, half=half: E.tensor_copy(
                        out=XHF.rearrange("p (f c) -> p f c", f=8)[
                            :, half * 4:(half + 1) * 4, 0:128],
                        in_=PTH[half].rearrange("p (f c) -> p f c", f=4)),
                        reads=(("aPT", half),), writes=(("aXHF",),),
                        out_free=512)
                for kc in range(8):
                    pl.op("t", lambda E, kc=kc: E.matmul(
                        PA[:, 0:512], XHF[:, kc * 512:kc * 512 + 128],
                        WQ1[:, kc * 512:(kc + 1) * 512], start=(kc == 0),
                        stop=(kc == 7)),
                        reads=(("aXHF",), ("aWQ",)), writes=(("aPA",),),
                        out_free=512)
                for half in range(2):
                    for kc in range(8):
                        pl.op("t", lambda E, kc=kc, half=half: E.matmul(
                            PA[:, 512:1024] if half == 0 else PL1[:, 0:512],
                            XHF[:, kc * 512:kc * 512 + 128],
                            WKV[:, kc * 1024 + half * 512:
                                kc * 1024 + (half + 1) * 512],
                            start=(kc == 0), stop=(kc == 7)),
                            reads=(("aXHF",), ("aWKV",)),
                            writes=(("aPA",) if half == 0 else ("aPL1", 0),),
                            out_free=512)
                pl.op("v", lambda E: E.tensor_copy(out=QTM, in_=PA[:, 0:512]),
                      reads=(("aPA",),), writes=(("aQTM",),), out_free=512)
                pl.op("v", lambda E: E.tensor_copy(out=KVTM[:, 0:512],
                      in_=PA[:, 512:1024]), reads=(("aPA",),),
                      writes=(("aKVTM",),), out_free=512)
                pl.op("v", lambda E: E.tensor_copy(out=KVTM[:, 512:1024],
                      in_=PL1[:, 0:512]), reads=(("aPL1", 0),),
                      writes=(("aKVTM",),), out_free=512)
                for (TMM, SS, RSD, kss, krs, ktm) in (
                        (QTM, QSS, QRS, "aQSS", "aQRS", "aQTM"),
                        (KVTM, KSS, KRS, "aKSS", "aKRS", "aKVTM")):
                    tin = TMM[:, 0:512]
                    pl.op("v", lambda E, tin=tin: E.scalar_tensor_tensor(
                        out=SQSC, in0=tin, scalar=1.0, in1=tin, op0=ALU.mult,
                        op1=ALU.mult), reads=((ktm,),), writes=(("aSQSC",),),
                        out_free=512)
                    pl.op("v", lambda E, SS=SS: E.reduce_sum(
                        out=SS.unsqueeze(2), in_=SQSC.rearrange(
                            "p (h d) -> p h d", h=8), axis=AX),
                        reads=(("aSQSC",),), writes=((kss,),), out_free=8)
                    pl.op("v", lambda E, SS=SS: E.tensor_scalar(
                        out=SS, in0=SS, scalar1=1e-24, scalar2=None,
                        op0=ALU.max), reads=((kss,),), writes=((kss,),),
                        out_free=8)
                    emit_rsqrt(pl, RSD, SS, SST, (krs,), (kss,), ("aSST",))
                pl.op("v", lambda E: E.tensor_tensor(
                    out=QHB.rearrange("p (h d) -> p d h", h=8),
                    in0=QTM.rearrange("p (h d) -> p d h", h=8),
                    in1=QRS.unsqueeze(1).broadcast_to([128, 64, 8]),
                    op=ALU.mult), reads=(("aQTM",),), sreads=(("aQRS",),),
                    writes=(("aQHB",),), out_free=512)
                pl.op("v", lambda E: E.tensor_tensor(
                    out=KHB.rearrange("p (h d) -> p d h", h=8),
                    in0=KVTM[:, 0:512].rearrange("p (h d) -> p d h", h=8),
                    in1=KRS.unsqueeze(1).broadcast_to([128, 64, 8]),
                    op=ALU.mult), reads=(("aKVTM",),), sreads=(("aKRS",),),
                    writes=(("aKHB",),), out_free=512)
                pl.op("v", lambda E, tls=tls: E.tensor_copy(
                    out=VEX[:, tls * 520:(tls + 1) * 520].rearrange(
                        "p (h e) -> p h e", h=8)[:, :, 0:64],
                    in_=KVTM[:, 512:1024].rearrange("p (h d) -> p h d", h=8)),
                    reads=(("aKVTM",),), writes=(("aVEX", tls),), out_free=512)
                for fc in range(4):
                    pl.op("t", lambda E, fc=fc: E.transpose(
                        PTH[0][:, fc * 128:(fc + 1) * 128],
                        QHB[:, fc * 128:(fc + 1) * 128], C["IDN"]),
                        reads=(("aQHB",), "IDN"), writes=(("aPT", 0),),
                        out_free=128)
                pl.op("v", lambda E, tls=tls: E.tensor_copy(
                    out=QHF.rearrange("p (f t) -> p f t", f=4)[
                        :, :, tls * 128:(tls + 1) * 128],
                    in_=PTH[0].rearrange("p (f t) -> p f t", f=4)),
                    reads=(("aPT", 0),), writes=(("aQHF", tls),), out_free=512)
                for fc in range(4):
                    pl.op("t", lambda E, fc=fc: E.transpose(
                        PTH[1][:, fc * 128:(fc + 1) * 128],
                        KHB[:, fc * 128:(fc + 1) * 128], C["IDN"]),
                        reads=(("aKHB",), "IDN"), writes=(("aPT", 1),),
                        out_free=128)
                pl.op("v", lambda E, tls=tls: E.tensor_copy(
                    out=KHF.rearrange("p (f t) -> p f t", f=4)[
                        :, :, tls * 128:(tls + 1) * 128],
                    in_=PTH[1].rearrange("p (f t) -> p f t", f=4)),
                    reads=(("aPT", 1),), writes=(("aKHF", tls),), out_free=512)

        QHF_R = tuple(("aQHF", i) for i in range(ntile_seq))
        KHF_R = tuple(("aKHF", i) for i in range(ntile_seq))
        VEX_R = tuple(("aVEX", i) for i in range(ntile_seq)) + (("aVEXones",),)
        for ib in range(SEQ_ST):
            for hx, h in enumerate([1, 3, 5, 7, 0, 2, 4, 6]):
                hb = (h % 2) * 64
                hf = h // 2
                po = PO1 if hx % 2 == 0 else PO2
                kpo = ("aPO1", hx % 2)
                def emit_pv(jg):
                    pj = PJ[jg % 2]
                    kpj = ("aPJ", jg % 2)
                    for j2 in range(2):
                        jc = jg * 2 + j2
                        pl.op("t", lambda E, jc=jc, h=h, pj=pj, po=po, j2=j2:
                              E.matmul(
                            po[0:65, :], VEX[:, jc * 520 + h * 65:
                                             jc * 520 + (h + 1) * 65],
                            pj[:, j2 * 512:(j2 + 1) * 512],
                            start=(jc == 0), stop=(jc == ntile_seq - 1)),
                            reads=VEX_R + (kpj,), writes=(kpo,), out_free=512)
                for jg in range(ntile_seq // 2):
                    plx = PL1 if jg % 2 == 0 else PL2
                    kpl = ("aPL1", jg % 2)
                    pj = PJ[jg % 2]
                    kpj = ("aPJ", jg % 2)
                    for j2 in range(2):
                        jc = jg * 2 + j2
                        pl.op("t", lambda E, hb=hb, hf=hf, jc=jc, ib=ib,
                              plx=plx, j2=j2: E.matmul(
                            plx[:, j2 * 512:(j2 + 1) * 512],
                            KHF[hb:hb + 64,
                                hf * slen + jc * 128: hf * slen + (jc + 1) * 128],
                            QHF[hb:hb + 64,
                                hf * slen + ib * 512: hf * slen + (ib + 1) * 512],
                            start=True, stop=True),
                            reads=KHF_R + QHF_R, writes=(kpl,), out_free=512)
                    pl.op("s", lambda E, pj=pj, plx=plx: E.activation(
                        out=pj, in_=plx, func=AF.Exp, scale=16.0, bias=NEG16),
                        reads=(kpl,), sreads=(("aNEG16",),),
                        writes=(kpj,), out_free=1024)
                    if jg > 0:
                        emit_pv(jg - 1)
                emit_pv(ntile_seq // 2 - 1)
                pl.op("v", lambda E, po=po: E.reciprocal(out=RS[64:65, :],
                      in_=po[64:65, :]), reads=(kpo,), writes=(("aRS",),),
                      out_free=512)
                pl.op("v", lambda E: E.tensor_copy(out=RH[64:65, :],
                      in_=RS[64:65, :]), reads=(("aRS",),), writes=(("aRH",),),
                      out_free=512)
                pl.op("v", lambda E: E.scalar_tensor_tensor(
                    out=RL[64:65, :], in0=RH[64:65, :], scalar=-1.0,
                    in1=RS[64:65, :], op0=ALU.mult, op1=ALU.add),
                    reads=(("aRH",), ("aRS",)), writes=(("aRL",),), out_free=512)
                pl.op("v", lambda E, po=po: E.tensor_copy(out=OSB[0:64, :],
                      in_=po[0:64, :]), reads=(kpo,),
                      writes=(("aOSB",),), out_free=512)
                pl.op("t", lambda E: E.matmul(
                    PA[0:64, 0:512], ONESB[64:65, 0:64], RH[64:65, :],
                    start=True, stop=False),
                    reads=(("aRH",), "aONESB"), writes=(("aPA",),),
                    out_free=512)
                pl.op("t", lambda E: E.matmul(
                    PA[0:64, 0:512], ONESB[64:65, 0:64], RL[64:65, :],
                    start=False, stop=True),
                    reads=(("aRL",), "aONESB"), writes=(("aPA",),),
                    out_free=512)
                pl.op("v", lambda E, hb=hb, hf=hf: E.tensor_tensor(
                    out=OFM[hb:hb + 64, hf * 512:(hf + 1) * 512],
                    in0=OSB[0:64, :], in1=PA[0:64, 0:512], op=ALU.mult),
                    reads=(("aOSB",), ("aPA",)), writes=(("aOFM", h),),
                    out_free=512)
                pl.op("t", lambda E: E.matmul(PL1[0:1, 0:1], ONESC, ONESC,
                      start=True, stop=True),
                      reads=(("aONESC",),), writes=(("aPL1", 0),), out_free=1)
            OFM_R = tuple(("aOFM", h) for h in range(8))
            for u in range(4):
                tls = ib * 4 + u
                tl = (s * SEQ_ST) * 4 + tls
                pwo = PA if u % 2 == 0 else PL1
                kpwo = ("aPA",) if u % 2 == 0 else ("aPL1", 0)
                for half in range(2):
                    for fc in range(4):
                        pl.op("t", lambda E, fc=fc, half=half, u=u, pwo=pwo:
                              E.matmul(
                            pwo[:, half * 512:(half + 1) * 512],
                            OFM[:, fc * 512 + u * 128: fc * 512 + (u + 1) * 128],
                            WO1[:, fc * 1024 + half * 512:
                                fc * 1024 + (half + 1) * 512],
                            start=(fc == 0), stop=(fc == 3)),
                            reads=OFM_R + (("aWO",),), writes=(kpwo,),
                            out_free=512)
                pl.op("s", lambda E, u=u, pwo=pwo: E.activation(
                    out=OB[:, u * 1024:(u + 1) * 1024], in_=pwo, func=AF.Copy,
                    accum_out=B_SUM[:, u:u + 1]), reads=(kpwo,),
                    writes=(("aOB", u), ("aB_SUM", u)), out_free=1024)
                pl.op("s", lambda E, u=u: E.activation(
                    out=JV, in_=OB[:, u * 1024:(u + 1) * 1024], func=AF.Square,
                    accum_out=B_SQ[:, u:u + 1]),
                    reads=(("aOB", u),), writes=(("aJV",), ("aB_SQ", u)),
                    out_free=1024)
                pl.op("s", lambda E, tl=tl, u=u: E.dma_start(
                    out=XT[u], in_=x_in[tl * 128:(tl + 1) * 128, :]), dma=True,
                    reads=((kx_in, tl),), writes=(("aXT", u),))
            ln_finalize(pl, B_SUM, B_SQ, B_M, B_R, S_T, "aBst",
                        [("aB_SUM", u) for u in range(4)],
                        [("aB_SQ", u) for u in range(4)])
            for u in range(4):
                tls = ib * 4 + u
                tl = (s * SEQ_ST) * 4 + tls
                pl.op("v", lambda E, u=u: E.tensor_scalar(
                    out=OB[:, u * 1024:(u + 1) * 1024],
                    in0=OB[:, u * 1024:(u + 1) * 1024],
                    scalar1=B_M[:, u:u + 1], scalar2=B_R[:, u:u + 1],
                    op0=ALU.subtract, op1=ALU.mult), reads=(("aOB", u),),
                    sreads=(("aBst", "M"), ("aBst", "R")), writes=(("aOB", u),),
                    out_free=1024)
                pl.op("p", lambda E, u=u: E.tensor_tensor(
                    out=XT[u], in0=OB[:, u * 1024:(u + 1) * 1024], in1=XT[u],
                    op=ALU.add), reads=(("aOB", u), ("aXT", u)),
                    writes=(("aXT", u),), out_free=1024)
                tok = pl.op("y", lambda E, tl=tl, u=u: E.dma_start(
                    out=x_out[tl * 128:(tl + 1) * 128, :], in_=XT[u]), dma=True,
                    reads=(("aXT", u),), writes=((kx_out, tl),))
                out_toks.append(tok)
    return out_toks



def emit_l1_ffn(pl, nc, ctx, x_in, x_out, kx_in, kx_out, W, C):
    E_ = ctx.enter_context
    sb = lambda name, shape, dt: E_(nc.sbuf_tensor(name, shape, dt)).ap()
    ps = lambda name, shape, dt: E_(nc.psum_tensor(name, shape, dt)).ap()
    ntok = N_ST * 512

    x1f_d = nc.dram_tensor("l1f_x1f", [N_ST * 128, 4096], BF16, kind="Internal").ap()
    x1t_d = nc.dram_tensor("l1f_x1t", [ntok, D], F32, kind="Internal").ap()
    accC = nc.dram_tensor("l1f_accC", [ntok, D], F32, kind="Internal").ap()
    accD = nc.dram_tensor("l1f_accD", [ntok, D], F32, kind="Internal").ap()

    ONES1 = sb("nones1", [1, 128], BF16)
    XT = [sb(f"nxt{i}", [128, 1024], F32) for i in range(4)]
    X1F = sb("nx1f", [128, 4096], BF16)
    X1T = sb("nx1t", [128, 1024], F32)
    COEF_ALL = sb("ncoef", [128, NT * 8], F32)
    JV = sb("njv", [128, 1024], BF16)
    JS = sb("njs", [128, 1024], BF16)
    S_SUM = sb("ns_sum", [128, 8], F32)
    S_SQ = sb("ns_sq", [128, 8], F32)
    S_M = sb("ns_m", [128, 8], F32)
    S_R = sb("ns_r", [128, 8], F32)
    S_T = sb("ns_t", [128, 8], F32)
    PA = ps("npa", [128, 1024], F32)
    PB = ps("npb", [128, 1024], F32)
    PT = ps("npt", [128, 1024], BF16)
    PT2 = ps("npt2", [128, 1024], BF16)
    PTH = [PT[:, 0:512], PT2[:, 0:512]]
    out_toks = []
    pl.op("v", lambda E: E.memset(ONES1, 1.0), writes=("nONES1",), out_free=128)

    def wld(dst, name, kname, src=None):
        src = W[name] if src is None else src
        if src.shape[0] > 128:
            kc = src.shape[0] // 128
            view = src.rearrange("(kc p) o -> p kc o", p=128)
            d = dst.rearrange("p (kc o) -> p kc o", kc=kc)
        else:
            view, d = src, dst
        pl.op("p", lambda E, d=d, s=view: E.dma_start(out=d, in_=s), dma=True,
              writes=(kname,), tag=f"ld_{kname}")

    with contextlib.ExitStack() as c1:
        sb1 = lambda name, shape, dt: c1.enter_context(
            nc.sbuf_tensor(name, shape, dt)).ap()
        PWT = sb1("npwt", [128, 8 * 1024], BF16)
        RWT = sb1("nrwt", [128, 64], BF16)
        RBB = sb1("nrbb", [128, 8], F32)
        DWK = sb1("ndwk", [128, 24], F32)
        DWKB = sb1("ndwkb", [128, 8], F32)
        PWBC = sb1("npwbc", [128, 8], F32)
        XHB = sb1("nxhb", [128, 1024], BF16)
        XHB2n = sb1("nxhb2", [128, 1024], BF16)
        NMRN = sb1("nnmrn", [128, 8], F32)
        XCF = [sb1(f"nxcf{i}", [128, 8 * 514], BF16) for i in range(2)]
        XCONV = sb1("nxconv", [128, 4096], BF16)
        CVT = [sb1(f"ncvt{i}", [128, 512], F32) for i in range(2)]
        XPF = sb1("nxpf", [128, 4096], BF16)
        XPB = sb1("nxpb", [128, 1024], BF16)
        LG = sb1("nlg", [128, 8], F32)
        EQ1 = sb1("neq1", [128, 8], F32)
        EQ2 = sb1("neq2", [128, 8], F32)
        LGM = sb1("nlgm", [128, 8], F32)
        M12 = sb1("nm12", [128, 2], F32)
        TH = sb1("nth", [128, 1], F32)
        W12 = sb1("nw12", [128, 2], F32)
        DWDG = sb1("ndwdg", [128, 3072], BF16)
        X1T4 = sb1("nx1t4", [128, 4096], F32)
        wld(PWT, "d_pwT", "nPWT")
        wld(RWT, "d_rwT", "nRWT")
        wld(DWDG, "d_dwdg", "nDWDG")
        pl.op("y", lambda E: E.dma_start(out=RBB, in_=W["L1_ffn_router_b"]
              .unsqueeze(0).broadcast_to([128, 8])), dma=True, writes=("nRBB",))
        pl.op("y", lambda E: E.dma_start(out=DWK, in_=W["d_dwk"]), dma=True,
              writes=("nDWK",))
        pl.op("y", lambda E: E.dma_start(out=DWKB, in_=W["d_dwbc"]), dma=True,
              writes=("nDWKB",))
        pl.op("y", lambda E: E.dma_start(out=PWBC, in_=W["d_pwbc"]), dma=True,
              writes=("nPWBC",))

        def pre_stats1(st):
            q = st % 2
            for u in range(4):
                tl = st * 4 + u
                kx = ("nXT", u)
                pl.op("y", lambda E, tl=tl, u=u: E.dma_start(
                    out=XT[u], in_=x_in[tl * 128:(tl + 1) * 128, :]), dma=True,
                    reads=((kx_in, tl),), writes=(kx,))
                pl.op("s", lambda E, u=u, q=q: E.activation(
                    out=JV, in_=XT[u], func=AF.Square,
                    accum_out=S_SQ[:, q * 4 + u:q * 4 + u + 1]), reads=(kx,),
                    writes=(("nJV",), ("nS_SQ", q, u)), out_free=1024)
                pl.op("s", lambda E, u=u, q=q: E.activation(
                    out=JS, in_=XT[u], func=AF.Copy,
                    accum_out=S_SUM[:, q * 4 + u:q * 4 + u + 1]),
                    reads=(kx,), writes=(("nJS",), ("nS_SUM", q, u)),
                    out_free=1024)

        for st in range(N_ST):
            seq_start = (st % SEQ_ST == 0)
            cur = XCF[st % 2]
            prv = XCF[1 - st % 2]
            kcur = ("nXCF", st % 2)
            kprv = ("nXCF", 1 - st % 2)
            if seq_start:
                pl.op("v", lambda E, cur=cur: E.memset(
                    cur.rearrange("p (f t) -> p f t", f=8)[:, :, 0:2], 0.0),
                    writes=(kcur,), out_free=16)
            else:
                pl.op("v", lambda E, cur=cur, prv=prv: E.tensor_copy(
                    out=cur.rearrange("p (f t) -> p f t", f=8)[:, :, 0:2],
                    in_=prv.rearrange("p (f t) -> p f t", f=8)[:, :, 512:514]),
                    reads=(kprv,), writes=(kcur,), out_free=16)
            q = st % 2
            if st == 0:
                pre_stats1(0)
            ln_finalize(pl, S_SUM[:, q * 4:q * 4 + 4], S_SQ[:, q * 4:q * 4 + 4],
                        S_M[:, q * 4:q * 4 + 4], S_R[:, q * 4:q * 4 + 4],
                        S_T[:, q * 4:q * 4 + 4], f"nSst{q}",
                        [("nS_SUM", q, u) for u in range(4)],
                        [("nS_SQ", q, u) for u in range(4)])
            pl.op("v", lambda E, q=q: E.scalar_tensor_tensor(
                out=NMRN[:, q * 4:q * 4 + 4], in0=S_M[:, q * 4:q * 4 + 4],
                scalar=-1.0, in1=S_R[:, q * 4:q * 4 + 4], op0=ALU.mult,
                op1=ALU.mult), reads=((f"nSst{q}", "M"), (f"nSst{q}", "R")),
                writes=(("nNMRN", q),), out_free=4)
            for u in range(4):
                kx = ("nXT", u)
                XHBu = XHB if u % 2 == 0 else XHB2n
                kxhb = ("nXHB", u % 2)
                pl.op("s", lambda E, u=u, q=q, XHBu=XHBu: E.activation(
                    out=XHBu, in_=XT[u], func=AF.Identity,
                    scale=S_R[:, q * 4 + u:q * 4 + u + 1],
                    bias=NMRN[:, q * 4 + u:q * 4 + u + 1]),
                    reads=(kx,), sreads=((f"nSst{q}", "R"), ("nNMRN", q)),
                    writes=(kxhb,), out_free=1024)
                pl.op("v", lambda E, u=u, q=q: E.tensor_scalar(
                    out=X1T4[:, u * 1024:(u + 1) * 1024], in0=XT[u],
                    scalar1=S_M[:, q * 4 + u:q * 4 + u + 1],
                    scalar2=S_R[:, q * 4 + u:q * 4 + u + 1],
                    op0=ALU.subtract, op1=ALU.mult),
                    reads=(kx,), sreads=((f"nSst{q}", "M"), (f"nSst{q}", "R")),
                    writes=(("nX1T4", u),), out_free=512)
                for half in range(2):
                    for fc2 in range(4):
                        fc = half * 4 + fc2
                        pl.op("t", lambda E, fc=fc, fc2=fc2, half=half,
                              XHBu=XHBu, kxhb=kxhb: E.transpose(
                            PTH[half][:, fc2 * 128:(fc2 + 1) * 128],
                            XHBu[:, fc * 128:(fc + 1) * 128], C["IDN"]),
                            reads=(kxhb, "IDN"), writes=(("nPT", half),),
                            out_free=128)
                    pl.op("v", lambda E, u=u, cur=cur, half=half: E.tensor_copy(
                        out=cur.rearrange("p (f t) -> p f t", f=8)[
                            :, half * 4:(half + 1) * 4,
                            2 + u * 128:2 + (u + 1) * 128],
                        in_=PTH[half].rearrange("p (f c) -> p f c", f=4)),
                        reads=(("nPT", half),), writes=(kcur,), out_free=512)
            if st + 1 < N_ST:
                pre_stats1(st + 1)
            for fc in range(8):
                pb = PB[:, (fc % 2) * 512:(fc % 2) * 512 + 512]
                kpb = ("nPBc", fc % 2)
                for t in range(3):
                    pl.op("t", lambda E, fc=fc, t=t, cur=cur, pb=pb: E.matmul(
                        pb, DWDG[:, (fc * 3 + t) * 128:(fc * 3 + t + 1) * 128],
                        cur[:, fc * 514 + t: fc * 514 + t + 512],
                        start=(t == 0), stop=(t == 2)),
                        reads=(kcur, ("nDWDG",)), writes=(kpb,), out_free=512)
                pl.op("s", lambda E, fc=fc, pb=pb: E.activation(
                    out=XCONV[:, fc * 512:(fc + 1) * 512], in_=pb,
                    func=AF.Identity, scale=1.0, bias=DWKB[:, fc:fc + 1]),
                    reads=(kpb,), sreads=(("nDWKB",),),
                    writes=(("nXCONV",),), out_free=512)
            for ofc in range(8):
                for kc in range(8):
                    pl.op("t", lambda E, ofc=ofc, kc=kc: E.matmul(
                        PA[:, 0:512],
                        PWT[:, kc * 1024 + ofc * 128: kc * 1024 + (ofc + 1) * 128],
                        XCONV[:, kc * 512:(kc + 1) * 512], start=(kc == 0),
                        stop=(kc == 7)),
                        reads=(("nPWT",), ("nXCONV",)), writes=(("nPA",),),
                        out_free=512)
                pl.op("s", lambda E, ofc=ofc: E.activation(
                    out=XPF[:, ofc * 512:(ofc + 1) * 512], in_=PA[:, 0:512],
                    func=AF.Silu, bias=PWBC[:, ofc:ofc + 1]),
                    reads=(("nPA",),), sreads=(("nPWBC",),),
                    writes=(("nXPF", ofc),), out_free=512)
            pl.op("v", lambda E, cur=cur: E.tensor_tensor(
                out=X1F.rearrange("p (f t) -> p f t", f=8),
                in0=cur.rearrange("p (f t) -> p f t", f=8)[:, :, 2:514],
                in1=XPF.rearrange("p (f t) -> p f t", f=8), op=ALU.add),
                reads=(kcur,) + tuple(("nXPF", o) for o in range(8)),
                writes=(("nX1F",),), out_free=4096)
            pl.op("y", lambda E, st=st: E.dma_start(
                out=x1f_d[st * 128:(st + 1) * 128, :], in_=X1F), dma=True,
                reads=(("nX1F",),), writes=(("x1f", st),))
            for u in range(4):
                tl = st * 4 + u
                for half in range(2):
                    for fc2 in range(4):
                        fc = half * 4 + fc2
                        pl.op("t", lambda E, fc=fc, fc2=fc2, half=half, u=u:
                              E.transpose(
                            PTH[half][:, fc2 * 128:(fc2 + 1) * 128],
                            XPF[:, fc * 512 + u * 128: fc * 512 + (u + 1) * 128],
                            C["IDN"]), reads=(("nXPF", fc), "IDN"),
                            writes=(("nPT", half),), out_free=128)
                    pl.op("v", lambda E, half=half: E.tensor_copy(
                        out=XPB[:, half * 512:(half + 1) * 512],
                        in_=PTH[half]),
                        reads=(("nPT", half),), writes=(("nXPB",),),
                        out_free=512)
                pl.op("p", lambda E, u=u: E.tensor_tensor(
                    out=X1T, in0=X1T4[:, u * 1024:(u + 1) * 1024], in1=XPB,
                    op=ALU.add),
                    reads=(("nX1T4", u), ("nXPB",)), writes=(("nX1T",),),
                    out_free=1024)
                pl.op("y", lambda E, tl=tl: E.dma_start(
                    out=x1t_d[tl * 128:(tl + 1) * 128, :], in_=X1T), dma=True,
                    reads=(("nX1T",),), writes=(("x1t", tl),))
                for kc in range(8):
                    pl.op("t", lambda E, kc=kc, u=u: E.matmul(
                        PB[:, 0:8],
                        X1F[:, kc * 512 + u * 128: kc * 512 + (u + 1) * 128],
                        RWT[:, kc * 8:(kc + 1) * 8], start=(kc == 0),
                        stop=(kc == 7)),
                        reads=(("nX1F",), ("nRWT",)), writes=(("nPB",),),
                        out_free=8)
                pl.op("v", lambda E: E.tensor_tensor(out=LG, in0=PB[:, 0:8],
                      in1=RBB, op=ALU.add), reads=(("nPB",), ("nRBB",)),
                      writes=(("nLG",),), out_free=8)
                pl.op("v", lambda E: E.reduce_max(out=M12[:, 0:1], in_=LG,
                      axis=AX), reads=(("nLG",),), writes=(("nM12",),),
                      out_free=1)
                pl.op("v", lambda E: E.tensor_scalar(out=EQ1, in0=LG,
                      scalar1=M12[:, 0:1], scalar2=None, op0=ALU.is_equal),
                      reads=(("nLG",),), sreads=(("nM12",),),
                      writes=(("nEQ1",),), out_free=8)
                pl.op("v", lambda E: E.scalar_tensor_tensor(out=LGM, in0=EQ1,
                      scalar=-1e30, in1=LG, op0=ALU.mult, op1=ALU.add),
                      reads=(("nEQ1",), ("nLG",)), writes=(("nLGM",),),
                      out_free=8)
                pl.op("v", lambda E: E.reduce_max(out=M12[:, 1:2], in_=LGM,
                      axis=AX), reads=(("nLGM",),), writes=(("nM12",),),
                      out_free=1)
                pl.op("v", lambda E: E.tensor_scalar(out=EQ2, in0=LGM,
                      scalar1=M12[:, 1:2], scalar2=None, op0=ALU.is_equal),
                      reads=(("nLGM",),), sreads=(("nM12",),),
                      writes=(("nEQ2",),), out_free=8)
                pl.op("v", lambda E: E.tensor_scalar(out=TH, in0=M12[:, 0:1],
                      scalar1=M12[:, 1:2], scalar2=0.5, op0=ALU.subtract,
                      op1=ALU.mult), reads=(("nM12",),), sreads=(("nM12",),),
                      writes=(("nTH",),), out_free=1)
                pl.op("s", lambda E: E.activation(out=TH, in_=TH, func=AF.Tanh),
                      reads=(("nTH",),), writes=(("nTH",),), out_free=1)
                pl.op("v", lambda E: E.tensor_scalar(out=W12[:, 0:1], in0=TH,
                      scalar1=0.5, scalar2=0.5, op0=ALU.mult, op1=ALU.add),
                      reads=(("nTH",),), writes=(("nW12",),), out_free=1)
                pl.op("v", lambda E: E.tensor_scalar(out=W12[:, 1:2], in0=TH,
                      scalar1=-0.5, scalar2=0.5, op0=ALU.mult, op1=ALU.add),
                      reads=(("nTH",),), writes=(("nW12",),), out_free=1)
                pl.op("v", lambda E, tl=tl: E.tensor_scalar(
                    out=COEF_ALL[:, tl * 8:(tl + 1) * 8], in0=EQ1,
                    scalar1=W12[:, 0:1], scalar2=None, op0=ALU.mult),
                    reads=(("nEQ1",),), sreads=(("nW12",),),
                    writes=(("nCOEF", tl),), out_free=8)
                pl.op("v", lambda E, tl=tl: E.scalar_tensor_tensor(
                    out=COEF_ALL[:, tl * 8:(tl + 1) * 8], in0=EQ2,
                    scalar=W12[:, 1:2], in1=COEF_ALL[:, tl * 8:(tl + 1) * 8],
                    op0=ALU.mult, op1=ALU.add),
                    reads=(("nEQ2",), ("nCOEF", tl)), sreads=(("nW12",),),
                    writes=(("nCOEF", tl),), out_free=8)
    pl.block_barrier()

    with contextlib.ExitStack() as c2:
        sb2 = lambda name, shape, dt: c2.enter_context(
            nc.sbuf_tensor(name, shape, dt)).ap()
        EWT = [sb2(f"newt{i}", [128, 8 * 1024], BF16) for i in range(8)]
        EBH = sb2("nebh", [1, 8192], BF16)
        YE = [sb2(f"nye{i}", [128, 1024], BF16) for i in range(2)]
        X1F2 = sb2("nx1f2", [128, 4096], BF16)
        ACC = sb2("nacc", [128, 4 * 1024], F32)
        PA2 = ps("npa2", [128, 1024], F32)
        pl.op("p", lambda E: E.dma_start(
            out=EBH, in_=W["L1_ffn_expert_b"].rearrange("a b -> (a b)").unsqueeze(0)),
            dma=True, writes=(("nEB",),))

        def expert_load(experts, base):
            for i, e in enumerate(experts):
                wld(EWT[base + i], None, ("nEWT", base + i), src=W["d_ewT"][e])

        def expert_pass(experts, acc, acc_key, base):
            def ldx1f(st):
                xf = X1F if st % 2 == 0 else X1F2
                pl.op("y", lambda E, st=st, xf=xf: E.dma_start(
                    out=xf, in_=x1f_d[st * 128:(st + 1) * 128, :]), dma=True,
                    reads=(("x1f", st),), writes=(("nX1F", st % 2),))
            ldx1f(0)
            for st in range(N_ST):
                xf = X1F if st % 2 == 0 else X1F2
                kxf = ("nX1F", st % 2)
                if st + 1 < N_ST:
                    ldx1f(st + 1)
                for u in range(4):
                    tl = st * 4 + u
                    for i, e in enumerate(experts):
                        pa = PA if i % 2 == 0 else PA2
                        kpa = ("nPA", i % 2)
                        ye = YE[i % 2]
                        kye = ("nYE", i % 2)
                        for kc in range(8):
                            for half in range(2):
                                pl.op("t", lambda E, i=i, kc=kc, half=half, u=u,
                                      pa=pa, xf=xf, base=base: E.matmul(
                                    pa[:, half * 512:(half + 1) * 512],
                                    xf[:, kc * 512 + u * 128:
                                       kc * 512 + (u + 1) * 128],
                                    EWT[base + i][:, kc * 1024 + half * 512:
                                                  kc * 1024 + (half + 1) * 512],
                                    start=(kc == 0), stop=False),
                                    reads=(kxf, ("nEWT", base + i)),
                                    writes=(kpa,), out_free=512)
                        for half in range(2):
                            pl.op("t", lambda E, e=e, half=half, pa=pa: E.matmul(
                                pa[:, half * 512:(half + 1) * 512], ONES1,
                                EBH[0:1, e * 1024 + half * 512:
                                     e * 1024 + (half + 1) * 512],
                                start=False, stop=True),
                                reads=(("nEB",), ("nONES1",)),
                                writes=(kpa,), out_free=512)
                        pl.op("s", lambda E, pa=pa, ye=ye: E.activation(
                              out=ye, in_=pa, func=AF.Silu), reads=(kpa,),
                              writes=(kye,), out_free=1024)
                        if i == 0:
                            pl.op("v", lambda E, u=u, tl=tl, e=e, ye=ye:
                                  E.tensor_scalar(
                                out=ACC[:, u * 1024:(u + 1) * 1024], in0=ye,
                                scalar1=COEF_ALL[:, tl * 8 + e: tl * 8 + e + 1],
                                scalar2=None, op0=ALU.mult),
                                reads=(kye,), sreads=(("nCOEF", tl),),
                                writes=(("nACC", u),), out_free=1024)
                        else:
                            pl.op("v", lambda E, u=u, tl=tl, e=e, ye=ye:
                                  E.scalar_tensor_tensor(
                                out=ACC[:, u * 1024:(u + 1) * 1024], in0=ye,
                                scalar=COEF_ALL[:, tl * 8 + e: tl * 8 + e + 1],
                                in1=ACC[:, u * 1024:(u + 1) * 1024],
                                op0=ALU.mult, op1=ALU.add),
                                reads=(kye, ("nACC", u)),
                                sreads=(("nCOEF", tl),), writes=(("nACC", u),),
                                out_free=1024)
                    tok = pl.op("s", lambda E, tl=tl, u=u, acc=acc: E.dma_start(
                        out=acc[tl * 128:(tl + 1) * 128, :],
                        in_=ACC[:, u * 1024:(u + 1) * 1024]), dma=True,
                        reads=(("nACC", u),), writes=((acc_key, tl),))

        expert_load([0, 1, 2, 3], 0)
        expert_load([4, 5, 6, 7], 4)
        expert_pass([0, 1, 2, 3], accC, "accC", 0)
        expert_pass([4, 5, 6, 7], accD, "accD", 4)
    pl.block_barrier()

    with contextlib.ExitStack() as c3:
        sb3 = lambda name, shape, dt: c3.enter_context(
            nc.sbuf_tensor(name, shape, dt)).ap()
        OWT = sb3("nowt", [128, 8 * 1024], BF16)
        FDWT = sb3("nfdwt", [128, 8 * 256], BF16)
        FUWT = sb3("nfuwt", [128, 2 * 1024], BF16)
        NWREP = sb3("nnwrep", [128, 1024], F32)
        OBH = sb3("nobh", [1, 1024], BF16)
        FDBH = sb3("nfdbh", [1, 256], BF16)
        FUBH = sb3("nfubh", [1, 1024], BF16)
        ACCR = [[sb3(f"caccr{q}{j}", [128, 1024], F32) for j in range(2)]
                for q in range(2)]
        ACCB = sb3("caccb", [128, 2 * 1024], BF16)
        ACCFM = sb3("caccfm", [128, 2 * 1024], BF16)
        H1 = sb3("ch1", [128, 2 * 1024], F32)
        HN = sb3("chn", [128, 2 * 1024], F32)
        HNB = sb3("chnb", [128, 2 * 1024], BF16)
        HFM = sb3("chfm", [128, 2 * 1024], BF16)
        FF1 = sb3("cff1", [128, 2 * 256], BF16)
        FF1F = sb3("cff1f", [128, 2 * 256], BF16)
        OB = sb3("cob", [128, 2 * 1024], F32)
        X1TC = sb3("cx1tc", [128, 2 * 1024], F32)
        JS2 = sb3("cjs2", [128, 1024], BF16)
        NMR = sb3("cnmr", [128, 4], F32)
        _PA2B = PA2.bitcast(BF16)
        PTHQ = [[PT[:, 0:512], PT2[:, 0:512]],
                [_PA2B[:, 0:512], _PA2B[:, 1024:1536]]]
        B1Q = [PA, PB]
        RT = sb3("nrt", [128, 4], F32)
        RSCR = sb3("nrscr", [128, 4], F32)
        R_R = sb3("nr_r", [128, 4], F32)
        B_SUM = sb3("nb_sum", [128, 4], F32)
        B_SQ = sb3("nb_sq", [128, 4], F32)
        B_M = sb3("nb_m", [128, 4], F32)
        B_R = sb3("nb_r", [128, 4], F32)
        BT = sb3("nbt", [128, 4], F32)
        BSCR = sb3("nbscr", [128, 4], F32)
        wld(OWT, "d_owT", "nOWT")
        wld(FDWT, "d_fdwT", "nFDWT")
        wld(FUWT, "d_fuwT", "nFUWT")
        pl.op("y", lambda E: E.dma_start(out=NWREP, in_=W["L1_ffn_norm_w"]
              .unsqueeze(0).broadcast_to([128, 1024])), dma=True,
              writes=("nNWREP",))
        pl.op("p", lambda E: E.dma_start(out=OBH,
              in_=W["L1_ffn_out_b"].unsqueeze(0)), dma=True, writes=("nOBb2",))
        pl.op("p", lambda E: E.dma_start(out=FDBH,
              in_=W["L1_ffn_ff_down_b"].unsqueeze(0)), dma=True,
              writes=("nFDB",))
        pl.op("p", lambda E: E.dma_start(out=FUBH,
              in_=W["L1_ffn_ff_up_b"].unsqueeze(0)), dma=True, writes=("nFUB",))

        def stageA(tl):
            q, u = tl % 2, tl % 4
            B1, acb = B1Q[q], ACCB[:, q * 1024:(q + 1) * 1024]
            acf = ACCFM[:, q * 1024:(q + 1) * 1024]
            h1 = H1[:, q * 1024:(q + 1) * 1024]
            x1c = X1TC[:, q * 1024:(q + 1) * 1024]
            for j, (acc, ak) in enumerate(((accC, "accC"), (accD, "accD"))):
                pl.op("y" if j == 0 else "s",
                    lambda E, tl=tl, j=j, q=q, acc=acc: E.dma_start(
                    out=ACCR[q][j], in_=acc[tl * 128:(tl + 1) * 128, :]),
                    dma=True, reads=((ak, tl),), writes=(("cACCR", q, j),))
            pl.op("p", lambda E, q=q, acb=acb: E.tensor_tensor(
                out=acb, in0=ACCR[q][0], in1=ACCR[q][1], op=ALU.add),
                reads=(("cACCR", q, 0), ("cACCR", q, 1)),
                writes=(("cACCB", q),), out_free=1024)
            for half in range(2):
                for fc2 in range(4):
                    fc = half * 4 + fc2
                    pl.op("t", lambda E, fc=fc, fc2=fc2, half=half, q=q,
                          acb=acb: E.transpose(
                        PTHQ[q][half][:, fc2 * 128:(fc2 + 1) * 128],
                        acb[:, fc * 128:(fc + 1) * 128], C["IDN"]),
                        reads=(("cACCB", q), "IDN"), writes=(("cPT", q, half),),
                        out_free=128)
                pl.op("v", lambda E, half=half, q=q, acf=acf: E.tensor_copy(
                    out=acf[:, half * 512:(half + 1) * 512], in_=PTHQ[q][half]),
                    reads=(("cPT", q, half),), writes=(("cACCFM", q),),
                    out_free=512)
            for half in range(2):
                for kc in range(8):
                    pl.op("t", lambda E, kc=kc, half=half, q=q, B1=B1,
                          acf=acf: E.matmul(
                        B1[:, half * 512:(half + 1) * 512],
                        acf[:, kc * 128:(kc + 1) * 128],
                        OWT[:, kc * 1024 + half * 512:
                            kc * 1024 + (half + 1) * 512],
                        start=(kc == 0), stop=False),
                        reads=(("cACCFM", q), ("nOWT",)), writes=(("cP", q),),
                        out_free=512)
                pl.op("t", lambda E, half=half, q=q, B1=B1: E.matmul(
                    B1[:, half * 512:(half + 1) * 512], ONES1,
                    OBH[0:1, half * 512:(half + 1) * 512],
                    start=False, stop=True),
                    reads=(("nOBb2",), ("nONES1",)), writes=(("cP", q),),
                    out_free=512)
            pl.op("p", lambda E, tl=tl, x1c=x1c: E.dma_start(
                out=x1c, in_=x1t_d[tl * 128:(tl + 1) * 128, :]), dma=True,
                reads=(("x1t", tl),), writes=(("cX1T", q),))

        def stageA2(tl):
            q, u = tl % 2, tl % 4
            B1 = B1Q[q]
            h1 = H1[:, q * 1024:(q + 1) * 1024]
            x1c = X1TC[:, q * 1024:(q + 1) * 1024]
            pl.op("v", lambda E, q=q, B1=B1, h1=h1, x1c=x1c: E.tensor_tensor(
                out=h1, in0=B1, in1=x1c, op=ALU.add),
                reads=(("cP", q), ("cX1T", q)), writes=(("cH1", q),),
                out_free=1024)
            pl.op("s", lambda E, q=q, u=u, h1=h1: E.activation(
                out=JS2, in_=h1, func=AF.Square, accum_out=RT[:, u:u + 1]),
                reads=(("cH1", q),), writes=(("nJS2",), ("nRT", u)),
                out_free=1024)

        def stageB1(tl):
            q, u = tl % 2, tl % 4
            B1 = B1Q[q]
            h1 = H1[:, q * 1024:(q + 1) * 1024]
            hn = HN[:, q * 1024:(q + 1) * 1024]
            hnb = HNB[:, q * 1024:(q + 1) * 1024]
            hfm = HFM[:, q * 1024:(q + 1) * 1024]
            ff1 = FF1[:, q * 256:(q + 1) * 256]
            ff1f = FF1F[:, q * 256:(q + 1) * 256]
            ob = OB[:, q * 1024:(q + 1) * 1024]
            pl.op("v", lambda E, u=u: E.tensor_scalar(
                out=RT[:, u:u + 1], in0=RT[:, u:u + 1], scalar1=1.0 / 1024,
                scalar2=1e-6, op0=ALU.mult, op1=ALU.add),
                reads=(("nRT", u),), writes=(("nRT", u),), out_free=1)
            emit_rsqrt(pl, R_R[:, u:u + 1], RT[:, u:u + 1],
                       RSCR[:, u:u + 1], ("nR_R", u), ("nRT", u),
                       ("nRscr", u))
            pl.op("v", lambda E, q=q, u=u, h1=h1, hn=hn: E.tensor_scalar(
                out=hn, in0=h1, scalar1=R_R[:, u:u + 1], scalar2=None,
                op0=ALU.mult), reads=(("cH1", q),), sreads=(("nR_R", u),),
                writes=(("cHN", q),), out_free=512)
            pl.op("v", lambda E, q=q, hn=hn: E.tensor_tensor(out=hn, in0=hn,
                  in1=NWREP, op=ALU.mult), reads=(("cHN", q), ("nNWREP",)),
                  writes=(("cHN", q),), out_free=1024)
            pl.op("s", lambda E, q=q, hn=hn, hnb=hnb: E.activation(
                out=hnb, in_=hn, func=AF.Copy), reads=(("cHN", q),),
                writes=(("cHNB", q),), out_free=1024)

        def stageB2(tl):
            q, u = tl % 2, tl % 4
            B1 = B1Q[q]
            hn = HN[:, q * 1024:(q + 1) * 1024]
            hnb = HNB[:, q * 1024:(q + 1) * 1024]
            hfm = HFM[:, q * 1024:(q + 1) * 1024]
            ff1 = FF1[:, q * 256:(q + 1) * 256]
            ff1f = FF1F[:, q * 256:(q + 1) * 256]
            ob = OB[:, q * 1024:(q + 1) * 1024]
            for half in range(2):
                for fc2 in range(4):
                    fc = half * 4 + fc2
                    pl.op("t", lambda E, fc=fc, fc2=fc2, half=half, q=q,
                          hnb=hnb: E.transpose(
                        PTHQ[q][half][:, fc2 * 128:(fc2 + 1) * 128],
                        hnb[:, fc * 128:(fc + 1) * 128], C["IDN"]),
                        reads=(("cHNB", q), "IDN"), writes=(("cPT", q, half),),
                        out_free=128)
                pl.op("v", lambda E, half=half, q=q, hfm=hfm: E.tensor_copy(
                    out=hfm[:, half * 512:(half + 1) * 512], in_=PTHQ[q][half]),
                    reads=(("cPT", q, half),), writes=(("cHFM", q),),
                    out_free=512)
            for kc in range(8):
                pl.op("t", lambda E, kc=kc, q=q, B1=B1, hfm=hfm: E.matmul(
                    B1[:, 0:256], hfm[:, kc * 128:(kc + 1) * 128],
                    FDWT[:, kc * 256:(kc + 1) * 256], start=(kc == 0),
                    stop=False),
                    reads=(("cHFM", q), ("nFDWT",)), writes=(("cP", q),),
                    out_free=256)
            pl.op("t", lambda E, q=q, B1=B1: E.matmul(
                B1[:, 0:256], ONES1, FDBH, start=False, stop=True),
                reads=(("nFDB",), ("nONES1",)), writes=(("cP", q),),
                out_free=256)
            pl.op("s", lambda E, q=q, B1=B1, ff1=ff1: E.activation(
                out=ff1, in_=B1[:, 0:256], func=AF.Gelu),
                reads=(("cP", q),), writes=(("cFF1", q),), out_free=256)
            for fc in range(2):
                pl.op("t", lambda E, fc=fc, q=q, ff1=ff1: E.transpose(
                    PTHQ[q][0][:, fc * 128:(fc + 1) * 128],
                    ff1[:, fc * 128:(fc + 1) * 128], C["IDN"]),
                    reads=(("cFF1", q), "IDN"), writes=(("cPT", q, 0),),
                    out_free=128)
            pl.op("v", lambda E, q=q, ff1f=ff1f: E.tensor_copy(
                out=ff1f, in_=PTHQ[q][0][:, 0:256]),
                reads=(("cPT", q, 0),), writes=(("cFF1F", q),), out_free=256)
            for half in range(2):
                for kc in range(2):
                    pl.op("t", lambda E, kc=kc, half=half, q=q, B1=B1,
                          ff1f=ff1f: E.matmul(
                        B1[:, half * 512:(half + 1) * 512],
                        ff1f[:, kc * 128:(kc + 1) * 128],
                        FUWT[:, kc * 1024 + half * 512:
                             kc * 1024 + (half + 1) * 512],
                        start=(kc == 0), stop=False),
                        reads=(("cFF1F", q), ("nFUWT",)), writes=(("cP", q),),
                        out_free=512)
                pl.op("t", lambda E, half=half, q=q, B1=B1: E.matmul(
                    B1[:, half * 512:(half + 1) * 512], ONES1,
                    FUBH[0:1, half * 512:(half + 1) * 512],
                    start=False, stop=True),
                    reads=(("nFUB",), ("nONES1",)), writes=(("cP", q),),
                    out_free=512)
            pl.op("v", lambda E, q=q, B1=B1, ob=ob, hn=hn: E.tensor_tensor(
                out=ob, in0=B1, in1=hn, op=ALU.add),
                reads=(("cP", q), ("cHN", q)), writes=(("cOB", q),),
                out_free=1024)
            pl.op("s", lambda E, q=q, u=u, ob=ob: E.activation(out=JS, in_=ob,
                  func=AF.Copy, accum_out=B_SUM[:, u:u + 1]),
                  reads=(("cOB", q),), writes=(("nJS",), ("nB_SUM", u)),
                  out_free=1024)
            pl.op("s", lambda E, q=q, u=u, ob=ob: E.activation(
                out=JS2, in_=ob, func=AF.Square, accum_out=B_SQ[:, u:u + 1]),
                reads=(("cOB", q),), writes=(("nJS2",), ("nB_SQ", u)),
                out_free=1024)
            pl.op("s", lambda E, tl=tl, u=u: E.dma_start(
                out=XT[u], in_=x_in[tl * 128:(tl + 1) * 128, :]), dma=True,
                reads=((kx_in, tl),), writes=(("nXT", u),))
            pl.op("v", lambda E, u=u: E.tensor_scalar(
                out=B_M[:, u:u + 1], in0=B_SUM[:, u:u + 1],
                scalar1=1.0 / 1024, scalar2=None, op0=ALU.mult),
                reads=(("nB_SUM", u),), writes=(("nB_M", u),), out_free=1)
            pl.op("v", lambda E, u=u: E.tensor_scalar(
                out=BT[:, u:u + 1], in0=B_SQ[:, u:u + 1],
                scalar1=1.0 / 1024, scalar2=None, op0=ALU.mult),
                reads=(("nB_SQ", u),), writes=(("nBT", u),), out_free=1)
            pl.op("v", lambda E, u=u: E.tensor_tensor(
                out=B_R[:, u:u + 1], in0=B_M[:, u:u + 1],
                in1=B_M[:, u:u + 1], op=ALU.mult), reads=(("nB_M", u),),
                writes=(("nB_R", u),), out_free=1)
            pl.op("v", lambda E, u=u: E.tensor_tensor(
                out=BT[:, u:u + 1], in0=BT[:, u:u + 1],
                in1=B_R[:, u:u + 1], op=ALU.subtract),
                reads=(("nBT", u), ("nB_R", u)), writes=(("nBT", u),),
                out_free=1)
            pl.op("v", lambda E, u=u: E.tensor_scalar(
                out=BT[:, u:u + 1], in0=BT[:, u:u + 1], scalar1=1e-5,
                scalar2=None, op0=ALU.add), reads=(("nBT", u),),
                writes=(("nBT", u),), out_free=1)
            emit_rsqrt(pl, B_R[:, u:u + 1], BT[:, u:u + 1],
                       BSCR[:, u:u + 1], ("nB_R", u), ("nBT", u),
                       ("nBscr", u))
            pl.op("v", lambda E, u=u: E.scalar_tensor_tensor(
                out=NMR[:, u:u + 1], in0=B_M[:, u:u + 1], scalar=-1.0,
                in1=B_R[:, u:u + 1], op0=ALU.mult, op1=ALU.mult),
                reads=(("nB_M", u), ("nB_R", u)), writes=(("nNMR", u),),
                out_free=1)
            pl.op("s", lambda E, q=q, u=u, ob=ob: E.activation(
                out=ob, in_=ob, func=AF.Identity, scale=B_R[:, u:u + 1],
                bias=NMR[:, u:u + 1]),
                reads=(("cOB", q),), sreads=(("nB_R", u), ("nNMR", u)),
                writes=(("cOB", q),), out_free=1024)
            pl.op("p", lambda E, q=q, u=u, ob=ob: E.tensor_tensor(
                out=XT[u], in0=ob, in1=XT[u], op=ALU.add),
                reads=(("cOB", q), ("nXT", u)), writes=(("nXT", u),),
                out_free=1024)
            tok = pl.op("y", lambda E, tl=tl, u=u: E.dma_start(
                out=x_out[tl * 128:(tl + 1) * 128, :], in_=XT[u]), dma=True,
                reads=(("nXT", u),), writes=((kx_out, tl),))
            out_toks.append(tok)

        stageA(0)
        stageA2(0)
        for tl in range(1, N_ST * 4):
            stageA(tl)
            stageB1(tl - 1)
            stageA2(tl)
            stageB2(tl - 1)
        stageB1(N_ST * 4 - 1)
        stageB2(N_ST * 4 - 1)
    return out_toks



WEIGHT_SPECS = {
    "L0_attn_Wq": [1024, 1024], "L0_attn_Wk": [1024, 1024],
    "L0_attn_Wv": [1024, 1024], "L0_attn_Wo": [1024, 1024],
    "L0_attn_gamma": [1024], "L0_attn_beta": [1024],
    "L0_attn_Wg1": [1024, 64], "L0_attn_Wg2": [64, 1024],
    "L0_ffn_gate_w": [8, 1024], "L0_ffn_lb_bias": [8],
    "L0_ffn_expert_gate_w": [8, 512, 1024], "L0_ffn_expert_up_w": [8, 512, 1024],
    "L0_ffn_expert_down_w": [8, 1024, 512],
    "L0_ffn_shared_gate_w": [1024, 1024], "L0_ffn_shared_up_w": [1024, 1024],
    "L0_ffn_shared_down_w": [1024, 1024],
    "L1_attn_Wq": [1024, 512], "L1_attn_Wkv": [1024, 1024], "L1_attn_Wo": [512, 1024],
    "L1_ffn_dw_kernel": [1024, 1, 3], "L1_ffn_dw_bias": [1024],
    "L1_ffn_pw_w": [1024, 1024], "L1_ffn_pw_b": [1024],
    "L1_ffn_router_w": [8, 1024], "L1_ffn_router_b": [8],
    "L1_ffn_expert_w": [8, 1024, 1024], "L1_ffn_expert_b": [8, 1024],
    "L1_ffn_out_w": [1024, 1024], "L1_ffn_out_b": [1024],
    "L1_ffn_norm_w": [1024],
    "L1_ffn_ff_down_w": [256, 1024], "L1_ffn_ff_down_b": [256],
    "L1_ffn_ff_up_w": [1024, 256], "L1_ffn_ff_up_b": [1024],
}
CONST_SPECS = {
    "c_idn": ([128, 128], BF16), "c_maskT": ([128, 16 * 128], BF16),
    "c_lamq": ([128, 1024], BF16), "c_lamk": ([128, 1024], BF16),
    "c_lamc": ([128, 512], F32), "c_sel8": ([8, 1024], BF16),
}
def _dwdg(inp):
    kern = np.asarray(inp["L1_ffn_dw_kernel"], np.float32)
    out = np.zeros((128, 24 * 128), np.float32)
    for fc in range(8):
        for t in range(3):
            blk = out[:, (fc * 3 + t) * 128:(fc * 3 + t + 1) * 128]
            np.fill_diagonal(blk, kern[fc * 128:(fc + 1) * 128, 0, t])
    return out


DERIVED_SPECS = {
    "d_dwdg": ([128, 3072], F32, _dwdg),
    "d_gcol": ([128, 8], F32, lambda inp: np.ascontiguousarray(
        np.asarray(inp["L0_attn_gamma"], np.float32).reshape(8, 128).T)),
    "d_gwT": ([128, 64], F32, lambda inp: np.ascontiguousarray(
        np.asarray(inp["L0_ffn_gate_w"], np.float32).T.reshape(8, 128, 8)
        .transpose(1, 0, 2).reshape(128, 64))),
    "d_pwT": ([1024, 1024], F32, lambda inp: np.ascontiguousarray(
        np.asarray(inp["L1_ffn_pw_w"], np.float32).T)),
    "d_rwT": ([128, 64], F32, lambda inp: np.ascontiguousarray(
        np.asarray(inp["L1_ffn_router_w"], np.float32).T.reshape(8, 128, 8)
        .transpose(1, 0, 2).reshape(128, 64))),
    "d_owT": ([1024, 1024], F32, lambda inp: np.ascontiguousarray(
        np.asarray(inp["L1_ffn_out_w"], np.float32).T)),
    "d_fdwT": ([1024, 256], F32, lambda inp: np.ascontiguousarray(
        np.asarray(inp["L1_ffn_ff_down_w"], np.float32).T)),
    "d_fuwT": ([256, 1024], F32, lambda inp: np.ascontiguousarray(
        np.asarray(inp["L1_ffn_ff_up_w"], np.float32).T)),
    "d_ewT": ([8, 1024, 1024], F32, lambda inp: np.ascontiguousarray(
        np.asarray(inp["L1_ffn_expert_w"], np.float32).transpose(0, 2, 1))),
    "d_pwbc": ([128, 8], F32, lambda inp: np.ascontiguousarray(
        np.asarray(inp["L1_ffn_pw_b"], np.float32).reshape(8, 128).T)),
    "d_dwk": ([128, 24], F32, lambda inp: np.ascontiguousarray(
        np.asarray(inp["L1_ffn_dw_kernel"], np.float32)[:, 0, :]
        .reshape(8, 128, 3).transpose(1, 0, 2).reshape(128, 24))),
    "d_dwbc": ([128, 8], F32, lambda inp: np.ascontiguousarray(
        np.asarray(inp["L1_ffn_dw_bias"], np.float32).reshape(8, 128).T)),
}
DERIVED_BLOCKS = {"d_gcol": "l0a", "d_gwT": "l0f", "d_pwT": "l1f", "d_dwdg": "l1f",
                  "d_rwT": "l1f", "d_owT": "l1f", "d_fdwT": "l1f",
                  "d_fuwT": "l1f", "d_ewT": "l1f", "d_pwbc": "l1f",
                  "d_dwk": "l1f", "d_dwbc": "l1f"}
BLOCK_WEIGHTS = {
    "l0a": ["L0_attn_Wq", "L0_attn_Wk", "L0_attn_Wv", "L0_attn_Wo", "L0_attn_gamma",
            "L0_attn_beta", "L0_attn_Wg1", "L0_attn_Wg2"],
    "l0f": ["L0_ffn_gate_w", "L0_ffn_lb_bias", "L0_ffn_expert_gate_w",
            "L0_ffn_expert_up_w", "L0_ffn_expert_down_w", "L0_ffn_shared_gate_w",
            "L0_ffn_shared_up_w", "L0_ffn_shared_down_w"],
    "l1a": ["L1_attn_Wq", "L1_attn_Wkv", "L1_attn_Wo"],
    "l1f": ["L1_ffn_dw_kernel", "L1_ffn_dw_bias", "L1_ffn_pw_w", "L1_ffn_pw_b",
            "L1_ffn_router_w", "L1_ffn_router_b", "L1_ffn_expert_w", "L1_ffn_expert_b",
            "L1_ffn_out_w", "L1_ffn_out_b", "L1_ffn_norm_w", "L1_ffn_ff_down_w",
            "L1_ffn_ff_down_b", "L1_ffn_ff_up_w", "L1_ffn_ff_up_b"],
}
BLOCK_EMIT = {"l0a": emit_l0_attn, "l0f": emit_l0_ffn, "l1a": emit_l1_attn,
              "l1f": emit_l1_ffn}


def build_program(blocks):
    nc = bass.Bass("TRN2", target_bir_lowering=False)
    W = {}
    for b in blocks:
        for name in BLOCK_WEIGHTS[b]:
            W[name] = nc.dram_tensor(name, WEIGHT_SPECS[name], F32,
                                     kind="ExternalInput").ap()
    for name, (shape, dt) in CONST_SPECS.items():
        W[name] = nc.dram_tensor(name, shape, dt, kind="ExternalInput").ap()
    for name, (shape, dt, _fn) in DERIVED_SPECS.items():
        if DERIVED_BLOCKS[name] in blocks:
            W[name] = nc.dram_tensor(name, shape, dt, kind="ExternalInput").ap()
    x_in = nc.dram_tensor("x", [N_ST * 512, D], F32, kind="ExternalInput").ap()
    x_out = nc.dram_tensor("xo", [N_ST * 512, D], F32, kind="ExternalOutput").ap()
    mids = [nc.dram_tensor(f"xmid{i}", [N_ST * 512, D], F32, kind="Internal").ap()
            for i in range(len(blocks) - 1)]
    chain = [x_in] + mids + [x_out]
    knames = ["x"] + [f"xmid{i}" for i in range(len(mids))] + ["xo"]

    pl = Plan(nc)
    ctx = contextlib.ExitStack()
    C = {}
    idn_sb = ctx.enter_context(nc.sbuf_tensor("c_idn_sb", [128, 128], BF16)).ap()
    C["IDN"] = idn_sb
    pl.op("y", lambda E: E.dma_start(out=idn_sb, in_=W["c_idn"]), dma=True,
          writes=("IDN",))

    out_toks = []
    for i, b in enumerate(blocks):
        if i > 0:
            pl.block_barrier()
        with contextlib.ExitStack() as bctx:
            toks = BLOCK_EMIT[b](pl, nc, bctx, chain[i], chain[i + 1],
                                 knames[i], knames[i + 1], W, C)
        if i == len(blocks) - 1:
            out_toks = toks
    pl.finalize(out_toks)
    pl.emit(ctx)
    ctx.close()
    return nc


def _shard_inputs(inputs):
    x = np.ascontiguousarray(np.asarray(inputs["L0_attn_x"], np.float32))
    xs = x.reshape(NCORES, TOK, D)
    consts = host_consts()
    maps = []
    for i in range(NCORES):
        m = {"x": np.ascontiguousarray(xs[i])}
        for name in WEIGHT_SPECS:
            if name in inputs:
                m[name] = np.ascontiguousarray(np.asarray(inputs[name], np.float32))
        m.update(consts)
        for name, (shape, dt, fn) in DERIVED_SPECS.items():
            m[name] = fn(inputs)
        maps.append(m)
    return maps


def run_blocks(inputs, blocks):
    maps = _shard_inputs(inputs)
    nc = build_program(blocks)
    need = set()
    for b in blocks:
        need.update(BLOCK_WEIGHTS[b])
    dneed = {n for n, b in DERIVED_BLOCKS.items() if b in blocks}
    in_maps = [{k: v for k, v in m.items()
                if k == "x" or k in need or k in CONST_SPECS or k in dneed}
               for m in maps]
    res = run_bass_kernel_spmd(nc, in_maps, core_ids=list(range(NCORES)))
    out = np.stack([np.asarray(r["xo"]) for r in res.results], axis=0)
    return out.reshape(16, 2048, 1024).astype(np.float32)


def kernel(**inputs):
    return run_blocks(inputs, ["l0a", "l0f", "l1a", "l1f"])


# version 20 (final)
# speedup vs baseline: 1.2222x; 1.0116x over previous
import contextlib
import math
import numpy as np
import ml_dtypes
import concourse.bass as bass
import concourse.mybir as mybir
from concourse.bass_utils import run_bass_kernel_spmd

F32 = mybir.dt.float32
BF16 = mybir.dt.bfloat16
I32 = mybir.dt.int32
AF = mybir.ActivationFunctionType
ALU = mybir.AluOpType
AX = mybir.AxisListType.X

NCORES = 8
TOK = 4096
D = 1024
NT = TOK // 128
N_ST = 8
SEQ_ST = 4

_BLOCK_METH = {"t": "tensor", "v": "vector", "s": "scalar", "p": "gpsimd", "y": "sync"}
DRAIN_FREE_THRESH = 2048


class Plan:

    def __init__(self, nc):
        self.nc = nc
        self.engs = list("tvspy")
        self.ops = {e: [] for e in self.engs}
        self.dma_idx = {e: 0 for e in self.engs}
        self.last_w = {}
        self.readers = {}
        self.tok_free = {}
        self.last_drain = {e: 0 for e in self.engs}
        self.marked = set()

    def _drain(self, e):
        self.ops[e].append(dict(fn=lambda eng: eng.drain(), deps=set(), dma=False,
                                tag="drain"))
        self.last_drain[e] = len(self.ops[e])

    @staticmethod
    def _ck(b):
        return (b,) if isinstance(b, str) else tuple(b)

    def op(self, e, fn, reads=(), writes=(), sreads=(), dma=False, out_free=1 << 30,
           tag=""):
        reads = tuple(self._ck(b) for b in reads)
        sreads = tuple(self._ck(b) for b in sreads)
        writes = tuple(self._ck(b) for b in writes)
        deps = set()
        raw_deps = set()
        for b in tuple(reads) + tuple(sreads):
            t = self.last_w.get(b)
            if t is not None:
                deps.add(t)
                raw_deps.add(t)
        for b in tuple(writes):
            t = self.last_w.get(b)
            if t is not None:
                deps.add(t)
            for t2 in self.readers.get(b, ()):
                deps.add(t2)
        if e in "vsp":
            need = False
            for d in deps:
                if d[0] == e and d[1] >= self.last_drain[e]:
                    if (d in raw_deps) or (self.tok_free.get(d, 1 << 30) < DRAIN_FREE_THRESH):
                        need = True
            if need:
                self._drain(e)
        deps = {d for d in deps if d[0] != e}
        for d in deps:
            self.marked.add(d)
        idx = len(self.ops[e])
        rec = dict(fn=fn, deps=deps, dma=dma, tag=tag)
        self.ops[e].append(rec)
        if dma:
            i = self.dma_idx[e]
            token = (e + "D" + str(i % 8), i // 8)
            self.dma_idx[e] += 1
            rec["dtok"] = token
            self.marked.add(token)
        else:
            token = (e, idx)
        self.tok_free[token] = out_free
        for b in tuple(reads) + tuple(sreads):
            self.readers.setdefault(b, []).append(token)
        for b in tuple(writes):
            self.last_w[b] = token
            self.readers[b] = []
        return token

    def tail_token(self, e):
        for i in range(len(self.ops[e]) - 1, -1, -1):
            o = self.ops[e][i]
            if o["tag"] == "barrier":
                continue
            if o["dma"]:
                return o["dtok"]
            return (e, i)
        return None

    def dma_tail_tokens(self):
        toks = []
        for e in self.engs:
            n = self.dma_idx[e]
            for k in range(8):
                nk = (n + 7 - k) // 8
                if nk > 0:
                    toks.append((e + "D" + str(k), nk - 1))
        return toks

    def block_barrier(self):
        tails = [self.tail_token(e) for e in self.engs] + self.dma_tail_tokens()
        tails = [t for t in tails if t is not None]
        for e in self.engs:
            deps = {t for t in tails if t[0] != e}
            for d in deps:
                self.marked.add(d)
            self.ops[e].append(dict(fn=lambda eng: eng.nop(), deps=deps,
                                    dma=False, tag="barrier"))

    def finalize(self, out_tokens):
        deps = set(t for t in out_tokens if t is not None)
        deps.update(self.dma_tail_tokens())
        for e in self.engs:
            if e == "y":
                continue
            t = self.tail_token(e)
            if t is not None:
                deps.add(t)
        for d in deps:
            self.marked.add(d)
        self.ops["y"].append(dict(fn=lambda eng: eng.nop(), deps=deps,
                                  dma=False, tag="final"))

    def emit(self, ctx):
        nc = self.nc
        doms = {d[0] for d in self.marked}
        sems = {d: ctx.enter_context(nc.semaphore(f"sem_{d}")) for d in sorted(doms)}
        rank = {}
        for d in self.marked:
            rank.setdefault(d[0], {})[d[1]] = 0
        for dom in rank:
            for r, idx in enumerate(sorted(rank[dom].keys())):
                rank[dom][idx] = r + 1

        def wait_val(tok):
            return rank[tok[0]][tok[1]] * (16 if len(tok[0]) > 1 else 1)

        plan = self

        def make_runner(e):
            def run(eng):
                wm = {}
                for i, o in enumerate(plan.ops[e]):
                    for tok in sorted(o["deps"]):
                        dom = tok[0]
                        v = wait_val(tok)
                        if wm.get(dom, 0) < v:
                            eng.wait_ge(sems[dom], v)
                            wm[dom] = v
                    if o["dma"]:
                        dom, j = o["dtok"]
                        if j > 0 and wm.get(dom, 0) < 16 * j:
                            eng.wait_ge(sems[dom], 16 * j)
                            wm[dom] = 16 * j
                    inst = o["fn"](eng)
                    if o["dma"]:
                        inst.then_inc(sems[o["dtok"][0]], 16)
                    elif (e, i) in plan.marked:
                        inst.then_inc(sems[e], 1)
            return run

        with nc.Block() as block:
            for e in self.engs:
                getattr(block, _BLOCK_METH[e])(make_runner(e))


def emit_rsqrt(pl, out_ap, in_ap, t_ap, key_out, key_in, key_t, newton=2):
    MAGIC = 0x5F3759DF
    fr = 1
    pl.op("v", lambda E, o=out_ap, i=in_ap: E.tensor_scalar(
        out=o.bitcast(I32), in0=i.bitcast(I32), scalar1=1, scalar2=None,
        op0=ALU.arith_shift_right), reads=(key_in,), writes=(key_out,), out_free=fr)
    pl.op("v", lambda E, o=out_ap: E.tensor_scalar(
        out=o.bitcast(I32), in0=o.bitcast(I32), scalar1=-1, scalar2=None,
        op0=ALU.bitwise_xor), reads=(key_out,), writes=(key_out,), out_free=fr)
    pl.op("v", lambda E, o=out_ap: E.tensor_scalar(
        out=o.bitcast(I32), in0=o.bitcast(I32), scalar1=MAGIC + 1, scalar2=None,
        op0=ALU.add), reads=(key_out,), writes=(key_out,), out_free=fr)
    for _ in range(newton):
        pl.op("v", lambda E, o=out_ap, t=t_ap: E.tensor_tensor(
            out=t, in0=o, in1=o, op=ALU.mult), reads=(key_out,), writes=(key_t,),
            out_free=fr)
        pl.op("v", lambda E, t=t_ap, i=in_ap: E.tensor_tensor(
            out=t, in0=t, in1=i, op=ALU.mult), reads=(key_t, key_in), writes=(key_t,),
            out_free=fr)
        pl.op("v", lambda E, t=t_ap: E.tensor_scalar(
            out=t, in0=t, scalar1=-0.5, scalar2=1.5, op0=ALU.mult, op1=ALU.add),
            reads=(key_t,), writes=(key_t,), out_free=fr)
        pl.op("v", lambda E, o=out_ap, t=t_ap: E.tensor_tensor(
            out=o, in0=o, in1=t, op=ALU.mult), reads=(key_out, key_t),
            writes=(key_out,), out_free=fr)


def ln_finalize(pl, SUM, SQ, M, R, T, pref, sum_keys, sq_keys, eps=1e-5, ncols=4,
                inv_n=1.0 / 1024):
    kM, kR, kT = (pref, "M"), (pref, "R"), (pref, "T")
    sl = (slice(None), slice(0, ncols))
    pl.op("v", lambda E: E.tensor_scalar(out=M[sl], in0=SUM[sl], scalar1=inv_n,
          scalar2=None, op0=ALU.mult), reads=tuple(sum_keys), writes=(kM,),
          out_free=ncols)
    pl.op("v", lambda E: E.tensor_scalar(out=T[sl], in0=SQ[sl], scalar1=inv_n,
          scalar2=None, op0=ALU.mult), reads=tuple(sq_keys), writes=(kT,),
          out_free=ncols)
    pl.op("v", lambda E: E.tensor_tensor(out=R[sl], in0=M[sl], in1=M[sl],
          op=ALU.mult), reads=(kM,), writes=(kR,), out_free=ncols)
    pl.op("v", lambda E: E.tensor_tensor(out=T[sl], in0=T[sl], in1=R[sl],
          op=ALU.subtract), reads=(kR, kT), writes=(kT,), out_free=ncols)
    pl.op("v", lambda E: E.tensor_scalar(out=T[sl], in0=T[sl], scalar1=eps,
          scalar2=None, op0=ALU.add), reads=(kT,), writes=(kT,), out_free=ncols)
    emit_rsqrt(pl, R[sl], T[sl], SUM[sl], kR, kT, (pref, "scr"))
    return kM, kR


def slopes16():
    start = 2.0 ** (-(2.0 ** (-(math.log2(16) - 3))))
    s = np.array([start ** (i + 1) for i in range(16)], dtype=np.float64)
    return s * (1.0 + 1e-5)


def host_consts():
    s = slopes16()
    i = np.arange(128.0)
    diff = i[None, :] - i[:, None]
    maskT = np.where(diff >= 0, np.exp(-s[:, None, None] * diff[None]), 0.0)
    maskT_t = maskT.transpose(1, 0, 2).reshape(128, 16 * 128)
    lamq = np.zeros((128, 1024))
    for ofc in range(8):
        for half in range(2):
            h = 2 * ofc + half
            lamq[half * 64:(half + 1) * 64, ofc * 128:(ofc + 1) * 128] = \
                np.exp(-s[h] * (i[None, :] + 1.0))
    lamk = np.zeros((128, 1024))
    for h in range(16):
        lamk[:, h * 64:(h + 1) * 64] = np.exp(-s[h] * (127.0 - i))[:, None]
    lamc = np.zeros((128, 512))
    for p2 in range(2):
        for f8 in range(8):
            h = p2 + 2 * f8
            lamc[p2 * 64:(p2 + 1) * 64, f8 * 64:(f8 + 1) * 64] = np.exp(-128.0 * s[h])
    sel8 = np.zeros((8, 1024))
    for e in range(8):
        sel8[e, e * 128:(e + 1) * 128] = 1.0
    bf = ml_dtypes.bfloat16
    return {
        "c_idn": np.eye(128, dtype=np.float32).astype(bf),
        "c_maskT": maskT_t.astype(np.float32).astype(bf),
        "c_lamq": lamq.astype(np.float32).astype(bf),
        "c_lamk": lamk.astype(np.float32).astype(bf),
        "c_lamc": lamc.astype(np.float32),
        "c_sel8": sel8.astype(np.float32).astype(bf),
    }



def emit_l0_attn(pl, nc, ctx, x_in, x_out, kx_in, kx_out, W, C):
    import os
    SKIP = set(os.environ.get("L0A_SKIP", "").split(","))
    E_ = ctx.enter_context
    sb = lambda name, shape, dt: E_(nc.sbuf_tensor(name, shape, dt)).ap()
    ps = lambda name, shape, dt: E_(nc.psum_tensor(name, shape, dt)).ap()

    WQ = sb("wq", [128, 8 * 1024], BF16)
    WK = sb("wk", [128, 8 * 1024], BF16)
    WV = sb("wv", [128, 8 * 1024], BF16)
    WO = sb("wo", [128, 8 * 1024], BF16)
    WG1 = sb("wg1", [128, 8 * 64], BF16)
    WG2 = sb("wg2", [64, 1024], BF16)
    GREP = sb("gamma_rep", [128, 1024], F32)
    BETA = sb("beta_rep", [128, 1024], F32)
    MASKT = sb("maskt", [128, 16 * 128], BF16)
    LAMQ = sb("lamq", [128, 1024], BF16)
    LAMK = sb("lamk", [128, 1024], BF16)
    LAMC = sb("lamc", [128, 512], F32)

    def wld(dst, src_name, kname):
        src = W[src_name]
        if src.shape[0] > 128:
            kc = src.shape[0] // 128
            view = src.rearrange("(kc p) o -> p kc o", p=128)
            d = dst.rearrange("p (kc o) -> p kc o", kc=kc)
        else:
            view, d = src, dst
        pl.op("p", lambda E, d=d, s=view: E.dma_start(out=d, in_=s), dma=True,
              writes=(kname,), tag=f"ld_{kname}")

    wld(WQ, "L0_attn_Wq", "WQ")
    wld(WK, "L0_attn_Wk", "WK")
    wld(WV, "L0_attn_Wv", "WV")
    wld(WO, "L0_attn_Wo", "WO")
    wld(WG1, "L0_attn_Wg1", "WG1")
    wld(WG2, "L0_attn_Wg2", "WG2")
    pl.op("y", lambda E: E.dma_start(
        out=GREP, in_=W["L0_attn_gamma"].unsqueeze(0).broadcast_to([128, 1024])),
        dma=True, writes=("GREP",))
    pl.op("y", lambda E: E.dma_start(
        out=BETA, in_=W["L0_attn_beta"].unsqueeze(0).broadcast_to([128, 1024])),
        dma=True, writes=("BETA",))
    pl.op("y", lambda E: E.dma_start(out=MASKT, in_=W["c_maskT"]), dma=True,
          writes=("MASKT",))
    pl.op("y", lambda E: E.dma_start(out=LAMQ, in_=W["c_lamq"]), dma=True,
          writes=("LAMQ",))
    pl.op("y", lambda E: E.dma_start(out=LAMK, in_=W["c_lamk"]), dma=True,
          writes=("LAMK",))
    pl.op("y", lambda E: E.dma_start(out=LAMC, in_=W["c_lamc"]), dma=True,
          writes=("LAMC",))
    for kc in range(8):
        pl.op("v", lambda E, kc=kc: E.tensor_scalar(
            out=WO[:, kc * 1024:(kc + 1) * 1024], in0=WO[:, kc * 1024:(kc + 1) * 1024],
            scalar1=0.5, scalar2=None, op0=ALU.mult),
            reads=("WO",), writes=("WO",), out_free=1024)

    XT = [sb(f"xt{i}", [128, 1024], F32) for i in range(4)]
    XTB = [sb(f"xtb{i}", [128, 1024], F32) for i in range(2)]
    XHB = sb("xhb", [128, 1024], BF16)
    XHB2 = sb("xhb2", [128, 1024], BF16)
    NMRS = sb("nmrs", [128, 8], F32)
    XHF = sb("xhf", [128, 8 * 512], BF16)
    QFM = sb("qfm", [128, 8 * 512], BF16)
    KFM = sb("kfm", [128, 8 * 512], BF16)
    QLAM = sb("qlam", [128, 8 * 512], BF16)
    KLAM = sb("klam", [128, 4 * 1024], BF16)
    VTM = sb("vtm", [128, 4 * 1024], BF16)
    SCBF = [sb(f"scbf{i}", [128, 512], BF16) for i in range(2)]
    OTM = sb("otm", [128, 4 * 1024], BF16)
    OB = sb("ob", [128, 4 * 1024], F32)
    ZBF = sb("zbf", [128, 1024], BF16)
    ZFM = sb("zfm", [128, 1024], BF16)
    G1BF = sb("g1bf", [64, 512], BF16)
    GT = [sb(f"gt{i}", [128, 1024], BF16) for i in range(4)]
    KV32 = sb("kv32", [128, 512], F32)
    KVBF = sb("kvbf", [128, 512], BF16)
    JV = sb("jv", [128, 1024], BF16)
    JS = sb("js", [128, 1024], BF16)
    S_SUM = sb("s_sum", [128, 8], F32)
    S_SQ = sb("s_sq", [128, 8], F32)
    S_M = sb("s_m", [128, 8], F32)
    S_R = sb("s_r", [128, 8], F32)
    S_T = sb("s_t", [128, 8], F32)
    I_SUM = sb("i_sum", [128, 4], F32)
    I_SQ = sb("i_sq", [128, 4], F32)
    I_M = sb("i_m", [128, 4], F32)
    I_R = sb("i_r", [128, 4], F32)
    B_SUM = sb("b_sum", [128, 4], F32)
    B_SQ = sb("b_sq", [128, 4], F32)
    B_M = sb("b_m", [128, 4], F32)
    B_R = sb("b_r", [128, 4], F32)

    PA = ps("pa", [128, 1024], F32)
    PB = ps("pb", [128, 1024], F32)
    PC = ps("pc", [128, 512], F32)
    PC2 = ps("pc2", [128, 512], F32)
    PD = ps("pd", [128, 512], F32)
    PT = ps("pt", [128, 1024], BF16)

    out_toks = []
    XHF_R = tuple(("XHF", u) for u in range(4))

    def pre_stats(st):
        q = st % 2
        for u in range(4):
            tl = st * 4 + u
            kx = ("XT", u)
            pl.op("y", lambda E, tl=tl, u=u: E.dma_start(
                out=XT[u], in_=x_in[tl * 128:(tl + 1) * 128, :]), dma=True,
                reads=((kx_in, tl),), writes=(kx,), tag=f"ldx{tl}")
            pl.op("s", lambda E, u=u, q=q: E.activation(
                out=JV, in_=XT[u], func=AF.Square,
                accum_out=S_SQ[:, q * 4 + u:q * 4 + u + 1]),
                reads=(kx,), writes=(("JV",), ("S_SQ", q, u)), out_free=1024)
            pl.op("s", lambda E, u=u, q=q: E.activation(
                out=JS, in_=XT[u], func=AF.Copy,
                accum_out=S_SUM[:, q * 4 + u:q * 4 + u + 1]),
                reads=(kx,), writes=(("JS",), ("S_SUM", q, u)), out_free=1024)

    for st in range(N_ST):
        if st % SEQ_ST == 0:
            pl.op("v", lambda E: E.memset(KV32, 0.0), writes=(("KV32",),),
                  out_free=512)
            pl.op("v", lambda E: E.memset(KVBF, 0.0), writes=(("KVBF",),),
                  out_free=512)

        q = st % 2
        qs = slice(q * 4, q * 4 + 4)
        if st == 0:
            pre_stats(0)
        ln_finalize(pl, S_SUM[:, qs], S_SQ[:, qs], S_M[:, qs], S_R[:, qs],
                    S_T[:, qs], f"Sst{q}",
                    [("S_SUM", q, u) for u in range(4)],
                    [("S_SQ", q, u) for u in range(4)])
        if 'lnT' not in SKIP:
            pl.op("v", lambda E, q=q, qs=qs: E.scalar_tensor_tensor(
                out=NMRS[:, qs], in0=S_M[:, qs], scalar=-1.0, in1=S_R[:, qs],
                op0=ALU.mult,
                op1=ALU.mult), reads=((f"Sst{q}", "M"), (f"Sst{q}", "R")),
                writes=(("NMRS", q),), out_free=4)
        for u in range(4) if 'lnT' not in SKIP else []:
            kx = ("XT", u)
            xhb = XHB if u % 2 == 0 else XHB2
            kxhb = ("XHB", u % 2)
            pl.op("s", lambda E, u=u, q=q, xhb=xhb: E.activation(
                out=xhb, in_=XT[u], func=AF.Identity,
                scale=S_R[:, q * 4 + u:q * 4 + u + 1],
                bias=NMRS[:, q * 4 + u:q * 4 + u + 1]), reads=(kx,),
                sreads=((f"Sst{q}", "R"), ("NMRS", q)), writes=(kxhb,),
                out_free=1024)
            for fc in range(8):
                pl.op("t", lambda E, fc=fc, xhb=xhb, kxhb=kxhb: E.transpose(
                    PT[:, fc * 128:(fc + 1) * 128], xhb[:, fc * 128:(fc + 1) * 128],
                    C["IDN"]), reads=(kxhb, "IDN"), writes=(("PT",),),
                    out_free=128)
            pl.op("v", lambda E, u=u: E.tensor_copy(
                out=XHF.rearrange("p (f c) -> p f c", f=8)[:, :, u * 128:(u + 1) * 128],
                in_=PT.rearrange("p (f c) -> p f c", f=8)),
                reads=(("PT",),), writes=(("XHF", u),), out_free=1024)
        if st + 1 < N_ST:
            pre_stats(st + 1)

        for ofc in range(8) if 'proj' not in SKIP else []:
            for kc in range(8):
                pl.op("t", lambda E, ofc=ofc, kc=kc: E.matmul(
                    PA[:, 0:512],
                    WQ[:, kc * 1024 + ofc * 128: kc * 1024 + (ofc + 1) * 128],
                    XHF[:, kc * 512:(kc + 1) * 512], start=(kc == 0), stop=(kc == 7)),
                    reads=(("WQ",),) + XHF_R, writes=(("PA", 0),), out_free=512)
            pl.op("s", lambda E, ofc=ofc: E.activation(
                out=QFM[:, ofc * 512:(ofc + 1) * 512], in_=PA[:, 0:512], func=AF.Silu),
                reads=(("PA", 0),), writes=(("QFM", ofc),), out_free=512)
            for kc in range(8):
                pl.op("t", lambda E, ofc=ofc, kc=kc: E.matmul(
                    PA[:, 512:1024],
                    WK[:, kc * 1024 + ofc * 128: kc * 1024 + (ofc + 1) * 128],
                    XHF[:, kc * 512:(kc + 1) * 512], start=(kc == 0), stop=(kc == 7)),
                    reads=(("WK",),) + XHF_R, writes=(("PA", 1),), out_free=512)
            pl.op("s", lambda E, ofc=ofc: E.activation(
                out=KFM[:, ofc * 512:(ofc + 1) * 512], in_=PA[:, 512:1024],
                func=AF.Silu),
                reads=(("PA", 1),), writes=(("KFM", ofc),), out_free=512)
        if 'proj' not in SKIP: pl.op("v", lambda E: E.tensor_tensor(
            out=QLAM.rearrange("p (o c i) -> p o c i", o=8, c=4),
            in0=QFM.rearrange("p (o c i) -> p o c i", o=8, c=4),
            in1=LAMQ.rearrange("p (o i) -> p o i", o=8).unsqueeze(2).broadcast_to(
                [128, 8, 4, 128]),
            op=ALU.mult),
            reads=tuple(("QFM", ofc) for ofc in range(8)) + (("LAMQ",),),
            writes=(("QLAM",),), out_free=4096)
        for u in range(4) if 'proj' not in SKIP else []:
            for half in range(2):
                for kc in range(8):
                    pl.op("t", lambda E, u=u, kc=kc, half=half: E.matmul(
                        PB[:, half * 512:(half + 1) * 512],
                        XHF[:, kc * 512 + u * 128: kc * 512 + (u + 1) * 128],
                        WV[:, kc * 1024 + half * 512: kc * 1024 + (half + 1) * 512],
                        start=(kc == 0), stop=(kc == 7)),
                        reads=(("WV",), ("XHF", u)), writes=(("PB",),), out_free=512)
            pl.op("s", lambda E, u=u: E.activation(
                out=VTM[:, u * 1024:(u + 1) * 1024], in_=PB, func=AF.Silu),
                reads=(("PB",),), writes=(("VTM", u),), out_free=1024)
            for fc in range(8):
                pl.op("t", lambda E, u=u, fc=fc: E.transpose(
                    PT[:, fc * 128:(fc + 1) * 128],
                    KFM[:, fc * 512 + u * 128: fc * 512 + (u + 1) * 128], C["IDN"]),
                    reads=(("KFM", fc), "IDN"), writes=(("PT",),), out_free=128)
            pl.op("v", lambda E, u=u: E.tensor_tensor(
                out=KLAM[:, u * 1024:(u + 1) * 1024], in0=PT, in1=LAMK, op=ALU.mult),
                reads=(("PT",), ("LAMK",)), writes=(("KLAM", u),), out_free=1024)
        for kc in range(8) if 'proj' not in SKIP else []:
            pl.op("t", lambda E, kc=kc: E.matmul(
                PC[0:64, :], WG1[:, kc * 64:(kc + 1) * 64],
                XHF[:, kc * 512:(kc + 1) * 512], start=(kc == 0), stop=(kc == 7)),
                reads=(("WG1",),) + XHF_R, writes=(("PC",),), out_free=512)
        if 'proj' not in SKIP: pl.op("v", lambda E: E.tensor_copy(out=G1BF, in_=PC[0:64, :]),
              reads=(("PC",),), writes=(("G1BF",),), out_free=512)
        for u in range(4):
            for half in range(2):
                pl.op("t", lambda E, u=u, half=half: E.matmul(
                    PB[:, half * 512:(half + 1) * 512], G1BF[:, u * 128:(u + 1) * 128],
                    WG2[:, half * 512:(half + 1) * 512], start=True, stop=True),
                    reads=(("G1BF",), ("WG2",)), writes=(("PB",),), out_free=512)
            pl.op("s", lambda E, u=u: E.activation(
                out=GT[u], in_=PB, func=AF.Tanh, scale=0.5),
                reads=(("PB",),), writes=(("GT", u),), out_free=1024)

        for u in range(4) if 'scan' not in SKIP else []:
            for hg in range(4):
                sc = SCBF[hg % 2]
                ksc = ("SCBF", hg % 2)
                pc = PC if hg % 2 == 0 else PC2
                kpc = ("PC", hg % 2)
                for hh in range(4):
                    h = hg * 4 + hh
                    tp = (0, (h % 2) * 64)
                    pl.op("t", lambda E, h=h, u=u, tp=tp: E.matmul(
                        PD[(h % 2) * 64:(h % 2) * 64 + 64,
                           (h // 2) * 64:(h // 2) * 64 + 64],
                        KLAM[:, u * 1024 + h * 64: u * 1024 + (h + 1) * 64],
                        VTM[:, u * 1024 + h * 64: u * 1024 + (h + 1) * 64],
                        start=True, stop=True, tile_position=tp),
                        reads=(("KLAM", u), ("VTM", u)), writes=(("PD", h),),
                        out_free=64)
                    qs = QFM[(h % 2) * 64:(h % 2) * 64 + 64,
                             (h // 2) * 512 + u * 128:(h // 2) * 512 + (u + 1) * 128]
                    ks = KFM[(h % 2) * 64:(h % 2) * 64 + 64,
                             (h // 2) * 512 + u * 128:(h // 2) * 512 + (u + 1) * 128]
                    pl.op("t", lambda E, hh=hh, qs=qs, ks=ks, pc=pc: E.matmul(
                        pc[:, hh * 128:(hh + 1) * 128], ks, qs, start=True, stop=True),
                        reads=(("KFM", h // 2), ("QFM", h // 2)), writes=(kpc,),
                        out_free=128)
                pl.op("v", lambda E, hg=hg, sc=sc, pc=pc: E.tensor_tensor(
                    out=sc, in0=pc, in1=MASKT[:, hg * 512:(hg + 1) * 512],
                    op=ALU.mult),
                    reads=(kpc, ("MASKT",)), writes=(ksc,), out_free=512)
                for hh in range(4):
                    h = hg * 4 + hh
                    pl.op("t", lambda E, h=h, u=u, sc=sc, hh=hh: E.matmul(
                        PA[:, h * 64:(h + 1) * 64], sc[:, hh * 128:(hh + 1) * 128],
                        VTM[:, u * 1024 + h * 64: u * 1024 + (h + 1) * 64],
                        start=True, stop=False),
                        reads=(ksc, ("VTM", u)), writes=(("PA", h // 8),),
                        out_free=64)
                    qlam_s = QLAM[(h % 2) * 64:(h % 2) * 64 + 64,
                                  (h // 2) * 512 + u * 128:(h // 2) * 512 + (u + 1) * 128]
                    kv_s = KVBF[(h % 2) * 64:(h % 2) * 64 + 64,
                                (h // 2) * 64:(h // 2) * 64 + 64]
                    pl.op("t", lambda E, qlam_s=qlam_s, kv_s=kv_s, h=h: E.matmul(
                        PA[:, h * 64:(h + 1) * 64], qlam_s, kv_s, start=False,
                        stop=True),
                        reads=(("QLAM",), ("KVBF",)), writes=(("PA", h // 8),),
                        out_free=64)
            pl.op("v", lambda E: E.tensor_tensor(out=KV32, in0=KV32, in1=LAMC,
                  op=ALU.mult), reads=(("KV32",), ("LAMC",)), writes=(("KV32",),),
                  out_free=512)
            pl.op("v", lambda E: E.tensor_tensor(out=KV32, in0=KV32, in1=PD,
                  op=ALU.add),
                  reads=(("KV32",),) + tuple(("PD", h) for h in range(16)),
                  writes=(("KV32",),), out_free=512)
            pl.op("v", lambda E: E.tensor_copy(out=KVBF, in_=KV32),
                  reads=(("KV32",),), writes=(("KVBF",),), out_free=512)
            pl.op("s", lambda E, u=u: E.activation(
                out=OTM[:, u * 1024:(u + 1) * 1024], in_=PA, func=AF.Copy,
                accum_out=I_SUM[:, u:u + 1]),
                reads=(("PA", 0), ("PA", 1)), writes=(("OTM", u), ("I_SUM", u)),
                out_free=1024)
            pl.op("v", lambda E, u=u: E.scalar_tensor_tensor(
                out=JV, in0=OTM[:, u * 1024:(u + 1) * 1024], scalar=1.0,
                in1=OTM[:, u * 1024:(u + 1) * 1024], op0=ALU.mult, op1=ALU.mult,
                accum_out=I_SQ[:, u:u + 1]), reads=(("OTM", u),),
                writes=(("JV",), ("I_SQ", u)), out_free=1024)
        ln_finalize(pl, I_SUM, I_SQ, I_M, I_R, S_T, "Ist",
                    [("I_SUM", u) for u in range(4)], [("I_SQ", u) for u in range(4)])

        for u in range(4) if 'gz' not in SKIP else []:
            tl = st * 4 + u
            gt = GT[u]
            kgt = ("GT", u)
            pl.op("v", lambda E, u=u: E.tensor_scalar(
                out=OB[:, u * 1024:(u + 1) * 1024],
                in0=OTM[:, u * 1024:(u + 1) * 1024],
                scalar1=I_M[:, u:u + 1], scalar2=I_R[:, u:u + 1],
                op0=ALU.subtract, op1=ALU.mult), reads=(("OTM", u),),
                sreads=(("Ist", "M"), ("Ist", "R")), writes=(("OB", u),),
                out_free=1024)
            pl.op("v", lambda E, u=u: E.tensor_tensor(
                out=OB[:, u * 1024:(u + 1) * 1024],
                in0=OB[:, u * 1024:(u + 1) * 1024], in1=GREP, op=ALU.mult),
                reads=(("OB", u), ("GREP",)), writes=(("OB", u),), out_free=1024)
            pl.op("v", lambda E, u=u: E.tensor_tensor(
                out=OB[:, u * 1024:(u + 1) * 1024],
                in0=OB[:, u * 1024:(u + 1) * 1024], in1=BETA, op=ALU.add),
                reads=(("OB", u), ("BETA",)), writes=(("OB", u),), out_free=1024)
            pl.op("v", lambda E, u=u, gt=gt: E.scalar_tensor_tensor(
                out=ZBF, in0=gt, scalar=1.0, in1=OB[:, u * 1024:(u + 1) * 1024],
                op0=ALU.add, op1=ALU.mult), reads=(kgt, ("OB", u)),
                writes=(("ZBF",),), out_free=1024)
            for fc in range(8):
                pl.op("t", lambda E, fc=fc: E.transpose(
                    PT[:, fc * 128:(fc + 1) * 128], ZBF[:, fc * 128:(fc + 1) * 128],
                    C["IDN"]), reads=(("ZBF",), "IDN"), writes=(("PT",),),
                    out_free=128)
            pl.op("v", lambda E: E.tensor_copy(out=ZFM, in_=PT),
                  reads=(("PT",),), writes=(("ZFM",),), out_free=1024)
            for half in range(2):
                for fc in range(8):
                    pl.op("t", lambda E, fc=fc, half=half: E.matmul(
                        PB[:, half * 512:(half + 1) * 512],
                        ZFM[:, fc * 128:(fc + 1) * 128],
                        WO[:, fc * 1024 + half * 512: fc * 1024 + (half + 1) * 512],
                        start=(fc == 0), stop=(fc == 7)),
                        reads=(("ZFM",), ("WO",)), writes=(("PB",),), out_free=512)
            pl.op("s", lambda E, u=u: E.activation(
                out=OB[:, u * 1024:(u + 1) * 1024], in_=PB, func=AF.Copy,
                accum_out=B_SUM[:, u:u + 1]), reads=(("PB",),),
                writes=(("OB", u), ("B_SUM", u)), out_free=1024)
            pl.op("s", lambda E, u=u: E.activation(
                out=JV, in_=OB[:, u * 1024:(u + 1) * 1024], func=AF.Square,
                accum_out=B_SQ[:, u:u + 1]), reads=(("OB", u),),
                writes=(("JV",), ("B_SQ", u)), out_free=1024)
        ln_finalize(pl, B_SUM, B_SQ, B_M, B_R, S_T, "Bst",
                    [("B_SUM", u) for u in range(4)], [("B_SQ", u) for u in range(4)])
        for u in range(4):
            tl = st * 4 + u
            xb = XTB[u % 2]
            kx = ("XTB", u % 2)
            pl.op("s", lambda E, tl=tl, xb=xb: E.dma_start(
                out=xb, in_=x_in[tl * 128:(tl + 1) * 128, :]), dma=True,
                reads=((kx_in, tl),), writes=(kx,))
            pl.op("v", lambda E, u=u: E.tensor_scalar(
                out=OB[:, u * 1024:(u + 1) * 1024],
                in0=OB[:, u * 1024:(u + 1) * 1024],
                scalar1=B_M[:, u:u + 1], scalar2=B_R[:, u:u + 1],
                op0=ALU.subtract, op1=ALU.mult), reads=(("OB", u),),
                sreads=(("Bst", "M"), ("Bst", "R")), writes=(("OB", u),),
                out_free=1024)
            pl.op("p", lambda E, u=u, xb=xb: E.tensor_tensor(
                out=xb, in0=OB[:, u * 1024:(u + 1) * 1024], in1=xb, op=ALU.add),
                reads=(("OB", u), kx), writes=(kx,), out_free=1024)
            tok = pl.op("y", lambda E, tl=tl, xb=xb: E.dma_start(
                out=x_out[tl * 128:(tl + 1) * 128, :], in_=xb), dma=True,
                reads=(kx,), writes=((kx_out, tl),), tag="stx")
            out_toks.append(tok)
    return out_toks



def emit_l0_ffn(pl, nc, ctx, x_in, x_out, kx_in, kx_out, W, C):
    E_ = ctx.enter_context
    sb = lambda name, shape, dt: E_(nc.sbuf_tensor(name, shape, dt)).ap()
    ps = lambda name, shape, dt: E_(nc.psum_tensor(name, shape, dt)).ap()
    ntok = N_ST * 512

    xhat = nc.dram_tensor("l0f_xhat", [N_ST * 128, 4096], BF16, kind="Internal").ap()
    accA = nc.dram_tensor("l0f_accA", [ntok, D], F32, kind="Internal").ap()
    accB = nc.dram_tensor("l0f_accB", [ntok, D], F32, kind="Internal").ap()
    cc_in = nc.dram_tensor("l0f_ccin", [128, 16], F32, kind="Internal").ap()
    cc_out = nc.dram_tensor("l0f_ccout", [128, 16], F32, kind="Internal").ap()

    GWT = sb("gwt", [128, 64], BF16)
    GW32 = sb("gw32", [128, 64], F32)
    GWLO = sb("gwlo", [128, 64], BF16)
    XH32 = sb("fxh32", [128, 1024], F32)
    XLOB = sb("fxlob", [128, 1024], BF16)
    XLOFS = sb("fxlofs", [128, 4096], BF16)
    LBB = sb("lbb", [128, 8], F32)
    SEL_ALL = sb("sel_all", [128, NT * 8], BF16)
    ONES1 = sb("ones1", [1, 128], BF16)
    ONESC = sb("onesc", [128, 1], BF16)
    WMR = sb("wmr", [128, 8], F32)
    WMS = sb("wms", [1, 16], F32)
    NAT2 = [sb(f"natw{i}", [128, 4096], BF16) for i in range(2)]
    XT = [sb(f"fxt{i}", [128, 1024], F32) for i in range(4)]
    XHB = sb("fxhb", [128, 1024], BF16)
    XHFS = sb("fxhfs", [128, 4096], BF16)
    GSB = sb("fgsb", [128, 2048], BF16)
    USB = sb("fusb", [128, 2048], BF16)
    H2ALL = sb("fh2", [128, 16 * 512], BF16)
    CB = [sb(f"fcb{i}", [128, 512], F32) for i in range(4)]
    CF1 = sb("fcf1", [1, 512], BF16)
    CF8 = sb("fcf8", [8, 512], BF16)
    SEL8 = sb("fsel8", [8, 1024], BF16)
    COEFT = sb("fcoeft", [128, 32], BF16)
    ACC_SB = sb("facc", [128, 1024], F32)
    ACCR = [sb(f"faccr{i}", [128, 1024], F32) for i in range(2)]
    LG = sb("flg", [128, 8], F32)
    EQ1 = sb("feq1", [128, 8], F32)
    EQ2 = sb("feq2", [128, 8], F32)
    LGM = sb("flgm", [128, 8], F32)
    M12 = sb("fm12", [128, 2], F32)
    TH = sb("fth", [128, 2], F32)
    S12 = sb("fs12", [128, 2], F32)
    DEN = sb("fden", [128, 1], F32)
    WEB = sb("fweb", [128, 8], BF16)
    ST2 = sb("fst2", [128, 16], BF16)
    WEF = sb("fwef", [128, 8], F32)
    JV = sb("fjv", [128, 1024], BF16)
    JS = sb("fjs", [128, 1024], BF16)
    STR = sb("fstr", [1, 16], F32)
    CCZ = sb("fccz", [128, 16], F32)
    S_SUM = sb("fs_sum", [128, 4], F32)
    NMRF = sb("fnmrf", [128, 4], F32)
    S_SQ = sb("fs_sq", [128, 4], F32)
    S_M = sb("fs_m", [128, 4], F32)
    S_R = sb("fs_r", [128, 4], F32)
    S_T = sb("fs_t", [128, 4], F32)
    B_SUM = sb("fb_sum", [128, 4], F32)
    B_SQ = sb("fb_sq", [128, 4], F32)
    B_M = sb("fb_m", [128, 4], F32)
    B_R = sb("fb_r", [128, 4], F32)

    PGa = ps("fpga", [128, 1024], F32)
    PGb = ps("fpgb", [128, 1024], F32)
    PACC = ps("fpacc", [128, 1024], F32)
    PT = ps("fpt", [128, 1024], BF16)
    PST = ps("fpst", [1, 16], F32)
    PTH = [PT[:, 0:512], PACC.bitcast(BF16)[:, 0:512]]

    out_toks = []

    pl.op("v", lambda E: E.memset(ONES1, 1.0), writes=("ONES1",), out_free=128)
    pl.op("v", lambda E: E.memset(ONESC, 1.0), writes=("ONESC",), out_free=1)
    pl.op("y", lambda E: E.dma_start(out=SEL8, in_=W["c_sel8"]), dma=True,
          writes=(("SEL8",),))
    pl.op("y", lambda E: E.dma_start(out=GW32, in_=W["d_gwT"]), dma=True,
          writes=("GW32",))
    pl.op("v", lambda E: E.tensor_copy(out=GWT, in_=GW32),
          reads=("GW32",), writes=("GWT",), out_free=64)
    pl.op("v", lambda E: E.scalar_tensor_tensor(out=GWLO, in0=GWT, scalar=-1.0,
          in1=GW32, op0=ALU.mult, op1=ALU.add), reads=("GWT", "GW32"),
          writes=("GWLO",), out_free=64)
    pl.op("v", lambda E: E.memset(CCZ, 0.0), writes=("CCZ",), out_free=16)
    pl.op("y", lambda E: E.dma_start(out=cc_in, in_=CCZ), dma=True,
          reads=("CCZ",), writes=(("cc_in",),))
    pl.op("y", lambda E: E.dma_start(
        out=LBB, in_=W["L0_ffn_lb_bias"].unsqueeze(0).broadcast_to([128, 8])),
        dma=True, writes=("LBB",))

    for st in range(N_ST):
        for u in range(4):
            tl = st * 4 + u
            kx = ("fXT", u)
            pl.op("y", lambda E, tl=tl, u=u: E.dma_start(
                out=XT[u], in_=x_in[tl * 128:(tl + 1) * 128, :]), dma=True,
                reads=((kx_in, tl),), writes=(kx,))
            pl.op("s", lambda E, u=u: E.activation(
                out=JV, in_=XT[u], func=AF.Square, accum_out=S_SQ[:, u:u + 1]),
                reads=(kx,), writes=(("fJV",), ("fS_SQ", u)), out_free=1024)
            pl.op("s", lambda E, u=u: E.activation(
                out=JS, in_=XT[u], func=AF.Copy, accum_out=S_SUM[:, u:u + 1]),
                reads=(kx,), writes=(("fJS",), ("fS_SUM", u)), out_free=1024)
        ln_finalize(pl, S_SUM, S_SQ, S_M, S_R, S_T, "fSst",
                    [("fS_SUM", u) for u in range(4)],
                    [("fS_SQ", u) for u in range(4)])
        pl.op("v", lambda E: E.scalar_tensor_tensor(
            out=NMRF, in0=S_M, scalar=-1.0, in1=S_R, op0=ALU.mult,
            op1=ALU.mult), reads=(("fSst", "M"), ("fSst", "R")),
            writes=(("fNMRF",),), out_free=4)
        for u in range(4):
            kx = ("fXT", u)
            pl.op("s", lambda E, u=u: E.activation(
                out=XH32, in_=XT[u], func=AF.Identity,
                scale=S_R[:, u:u + 1], bias=NMRF[:, u:u + 1]), reads=(kx,),
                sreads=(("fSst", "R"), ("fNMRF",)), writes=(("fXH32",),),
                out_free=1024)
            pl.op("s", lambda E: E.activation(out=XHB, in_=XH32, func=AF.Copy),
                  reads=(("fXH32",),), writes=(("fXHB",),), out_free=1024)
            pl.op("v", lambda E: E.scalar_tensor_tensor(out=XLOB, in0=XHB,
                  scalar=-1.0, in1=XH32, op0=ALU.mult, op1=ALU.add),
                  reads=(("fXHB",), ("fXH32",)), writes=(("fXLOB",),),
                  out_free=1024)
            for half in range(2):
                for fc2 in range(4):
                    fc = half * 4 + fc2
                    pl.op("t", lambda E, fc=fc, fc2=fc2, half=half: E.transpose(
                        PTH[half][:, fc2 * 128:(fc2 + 1) * 128],
                        XHB[:, fc * 128:(fc + 1) * 128],
                        C["IDN"]), reads=(("fXHB",), "IDN"),
                        writes=(("fPT", half),), out_free=128)
                pl.op("v", lambda E, u=u, half=half: E.tensor_copy(
                    out=XHFS.rearrange("p (f c) -> p f c", f=8)[
                        :, half * 4:(half + 1) * 4, u * 128:(u + 1) * 128],
                    in_=PTH[half].rearrange("p (f c) -> p f c", f=4)),
                    reads=(("fPT", half),), writes=(("fXHFS", 0),), out_free=512)
            for half in range(2):
                for fc2 in range(4):
                    fc = half * 4 + fc2
                    pl.op("t", lambda E, fc=fc, fc2=fc2, half=half: E.transpose(
                        PTH[half][:, fc2 * 128:(fc2 + 1) * 128],
                        XLOB[:, fc * 128:(fc + 1) * 128],
                        C["IDN"]), reads=(("fXLOB",), "IDN"),
                        writes=(("fPT", half),), out_free=128)
                pl.op("v", lambda E, u=u, half=half: E.tensor_copy(
                    out=XLOFS.rearrange("p (f c) -> p f c", f=8)[
                        :, half * 4:(half + 1) * 4, u * 128:(u + 1) * 128],
                    in_=PTH[half].rearrange("p (f c) -> p f c", f=4)),
                    reads=(("fPT", half),), writes=(("fXLOFS",),), out_free=512)
        pl.op("y", lambda E, st=st: E.dma_start(
            out=xhat[st * 128:(st + 1) * 128, :], in_=XHFS), dma=True,
            reads=(("fXHFS", 0),), writes=(("xhat", st),))
        for u in range(4):
            tl = st * 4 + u
            for trm in range(3):
                xs, ws, kxs = ((XHFS, GWT, "fXHFS"), (XHFS, GWLO, "fXHFS"),
                               (XLOFS, GWT, "fXLOFS"))[trm]
                for kc in range(8):
                    pl.op("t", lambda E, kc=kc, u=u, xs=xs, ws=ws, trm=trm:
                          E.matmul(
                        PGa[:, 0:8], xs[:, kc * 512 + u * 128: kc * 512 + (u + 1) * 128],
                        ws[:, kc * 8:(kc + 1) * 8],
                        start=(trm == 0 and kc == 0), stop=(trm == 2 and kc == 7)),
                        reads=((kxs if kxs != "fXHFS" else ("fXHFS", 0)),
                               ("GWT",), ("GWLO",)), writes=(("fPGpl",),),
                        out_free=8)
            pl.op("v", lambda E: E.tensor_tensor(out=LG, in0=PGa[:, 0:8], in1=LBB,
                  op=ALU.add), reads=(("fPGpl",), ("LBB",)), writes=(("fLG",),),
                  out_free=8)
            pl.op("v", lambda E: E.reduce_max(out=M12[:, 0:1], in_=LG, axis=AX),
                  reads=(("fLG",),), writes=(("fM12",),), out_free=1)
            pl.op("v", lambda E: E.tensor_scalar(out=EQ1, in0=LG,
                  scalar1=M12[:, 0:1], scalar2=None, op0=ALU.is_equal),
                  reads=(("fLG",),), sreads=(("fM12",),), writes=(("fEQ1",),),
                  out_free=8)
            pl.op("v", lambda E: E.scalar_tensor_tensor(out=LGM, in0=EQ1,
                  scalar=-1e30, in1=LG, op0=ALU.mult, op1=ALU.add),
                  reads=(("fEQ1",), ("fLG",)), writes=(("fLGM",),), out_free=8)
            pl.op("v", lambda E: E.reduce_max(out=M12[:, 1:2], in_=LGM, axis=AX),
                  reads=(("fLGM",),), writes=(("fM12",),), out_free=1)
            pl.op("v", lambda E: E.tensor_scalar(out=EQ2, in0=LGM,
                  scalar1=M12[:, 1:2], scalar2=None, op0=ALU.is_equal),
                  reads=(("fLGM",),), sreads=(("fM12",),), writes=(("fEQ2",),),
                  out_free=8)
            pl.op("s", lambda E: E.activation(out=TH, in_=M12, func=AF.Tanh,
                  scale=0.5), reads=(("fM12",),), writes=(("fTH",),), out_free=2)
            pl.op("v", lambda E: E.tensor_scalar(out=S12, in0=TH, scalar1=0.5,
                  scalar2=0.5, op0=ALU.mult, op1=ALU.add), reads=(("fTH",),),
                  writes=(("fS12",),), out_free=2)
            pl.op("v", lambda E: E.reduce_sum(out=DEN, in_=S12, axis=AX),
                  reads=(("fS12",),), writes=(("fDEN",),), out_free=1)
            pl.op("v", lambda E: E.tensor_scalar(out=DEN, in0=DEN, scalar1=1e-8,
                  scalar2=None, op0=ALU.add), reads=(("fDEN",),),
                  writes=(("fDEN",),), out_free=1)
            pl.op("v", lambda E: E.reciprocal(out=DEN, in_=DEN),
                  reads=(("fDEN",),), writes=(("fDEN",),), out_free=1)
            pl.op("v", lambda E: E.tensor_scalar(out=S12, in0=S12, scalar1=DEN,
                  scalar2=None, op0=ALU.mult), reads=(("fS12",),),
                  sreads=(("fDEN",),), writes=(("fS12",),), out_free=2)
            pl.op("v", lambda E: E.tensor_scalar(out=WEF, in0=EQ1,
                  scalar1=S12[:, 0:1], scalar2=None, op0=ALU.mult),
                  reads=(("fEQ1",),), sreads=(("fS12",),), writes=(("fWEF",),),
                  out_free=8)
            pl.op("v", lambda E: E.scalar_tensor_tensor(out=WEF, in0=EQ2,
                  scalar=S12[:, 1:2], in1=WEF, op0=ALU.mult, op1=ALU.add),
                  reads=(("fEQ2",), ("fWEF",)), sreads=(("fS12",),),
                  writes=(("fWEF",),), out_free=8)
            pl.op("v", lambda E: E.tensor_copy(out=ST2[:, 8:16], in_=WEF),
                  reads=(("fWEF",),), writes=(("fST2",),), out_free=8)
            pl.op("v", lambda E, tl=tl: E.tensor_tensor(
                out=ST2[:, 0:8], in0=EQ1, in1=EQ2, op=ALU.add),
                reads=(("fEQ1",), ("fEQ2",)), writes=(("fST2",),), out_free=8)
            pl.op("v", lambda E, tl=tl: E.tensor_copy(
                out=SEL_ALL[:, tl * 8:(tl + 1) * 8], in_=ST2[:, 0:8]),
                reads=(("fST2",),), writes=(("SEL", tl),), out_free=8)
            pl.op("t", lambda E, tl=tl: E.matmul(
                PST[0:1, 0:16], ONESC, ST2,
                start=(tl == 0), stop=(tl == N_ST * 4 - 1)),
                reads=(("fST2",), ("ONESC",)), writes=(("fPST",),), out_free=16)
    pl.op("v", lambda E: E.tensor_copy(out=STR, in_=PST),
          reads=(("fPST",),), writes=(("STR",),), out_free=16)
    pl.op("y", lambda E: E.dma_start(out=cc_in[0:1, 0:16], in_=STR), dma=True,
          reads=(("STR",),), writes=(("cc_in",),))
    pl.op("p", lambda E: E.collective_compute(
        "AllReduce", ALU.add, replica_groups=[[0, 1, 2, 3, 4, 5, 6, 7]],
        ins=[cc_in.opt()], outs=[cc_out.opt()]),
        reads=(("cc_in",),), writes=(("cc_out",),), out_free=16, tag="allreduce")
    pl.op("y", lambda E: E.dma_start(out=WMS, in_=cc_out[0:1, 0:16]), dma=True,
          reads=(("cc_out",),), writes=(("WMS",),))
    pl.op("v", lambda E: E.tensor_scalar(out=WMS[0:1, 0:8], in0=WMS[0:1, 0:8],
          scalar1=1.0, scalar2=None, op0=ALU.max), reads=(("WMS",),),
          writes=(("WMS",),), out_free=8)
    pl.op("v", lambda E: E.reciprocal(out=WMS[0:1, 0:8], in_=WMS[0:1, 0:8]),
          reads=(("WMS",),), writes=(("WMS",),), out_free=8)
    pl.op("v", lambda E: E.tensor_tensor(out=WMS[0:1, 0:8], in0=WMS[0:1, 8:16],
          in1=WMS[0:1, 0:8], op=ALU.mult), reads=(("WMS",),), writes=(("WMS",),),
          out_free=8)
    pl.op("v", lambda E: E.tensor_copy(out=CF1[0:1, 0:8], in_=WMS[0:1, 0:8]),
          reads=(("WMS",),), writes=(("CF1",),), out_free=8)
    pl.op("t", lambda E: E.matmul(PGa[:, 0:8], ONES1, CF1[0:1, 0:8], start=True,
          stop=True), reads=(("CF1",), ("ONES1",)), writes=(("fPGpl",),),
          out_free=8)
    pl.op("v", lambda E: E.tensor_copy(out=WMR, in_=PGa[:, 0:8]),
          reads=(("fPGpl",),), writes=(("WMR",),), out_free=8)
    pl.block_barrier()

    natctr = [0]

    def load_mat_T(dst, kdst, src_ap, out_dim, in_dim, tag):
        noc = out_dim // 128
        nkc = in_dim // 128
        oc_grp = max(1, 4096 // in_dim)
        for g0 in range(0, noc, oc_grp):
            g1 = min(noc, g0 + oc_grp)
            natctr[0] += 1
            NAT = NAT2[natctr[0] % 2]
            knat = ("NAT", natctr[0] % 2)
            view = src_ap[g0 * 128:g1 * 128, :].rearrange("(oc p) i -> p oc i",
                                                          p=128)
            pl.op("p", lambda E, view=view, n=(g1 - g0), NAT=NAT: E.dma_start(
                out=NAT.rearrange("p (oc i) -> p oc i", oc=oc_grp if in_dim * oc_grp == 4096 else 1)[:, 0:n, 0:in_dim]
                if in_dim <= 4096 else None, in_=view),
                dma=True, writes=(knat,), tag=f"ld_{tag}")
            kw = (("fPT", 0), ("fPT", 1))
            for kc in range(nkc):
                for oc in range(g0, g1):
                    pl.op("t", lambda E, kc=kc, oc=oc, g0=g0,
                          NAT=NAT: E.transpose(
                        PT[:, (oc - g0) * 128:(oc - g0 + 1) * 128],
                        NAT[:, (oc - g0) * in_dim + kc * 128:
                            (oc - g0) * in_dim + (kc + 1) * 128],
                        C["IDN"]), reads=(knat, "IDN"),
                        writes=kw, out_free=128)
                pl.op("v", lambda E, kc=kc, g0=g0, g1=g1: E.tensor_copy(
                    out=dst[:, kc * out_dim + g0 * 128: kc * out_dim + g1 * 128],
                    in_=PT[:, 0:(g1 - g0) * 128]),
                    reads=kw,
                    writes=(kdst,), out_free=(g1 - g0) * 128)

    def ldxh(st):
        pl.op("p", lambda E, st=st: E.dma_start(
            out=XHFS, in_=xhat[st * 128:(st + 1) * 128, :]), dma=True,
            reads=(("xhat", st),), writes=(("fXHFS", 0),))

    def expert_pass(EG, EU, DW, experts, acc, acc_key):
        for i, e in enumerate(experts):
            load_mat_T(EG[i], ("EGWT", i), W["L0_ffn_expert_gate_w"][e], 512,
                       1024, f"eg{e}")
            load_mat_T(EU[i], ("EUWT", i), W["L0_ffn_expert_up_w"][e], 512,
                       1024, f"eu{e}")
            load_mat_T(DW[i], ("DWT", i), W["L0_ffn_expert_down_w"][e], 1024,
                       512, f"ed{e}")
        for st in range(N_ST):
            XHFSx = XHFS
            kxh = ("fXHFS", 0)
            ldxh(st)
            for u in range(4):
                tl = st * 4 + u
                pl.op("v", lambda E, tl=tl, u=u: E.tensor_tensor(
                    out=COEFT[:, u * 8:(u + 1) * 8],
                    in0=SEL_ALL[:, tl * 8:(tl + 1) * 8], in1=WMR,
                    op=ALU.mult), reads=(("SEL", tl), ("WMR",)),
                    writes=(("COEFT", u),), out_free=8)
                pl.op("t", lambda E, u=u: E.transpose(
                    PT[0:8, u * 128:(u + 1) * 128],
                    COEFT[:, u * 8:(u + 1) * 8],
                    C["IDN"]), reads=(("COEFT", u), "IDN"),
                    writes=(("fPT", 0), ("fPT", 1)), out_free=128)
            pl.op("v", lambda E: E.tensor_copy(out=CF8, in_=PT[0:8, 0:512]),
                  reads=(("fPT", 0), ("fPT", 1)), writes=(("CF8",),),
                  out_free=512)
            for i, e in enumerate(experts):
                pl.op("t", lambda E, e=e: E.matmul(
                    PACC[:, 0:512], SEL8[:, e * 128:(e + 1) * 128], CF8,
                    start=True, stop=True), reads=(("CF8",), ("SEL8",)),
                    writes=(("fPACC",),), out_free=512)
                pl.op("s", lambda E, i=i: E.activation(out=CB[i],
                      in_=PACC[:, 0:512], func=AF.Copy), reads=(("fPACC",),),
                      writes=(("CBk", i),), out_free=512)
                for hc in range(4):
                    pg = PGa if hc < 2 else PGb
                    kpg = ("fPG", hc // 2)
                    for kc in range(8):
                        pl.op("t", lambda E, i=i, hc=hc, kc=kc, pg=pg,
                              XHFSx=XHFSx: E.matmul(
                            pg[:, (hc % 2) * 512:(hc % 2 + 1) * 512],
                            EG[i][:, kc * 512 + hc * 128: kc * 512 + (hc + 1) * 128],
                            XHFSx[:, kc * 512:(kc + 1) * 512],
                            start=(kc == 0), stop=(kc == 7)),
                            reads=(("EGWT", i), kxh), writes=(kpg,),
                            out_free=512)
                    if hc % 2 == 1:
                        pl.op("s", lambda E, hc=hc, pg=pg: E.activation(
                            out=GSB[:, (hc - 1) * 512:(hc + 1) * 512], in_=pg,
                            func=AF.Silu), reads=(kpg,), writes=(("GSB",),),
                            out_free=1024)
                for hc in range(4):
                    pg = PGa if hc < 2 else PGb
                    kpg = ("fPG", hc // 2)
                    for kc in range(8):
                        pl.op("t", lambda E, i=i, hc=hc, kc=kc, pg=pg,
                              XHFSx=XHFSx: E.matmul(
                            pg[:, (hc % 2) * 512:(hc % 2 + 1) * 512],
                            EU[i][:, kc * 512 + hc * 128: kc * 512 + (hc + 1) * 128],
                            XHFSx[:, kc * 512:(kc + 1) * 512],
                            start=(kc == 0), stop=(kc == 7)),
                            reads=(("EUWT", i), kxh, ("GSB",)),
                            writes=(kpg,), out_free=512)
                    if hc % 2 == 1:
                        for h2 in (hc - 1, hc):
                            pl.op("v", lambda E, i=i, h2=h2, pg=pg: E.tensor_tensor(
                                out=USB[:, h2 * 512:(h2 + 1) * 512],
                                in0=pg[:, (h2 % 2) * 512:(h2 % 2 + 1) * 512],
                                in1=CB[i], op=ALU.mult),
                                reads=(kpg, ("CBk", i)),
                                writes=(("USB",),), out_free=512)
                pl.op("v", lambda E, i=i: E.tensor_tensor(
                    out=H2ALL[:, i * 2048:(i + 1) * 2048], in0=USB, in1=GSB,
                    op=ALU.mult), reads=(("USB",), ("GSB",)),
                    writes=(("H2", i),), out_free=2048)
            for tt in range(4):
                for half in range(2):
                    for ch in range(16):
                        i, hc = ch // 4, ch % 4
                        pl.op("t", lambda E, i=i, hc=hc, tt=tt, half=half, ch=ch:
                              E.matmul(
                            PACC[:, half * 512:(half + 1) * 512],
                            H2ALL[:, (i * 4 + hc) * 512 + tt * 128:
                                  (i * 4 + hc) * 512 + (tt + 1) * 128],
                            DW[i][:, hc * 1024 + half * 512:
                                  hc * 1024 + (half + 1) * 512],
                            start=(ch == 0), stop=(ch == 15)),
                            reads=(("H2", i), ("DWT", i)), writes=(("fPACC",),),
                            out_free=512)
                pl.op("s", lambda E: E.activation(out=ACC_SB, in_=PACC,
                      func=AF.Copy), reads=(("fPACC",),), writes=(("ACC_SB",),),
                      out_free=1024)
                tl = st * 4 + tt
                tok = pl.op("s", lambda E, tl=tl, acc=acc: E.dma_start(
                    out=acc[tl * 128:(tl + 1) * 128, :], in_=ACC_SB), dma=True,
                    reads=(("ACC_SB",),), writes=((acc_key, tl),))

    with contextlib.ExitStack() as ctx_ab:
        sb2 = lambda name, shape, dt: ctx_ab.enter_context(
            nc.sbuf_tensor(name, shape, dt)).ap()
        EGWT = [sb2(f"egwt{i}", [128, 8 * 512], BF16) for i in range(4)]
        EUWT = [sb2(f"euwt{i}", [128, 8 * 512], BF16) for i in range(4)]
        DWT = [sb2(f"dwt{i}", [128, 4 * 1024], BF16) for i in range(4)]
        expert_pass(EGWT, EUWT, DWT, [0, 1, 2, 3], accA, "accA")
        expert_pass(EGWT, EUWT, DWT, [4, 5, 6, 7], accB, "accB")
    pl.block_barrier()

    with contextlib.ExitStack() as ctx_c:
        sb3 = lambda name, shape, dt: ctx_c.enter_context(
            nc.sbuf_tensor(name, shape, dt)).ap()
        SGWT = sb3("sgwt", [128, 8 * 1024], BF16)
        SUWT = sb3("suwt", [128, 8 * 1024], BF16)
        SDWT = sb3("sdwt", [128, 8 * 1024], BF16)
        OB4 = [sb3(f"fob{i}", [128, 1024], F32) for i in range(4)]
        load_mat_T(SGWT, ("SGWT",), W["L0_ffn_shared_gate_w"], 1024, 1024, "sg")
        load_mat_T(SUWT, ("SUWT",), W["L0_ffn_shared_up_w"], 1024, 1024, "su")
        load_mat_T(SDWT, ("SDWT",), W["L0_ffn_shared_down_w"], 1024, 1024, "sd")
        XHFS2 = sb3("fxhfs2c", [128, 4096], BF16)

        def stageG(st):
            q = st % 2
            XHFSx = XHFS if q == 0 else XHFS2
            kxh = ("fXHFS", q)
            pl.op("p", lambda E, st=st, XHFSx=XHFSx: E.dma_start(
                out=XHFSx, in_=xhat[st * 128:(st + 1) * 128, :]), dma=True,
                reads=(("xhat", st),), writes=(kxh,))
            for rnd in range(2):
                for hc in range(4):
                    hco = rnd * 4 + hc
                    pg = PGa if hc < 2 else PGb
                    kpg = ("fPG", hc // 2)
                    for kc in range(8):
                        pl.op("t", lambda E, hco=hco, hc=hc, kc=kc, pg=pg,
                              XHFSx=XHFSx, kxh=kxh: E.matmul(
                            pg[:, (hc % 2) * 512:(hc % 2 + 1) * 512],
                            SGWT[:, kc * 1024 + hco * 128: kc * 1024 + (hco + 1) * 128],
                            XHFSx[:, kc * 512:(kc + 1) * 512],
                            start=(kc == 0), stop=(kc == 7)),
                            reads=(("SGWT",), kxh), writes=(kpg,),
                            out_free=512)
                    if hc % 2 == 1:
                        pl.op("s", lambda E, hc=hc, pg=pg: E.activation(
                            out=GSB[:, (hc - 1) * 512:(hc + 1) * 512], in_=pg,
                            func=AF.Silu), reads=(kpg,), writes=(("GSB",),),
                            out_free=1024)
                for hc in range(4):
                    hco = rnd * 4 + hc
                    pg = PGa if hc < 2 else PGb
                    kpg = ("fPG", hc // 2)
                    for kc in range(8):
                        pl.op("t", lambda E, hco=hco, hc=hc, kc=kc, pg=pg,
                              XHFSx=XHFSx, kxh=kxh: E.matmul(
                            pg[:, (hc % 2) * 512:(hc % 2 + 1) * 512],
                            SUWT[:, kc * 1024 + hco * 128: kc * 1024 + (hco + 1) * 128],
                            XHFSx[:, kc * 512:(kc + 1) * 512],
                            start=(kc == 0), stop=(kc == 7)),
                            reads=(("SUWT",), kxh, ("GSB",)),
                            writes=(kpg,), out_free=512)
                    if hc % 2 == 1:
                        pl.op("s", lambda E, hc=hc, pg=pg: E.activation(
                            out=USB[:, (hc - 1) * 512:(hc + 1) * 512], in_=pg,
                            func=AF.Copy), reads=(kpg,), writes=(("USB",),),
                            out_free=1024)
                pl.op("v", lambda E, rnd=rnd, q=q: E.tensor_tensor(
                    out=H2ALL[:, (q * 2 + rnd) * 2048:(q * 2 + rnd + 1) * 2048],
                    in0=USB, in1=GSB,
                    op=ALU.mult), reads=(("USB",), ("GSB",)),
                    writes=(("H2", q * 2 + rnd),), out_free=2048)

        def stageD(st):
            q = st % 2
            for tt in range(4):
                tl = st * 4 + tt
                for j in range(2):
                    pl.op("y" if j == 0 else "s", lambda E, tl=tl, j=j,
                          acc=(accA if j == 0 else accB): E.dma_start(
                          out=ACCR[j], in_=acc[tl * 128:(tl + 1) * 128, :]),
                          dma=True, reads=((("accA" if j == 0 else "accB"), tl),),
                          writes=(("ACCR", j),))
                for half in range(2):
                    for ch in range(8):
                        rnd, hc = ch // 4, ch % 4
                        pl.op("t", lambda E, rnd=rnd, hc=hc, tt=tt, half=half,
                              ch=ch, q=q: E.matmul(
                            PACC[:, half * 512:(half + 1) * 512],
                            H2ALL[:, (q * 8 + rnd * 4 + hc) * 512 + tt * 128:
                                  (q * 8 + rnd * 4 + hc) * 512 + (tt + 1) * 128],
                            SDWT[:, (rnd * 4 + hc) * 1024 + half * 512:
                                  (rnd * 4 + hc) * 1024 + (half + 1) * 512],
                            start=(ch == 0), stop=(ch == 7)),
                            reads=(("H2", q * 2 + rnd), ("SDWT",)),
                            writes=(("fPACC",),),
                            out_free=512)
                pl.op("v", lambda E, tt=tt: E.tensor_tensor(
                    out=OB4[tt], in0=PACC, in1=ACCR[0], op=ALU.add),
                    reads=(("fPACC",), ("ACCR", 0)), writes=(("OBS", tt),),
                    out_free=1024)
                pl.op("p", lambda E, tt=tt: E.tensor_tensor(
                    out=OB4[tt], in0=OB4[tt], in1=ACCR[1], op=ALU.add),
                    reads=(("OBS", tt), ("ACCR", 1)), writes=(("OBS", tt),),
                    out_free=1024)
                pl.op("s", lambda E, tt=tt: E.activation(out=JS, in_=OB4[tt],
                      func=AF.Copy, accum_out=B_SUM[:, tt:tt + 1]),
                      reads=(("OBS", tt),), writes=(("fJS",), ("fB_SUM", tt)),
                      out_free=1024)
                pl.op("s", lambda E, tt=tt: E.activation(
                    out=JV, in_=OB4[tt], func=AF.Square,
                    accum_out=B_SQ[:, tt:tt + 1]),
                    reads=(("OBS", tt),), writes=(("fJV",), ("fB_SQ", tt)),
                    out_free=1024)
                pl.op("p", lambda E, tl=tl, tt=tt: E.dma_start(
                    out=XT[tt], in_=x_in[tl * 128:(tl + 1) * 128, :]), dma=True,
                    reads=((kx_in, tl),), writes=(("fXT", tt),))

        stageG(0)
        for st in range(N_ST):
            if st + 1 < N_ST:
                stageG(st + 1)
            stageD(st)
            ln_finalize(pl, B_SUM, B_SQ, B_M, B_R, S_T, "fBst",
                        [("fB_SUM", tt) for tt in range(4)],
                        [("fB_SQ", tt) for tt in range(4)])
            for tt in range(4):
                tl = st * 4 + tt
                pl.op("v", lambda E, tt=tt: E.tensor_scalar(
                    out=OB4[tt], in0=OB4[tt], scalar1=B_M[:, tt:tt + 1],
                    scalar2=B_R[:, tt:tt + 1], op0=ALU.subtract, op1=ALU.mult),
                    reads=(("OBS", tt),), sreads=(("fBst", "M"), ("fBst", "R")),
                    writes=(("OBS", tt),), out_free=1024)
                pl.op("p", lambda E, tt=tt: E.tensor_tensor(
                    out=XT[tt], in0=OB4[tt], in1=XT[tt], op=ALU.add),
                    reads=(("OBS", tt), ("fXT", tt)), writes=(("fXT", tt),),
                    out_free=1024)
                tok = pl.op("y", lambda E, tl=tl, tt=tt: E.dma_start(
                    out=x_out[tl * 128:(tl + 1) * 128, :], in_=XT[tt]), dma=True,
                    reads=(("fXT", tt),), writes=((kx_out, tl),))
                out_toks.append(tok)
    return out_toks



def emit_l1_attn(pl, nc, ctx, x_in, x_out, kx_in, kx_out, W, C):
    E_ = ctx.enter_context
    sb = lambda name, shape, dt: E_(nc.sbuf_tensor(name, shape, dt)).ap()
    ps = lambda name, shape, dt: E_(nc.psum_tensor(name, shape, dt)).ap()
    nseq = max(1, N_ST // SEQ_ST)
    ntile_seq = SEQ_ST * 4
    slen = ntile_seq * 128

    WQ1 = sb("awq", [128, 8 * 512], BF16)
    WKV = sb("awkv", [128, 8 * 1024], BF16)
    WO1 = sb("awo", [128, 4 * 1024], BF16)
    ONESC = sb("aonesc", [128, 1], BF16)
    XT = [sb(f"axt{i}", [128, 1024], F32) for i in range(4)]
    XHB = sb("axhb", [128, 1024], BF16)
    XHB2a = sb("axhb2", [128, 1024], BF16)
    NMRA = sb("anmra", [128, 4], F32)
    XHF = sb("axhf", [128, 4096], BF16)
    QTM = sb("aqtm", [128, 512], F32)
    KVTM = sb("akvtm", [128, 1024], F32)
    SQSC = sb("asqsc", [128, 512], F32)
    QSS = sb("aqss", [128, 8], F32)
    KSS = sb("akss", [128, 8], F32)
    QRS = sb("aqrs", [128, 8], F32)
    KRS = sb("akrs", [128, 8], F32)
    SST = sb("asst", [128, 8], F32)
    ONESB = sb("aonesb", [128, 64], BF16)
    NEG16 = sb("aneg16", [128, 1], F32)
    QHB = sb("aqhb", [128, 512], BF16)
    KHB = sb("akhb", [128, 512], BF16)
    VEX = sb("avex", [128, ntile_seq * 520], BF16)
    QHF = sb("aqhf", [128, 4 * slen], BF16)
    KHF = sb("akhf", [128, 4 * slen], BF16)
    PJ = [sb(f"apj{i}", [128, 1024], BF16) for i in range(2)]
    OSB = sb("aosb", [128, 512], F32)
    RS = sb("ars", [128, 512], F32)
    RH = sb("arh", [128, 512], BF16)
    RL = sb("arl", [128, 512], BF16)
    OFM = sb("aofm", [128, 4 * 512], BF16)
    OB = sb("aob", [128, 4 * 1024], F32)
    JV = sb("ajv", [128, 1024], BF16)
    JS = sb("ajs", [128, 1024], BF16)
    S_SUM = sb("as_sum", [128, 4], F32)
    S_SQ = sb("as_sq", [128, 4], F32)
    S_M = sb("as_m", [128, 4], F32)
    S_R = sb("as_r", [128, 4], F32)
    S_T = sb("as_t", [128, 4], F32)
    B_SUM = sb("ab_sum", [128, 4], F32)
    B_SQ = sb("ab_sq", [128, 4], F32)
    B_M = sb("ab_m", [128, 4], F32)
    B_R = sb("ab_r", [128, 4], F32)

    PA = ps("apa", [128, 1024], F32)
    PL1 = ps("apl1", [128, 1024], F32)
    PL2 = ps("apl2", [128, 1024], F32)
    PO1 = ps("apo1", [128, 512], F32)
    PO2 = ps("apo2", [128, 512], F32)
    PB1 = None
    _PLB = PL1.bitcast(BF16)
    PTH = [_PLB[:, 0:512], _PLB[:, 1024:1536]]

    out_toks = []

    def wld(dst, src_name, kname):
        src = W[src_name]
        kc = src.shape[0] // 128
        view = src.rearrange("(kc p) o -> p kc o", p=128)
        d = dst.rearrange("p (kc o) -> p kc o", kc=kc)
        pl.op("p", lambda E, d=d, s=view: E.dma_start(out=d, in_=s), dma=True,
              writes=(kname,), tag=f"ld_{kname}")

    wld(WQ1, "L1_attn_Wq", "aWQ")
    wld(WKV, "L1_attn_Wkv", "aWKV")
    wld(WO1, "L1_attn_Wo", "aWO")
    pl.op("v", lambda E: E.memset(ONESC, 1.0), writes=("aONESC",), out_free=1)
    pl.op("v", lambda E: E.memset(ONESB, 1.0), writes=("aONESB",), out_free=64)
    pl.op("v", lambda E: E.memset(NEG16, -16.0), writes=("aNEG16",), out_free=1)
    pl.op("v", lambda E: E.memset(
        VEX.rearrange("p (t h e) -> p t h e", t=ntile_seq, h=8)[:, :, :, 64:65],
        1.0), writes=(("aVEXones",),), out_free=1024)

    for s in range(nseq):
        for stq in range(SEQ_ST):
            st = s * SEQ_ST + stq
            for u in range(4):
                tl = st * 4 + u
                kx = ("aXT", u)
                pl.op("y", lambda E, tl=tl, u=u: E.dma_start(
                    out=XT[u], in_=x_in[tl * 128:(tl + 1) * 128, :]), dma=True,
                    reads=((kx_in, tl),), writes=(kx,))
                pl.op("s", lambda E, u=u: E.activation(
                    out=JV, in_=XT[u], func=AF.Square,
                    accum_out=S_SQ[:, u:u + 1]), reads=(kx,),
                    writes=(("aJV",), ("aS_SQ", u)), out_free=1024)
                pl.op("s", lambda E, u=u: E.activation(
                    out=JS, in_=XT[u], func=AF.Copy, accum_out=S_SUM[:, u:u + 1]),
                    reads=(kx,), writes=(("aJS",), ("aS_SUM", u)), out_free=1024)
            ln_finalize(pl, S_SUM, S_SQ, S_M, S_R, S_T, "aSst",
                        [("aS_SUM", u) for u in range(4)],
                        [("aS_SQ", u) for u in range(4)])
            pl.op("v", lambda E: E.scalar_tensor_tensor(
                out=NMRA, in0=S_M, scalar=-1.0, in1=S_R, op0=ALU.mult,
                op1=ALU.mult), reads=(("aSst", "M"), ("aSst", "R")),
                writes=(("aNMRA",),), out_free=4)
            for u in range(4):
                tl = st * 4 + u
                tls = stq * 4 + u
                kx = ("aXT", u)
                XHBu = XHB if u % 2 == 0 else XHB2a
                kxhb = ("aXHB", u % 2)
                pl.op("s", lambda E, u=u, XHBu=XHBu: E.activation(
                    out=XHBu, in_=XT[u], func=AF.Identity,
                    scale=S_R[:, u:u + 1], bias=NMRA[:, u:u + 1]),
                    reads=(kx,), sreads=(("aSst", "R"), ("aNMRA",)),
                    writes=(kxhb,), out_free=1024)
                for half in range(2):
                    for fc2 in range(4):
                        fc = half * 4 + fc2
                        pl.op("t", lambda E, fc=fc, fc2=fc2, half=half,
                              XHBu=XHBu, kxhb=kxhb: E.transpose(
                            PTH[half][:, fc2 * 128:(fc2 + 1) * 128],
                            XHBu[:, fc * 128:(fc + 1) * 128], C["IDN"]),
                            reads=(kxhb, "IDN"), writes=(("aPT", half),),
                            out_free=128)
                    pl.op("v", lambda E, half=half: E.tensor_copy(
                        out=XHF.rearrange("p (f c) -> p f c", f=8)[
                            :, half * 4:(half + 1) * 4, 0:128],
                        in_=PTH[half].rearrange("p (f c) -> p f c", f=4)),
                        reads=(("aPT", half),), writes=(("aXHF",),),
                        out_free=512)
                for kc in range(8):
                    pl.op("t", lambda E, kc=kc: E.matmul(
                        PA[:, 0:512], XHF[:, kc * 512:kc * 512 + 128],
                        WQ1[:, kc * 512:(kc + 1) * 512], start=(kc == 0),
                        stop=(kc == 7)),
                        reads=(("aXHF",), ("aWQ",)), writes=(("aPA",),),
                        out_free=512)
                for half in range(2):
                    for kc in range(8):
                        pl.op("t", lambda E, kc=kc, half=half: E.matmul(
                            PA[:, 512:1024] if half == 0 else PL1[:, 0:512],
                            XHF[:, kc * 512:kc * 512 + 128],
                            WKV[:, kc * 1024 + half * 512:
                                kc * 1024 + (half + 1) * 512],
                            start=(kc == 0), stop=(kc == 7)),
                            reads=(("aXHF",), ("aWKV",)),
                            writes=(("aPA",) if half == 0 else ("aPL1", 0),),
                            out_free=512)
                pl.op("v", lambda E: E.tensor_copy(out=QTM, in_=PA[:, 0:512]),
                      reads=(("aPA",),), writes=(("aQTM",),), out_free=512)
                pl.op("v", lambda E: E.tensor_copy(out=KVTM[:, 0:512],
                      in_=PA[:, 512:1024]), reads=(("aPA",),),
                      writes=(("aKVTM",),), out_free=512)
                pl.op("v", lambda E: E.tensor_copy(out=KVTM[:, 512:1024],
                      in_=PL1[:, 0:512]), reads=(("aPL1", 0),),
                      writes=(("aKVTM",),), out_free=512)
                for (TMM, SS, RSD, kss, krs, ktm) in (
                        (QTM, QSS, QRS, "aQSS", "aQRS", "aQTM"),
                        (KVTM, KSS, KRS, "aKSS", "aKRS", "aKVTM")):
                    tin = TMM[:, 0:512]
                    pl.op("v", lambda E, tin=tin: E.scalar_tensor_tensor(
                        out=SQSC, in0=tin, scalar=1.0, in1=tin, op0=ALU.mult,
                        op1=ALU.mult), reads=((ktm,),), writes=(("aSQSC",),),
                        out_free=512)
                    pl.op("v", lambda E, SS=SS: E.reduce_sum(
                        out=SS.unsqueeze(2), in_=SQSC.rearrange(
                            "p (h d) -> p h d", h=8), axis=AX),
                        reads=(("aSQSC",),), writes=((kss,),), out_free=8)
                    pl.op("v", lambda E, SS=SS: E.tensor_scalar(
                        out=SS, in0=SS, scalar1=1e-24, scalar2=None,
                        op0=ALU.max), reads=((kss,),), writes=((kss,),),
                        out_free=8)
                    emit_rsqrt(pl, RSD, SS, SST, (krs,), (kss,), ("aSST",))
                pl.op("v", lambda E: E.tensor_tensor(
                    out=QHB.rearrange("p (h d) -> p d h", h=8),
                    in0=QTM.rearrange("p (h d) -> p d h", h=8),
                    in1=QRS.unsqueeze(1).broadcast_to([128, 64, 8]),
                    op=ALU.mult), reads=(("aQTM",),), sreads=(("aQRS",),),
                    writes=(("aQHB",),), out_free=512)
                pl.op("v", lambda E: E.tensor_tensor(
                    out=KHB.rearrange("p (h d) -> p d h", h=8),
                    in0=KVTM[:, 0:512].rearrange("p (h d) -> p d h", h=8),
                    in1=KRS.unsqueeze(1).broadcast_to([128, 64, 8]),
                    op=ALU.mult), reads=(("aKVTM",),), sreads=(("aKRS",),),
                    writes=(("aKHB",),), out_free=512)
                pl.op("v", lambda E, tls=tls: E.tensor_copy(
                    out=VEX[:, tls * 520:(tls + 1) * 520].rearrange(
                        "p (h e) -> p h e", h=8)[:, :, 0:64],
                    in_=KVTM[:, 512:1024].rearrange("p (h d) -> p h d", h=8)),
                    reads=(("aKVTM",),), writes=(("aVEX", tls),), out_free=512)
                for fc in range(4):
                    pl.op("t", lambda E, fc=fc: E.transpose(
                        PTH[0][:, fc * 128:(fc + 1) * 128],
                        QHB[:, fc * 128:(fc + 1) * 128], C["IDN"]),
                        reads=(("aQHB",), "IDN"), writes=(("aPT", 0),),
                        out_free=128)
                pl.op("v", lambda E, tls=tls: E.tensor_copy(
                    out=QHF.rearrange("p (f t) -> p f t", f=4)[
                        :, :, tls * 128:(tls + 1) * 128],
                    in_=PTH[0].rearrange("p (f t) -> p f t", f=4)),
                    reads=(("aPT", 0),), writes=(("aQHF", tls),), out_free=512)
                for fc in range(4):
                    pl.op("t", lambda E, fc=fc: E.transpose(
                        PTH[1][:, fc * 128:(fc + 1) * 128],
                        KHB[:, fc * 128:(fc + 1) * 128], C["IDN"]),
                        reads=(("aKHB",), "IDN"), writes=(("aPT", 1),),
                        out_free=128)
                pl.op("v", lambda E, tls=tls: E.tensor_copy(
                    out=KHF.rearrange("p (f t) -> p f t", f=4)[
                        :, :, tls * 128:(tls + 1) * 128],
                    in_=PTH[1].rearrange("p (f t) -> p f t", f=4)),
                    reads=(("aPT", 1),), writes=(("aKHF", tls),), out_free=512)

        QHF_R = tuple(("aQHF", i) for i in range(ntile_seq))
        KHF_R = tuple(("aKHF", i) for i in range(ntile_seq))
        VEX_R = tuple(("aVEX", i) for i in range(ntile_seq)) + (("aVEXones",),)
        pending = [None]
        for ib in range(SEQ_ST):
            for hx, h in enumerate([1, 3, 5, 7, 0, 2, 4, 6]):
                hb = (h % 2) * 64
                hf = h // 2
                po = PO1 if hx % 2 == 0 else PO2
                kpo = ("aPO1", hx % 2)
                def emit_pv(jg):
                    pj = PJ[jg % 2]
                    kpj = ("aPJ", jg % 2)
                    for j2 in range(2):
                        jc = jg * 2 + j2
                        pl.op("t", lambda E, jc=jc, h=h, pj=pj, po=po, j2=j2:
                              E.matmul(
                            po[0:65, :], VEX[:, jc * 520 + h * 65:
                                             jc * 520 + (h + 1) * 65],
                            pj[:, j2 * 512:(j2 + 1) * 512],
                            start=(jc == 0), stop=(jc == ntile_seq - 1)),
                            reads=VEX_R + (kpj,), writes=(kpo,), out_free=512)
                for jg in range(ntile_seq // 2):
                    plx = PL1 if jg % 2 == 0 else PL2
                    kpl = ("aPL1", jg % 2)
                    pj = PJ[jg % 2]
                    kpj = ("aPJ", jg % 2)
                    for j2 in range(2):
                        jc = jg * 2 + j2
                        pl.op("t", lambda E, hb=hb, hf=hf, jc=jc, ib=ib,
                              plx=plx, j2=j2: E.matmul(
                            plx[:, j2 * 512:(j2 + 1) * 512],
                            KHF[hb:hb + 64,
                                hf * slen + jc * 128: hf * slen + (jc + 1) * 128],
                            QHF[hb:hb + 64,
                                hf * slen + ib * 512: hf * slen + (ib + 1) * 512],
                            start=True, stop=True),
                            reads=KHF_R + QHF_R, writes=(kpl,), out_free=512)
                    pl.op("s", lambda E, pj=pj, plx=plx: E.activation(
                        out=pj, in_=plx, func=AF.Exp, scale=16.0, bias=NEG16),
                        reads=(kpl,), sreads=(("aNEG16",),),
                        writes=(kpj,), out_free=1024)
                    if jg > 0:
                        emit_pv(jg - 1)
                    if jg == 1 and pending[0] is not None:
                        pending[0]()
                        pending[0] = None
                emit_pv(ntile_seq // 2 - 1)
                pl.op("v", lambda E, po=po: E.reciprocal(out=RS[64:65, :],
                      in_=po[64:65, :]), reads=(kpo,), writes=(("aRS",),),
                      out_free=512)
                pl.op("v", lambda E: E.tensor_copy(out=RH[64:65, :],
                      in_=RS[64:65, :]), reads=(("aRS",),), writes=(("aRH",),),
                      out_free=512)
                pl.op("v", lambda E: E.scalar_tensor_tensor(
                    out=RL[64:65, :], in0=RH[64:65, :], scalar=-1.0,
                    in1=RS[64:65, :], op0=ALU.mult, op1=ALU.add),
                    reads=(("aRH",), ("aRS",)), writes=(("aRL",),), out_free=512)
                pl.op("v", lambda E, po=po: E.tensor_copy(out=OSB[0:64, :],
                      in_=po[0:64, :]), reads=(kpo,),
                      writes=(("aOSB",),), out_free=512)

                def tail(h=h, hb=hb, hf=hf):
                    pl.op("t", lambda E: E.matmul(
                        PA[0:64, 0:512], ONESB[64:65, 0:64], RH[64:65, :],
                        start=True, stop=False),
                        reads=(("aRH",), "aONESB"), writes=(("aPA",),),
                        out_free=512)
                    pl.op("t", lambda E: E.matmul(
                        PA[0:64, 0:512], ONESB[64:65, 0:64], RL[64:65, :],
                        start=False, stop=True),
                        reads=(("aRL",), "aONESB"), writes=(("aPA",),),
                        out_free=512)
                    pl.op("v", lambda E, hb=hb, hf=hf: E.tensor_tensor(
                        out=OFM[hb:hb + 64, hf * 512:(hf + 1) * 512],
                        in0=OSB[0:64, :], in1=PA[0:64, 0:512], op=ALU.mult),
                        reads=(("aOSB",), ("aPA",)), writes=(("aOFM", h),),
                        out_free=512)
                    pl.op("t", lambda E: E.matmul(PL1[0:1, 0:1], ONESC, ONESC,
                          start=True, stop=True),
                          reads=(("aONESC",),), writes=(("aPL1", 0),),
                          out_free=1)
                pending[0] = tail
            if pending[0] is not None:
                pending[0]()
                pending[0] = None
            OFM_R = tuple(("aOFM", h) for h in range(8))
            for u in range(4):
                tls = ib * 4 + u
                tl = (s * SEQ_ST) * 4 + tls
                pwo = PA if u % 2 == 0 else PL1
                kpwo = ("aPA",) if u % 2 == 0 else ("aPL1", 0)
                for half in range(2):
                    for fc in range(4):
                        pl.op("t", lambda E, fc=fc, half=half, u=u, pwo=pwo:
                              E.matmul(
                            pwo[:, half * 512:(half + 1) * 512],
                            OFM[:, fc * 512 + u * 128: fc * 512 + (u + 1) * 128],
                            WO1[:, fc * 1024 + half * 512:
                                fc * 1024 + (half + 1) * 512],
                            start=(fc == 0), stop=(fc == 3)),
                            reads=OFM_R + (("aWO",),), writes=(kpwo,),
                            out_free=512)
                pl.op("s", lambda E, u=u, pwo=pwo: E.activation(
                    out=OB[:, u * 1024:(u + 1) * 1024], in_=pwo, func=AF.Copy,
                    accum_out=B_SUM[:, u:u + 1]), reads=(kpwo,),
                    writes=(("aOB", u), ("aB_SUM", u)), out_free=1024)
                pl.op("s", lambda E, u=u: E.activation(
                    out=JV, in_=OB[:, u * 1024:(u + 1) * 1024], func=AF.Square,
                    accum_out=B_SQ[:, u:u + 1]),
                    reads=(("aOB", u),), writes=(("aJV",), ("aB_SQ", u)),
                    out_free=1024)
                pl.op("s", lambda E, tl=tl, u=u: E.dma_start(
                    out=XT[u], in_=x_in[tl * 128:(tl + 1) * 128, :]), dma=True,
                    reads=((kx_in, tl),), writes=(("aXT", u),))
            ln_finalize(pl, B_SUM, B_SQ, B_M, B_R, S_T, "aBst",
                        [("aB_SUM", u) for u in range(4)],
                        [("aB_SQ", u) for u in range(4)])
            for u in range(4):
                tls = ib * 4 + u
                tl = (s * SEQ_ST) * 4 + tls
                pl.op("v", lambda E, u=u: E.tensor_scalar(
                    out=OB[:, u * 1024:(u + 1) * 1024],
                    in0=OB[:, u * 1024:(u + 1) * 1024],
                    scalar1=B_M[:, u:u + 1], scalar2=B_R[:, u:u + 1],
                    op0=ALU.subtract, op1=ALU.mult), reads=(("aOB", u),),
                    sreads=(("aBst", "M"), ("aBst", "R")), writes=(("aOB", u),),
                    out_free=1024)
                pl.op("p", lambda E, u=u: E.tensor_tensor(
                    out=XT[u], in0=OB[:, u * 1024:(u + 1) * 1024], in1=XT[u],
                    op=ALU.add), reads=(("aOB", u), ("aXT", u)),
                    writes=(("aXT", u),), out_free=1024)
                tok = pl.op("y", lambda E, tl=tl, u=u: E.dma_start(
                    out=x_out[tl * 128:(tl + 1) * 128, :], in_=XT[u]), dma=True,
                    reads=(("aXT", u),), writes=((kx_out, tl),))
                out_toks.append(tok)
    return out_toks



def emit_l1_ffn(pl, nc, ctx, x_in, x_out, kx_in, kx_out, W, C):
    E_ = ctx.enter_context
    sb = lambda name, shape, dt: E_(nc.sbuf_tensor(name, shape, dt)).ap()
    ps = lambda name, shape, dt: E_(nc.psum_tensor(name, shape, dt)).ap()
    ntok = N_ST * 512

    x1f_d = nc.dram_tensor("l1f_x1f", [N_ST * 128, 4096], BF16, kind="Internal").ap()
    x1t_d = nc.dram_tensor("l1f_x1t", [ntok, D], F32, kind="Internal").ap()
    accC = nc.dram_tensor("l1f_accC", [ntok, D], F32, kind="Internal").ap()
    accD = nc.dram_tensor("l1f_accD", [ntok, D], F32, kind="Internal").ap()

    ONES1 = sb("nones1", [1, 128], BF16)
    XT = [sb(f"nxt{i}", [128, 1024], F32) for i in range(4)]
    X1F = sb("nx1f", [128, 4096], BF16)
    X1T = sb("nx1t", [128, 1024], F32)
    COEF_ALL = sb("ncoef", [128, NT * 8], F32)
    JV = sb("njv", [128, 1024], BF16)
    JS = sb("njs", [128, 1024], BF16)
    S_SUM = sb("ns_sum", [128, 8], F32)
    S_SQ = sb("ns_sq", [128, 8], F32)
    S_M = sb("ns_m", [128, 8], F32)
    S_R = sb("ns_r", [128, 8], F32)
    S_T = sb("ns_t", [128, 8], F32)
    PA = ps("npa", [128, 1024], F32)
    PB = ps("npb", [128, 1024], F32)
    PT = ps("npt", [128, 1024], BF16)
    PT2 = ps("npt2", [128, 1024], BF16)
    PTH = [PT[:, 0:512], PT2[:, 0:512]]
    out_toks = []
    pl.op("v", lambda E: E.memset(ONES1, 1.0), writes=("nONES1",), out_free=128)

    def wld(dst, name, kname, src=None):
        src = W[name] if src is None else src
        if src.shape[0] > 128:
            kc = src.shape[0] // 128
            view = src.rearrange("(kc p) o -> p kc o", p=128)
            d = dst.rearrange("p (kc o) -> p kc o", kc=kc)
        else:
            view, d = src, dst
        pl.op("p", lambda E, d=d, s=view: E.dma_start(out=d, in_=s), dma=True,
              writes=(kname,), tag=f"ld_{kname}")

    with contextlib.ExitStack() as c1:
        sb1 = lambda name, shape, dt: c1.enter_context(
            nc.sbuf_tensor(name, shape, dt)).ap()
        PWT = sb1("npwt", [128, 8 * 1024], BF16)
        RWT = sb1("nrwt", [128, 64], BF16)
        RBB = sb1("nrbb", [128, 8], F32)
        DWK = sb1("ndwk", [128, 24], F32)
        DWKB = sb1("ndwkb", [128, 8], F32)
        PWBC = sb1("npwbc", [128, 8], F32)
        XHB = sb1("nxhb", [128, 1024], BF16)
        XHB2n = sb1("nxhb2", [128, 1024], BF16)
        NMRN = sb1("nnmrn", [128, 8], F32)
        XCF = [sb1(f"nxcf{i}", [128, 8 * 514], BF16) for i in range(2)]
        XCONV = sb1("nxconv", [128, 4096], BF16)
        CVT = [sb1(f"ncvt{i}", [128, 512], F32) for i in range(2)]
        XPF = sb1("nxpf", [128, 4096], BF16)
        XPB = sb1("nxpb", [128, 1024], BF16)
        LG = sb1("nlg", [128, 8], F32)
        EQ1 = sb1("neq1", [128, 8], F32)
        EQ2 = sb1("neq2", [128, 8], F32)
        LGM = sb1("nlgm", [128, 8], F32)
        M12 = sb1("nm12", [128, 2], F32)
        TH = sb1("nth", [128, 1], F32)
        W12 = sb1("nw12", [128, 2], F32)
        DWDG = sb1("ndwdg", [128, 3072], BF16)
        X1T4 = sb1("nx1t4", [128, 4096], F32)
        wld(PWT, "d_pwT", "nPWT")
        wld(RWT, "d_rwT", "nRWT")
        wld(DWDG, "d_dwdg", "nDWDG")
        pl.op("y", lambda E: E.dma_start(out=RBB, in_=W["L1_ffn_router_b"]
              .unsqueeze(0).broadcast_to([128, 8])), dma=True, writes=("nRBB",))
        pl.op("y", lambda E: E.dma_start(out=DWK, in_=W["d_dwk"]), dma=True,
              writes=("nDWK",))
        pl.op("y", lambda E: E.dma_start(out=DWKB, in_=W["d_dwbc"]), dma=True,
              writes=("nDWKB",))
        pl.op("y", lambda E: E.dma_start(out=PWBC, in_=W["d_pwbc"]), dma=True,
              writes=("nPWBC",))

        def pre_stats1(st):
            q = st % 2
            for u in range(4):
                tl = st * 4 + u
                kx = ("nXT", u)
                pl.op("y", lambda E, tl=tl, u=u: E.dma_start(
                    out=XT[u], in_=x_in[tl * 128:(tl + 1) * 128, :]), dma=True,
                    reads=((kx_in, tl),), writes=(kx,))
                pl.op("s", lambda E, u=u, q=q: E.activation(
                    out=JV, in_=XT[u], func=AF.Square,
                    accum_out=S_SQ[:, q * 4 + u:q * 4 + u + 1]), reads=(kx,),
                    writes=(("nJV",), ("nS_SQ", q, u)), out_free=1024)
                pl.op("s", lambda E, u=u, q=q: E.activation(
                    out=JS, in_=XT[u], func=AF.Copy,
                    accum_out=S_SUM[:, q * 4 + u:q * 4 + u + 1]),
                    reads=(kx,), writes=(("nJS",), ("nS_SUM", q, u)),
                    out_free=1024)

        for st in range(N_ST):
            seq_start = (st % SEQ_ST == 0)
            cur = XCF[st % 2]
            prv = XCF[1 - st % 2]
            kcur = ("nXCF", st % 2)
            kprv = ("nXCF", 1 - st % 2)
            if seq_start:
                pl.op("v", lambda E, cur=cur: E.memset(
                    cur.rearrange("p (f t) -> p f t", f=8)[:, :, 0:2], 0.0),
                    writes=(kcur,), out_free=16)
            else:
                pl.op("v", lambda E, cur=cur, prv=prv: E.tensor_copy(
                    out=cur.rearrange("p (f t) -> p f t", f=8)[:, :, 0:2],
                    in_=prv.rearrange("p (f t) -> p f t", f=8)[:, :, 512:514]),
                    reads=(kprv,), writes=(kcur,), out_free=16)
            q = st % 2
            if st == 0:
                pre_stats1(0)
            ln_finalize(pl, S_SUM[:, q * 4:q * 4 + 4], S_SQ[:, q * 4:q * 4 + 4],
                        S_M[:, q * 4:q * 4 + 4], S_R[:, q * 4:q * 4 + 4],
                        S_T[:, q * 4:q * 4 + 4], f"nSst{q}",
                        [("nS_SUM", q, u) for u in range(4)],
                        [("nS_SQ", q, u) for u in range(4)])
            pl.op("v", lambda E, q=q: E.scalar_tensor_tensor(
                out=NMRN[:, q * 4:q * 4 + 4], in0=S_M[:, q * 4:q * 4 + 4],
                scalar=-1.0, in1=S_R[:, q * 4:q * 4 + 4], op0=ALU.mult,
                op1=ALU.mult), reads=((f"nSst{q}", "M"), (f"nSst{q}", "R")),
                writes=(("nNMRN", q),), out_free=4)
            for u in range(4):
                kx = ("nXT", u)
                XHBu = XHB if u % 2 == 0 else XHB2n
                kxhb = ("nXHB", u % 2)
                pl.op("s", lambda E, u=u, q=q, XHBu=XHBu: E.activation(
                    out=XHBu, in_=XT[u], func=AF.Identity,
                    scale=S_R[:, q * 4 + u:q * 4 + u + 1],
                    bias=NMRN[:, q * 4 + u:q * 4 + u + 1]),
                    reads=(kx,), sreads=((f"nSst{q}", "R"), ("nNMRN", q)),
                    writes=(kxhb,), out_free=1024)
                pl.op("v", lambda E, u=u, q=q: E.tensor_scalar(
                    out=X1T4[:, u * 1024:(u + 1) * 1024], in0=XT[u],
                    scalar1=S_M[:, q * 4 + u:q * 4 + u + 1],
                    scalar2=S_R[:, q * 4 + u:q * 4 + u + 1],
                    op0=ALU.subtract, op1=ALU.mult),
                    reads=(kx,), sreads=((f"nSst{q}", "M"), (f"nSst{q}", "R")),
                    writes=(("nX1T4", u),), out_free=512)
                for half in range(2):
                    for fc2 in range(4):
                        fc = half * 4 + fc2
                        pl.op("t", lambda E, fc=fc, fc2=fc2, half=half,
                              XHBu=XHBu, kxhb=kxhb: E.transpose(
                            PTH[half][:, fc2 * 128:(fc2 + 1) * 128],
                            XHBu[:, fc * 128:(fc + 1) * 128], C["IDN"]),
                            reads=(kxhb, "IDN"), writes=(("nPT", half),),
                            out_free=128)
                    pl.op("v", lambda E, u=u, cur=cur, half=half: E.tensor_copy(
                        out=cur.rearrange("p (f t) -> p f t", f=8)[
                            :, half * 4:(half + 1) * 4,
                            2 + u * 128:2 + (u + 1) * 128],
                        in_=PTH[half].rearrange("p (f c) -> p f c", f=4)),
                        reads=(("nPT", half),), writes=(kcur,), out_free=512)
            if st + 1 < N_ST:
                pre_stats1(st + 1)
            for fc in range(8):
                pb = PB[:, (fc % 2) * 512:(fc % 2) * 512 + 512]
                kpb = ("nPBc", fc % 2)
                for t in range(3):
                    pl.op("t", lambda E, fc=fc, t=t, cur=cur, pb=pb: E.matmul(
                        pb, DWDG[:, (fc * 3 + t) * 128:(fc * 3 + t + 1) * 128],
                        cur[:, fc * 514 + t: fc * 514 + t + 512],
                        start=(t == 0), stop=(t == 2)),
                        reads=(kcur, ("nDWDG",)), writes=(kpb,), out_free=512)
                pl.op("s", lambda E, fc=fc, pb=pb: E.activation(
                    out=XCONV[:, fc * 512:(fc + 1) * 512], in_=pb,
                    func=AF.Identity, scale=1.0, bias=DWKB[:, fc:fc + 1]),
                    reads=(kpb,), sreads=(("nDWKB",),),
                    writes=(("nXCONV",),), out_free=512)
            for ofc in range(8):
                for kc in range(8):
                    pl.op("t", lambda E, ofc=ofc, kc=kc: E.matmul(
                        PA[:, 0:512],
                        PWT[:, kc * 1024 + ofc * 128: kc * 1024 + (ofc + 1) * 128],
                        XCONV[:, kc * 512:(kc + 1) * 512], start=(kc == 0),
                        stop=(kc == 7)),
                        reads=(("nPWT",), ("nXCONV",)), writes=(("nPA",),),
                        out_free=512)
                pl.op("s", lambda E, ofc=ofc: E.activation(
                    out=XPF[:, ofc * 512:(ofc + 1) * 512], in_=PA[:, 0:512],
                    func=AF.Silu, bias=PWBC[:, ofc:ofc + 1]),
                    reads=(("nPA",),), sreads=(("nPWBC",),),
                    writes=(("nXPF", ofc),), out_free=512)
            pl.op("v", lambda E, cur=cur: E.tensor_tensor(
                out=X1F.rearrange("p (f t) -> p f t", f=8),
                in0=cur.rearrange("p (f t) -> p f t", f=8)[:, :, 2:514],
                in1=XPF.rearrange("p (f t) -> p f t", f=8), op=ALU.add),
                reads=(kcur,) + tuple(("nXPF", o) for o in range(8)),
                writes=(("nX1F",),), out_free=4096)
            pl.op("y", lambda E, st=st: E.dma_start(
                out=x1f_d[st * 128:(st + 1) * 128, :], in_=X1F), dma=True,
                reads=(("nX1F",),), writes=(("x1f", st),))
            for u in range(4):
                tl = st * 4 + u
                for half in range(2):
                    for fc2 in range(4):
                        fc = half * 4 + fc2
                        pl.op("t", lambda E, fc=fc, fc2=fc2, half=half, u=u:
                              E.transpose(
                            PTH[half][:, fc2 * 128:(fc2 + 1) * 128],
                            XPF[:, fc * 512 + u * 128: fc * 512 + (u + 1) * 128],
                            C["IDN"]), reads=(("nXPF", fc), "IDN"),
                            writes=(("nPT", half),), out_free=128)
                    pl.op("v", lambda E, half=half: E.tensor_copy(
                        out=XPB[:, half * 512:(half + 1) * 512],
                        in_=PTH[half]),
                        reads=(("nPT", half),), writes=(("nXPB",),),
                        out_free=512)
                pl.op("p", lambda E, u=u: E.tensor_tensor(
                    out=X1T, in0=X1T4[:, u * 1024:(u + 1) * 1024], in1=XPB,
                    op=ALU.add),
                    reads=(("nX1T4", u), ("nXPB",)), writes=(("nX1T",),),
                    out_free=1024)
                pl.op("y", lambda E, tl=tl: E.dma_start(
                    out=x1t_d[tl * 128:(tl + 1) * 128, :], in_=X1T), dma=True,
                    reads=(("nX1T",),), writes=(("x1t", tl),))
                for kc in range(8):
                    pl.op("t", lambda E, kc=kc, u=u: E.matmul(
                        PB[:, 0:8],
                        X1F[:, kc * 512 + u * 128: kc * 512 + (u + 1) * 128],
                        RWT[:, kc * 8:(kc + 1) * 8], start=(kc == 0),
                        stop=(kc == 7)),
                        reads=(("nX1F",), ("nRWT",)), writes=(("nPB",),),
                        out_free=8)
                pl.op("v", lambda E: E.tensor_tensor(out=LG, in0=PB[:, 0:8],
                      in1=RBB, op=ALU.add), reads=(("nPB",), ("nRBB",)),
                      writes=(("nLG",),), out_free=8)
                pl.op("v", lambda E: E.reduce_max(out=M12[:, 0:1], in_=LG,
                      axis=AX), reads=(("nLG",),), writes=(("nM12",),),
                      out_free=1)
                pl.op("v", lambda E: E.tensor_scalar(out=EQ1, in0=LG,
                      scalar1=M12[:, 0:1], scalar2=None, op0=ALU.is_equal),
                      reads=(("nLG",),), sreads=(("nM12",),),
                      writes=(("nEQ1",),), out_free=8)
                pl.op("v", lambda E: E.scalar_tensor_tensor(out=LGM, in0=EQ1,
                      scalar=-1e30, in1=LG, op0=ALU.mult, op1=ALU.add),
                      reads=(("nEQ1",), ("nLG",)), writes=(("nLGM",),),
                      out_free=8)
                pl.op("v", lambda E: E.reduce_max(out=M12[:, 1:2], in_=LGM,
                      axis=AX), reads=(("nLGM",),), writes=(("nM12",),),
                      out_free=1)
                pl.op("v", lambda E: E.tensor_scalar(out=EQ2, in0=LGM,
                      scalar1=M12[:, 1:2], scalar2=None, op0=ALU.is_equal),
                      reads=(("nLGM",),), sreads=(("nM12",),),
                      writes=(("nEQ2",),), out_free=8)
                pl.op("v", lambda E: E.tensor_scalar(out=TH, in0=M12[:, 0:1],
                      scalar1=M12[:, 1:2], scalar2=0.5, op0=ALU.subtract,
                      op1=ALU.mult), reads=(("nM12",),), sreads=(("nM12",),),
                      writes=(("nTH",),), out_free=1)
                pl.op("s", lambda E: E.activation(out=TH, in_=TH, func=AF.Tanh),
                      reads=(("nTH",),), writes=(("nTH",),), out_free=1)
                pl.op("v", lambda E: E.tensor_scalar(out=W12[:, 0:1], in0=TH,
                      scalar1=0.5, scalar2=0.5, op0=ALU.mult, op1=ALU.add),
                      reads=(("nTH",),), writes=(("nW12",),), out_free=1)
                pl.op("v", lambda E: E.tensor_scalar(out=W12[:, 1:2], in0=TH,
                      scalar1=-0.5, scalar2=0.5, op0=ALU.mult, op1=ALU.add),
                      reads=(("nTH",),), writes=(("nW12",),), out_free=1)
                pl.op("v", lambda E, tl=tl: E.tensor_scalar(
                    out=COEF_ALL[:, tl * 8:(tl + 1) * 8], in0=EQ1,
                    scalar1=W12[:, 0:1], scalar2=None, op0=ALU.mult),
                    reads=(("nEQ1",),), sreads=(("nW12",),),
                    writes=(("nCOEF", tl),), out_free=8)
                pl.op("v", lambda E, tl=tl: E.scalar_tensor_tensor(
                    out=COEF_ALL[:, tl * 8:(tl + 1) * 8], in0=EQ2,
                    scalar=W12[:, 1:2], in1=COEF_ALL[:, tl * 8:(tl + 1) * 8],
                    op0=ALU.mult, op1=ALU.add),
                    reads=(("nEQ2",), ("nCOEF", tl)), sreads=(("nW12",),),
                    writes=(("nCOEF", tl),), out_free=8)
    pl.block_barrier()

    with contextlib.ExitStack() as c2:
        sb2 = lambda name, shape, dt: c2.enter_context(
            nc.sbuf_tensor(name, shape, dt)).ap()
        EWT = [sb2(f"newt{i}", [128, 8 * 1024], BF16) for i in range(8)]
        EBH = sb2("nebh", [1, 8192], BF16)
        YE = [sb2(f"nye{i}", [128, 1024], BF16) for i in range(2)]
        X1F2 = sb2("nx1f2", [128, 4096], BF16)
        ACC = sb2("nacc", [128, 4 * 1024], F32)
        PA2 = ps("npa2", [128, 1024], F32)
        pl.op("p", lambda E: E.dma_start(
            out=EBH, in_=W["L1_ffn_expert_b"].rearrange("a b -> (a b)").unsqueeze(0)),
            dma=True, writes=(("nEB",),))

        def expert_load(experts, base):
            for i, e in enumerate(experts):
                wld(EWT[base + i], None, ("nEWT", base + i), src=W["d_ewT"][e])

        def expert_pass(experts, acc, acc_key, base):
            def ldx1f(st):
                xf = X1F if st % 2 == 0 else X1F2
                pl.op("y", lambda E, st=st, xf=xf: E.dma_start(
                    out=xf, in_=x1f_d[st * 128:(st + 1) * 128, :]), dma=True,
                    reads=(("x1f", st),), writes=(("nX1F", st % 2),))
            ldx1f(0)
            for st in range(N_ST):
                xf = X1F if st % 2 == 0 else X1F2
                kxf = ("nX1F", st % 2)
                if st + 1 < N_ST:
                    ldx1f(st + 1)
                for u in range(4):
                    tl = st * 4 + u
                    for i, e in enumerate(experts):
                        pa = PA if i % 2 == 0 else PA2
                        kpa = ("nPA", i % 2)
                        ye = YE[i % 2]
                        kye = ("nYE", i % 2)
                        for kc in range(8):
                            for half in range(2):
                                pl.op("t", lambda E, i=i, kc=kc, half=half, u=u,
                                      pa=pa, xf=xf, base=base: E.matmul(
                                    pa[:, half * 512:(half + 1) * 512],
                                    xf[:, kc * 512 + u * 128:
                                       kc * 512 + (u + 1) * 128],
                                    EWT[base + i][:, kc * 1024 + half * 512:
                                                  kc * 1024 + (half + 1) * 512],
                                    start=(kc == 0), stop=False),
                                    reads=(kxf, ("nEWT", base + i)),
                                    writes=(kpa,), out_free=512)
                        for half in range(2):
                            pl.op("t", lambda E, e=e, half=half, pa=pa: E.matmul(
                                pa[:, half * 512:(half + 1) * 512], ONES1,
                                EBH[0:1, e * 1024 + half * 512:
                                     e * 1024 + (half + 1) * 512],
                                start=False, stop=True),
                                reads=(("nEB",), ("nONES1",)),
                                writes=(kpa,), out_free=512)
                        pl.op("s", lambda E, pa=pa, ye=ye: E.activation(
                              out=ye, in_=pa, func=AF.Silu), reads=(kpa,),
                              writes=(kye,), out_free=1024)
                        if i == 0:
                            pl.op("v", lambda E, u=u, tl=tl, e=e, ye=ye:
                                  E.tensor_scalar(
                                out=ACC[:, u * 1024:(u + 1) * 1024], in0=ye,
                                scalar1=COEF_ALL[:, tl * 8 + e: tl * 8 + e + 1],
                                scalar2=None, op0=ALU.mult),
                                reads=(kye,), sreads=(("nCOEF", tl),),
                                writes=(("nACC", u),), out_free=1024)
                        else:
                            pl.op("v", lambda E, u=u, tl=tl, e=e, ye=ye:
                                  E.scalar_tensor_tensor(
                                out=ACC[:, u * 1024:(u + 1) * 1024], in0=ye,
                                scalar=COEF_ALL[:, tl * 8 + e: tl * 8 + e + 1],
                                in1=ACC[:, u * 1024:(u + 1) * 1024],
                                op0=ALU.mult, op1=ALU.add),
                                reads=(kye, ("nACC", u)),
                                sreads=(("nCOEF", tl),), writes=(("nACC", u),),
                                out_free=1024)
                    tok = pl.op("s", lambda E, tl=tl, u=u, acc=acc: E.dma_start(
                        out=acc[tl * 128:(tl + 1) * 128, :],
                        in_=ACC[:, u * 1024:(u + 1) * 1024]), dma=True,
                        reads=(("nACC", u),), writes=((acc_key, tl),))

        expert_load([0, 1, 2, 3], 0)
        expert_load([4, 5, 6, 7], 4)
        expert_pass([0, 1, 2, 3], accC, "accC", 0)
        expert_pass([4, 5, 6, 7], accD, "accD", 4)
    pl.block_barrier()

    with contextlib.ExitStack() as c3:
        sb3 = lambda name, shape, dt: c3.enter_context(
            nc.sbuf_tensor(name, shape, dt)).ap()
        OWT = sb3("nowt", [128, 8 * 1024], BF16)
        FDWT = sb3("nfdwt", [128, 8 * 256], BF16)
        FUWT = sb3("nfuwt", [128, 2 * 1024], BF16)
        NWREP = sb3("nnwrep", [128, 1024], F32)
        OBH = sb3("nobh", [1, 1024], BF16)
        FDBH = sb3("nfdbh", [1, 256], BF16)
        FUBH = sb3("nfubh", [1, 1024], BF16)
        ACCR = [[sb3(f"caccr{q}{j}", [128, 1024], F32) for j in range(2)]
                for q in range(2)]
        ACCB = sb3("caccb", [128, 2 * 1024], BF16)
        ACCFM = sb3("caccfm", [128, 2 * 1024], BF16)
        H1 = sb3("ch1", [128, 2 * 1024], F32)
        HN = sb3("chn", [128, 2 * 1024], F32)
        HNB = sb3("chnb", [128, 2 * 1024], BF16)
        HFM = sb3("chfm", [128, 2 * 1024], BF16)
        FF1 = sb3("cff1", [128, 2 * 256], BF16)
        FF1F = sb3("cff1f", [128, 2 * 256], BF16)
        OB = sb3("cob", [128, 2 * 1024], F32)
        X1TC = sb3("cx1tc", [128, 2 * 1024], F32)
        JS2 = sb3("cjs2", [128, 1024], BF16)
        NMR = sb3("cnmr", [128, 4], F32)
        _PA2B = PA2.bitcast(BF16)
        PTHQ = [[PT[:, 0:512], PT2[:, 0:512]],
                [_PA2B[:, 0:512], _PA2B[:, 1024:1536]]]
        B1Q = [PA, PB]
        RT = sb3("nrt", [128, 4], F32)
        RSCR = sb3("nrscr", [128, 4], F32)
        R_R = sb3("nr_r", [128, 4], F32)
        B_SUM = sb3("nb_sum", [128, 4], F32)
        B_SQ = sb3("nb_sq", [128, 4], F32)
        B_M = sb3("nb_m", [128, 4], F32)
        B_R = sb3("nb_r", [128, 4], F32)
        BT = sb3("nbt", [128, 4], F32)
        BSCR = sb3("nbscr", [128, 4], F32)
        wld(OWT, "d_owT", "nOWT")
        wld(FDWT, "d_fdwT", "nFDWT")
        wld(FUWT, "d_fuwT", "nFUWT")
        pl.op("y", lambda E: E.dma_start(out=NWREP, in_=W["L1_ffn_norm_w"]
              .unsqueeze(0).broadcast_to([128, 1024])), dma=True,
              writes=("nNWREP",))
        pl.op("p", lambda E: E.dma_start(out=OBH,
              in_=W["L1_ffn_out_b"].unsqueeze(0)), dma=True, writes=("nOBb2",))
        pl.op("p", lambda E: E.dma_start(out=FDBH,
              in_=W["L1_ffn_ff_down_b"].unsqueeze(0)), dma=True,
              writes=("nFDB",))
        pl.op("p", lambda E: E.dma_start(out=FUBH,
              in_=W["L1_ffn_ff_up_b"].unsqueeze(0)), dma=True, writes=("nFUB",))

        def stageA(tl):
            q, u = tl % 2, tl % 4
            B1, acb = B1Q[q], ACCB[:, q * 1024:(q + 1) * 1024]
            acf = ACCFM[:, q * 1024:(q + 1) * 1024]
            h1 = H1[:, q * 1024:(q + 1) * 1024]
            x1c = X1TC[:, q * 1024:(q + 1) * 1024]
            for j, (acc, ak) in enumerate(((accC, "accC"), (accD, "accD"))):
                pl.op("y" if j == 0 else "s",
                    lambda E, tl=tl, j=j, q=q, acc=acc: E.dma_start(
                    out=ACCR[q][j], in_=acc[tl * 128:(tl + 1) * 128, :]),
                    dma=True, reads=((ak, tl),), writes=(("cACCR", q, j),))
            pl.op("p", lambda E, q=q, acb=acb: E.tensor_tensor(
                out=acb, in0=ACCR[q][0], in1=ACCR[q][1], op=ALU.add),
                reads=(("cACCR", q, 0), ("cACCR", q, 1)),
                writes=(("cACCB", q),), out_free=1024)
            for half in range(2):
                for fc2 in range(4):
                    fc = half * 4 + fc2
                    pl.op("t", lambda E, fc=fc, fc2=fc2, half=half, q=q,
                          acb=acb: E.transpose(
                        PTHQ[q][half][:, fc2 * 128:(fc2 + 1) * 128],
                        acb[:, fc * 128:(fc + 1) * 128], C["IDN"]),
                        reads=(("cACCB", q), "IDN"), writes=(("cPT", q, half),),
                        out_free=128)
                pl.op("v", lambda E, half=half, q=q, acf=acf: E.tensor_copy(
                    out=acf[:, half * 512:(half + 1) * 512], in_=PTHQ[q][half]),
                    reads=(("cPT", q, half),), writes=(("cACCFM", q),),
                    out_free=512)
            for half in range(2):
                for kc in range(8):
                    pl.op("t", lambda E, kc=kc, half=half, q=q, B1=B1,
                          acf=acf: E.matmul(
                        B1[:, half * 512:(half + 1) * 512],
                        acf[:, kc * 128:(kc + 1) * 128],
                        OWT[:, kc * 1024 + half * 512:
                            kc * 1024 + (half + 1) * 512],
                        start=(kc == 0), stop=False),
                        reads=(("cACCFM", q), ("nOWT",)), writes=(("cP", q),),
                        out_free=512)
                pl.op("t", lambda E, half=half, q=q, B1=B1: E.matmul(
                    B1[:, half * 512:(half + 1) * 512], ONES1,
                    OBH[0:1, half * 512:(half + 1) * 512],
                    start=False, stop=True),
                    reads=(("nOBb2",), ("nONES1",)), writes=(("cP", q),),
                    out_free=512)
            pl.op("p", lambda E, tl=tl, x1c=x1c: E.dma_start(
                out=x1c, in_=x1t_d[tl * 128:(tl + 1) * 128, :]), dma=True,
                reads=(("x1t", tl),), writes=(("cX1T", q),))

        def stageA2(tl):
            q, u = tl % 2, tl % 4
            B1 = B1Q[q]
            h1 = H1[:, q * 1024:(q + 1) * 1024]
            x1c = X1TC[:, q * 1024:(q + 1) * 1024]
            pl.op("v", lambda E, q=q, B1=B1, h1=h1, x1c=x1c: E.tensor_tensor(
                out=h1, in0=B1, in1=x1c, op=ALU.add),
                reads=(("cP", q), ("cX1T", q)), writes=(("cH1", q),),
                out_free=1024)
            pl.op("s", lambda E, q=q, u=u, h1=h1: E.activation(
                out=JS2, in_=h1, func=AF.Square, accum_out=RT[:, u:u + 1]),
                reads=(("cH1", q),), writes=(("nJS2",), ("nRT", u)),
                out_free=1024)

        def stageB1(tl):
            q, u = tl % 2, tl % 4
            B1 = B1Q[q]
            h1 = H1[:, q * 1024:(q + 1) * 1024]
            hn = HN[:, q * 1024:(q + 1) * 1024]
            hnb = HNB[:, q * 1024:(q + 1) * 1024]
            hfm = HFM[:, q * 1024:(q + 1) * 1024]
            ff1 = FF1[:, q * 256:(q + 1) * 256]
            ff1f = FF1F[:, q * 256:(q + 1) * 256]
            ob = OB[:, q * 1024:(q + 1) * 1024]
            pl.op("v", lambda E, u=u: E.tensor_scalar(
                out=RT[:, u:u + 1], in0=RT[:, u:u + 1], scalar1=1.0 / 1024,
                scalar2=1e-6, op0=ALU.mult, op1=ALU.add),
                reads=(("nRT", u),), writes=(("nRT", u),), out_free=1)
            emit_rsqrt(pl, R_R[:, u:u + 1], RT[:, u:u + 1],
                       RSCR[:, u:u + 1], ("nR_R", u), ("nRT", u),
                       ("nRscr", u))
            pl.op("v", lambda E, q=q, u=u, h1=h1, hn=hn: E.tensor_scalar(
                out=hn, in0=h1, scalar1=R_R[:, u:u + 1], scalar2=None,
                op0=ALU.mult), reads=(("cH1", q),), sreads=(("nR_R", u),),
                writes=(("cHN", q),), out_free=512)
            pl.op("v", lambda E, q=q, hn=hn: E.tensor_tensor(out=hn, in0=hn,
                  in1=NWREP, op=ALU.mult), reads=(("cHN", q), ("nNWREP",)),
                  writes=(("cHN", q),), out_free=1024)
            pl.op("s", lambda E, q=q, hn=hn, hnb=hnb: E.activation(
                out=hnb, in_=hn, func=AF.Copy), reads=(("cHN", q),),
                writes=(("cHNB", q),), out_free=1024)

        def stageB2(tl):
            q, u = tl % 2, tl % 4
            B1 = B1Q[q]
            hn = HN[:, q * 1024:(q + 1) * 1024]
            hnb = HNB[:, q * 1024:(q + 1) * 1024]
            hfm = HFM[:, q * 1024:(q + 1) * 1024]
            ff1 = FF1[:, q * 256:(q + 1) * 256]
            ff1f = FF1F[:, q * 256:(q + 1) * 256]
            ob = OB[:, q * 1024:(q + 1) * 1024]
            for half in range(2):
                for fc2 in range(4):
                    fc = half * 4 + fc2
                    pl.op("t", lambda E, fc=fc, fc2=fc2, half=half, q=q,
                          hnb=hnb: E.transpose(
                        PTHQ[q][half][:, fc2 * 128:(fc2 + 1) * 128],
                        hnb[:, fc * 128:(fc + 1) * 128], C["IDN"]),
                        reads=(("cHNB", q), "IDN"), writes=(("cPT", q, half),),
                        out_free=128)
                pl.op("v", lambda E, half=half, q=q, hfm=hfm: E.tensor_copy(
                    out=hfm[:, half * 512:(half + 1) * 512], in_=PTHQ[q][half]),
                    reads=(("cPT", q, half),), writes=(("cHFM", q),),
                    out_free=512)
            for kc in range(8):
                pl.op("t", lambda E, kc=kc, q=q, B1=B1, hfm=hfm: E.matmul(
                    B1[:, 0:256], hfm[:, kc * 128:(kc + 1) * 128],
                    FDWT[:, kc * 256:(kc + 1) * 256], start=(kc == 0),
                    stop=False),
                    reads=(("cHFM", q), ("nFDWT",)), writes=(("cP", q),),
                    out_free=256)
            pl.op("t", lambda E, q=q, B1=B1: E.matmul(
                B1[:, 0:256], ONES1, FDBH, start=False, stop=True),
                reads=(("nFDB",), ("nONES1",)), writes=(("cP", q),),
                out_free=256)
            pl.op("s", lambda E, q=q, B1=B1, ff1=ff1: E.activation(
                out=ff1, in_=B1[:, 0:256], func=AF.Gelu),
                reads=(("cP", q),), writes=(("cFF1", q),), out_free=256)
            for fc in range(2):
                pl.op("t", lambda E, fc=fc, q=q, ff1=ff1: E.transpose(
                    PTHQ[q][0][:, fc * 128:(fc + 1) * 128],
                    ff1[:, fc * 128:(fc + 1) * 128], C["IDN"]),
                    reads=(("cFF1", q), "IDN"), writes=(("cPT", q, 0),),
                    out_free=128)
            pl.op("v", lambda E, q=q, ff1f=ff1f: E.tensor_copy(
                out=ff1f, in_=PTHQ[q][0][:, 0:256]),
                reads=(("cPT", q, 0),), writes=(("cFF1F", q),), out_free=256)
            for half in range(2):
                for kc in range(2):
                    pl.op("t", lambda E, kc=kc, half=half, q=q, B1=B1,
                          ff1f=ff1f: E.matmul(
                        B1[:, half * 512:(half + 1) * 512],
                        ff1f[:, kc * 128:(kc + 1) * 128],
                        FUWT[:, kc * 1024 + half * 512:
                             kc * 1024 + (half + 1) * 512],
                        start=(kc == 0), stop=False),
                        reads=(("cFF1F", q), ("nFUWT",)), writes=(("cP", q),),
                        out_free=512)
                pl.op("t", lambda E, half=half, q=q, B1=B1: E.matmul(
                    B1[:, half * 512:(half + 1) * 512], ONES1,
                    FUBH[0:1, half * 512:(half + 1) * 512],
                    start=False, stop=True),
                    reads=(("nFUB",), ("nONES1",)), writes=(("cP", q),),
                    out_free=512)
            pl.op("v", lambda E, q=q, B1=B1, ob=ob, hn=hn: E.tensor_tensor(
                out=ob, in0=B1, in1=hn, op=ALU.add),
                reads=(("cP", q), ("cHN", q)), writes=(("cOB", q),),
                out_free=1024)
            pl.op("s", lambda E, q=q, u=u, ob=ob: E.activation(out=JS, in_=ob,
                  func=AF.Copy, accum_out=B_SUM[:, u:u + 1]),
                  reads=(("cOB", q),), writes=(("nJS",), ("nB_SUM", u)),
                  out_free=1024)
            pl.op("s", lambda E, q=q, u=u, ob=ob: E.activation(
                out=JS2, in_=ob, func=AF.Square, accum_out=B_SQ[:, u:u + 1]),
                reads=(("cOB", q),), writes=(("nJS2",), ("nB_SQ", u)),
                out_free=1024)
            pl.op("s", lambda E, tl=tl, u=u: E.dma_start(
                out=XT[u], in_=x_in[tl * 128:(tl + 1) * 128, :]), dma=True,
                reads=((kx_in, tl),), writes=(("nXT", u),))
            pl.op("v", lambda E, u=u: E.tensor_scalar(
                out=B_M[:, u:u + 1], in0=B_SUM[:, u:u + 1],
                scalar1=1.0 / 1024, scalar2=None, op0=ALU.mult),
                reads=(("nB_SUM", u),), writes=(("nB_M", u),), out_free=1)
            pl.op("v", lambda E, u=u: E.tensor_scalar(
                out=BT[:, u:u + 1], in0=B_SQ[:, u:u + 1],
                scalar1=1.0 / 1024, scalar2=None, op0=ALU.mult),
                reads=(("nB_SQ", u),), writes=(("nBT", u),), out_free=1)
            pl.op("v", lambda E, u=u: E.tensor_tensor(
                out=B_R[:, u:u + 1], in0=B_M[:, u:u + 1],
                in1=B_M[:, u:u + 1], op=ALU.mult), reads=(("nB_M", u),),
                writes=(("nB_R", u),), out_free=1)
            pl.op("v", lambda E, u=u: E.tensor_tensor(
                out=BT[:, u:u + 1], in0=BT[:, u:u + 1],
                in1=B_R[:, u:u + 1], op=ALU.subtract),
                reads=(("nBT", u), ("nB_R", u)), writes=(("nBT", u),),
                out_free=1)
            pl.op("v", lambda E, u=u: E.tensor_scalar(
                out=BT[:, u:u + 1], in0=BT[:, u:u + 1], scalar1=1e-5,
                scalar2=None, op0=ALU.add), reads=(("nBT", u),),
                writes=(("nBT", u),), out_free=1)
            emit_rsqrt(pl, B_R[:, u:u + 1], BT[:, u:u + 1],
                       BSCR[:, u:u + 1], ("nB_R", u), ("nBT", u),
                       ("nBscr", u))
            pl.op("v", lambda E, u=u: E.scalar_tensor_tensor(
                out=NMR[:, u:u + 1], in0=B_M[:, u:u + 1], scalar=-1.0,
                in1=B_R[:, u:u + 1], op0=ALU.mult, op1=ALU.mult),
                reads=(("nB_M", u), ("nB_R", u)), writes=(("nNMR", u),),
                out_free=1)
            pl.op("s", lambda E, q=q, u=u, ob=ob: E.activation(
                out=ob, in_=ob, func=AF.Identity, scale=B_R[:, u:u + 1],
                bias=NMR[:, u:u + 1]),
                reads=(("cOB", q),), sreads=(("nB_R", u), ("nNMR", u)),
                writes=(("cOB", q),), out_free=1024)
            pl.op("p", lambda E, q=q, u=u, ob=ob: E.tensor_tensor(
                out=XT[u], in0=ob, in1=XT[u], op=ALU.add),
                reads=(("cOB", q), ("nXT", u)), writes=(("nXT", u),),
                out_free=1024)
            tok = pl.op("y", lambda E, tl=tl, u=u: E.dma_start(
                out=x_out[tl * 128:(tl + 1) * 128, :], in_=XT[u]), dma=True,
                reads=(("nXT", u),), writes=((kx_out, tl),))
            out_toks.append(tok)

        stageA(0)
        stageA2(0)
        for tl in range(1, N_ST * 4):
            stageA(tl)
            stageB1(tl - 1)
            stageA2(tl)
            stageB2(tl - 1)
        stageB1(N_ST * 4 - 1)
        stageB2(N_ST * 4 - 1)
    return out_toks



WEIGHT_SPECS = {
    "L0_attn_Wq": [1024, 1024], "L0_attn_Wk": [1024, 1024],
    "L0_attn_Wv": [1024, 1024], "L0_attn_Wo": [1024, 1024],
    "L0_attn_gamma": [1024], "L0_attn_beta": [1024],
    "L0_attn_Wg1": [1024, 64], "L0_attn_Wg2": [64, 1024],
    "L0_ffn_gate_w": [8, 1024], "L0_ffn_lb_bias": [8],
    "L0_ffn_expert_gate_w": [8, 512, 1024], "L0_ffn_expert_up_w": [8, 512, 1024],
    "L0_ffn_expert_down_w": [8, 1024, 512],
    "L0_ffn_shared_gate_w": [1024, 1024], "L0_ffn_shared_up_w": [1024, 1024],
    "L0_ffn_shared_down_w": [1024, 1024],
    "L1_attn_Wq": [1024, 512], "L1_attn_Wkv": [1024, 1024], "L1_attn_Wo": [512, 1024],
    "L1_ffn_dw_kernel": [1024, 1, 3], "L1_ffn_dw_bias": [1024],
    "L1_ffn_pw_w": [1024, 1024], "L1_ffn_pw_b": [1024],
    "L1_ffn_router_w": [8, 1024], "L1_ffn_router_b": [8],
    "L1_ffn_expert_w": [8, 1024, 1024], "L1_ffn_expert_b": [8, 1024],
    "L1_ffn_out_w": [1024, 1024], "L1_ffn_out_b": [1024],
    "L1_ffn_norm_w": [1024],
    "L1_ffn_ff_down_w": [256, 1024], "L1_ffn_ff_down_b": [256],
    "L1_ffn_ff_up_w": [1024, 256], "L1_ffn_ff_up_b": [1024],
}
CONST_SPECS = {
    "c_idn": ([128, 128], BF16), "c_maskT": ([128, 16 * 128], BF16),
    "c_lamq": ([128, 1024], BF16), "c_lamk": ([128, 1024], BF16),
    "c_lamc": ([128, 512], F32), "c_sel8": ([8, 1024], BF16),
}
def _dwdg(inp):
    kern = np.asarray(inp["L1_ffn_dw_kernel"], np.float32)
    out = np.zeros((128, 24 * 128), np.float32)
    for fc in range(8):
        for t in range(3):
            blk = out[:, (fc * 3 + t) * 128:(fc * 3 + t + 1) * 128]
            np.fill_diagonal(blk, kern[fc * 128:(fc + 1) * 128, 0, t])
    return out


DERIVED_SPECS = {
    "d_dwdg": ([128, 3072], F32, _dwdg),
    "d_gcol": ([128, 8], F32, lambda inp: np.ascontiguousarray(
        np.asarray(inp["L0_attn_gamma"], np.float32).reshape(8, 128).T)),
    "d_gwT": ([128, 64], F32, lambda inp: np.ascontiguousarray(
        np.asarray(inp["L0_ffn_gate_w"], np.float32).T.reshape(8, 128, 8)
        .transpose(1, 0, 2).reshape(128, 64))),
    "d_pwT": ([1024, 1024], F32, lambda inp: np.ascontiguousarray(
        np.asarray(inp["L1_ffn_pw_w"], np.float32).T)),
    "d_rwT": ([128, 64], F32, lambda inp: np.ascontiguousarray(
        np.asarray(inp["L1_ffn_router_w"], np.float32).T.reshape(8, 128, 8)
        .transpose(1, 0, 2).reshape(128, 64))),
    "d_owT": ([1024, 1024], F32, lambda inp: np.ascontiguousarray(
        np.asarray(inp["L1_ffn_out_w"], np.float32).T)),
    "d_fdwT": ([1024, 256], F32, lambda inp: np.ascontiguousarray(
        np.asarray(inp["L1_ffn_ff_down_w"], np.float32).T)),
    "d_fuwT": ([256, 1024], F32, lambda inp: np.ascontiguousarray(
        np.asarray(inp["L1_ffn_ff_up_w"], np.float32).T)),
    "d_ewT": ([8, 1024, 1024], F32, lambda inp: np.ascontiguousarray(
        np.asarray(inp["L1_ffn_expert_w"], np.float32).transpose(0, 2, 1))),
    "d_pwbc": ([128, 8], F32, lambda inp: np.ascontiguousarray(
        np.asarray(inp["L1_ffn_pw_b"], np.float32).reshape(8, 128).T)),
    "d_dwk": ([128, 24], F32, lambda inp: np.ascontiguousarray(
        np.asarray(inp["L1_ffn_dw_kernel"], np.float32)[:, 0, :]
        .reshape(8, 128, 3).transpose(1, 0, 2).reshape(128, 24))),
    "d_dwbc": ([128, 8], F32, lambda inp: np.ascontiguousarray(
        np.asarray(inp["L1_ffn_dw_bias"], np.float32).reshape(8, 128).T)),
}
DERIVED_BLOCKS = {"d_gcol": "l0a", "d_gwT": "l0f", "d_pwT": "l1f", "d_dwdg": "l1f",
                  "d_rwT": "l1f", "d_owT": "l1f", "d_fdwT": "l1f",
                  "d_fuwT": "l1f", "d_ewT": "l1f", "d_pwbc": "l1f",
                  "d_dwk": "l1f", "d_dwbc": "l1f"}
BLOCK_WEIGHTS = {
    "l0a": ["L0_attn_Wq", "L0_attn_Wk", "L0_attn_Wv", "L0_attn_Wo", "L0_attn_gamma",
            "L0_attn_beta", "L0_attn_Wg1", "L0_attn_Wg2"],
    "l0f": ["L0_ffn_gate_w", "L0_ffn_lb_bias", "L0_ffn_expert_gate_w",
            "L0_ffn_expert_up_w", "L0_ffn_expert_down_w", "L0_ffn_shared_gate_w",
            "L0_ffn_shared_up_w", "L0_ffn_shared_down_w"],
    "l1a": ["L1_attn_Wq", "L1_attn_Wkv", "L1_attn_Wo"],
    "l1f": ["L1_ffn_dw_kernel", "L1_ffn_dw_bias", "L1_ffn_pw_w", "L1_ffn_pw_b",
            "L1_ffn_router_w", "L1_ffn_router_b", "L1_ffn_expert_w", "L1_ffn_expert_b",
            "L1_ffn_out_w", "L1_ffn_out_b", "L1_ffn_norm_w", "L1_ffn_ff_down_w",
            "L1_ffn_ff_down_b", "L1_ffn_ff_up_w", "L1_ffn_ff_up_b"],
}
BLOCK_EMIT = {"l0a": emit_l0_attn, "l0f": emit_l0_ffn, "l1a": emit_l1_attn,
              "l1f": emit_l1_ffn}


def build_program(blocks):
    nc = bass.Bass("TRN2", target_bir_lowering=False)
    W = {}
    for b in blocks:
        for name in BLOCK_WEIGHTS[b]:
            W[name] = nc.dram_tensor(name, WEIGHT_SPECS[name], F32,
                                     kind="ExternalInput").ap()
    for name, (shape, dt) in CONST_SPECS.items():
        W[name] = nc.dram_tensor(name, shape, dt, kind="ExternalInput").ap()
    for name, (shape, dt, _fn) in DERIVED_SPECS.items():
        if DERIVED_BLOCKS[name] in blocks:
            W[name] = nc.dram_tensor(name, shape, dt, kind="ExternalInput").ap()
    x_in = nc.dram_tensor("x", [N_ST * 512, D], F32, kind="ExternalInput").ap()
    x_out = nc.dram_tensor("xo", [N_ST * 512, D], F32, kind="ExternalOutput").ap()
    mids = [nc.dram_tensor(f"xmid{i}", [N_ST * 512, D], F32, kind="Internal").ap()
            for i in range(len(blocks) - 1)]
    chain = [x_in] + mids + [x_out]
    knames = ["x"] + [f"xmid{i}" for i in range(len(mids))] + ["xo"]

    pl = Plan(nc)
    ctx = contextlib.ExitStack()
    C = {}
    idn_sb = ctx.enter_context(nc.sbuf_tensor("c_idn_sb", [128, 128], BF16)).ap()
    C["IDN"] = idn_sb
    pl.op("y", lambda E: E.dma_start(out=idn_sb, in_=W["c_idn"]), dma=True,
          writes=("IDN",))

    out_toks = []
    for i, b in enumerate(blocks):
        if i > 0:
            pl.block_barrier()
        with contextlib.ExitStack() as bctx:
            toks = BLOCK_EMIT[b](pl, nc, bctx, chain[i], chain[i + 1],
                                 knames[i], knames[i + 1], W, C)
        if i == len(blocks) - 1:
            out_toks = toks
    pl.finalize(out_toks)
    pl.emit(ctx)
    ctx.close()
    return nc


def _shard_inputs(inputs):
    x = np.ascontiguousarray(np.asarray(inputs["L0_attn_x"], np.float32))
    xs = x.reshape(NCORES, TOK, D)
    consts = host_consts()
    maps = []
    for i in range(NCORES):
        m = {"x": np.ascontiguousarray(xs[i])}
        for name in WEIGHT_SPECS:
            if name in inputs:
                m[name] = np.ascontiguousarray(np.asarray(inputs[name], np.float32))
        m.update(consts)
        for name, (shape, dt, fn) in DERIVED_SPECS.items():
            m[name] = fn(inputs)
        maps.append(m)
    return maps


def run_blocks(inputs, blocks):
    maps = _shard_inputs(inputs)
    nc = build_program(blocks)
    need = set()
    for b in blocks:
        need.update(BLOCK_WEIGHTS[b])
    dneed = {n for n, b in DERIVED_BLOCKS.items() if b in blocks}
    in_maps = [{k: v for k, v in m.items()
                if k == "x" or k in need or k in CONST_SPECS or k in dneed}
               for m in maps]
    res = run_bass_kernel_spmd(nc, in_maps, core_ids=list(range(NCORES)))
    out = np.stack([np.asarray(r["xo"]) for r in res.results], axis=0)
    return out.reshape(16, 2048, 1024).astype(np.float32)


def kernel(**inputs):
    return run_blocks(inputs, ["l0a", "l0f", "l1a", "l1f"])
